# version 02
# baseline (speedup 1.0000x reference)
import numpy as np
import ml_dtypes
from contextlib import ExitStack

import concourse.bass as bass
import concourse.mybir as mybir
from concourse.bass_utils import run_bass_kernel_spmd
from concourse.aot_env import allow_ici_all_to_all

F32 = mybir.dt.float32
BF16 = mybir.dt.bfloat16
AF = mybir.ActivationFunctionType
ALU = mybir.AluOpType
BF = ml_dtypes.bfloat16

NCORE = 8
B, S, D = 2, 8192, 1024
DC = 8
CH = 512
NCH = 16
TLOC = 4 * CH
NRUN = 4
LN_EPS = 1e-5

BLOCKS = ["l0a", "l0f", "l1a", "l1f", "l2a", "l2f", "l3a", "l3f"]


def runs_of_core(c):
    return [(0, c), (0, 15 - c), (1, c), (1, 15 - c)]


def owner_of_chunk(j):
    return min(j, 15 - j)


def run_index(c, b, j):
    return runs_of_core(c).index((b, j))


class Prog:
    ENGINES = ("te", "act", "dve", "pool", "sp")

    def __init__(self, nc, stack):
        self.nc = nc
        self.stack = stack
        self.steps = {e: [] for e in self.ENGINES}
        self.semh = {}
        self.cnt = {}
        self.sb = {}
        self.dram = {}
        self.inputs = []
        self.ps = None
        self.pt = None

    def sem(self, name):
        if name not in self.semh:
            self.semh[name] = self.stack.enter_context(self.nc.semaphore(name))
            self.cnt[name] = 0
        return self.semh[name]

    def sbuf(self, name, shape, dtype):
        if name not in self.sb:
            self.sb[name] = self.stack.enter_context(self.nc.sbuf_tensor(name, shape, dtype))
        return self.sb[name]

    UNI_BYTES = 131008

    def alloc_uni(self):
        self.uni = self.stack.enter_context(
            self.nc.sbuf_tensor("UNI", [128, self.UNI_BYTES // 2], BF16))
        self.carve_off = 0
        self.carve_hw = 0
        self.carved = set()

    def carve_reset(self):
        for n in self.carved:
            self.sb.pop(n, None)
        self.carved = set()
        self.carve_off = 0

    def carve_mark(self):
        return self.carve_off

    def carve_rewind(self, mark, names):
        for n in names:
            if n in self.carved:
                self.carved.discard(n)
                self.sb.pop(n, None)
        self.carve_off = mark

    def carve(self, name, shape, dtype):
        if name in self.sb:
            return self.sb[name]
        import math as _m
        nb = int(np.prod(shape[1:])) * mybir.dt.size(dtype)
        nb_al = (nb + 63) & ~63
        off = self.carve_off
        assert off + nb_al <= self.UNI_BYTES, \
            f"UNI overflow carving {name}: off={off} need={nb_al} cap={self.UNI_BYTES}"
        view = self.uni[0:shape[0], off // 2:(off + nb) // 2]
        if dtype != BF16:
            view = view.bitcast(dtype)
        if len(shape) > 2:
            dims = " ".join(f"d{i}" for i in range(1, len(shape)))
            kw = {f"d{i}": shape[i] for i in range(1, len(shape))}
            view = view.rearrange(f"p ({dims}) -> p {dims}", **kw)
        self.sb[name] = view
        self.carved.add(name)
        self.carve_off = off + nb_al
        self.carve_hw = max(self.carve_hw, self.carve_off)
        return view

    def block_fence(self):
        sems = [s for s in ("te", "act", "dve", "xld", "wld", "cc") if s in self.cnt]
        for e in self.ENGINES:
            self.waitpt(e, *sems)

    def din(self, name, shape, dtype=F32):
        if name not in self.dram:
            self.dram[name] = self.nc.dram_tensor(name, shape, dtype, kind="ExternalInput")
            self.inputs.append(name)
        return self.dram[name].ap()

    def dout(self, name, shape, dtype=F32):
        if name not in self.dram:
            self.dram[name] = self.nc.dram_tensor(name, shape, dtype, kind="ExternalOutput")
        return self.dram[name].ap()

    def dint(self, name, shape, dtype=F32):
        if name not in self.dram:
            self.dram[name] = self.nc.dram_tensor(name, shape, dtype, kind="Internal")
        return self.dram[name].ap()

    def emit(self, eng, fn, wait=(), inc=None, inc_by=1):
        for s, _ in wait:
            self.sem(s)
        if inc:
            self.sem(inc)
        if eng in ("act", "dve"):
            wait = [(eng, self.cnt.get(eng, 0))] + list(wait)
        wait = tuple((s, v) for s, v in wait if v > 0)

        snap = dict(self.sb)

        def run(e, _fn=fn, _wait=wait, _inc=inc, _by=inc_by, _snap=snap):
            saved = self.sb
            self.sb = _snap
            try:
                for s, v in _wait:
                    e.wait_ge(self.semh[s], v)
                last = _fn(e)
                if _inc is not None:
                    assert last is not None, f"step on {eng} with inc={_inc} returned no instruction"
                    last.then_inc(self.semh[_inc], _by)
            finally:
                self.sb = saved
        self.steps[eng].append(run)
        if inc:
            self.cnt[inc] += inc_by
            return self.cnt[inc]
        return None

    def waitpt(self, eng, *sems):
        w = [(s, self.cnt.get(s, 0)) for s in sems]
        self.emit(eng, lambda e: None, wait=w)

    def build(self):
        eng_map = dict(te="tensor", act="scalar", dve="vector", pool="gpsimd", sp="sync")
        with self.nc.Block() as block:
            for eng in self.ENGINES:
                steps = self.steps[eng]
                getattr(block, eng_map[eng])(lambda e, _s=steps: [f(e) for f in _s])


def dma_pool(p, out_ap, in_ap, wait=(), sem="wld"):
    w = [(sem, p.cnt.get(sem, 0))] + list(wait)
    return p.emit("pool", lambda e: e.dma_start(out=out_ap, in_=in_ap),
                  wait=w, inc=sem, inc_by=16)


def dma_sp(p, out_ap, in_ap, wait=(), sem="xld"):
    w = [(sem, p.cnt.get(sem, 0))] + list(wait)
    return p.emit("sp", lambda e: e.dma_start(out=out_ap, in_=in_ap),
                  wait=w, inc=sem, inc_by=16)


def bank(p, i, rows=128, width=512):
    return p.ps[0:rows, i * 512:i * 512 + width]


HC_INV_D = 0
HC_EPS = 1
HC_HALF = 2
HC_Q2 = 3
HC_Q2L0 = 4
HC_ZERO = 5
HC_NEPS = 6
HC_ONE = 7


def alloc_common(p):
    p.sbuf("xres", [128, DC, TLOC], F32)
    p.sbuf("ones_bf", [128, 2], BF16)
    p.sbuf("ones_f", [128, 2], F32)
    p.sbuf("onesr_f", [1, 128], F32)
    p.sbuf("onesr_b", [1, 128], BF16)
    p.sbuf("hc", [1, 16], F32)
    p.sbuf("idf", [128, 128], F32)
    p.sbuf("idb", [128, 128], BF16)
    p.alloc_uni()
    p.ps = p.stack.enter_context(p.nc.psum_tensor("PS", [128, 7 * 512], F32))
    p.pt = p.stack.enter_context(p.nc.psum_tensor("PT", [128, 1024], BF16))

    idf = p.din("c_ident", [128, 128])
    hc = p.din("c_hc", [1, 16])
    dma_sp(p, p.sb["idf"][:, :], idf[:, :])
    dma_pool(p, p.sb["idb"][:, :], idf[:, :])
    ld = dma_sp(p, p.sb["hc"][:, :], hc[:, :])
    lw = p.cnt["wld"]
    p.emit("dve", lambda e: e.memset(p.sb["ones_bf"][:, :], 1.0),
           wait=[("wld", lw), ("xld", ld)], inc="dve")
    p.emit("dve", lambda e: e.memset(p.sb["ones_f"][:, :], 1.0), inc="dve")
    p.emit("dve", lambda e: e.memset(p.sb["onesr_f"][:, :], 1.0), inc="dve")
    p.emit("dve", lambda e: e.memset(p.sb["onesr_b"][:, :], 1.0), inc="dve")


def emit_ln(p, keep_xln=True, keep_bc=False, src="xres"):
    p.carve("stat_m", [1, TLOC], F32)
    p.carve("stat_r", [1, TLOC], F32)
    p.carve("lnst", [1, TLOC], F32)
    p.carve("lnsq", [128, 1024], BF16)
    p.carve("lnsq2", [128, 1024], BF16)
    xres, lnsq, lnsq2, hc = p.sb["xres"], p.sb["lnsq"], p.sb["lnsq2"], p.sb["hc"]
    if keep_bc:
        p.carve("mbc", [128, TLOC], F32)
        p.carve("rbc", [128, TLOC], F32)

    for half in range(2):
        c0 = half * 1024
        for dc in range(DC):
            if src == "xres":
                dv1 = p.emit("dve", lambda e, dc=dc, c0=c0: e.tensor_copy(
                    lnsq[:, :], xres[:, dc, c0:c0 + 1024]),
                    wait=[("te", p.cnt["te"])], inc="dve")
            else:
                dv1 = p.emit("dve", lambda e, dc=dc, c0=c0: e.tensor_copy(
                    lnsq[:, :], p.sb[src][:, dc, c0:c0 + 1024]),
                    wait=[("te", p.cnt["te"])], inc="dve")
            ac2 = p.emit("act", lambda e: e.activation(lnsq2[:, :], lnsq[:, :], AF.Square),
                         wait=[("dve", dv1)], inc="act")

            def mm(e, dc=dc):
                for tg2 in range(2):
                    sl = slice(tg2 * 512, tg2 * 512 + 512)
                    e.matmul(bank(p, tg2, rows=1), p.sb["ones_bf"][:, 0:1],
                             lnsq[:, sl], start=(dc == 0), stop=(dc == DC - 1))
                    last = e.matmul(bank(p, 2 + tg2, rows=1), p.sb["ones_bf"][:, 0:1],
                                    lnsq2[:, sl], start=(dc == 0), stop=(dc == DC - 1))
                return last
            p.emit("te", mm, wait=[("act", ac2), ("dve", dv1)], inc="te")
        te_done = p.cnt["te"]

        def evac(e, half=half):
            e.tensor_copy(p.sb["stat_m"][0:1, half * 1024:half * 1024 + 1024],
                          p.ps[0:1, 0:1024])
            return e.tensor_copy(p.sb["stat_r"][0:1, half * 1024:half * 1024 + 1024],
                                 p.ps[0:1, 1024:2048])
        p.emit("dve", evac, wait=[("te", te_done)], inc="dve")

    p.emit("dve", lambda e: e.tensor_scalar(p.sb["stat_m"][0:1, :], p.sb["stat_m"][0:1, :],
                                            hc[0:1, HC_INV_D:HC_INV_D + 1], None, ALU.mult), inc="dve")
    p.emit("dve", lambda e: e.tensor_scalar(p.sb["stat_r"][0:1, :], p.sb["stat_r"][0:1, :],
                                            hc[0:1, HC_INV_D:HC_INV_D + 1], None, ALU.mult), inc="dve")
    p.emit("dve", lambda e: e.tensor_mul(p.sb["lnst"][0:1, :], p.sb["stat_m"][0:1, :],
                                         p.sb["stat_m"][0:1, :]), inc="dve")
    dv = p.emit("dve", lambda e: e.tensor_sub(p.sb["stat_r"][0:1, :], p.sb["stat_r"][0:1, :],
                                              p.sb["lnst"][0:1, :]), inc="dve")
    ac = p.emit("act", lambda e: e.activation(p.sb["lnst"][0:1, :], p.sb["stat_r"][0:1, :],
                                              AF.Sqrt, bias=hc[0:1, HC_EPS:HC_EPS + 1]),
                wait=[("dve", dv)], inc="act")
    dv = p.emit("dve", lambda e: e.reciprocal(p.sb["stat_r"][0:1, :], p.sb["lnst"][0:1, :]),
                wait=[("act", ac)], inc="dve")

    if keep_xln or keep_bc:
        for tg in range(NRUN):
            sl = slice(tg * 512, tg * 512 + 512)

            def bc(e, sl=sl):
                e.matmul(bank(p, 4), p.sb["onesr_f"][0:1, :], p.sb["stat_m"][0:1, sl],
                         start=True, stop=True)
                return e.matmul(bank(p, 5), p.sb["onesr_f"][0:1, :], p.sb["stat_r"][0:1, sl],
                                start=True, stop=True)
            te = p.emit("te", bc, wait=[("dve", dv), ("dve", p.cnt["dve"]),
                                        ("act", p.cnt["act"])], inc="te")
            ac_sv = 0
            if keep_bc:
                def sv(e, sl=sl):
                    e.activation(p.sb["mbc"][:, sl], bank(p, 4), AF.Copy)
                    return e.activation(p.sb["rbc"][:, sl], bank(p, 5), AF.Copy)
                ac_sv = p.emit("act", sv, wait=[("te", te)], inc="act")
            if keep_xln:
                p.carve("xln", [128, DC, TLOC], BF16)
                xln = p.sb["xln"]

                def ap1(e, sl=sl):
                    mb = bank(p, 4).unsqueeze(1).broadcast_to([128, DC, 512])
                    return e.tensor_tensor(xln[:, :, sl], xres[:, :, sl], mb, ALU.subtract)
                p.emit("dve", ap1, wait=[("te", te), ("act", ac_sv)], inc="dve")

                def ap2(e, sl=sl):
                    rb = bank(p, 5).unsqueeze(1).broadcast_to([128, DC, 512])
                    return e.tensor_tensor(xln[:, :, sl], xln[:, :, sl], rb, ALU.mult)
                p.emit("dve", ap2, inc="dve")


def emit_rwkv(p, lw):
    tag = lw["tag"]
    WrT = p.din(lw["WrT"], [D, D])
    WkT = p.din(lw["WkT"], [D, D])
    WvT = p.din(lw["WvT"], [D, D])
    mur = p.din(lw["mur"], [128, DC])
    muk = p.din(lw["muk"], [128, DC])
    bndsel = p.din("c_bndsel", [33, 4])

    xres = p.sb["xres"]
    p.carve("mu2", [128, 2, DC], F32)
    p.carve("bsel33", [33, 4], F32)
    p.carve("xlf", [128, DC, CH], F32)
    p.carve("scr_f1", [128, DC, CH], F32)
    p.carve("sbbnd", [128, NRUN, DC], F32)
    p.carve("x0col", [128, DC, NRUN], F32)
    p.carve("bndt", [33, 1024], F32)
    p.carve("bndev", [4, 1024], F32)
    p.carve("xprev0", [128, DC, NRUN], F32)
    p.carve("xrmix", [128, DC, CH], BF16)
    p.carve("xkmix", [128, DC, CH], BF16)
    p.carve("wtile", [128, 2, DC, 128], BF16)
    p.carve("rgate", [128, DC, CH], F32)
    p.carve("kact", [128, DC, CH], BF16)
    p.carve("vtmp", [128, CH], F32)
    p.carve("vtmp2", [128, CH], F32)
    p.carve("d0scr", [128, DC], F32)

    bnd_in = p.dint(f"{tag}_bnd_in", [NRUN, 1024])
    bnd_all = p.dint(f"{tag}_bnd_all", [4 * NCORE, 1024])

    dma_sp(p, p.sb["mu2"][:, 0, :], mur[:, :])
    dma_sp(p, p.sb["mu2"][:, 1, :], muk[:, :])
    dma_sp(p, p.sb["bsel33"][:, :], bndsel[:, :])
    p.waitpt("dve", "xld")

    emit_ln(p, keep_xln=False, keep_bc=False)

    def bcast_stats(r):
        sl = slice(r * 512, r * 512 + 512)

        def bc(e, sl=sl):
            e.matmul(bank(p, 4), p.sb["onesr_f"][0:1, :], p.sb["stat_m"][0:1, sl],
                     start=True, stop=True)
            return e.matmul(bank(p, 5), p.sb["onesr_f"][0:1, :], p.sb["stat_r"][0:1, sl],
                            start=True, stop=True)
        return p.emit("te", bc, wait=[("dve", p.cnt["dve"]), ("act", p.cnt["act"])], inc="te")

    def mk_xlf(r, te):
        sl = slice(r * 512, r * 512 + 512)

        def f1(e, sl=sl):
            mb = bank(p, 4).unsqueeze(1).broadcast_to([128, DC, 512])
            return e.tensor_tensor(p.sb["xlf"][:, :, :], xres[:, :, sl], mb, ALU.subtract)

        def f2(e, sl=sl):
            rb = bank(p, 5).unsqueeze(1).broadcast_to([128, DC, 512])
            return e.tensor_tensor(p.sb["xlf"][:, :, :], p.sb["xlf"][:, :, :], rb, ALU.mult)
        p.emit("dve", f1, wait=[("te", te)], inc="dve")
        return p.emit("dve", f2, inc="dve")

    for r in range(NRUN):
        te = bcast_stats(r)
        mk_xlf(r, te)
        p.emit("dve", lambda e, r=r: e.tensor_copy(p.sb["sbbnd"][:, r, :], p.sb["xlf"][:, :, 511]),
               inc="dve")
        p.emit("dve", lambda e, r=r: e.tensor_copy(p.sb["x0col"][:, :, r], p.sb["xlf"][:, :, 0]),
               inc="dve")
    dv_pub = p.cnt["dve"]

    st = p.cnt["xld"]
    for r in range(NRUN):
        st = dma_sp(p, bnd_in[r].rearrange("(pp dcc) -> pp dcc", pp=128),
                    p.sb["sbbnd"][:, r, :],
                    wait=[("dve", dv_pub)] if r == 0 else ())

    def coll(e):
        return e.collective_compute(
            "AllGather", ALU.bypass, replica_groups=[list(range(NCORE))],
            ins=[bnd_in.opt()], outs=[bnd_all.opt()])
    cc = p.emit("pool", coll, wait=[("xld", st)], inc="cc")
    ld = dma_sp(p, p.sb["bndt"][0:32, :], bnd_all[:, :], wait=[("cc", cc)])
    dvz = p.emit("dve", lambda e: e.memset(p.sb["bndt"][32:33, :], 0.0),
                 wait=[("xld", ld)], inc="dve")

    def gmm(e):
        e.matmul(bank(p, 4, rows=4), p.sb["bsel33"][:, :], p.sb["bndt"][:, 0:512],
                 start=True, stop=True)
        return e.matmul(bank(p, 5, rows=4), p.sb["bsel33"][:, :], p.sb["bndt"][:, 512:1024],
                        start=True, stop=True)
    te = p.emit("te", gmm, wait=[("dve", dvz)], inc="te")

    def gev(e):
        e.tensor_copy(p.sb["bndev"][0:4, 0:512], bank(p, 4, rows=4))
        return e.tensor_copy(p.sb["bndev"][0:4, 512:1024], bank(p, 5, rows=4))
    dv = p.emit("dve", gev, wait=[("te", te)], inc="dve")

    ac_xp = p.cnt["act"]
    for dc in range(DC):
        def tr(e, dc=dc):
            src = p.sb["bndev"][0:4, :].rearrange("r (pp dcc) -> r pp dcc", pp=128)[:, :, dc]
            return e.transpose(bank(p, 6, rows=128, width=4), src, p.sb["idf"][0:4, 0:4])
        te = p.emit("te", tr, wait=[("dve", dv), ("act", ac_xp)], inc="te")
        ac_xp = p.emit("act", lambda e, dc=dc: e.activation(
            p.sb["xprev0"][:, dc, :], bank(p, 6, rows=128, width=4), AF.Copy),
            wait=[("te", te)], inc="act")

    wtile = p.sb["wtile"]
    slot_user = {0: 0, 1: 0}
    for tg in range(NRUN):
        sl = slice(tg * 512, tg * 512 + 512)
        te = bcast_stats(tg)
        p.waitpt("dve", "act")
        mk_xlf(tg, te)
        def mixsub(e, tg=tg):
            xv = p.sb["xlf"]
            e.tensor_tensor(p.sb["scr_f1"][:, :, 1:512], xv[:, :, 1:512], xv[:, :, 0:511],
                            ALU.subtract)
            return e.tensor_sub(p.sb["d0scr"][:, :], p.sb["x0col"][:, :, tg], p.sb["xprev0"][:, :, tg])
        p.emit("dve", mixsub, inc="dve")
        for mi, out in ((0, "xrmix"), (1, "xkmix")):
            def mix(e, tg=tg, mi=mi, out=out):
                xv = p.sb["xlf"]
                ov = p.sb[out]
                for dc in range(DC):
                    e.scalar_tensor_tensor(
                        ov[:, dc, 1:512], p.sb["scr_f1"][:, dc, 1:512],
                        p.sb["mu2"][:, mi, dc:dc + 1], xv[:, dc, 0:511], ALU.mult, ALU.add)
                    last = e.scalar_tensor_tensor(
                        ov[:, dc, 0:1], p.sb["d0scr"][:, dc:dc + 1],
                        p.sb["mu2"][:, mi, dc:dc + 1], p.sb["xprev0"][:, dc, tg:tg + 1],
                        ALU.mult, ALU.add)
                return last
            p.emit("dve", mix, inc="dve")
        dv_mix = p.cnt["dve"]

        for pi, (Wap, src) in enumerate(((WrT, "xrmix"), (WkT, "xkmix"))):
            for oc in range(DC):
                slot = (pi * 8 + oc) % 2
                wl = dma_pool(p, wtile[:, slot],
                              Wap.rearrange("(dcc pp) o -> pp dcc o", pp=128)[:, :, oc * 128:(oc + 1) * 128],
                              wait=[("te", slot_user[slot])])

                def mm(e, pi=pi, slot=slot, src=src):
                    for dcc in range(DC):
                        last = e.matmul(bank(p, pi), wtile[:, slot, dcc, :],
                                        p.sb[src][:, dcc, :],
                                        start=(dcc == 0), stop=(dcc == DC - 1))
                    return last
                te = p.emit("te", mm, wait=[("wld", wl), ("dve", dv_mix),
                                            ("act", p.cnt["act"])], inc="te")
                slot_user[slot] = te
                if pi == 0:
                    p.emit("act", lambda e, oc=oc: e.activation(
                        p.sb["rgate"][:, oc, :], bank(p, 0), AF.Sigmoid),
                        wait=[("te", te)], inc="act")
                else:
                    dvr = p.emit("dve", lambda e: e.tensor_scalar_max(
                        p.sb["vtmp"][:, :], bank(p, 1), 0.0), wait=[("te", te)], inc="dve")
                    p.emit("act", lambda e, oc=oc: e.activation(
                        p.sb["kact"][:, oc, :], p.sb["vtmp"][:, :], AF.Square),
                        wait=[("dve", dvr)], inc="act")
        ac_k = p.cnt["act"]
        for oc in range(DC):
            slot = oc % 2
            wl = dma_pool(p, wtile[:, slot],
                          WvT.rearrange("(dcc pp) o -> pp dcc o", pp=128)[:, :, oc * 128:(oc + 1) * 128],
                          wait=[("te", slot_user[slot])])

            def mmv(e, slot=slot):
                for dcc in range(DC):
                    last = e.matmul(bank(p, 2), wtile[:, slot, dcc, :], p.sb["kact"][:, dcc, :],
                                    start=(dcc == 0), stop=(dcc == DC - 1))
                return last
            te = p.emit("te", mmv, wait=[("wld", wl), ("act", ac_k),
                                         ("dve", p.cnt["dve"])], inc="te")
            slot_user[slot] = te

            p.emit("dve", lambda e, oc=oc: e.tensor_tensor(
                p.sb["vtmp2"][:, :], p.sb["rgate"][:, oc, :], bank(p, 2), ALU.mult),
                wait=[("te", te)], inc="dve")
            p.emit("dve", lambda e, oc=oc, sl=sl: e.tensor_add(
                xres[:, oc, sl], xres[:, oc, sl], p.sb["vtmp2"][:, :]), inc="dve")


def bcol(j):
    o = owner_of_chunk(j)
    return o * 1024 + (0 if j == o else 512)


def stagecol(b, j):
    o = owner_of_chunk(j)
    ro = 2 * b + (0 if j == o else 1)
    return o, ro


def va_slot(j, sub):
    return (bcol(j) // 128) + sub


def emit_attn(p, lw, debug_stop=None):
    tag = lw["tag"]
    kind = lw["kind"]
    xres = p.sb["xres"]

    NAUG = 65
    WoT = p.din(lw["WoT"], [D, D]) if kind != "l0" else None
    WqT = p.din(lw["WqT"], [D, 128])
    if kind == "l0":
        WkT = p.din(lw["WkT"], [D, 128])
        WvT = p.din(lw["WvT"], [D, 128])
        slopes_in = p.din("c_slopes", [128, 16])
        relh_in = p.din("c_relh", [128, 8, 512])
        rell_in = p.din("c_rell", [128, 8, 512])
        winm_in = p.din("c_winmask", [4, 128, 512])
    elif kind == "l2":
        WkT = p.din(lw["WkT"], [D, 128])
        WvT = p.din(lw["WvT"], [D, 128])
        bo = p.din(lw["bo"], [128, DC])
    else:
        Wkvd = p.din(lw["Wkvd"], [D, 256])
        WkuT = p.din(lw["WkuT"], [256, 128])
        WvuT = p.din(lw["WvuT"], [256, 128])
    diagm_in = p.din("c_diagmask", [4, 128, 512])

    p.carve("xln", [128, DC, TLOC], BF16)
    p.carve("xs", [128, DC, 512], BF16)
    if kind == "l0":
        p.carve("winm", [128, 4, 512], BF16)
        p.carve("slp", [128, 16], F32)
        p.carve("relh", [128, 8, 512], BF16)
        p.carve("rell", [128, 8, 512], BF16)
        p.carve("sidb", [128, 128], BF16)
    p.carve("wq_sb", [128, DC, 128], BF16)
    p.carve("wk_sb", [128, DC, 128], BF16)
    p.carve("wv_sb", [128, DC, 128], BF16)
    p.carve("diagm", [128, 4, 512], BF16)
    p.carve("qsc", [128, 1], F32)
    p.carve("kmx", [2, 4], F32)
    if kind != "l0":
        p.carve("wtile", [128, 2, DC, 128], BF16)
        p.carve("vtmp", [128, CH], F32)
    p.carve("bo_sb", [128, DC], F32)
    if kind == "l1":
        p.carve("lat_sb", [128, 2, 512], BF16)
        p.carve("wkvd_sb", [128, DC, 256], BF16)

    ag_in = p.dint(f"{tag}_ag_in", [128, DC * TLOC], BF16)
    xag = p.dint(f"{tag}_xag", [NCORE, 128, DC, TLOC], BF16)
    a2a_in = p.dint(f"{tag}_a2a_in", [NCORE * 128, TLOC], BF16)
    a2a_out = p.dint(f"{tag}_a2a_out", [NCORE * 128, TLOC], BF16)
    if kind == "l1":
        lat_in = p.dint(f"{tag}_lat_in", [128, 2 * TLOC], BF16)
        lat_all = p.dint(f"{tag}_lat_all", [NCORE, 128, 2, TLOC], BF16)

    dma_pool(p, p.sb["wq_sb"][:, :, :], WqT.rearrange("(dcc pp) o -> pp dcc o", pp=128))
    if kind == "l0":
        dma_pool(p, p.sb["wk_sb"][:, :, :], WkT.rearrange("(dcc pp) o -> pp dcc o", pp=128))
        dma_pool(p, p.sb["wv_sb"][:, :, :], WvT.rearrange("(dcc pp) o -> pp dcc o", pp=128))
        dma_pool(p, p.sb["winm"][:, :, :], winm_in.rearrange("m pp q -> pp m q"))
        dma_sp(p, p.sb["slp"][:, :], slopes_in[:, :])
        dma_pool(p, p.sb["relh"][:, :, :], relh_in[:, :, :])
        dma_pool(p, p.sb["rell"][:, :, :], rell_in[:, :, :])
        p.emit("dve", lambda e: e.memset(p.sb["bo_sb"][:, :], 0.0), inc="dve")
    elif kind == "l2":
        dma_pool(p, p.sb["wk_sb"][:, :, :], WkT.rearrange("(dcc pp) o -> pp dcc o", pp=128))
        dma_pool(p, p.sb["wv_sb"][:, :, :], WvT.rearrange("(dcc pp) o -> pp dcc o", pp=128))
        dma_pool(p, p.sb["bo_sb"][:, :], bo[:, :])
    else:
        dma_pool(p, p.sb["wk_sb"][:, 0:2, :], WkuT.rearrange("(lc pp) o -> pp lc o", pp=128))
        dma_pool(p, p.sb["wv_sb"][:, 0:2, :], WvuT.rearrange("(lc pp) o -> pp lc o", pp=128))
        p.emit("dve", lambda e: e.memset(p.sb["bo_sb"][:, :], 0.0), inc="dve")
    dma_pool(p, p.sb["diagm"][:, :, :], diagm_in.rearrange("m pp q -> pp m q"))
    p.emit("dve", lambda e: e.memset(p.sb["qsc"][:, :], 0.125 if kind != "l0" else 0.03125),
           inc="dve")

    mark = p.carve_mark()
    emit_ln(p, keep_xln=True)
    st = dma_sp(p, ag_in[:, :], p.sb["xln"].rearrange("p dc t -> p (dc t)"),
                wait=[("dve", p.cnt["dve"])])
    cc = p.emit("pool", lambda e: e.collective_compute(
        "AllGather", ALU.bypass, replica_groups=[list(range(NCORE))],
        ins=[ag_in.opt()], outs=[xag.rearrange("o pp f1 f2 -> (o pp) (f1 f2)").opt()]),
        wait=[("xld", st), ("wld", p.cnt["wld"])], inc="cc")

    if debug_stop == "ag":
        return
    if kind == "l1":
        dma_pool(p, p.sb["wkvd_sb"][:, :, :], Wkvd.rearrange("(dcc pp) o -> pp dcc o", pp=128))
        wl = p.cnt["wld"]
        for tg in range(NRUN):
            sl = slice(tg * 512, tg * 512 + 512)
            for lc in range(2):
                def mml(e, sl=sl, lc=lc):
                    for dcc in range(DC):
                        last = e.matmul(bank(p, 2), p.sb["wkvd_sb"][:, dcc, lc * 128:(lc + 1) * 128],
                                        p.sb["xln"][:, dcc, sl], start=(dcc == 0), stop=(dcc == DC - 1))
                    return last
                te = p.emit("te", mml, wait=[("wld", wl), ("dve", p.cnt["dve"]),
                                             ("act", p.cnt["act"]), ("xld", p.cnt["xld"])], inc="te")
                p.emit("act", lambda e, lc=lc: e.activation(
                    p.sb["lat_sb"][:, lc, :], bank(p, 2), AF.Copy),
                    wait=[("te", te), ("xld", p.cnt["xld"])], inc="act")
            dma_sp(p, lat_in.rearrange("p (l t) -> p l t", l=2)[:, :, sl],
                   p.sb["lat_sb"][:, :, :], wait=[("act", p.cnt["act"])])
        cc = p.emit("pool", lambda e: e.collective_compute(
            "AllGather", ALU.bypass, replica_groups=[list(range(NCORE))],
            ins=[lat_in.opt()], outs=[lat_all.rearrange("o pp f1 f2 -> (o pp) (f1 f2)").opt()]),
            wait=[("xld", p.cnt["xld"])], inc="cc")

    p.carve_rewind(mark, ["stat_m", "stat_r", "lnst", "lnsq", "lnsq2"])
    p.carve("lnsq", [128, 512], BF16)
    p.carve("KA", [NAUG, 8192], BF16)
    p.carve("QA", [NAUG, 8192], BF16)
    p.carve("VA", [128, 64, 65], BF16)
    p.carve("ptile", [128, 2, 1024], BF16)
    p.carve("vfm", [64, 512], BF16)
    p.carve("ofsb", [64, 512], F32)
    p.carve("rden", [1, 512], F32)

    def initk(e):
        e.memset(p.sb["KA"][64:65, :], -1.0)
        return e.memset(p.sb["VA"][:, :, 64:65], 1.0)
    p.emit("dve", initk, wait=[("xld", p.cnt["xld"]), ("te", p.cnt["te"]),
                               ("act", p.cnt["act"])], inc="dve")

    if debug_stop == "init":
        return
    if debug_stop == "xsdma":
        dma_sp(p, p.sb["xs"][:, :, :], xag[3][:, :, 512:1024], wait=[("cc", p.cnt["cc"])])
        dma_sp(p, p.sb["lnsq"][0:64, :], xag[5][0:64, 2, 1024:1536], wait=[("cc", p.cnt["cc"])])
        return

    for b in range(2):
        for hl in range(2):
            hsl = slice(hl * 64, hl * 64 + 64)
            dvk = p.emit("dve", lambda e: e.memset(p.sb["kmx"][0:1, 0:4], 0.0), inc="dve")
            if kind == "l0":
                p.emit("dve", lambda e, hl=hl: e.tensor_scalar(
                    p.sb["sidb"][:, :], p.sb["idb"][:, :], p.sb["slp"][:, hl:hl + 1],
                    None, ALU.mult), inc="dve")
            xs_user = p.cnt["te"]
            ntg16 = 1 if (debug_stop or "").startswith("projk1") else 16
            if (debug_stop or "").startswith("projk1") and (b > 0 or hl > 0):
                ntg16 = 0
            for tg16 in range(ntg16):
                o, half = tg16 // 2, tg16 % 2
                lcol = 2 * b * 512 + half * 512
                gcol = o * 1024 + half * 512
                xsl = dma_sp(p, p.sb["xs"][:, :, :],
                             xag[o][:, :, lcol:lcol + 512],
                             wait=[("te", xs_user), ("cc", p.cnt["cc"])])
                if kind == "l1":
                    xsl = dma_sp(p, p.sb["lat_sb"][:, :, :],
                                 lat_all[o][:, :, lcol:lcol + 512],
                                 wait=[("te", xs_user), ("cc", p.cnt["cc"])])

                def mmk(e, hsl=hsl):
                    if kind != "l1":
                        for dcc in range(DC):
                            last = e.matmul(bank(p, 2, rows=64), p.sb["wk_sb"][:, dcc, hsl],
                                            p.sb["xs"][:, dcc, :], start=(dcc == 0), stop=(dcc == DC - 1))
                    else:
                        for lc in range(2):
                            last = e.matmul(bank(p, 2, rows=64), p.sb["wk_sb"][:, lc, hsl],
                                            p.sb["lat_sb"][:, lc, :], start=(lc == 0), stop=(lc == 1))
                    return last
                te = p.emit("te", mmk, wait=[("xld", xsl), ("wld", p.cnt["wld"]),
                                             ("dve", p.cnt["dve"]), ("act", p.cnt["act"])], inc="te")
                dvka = p.emit("dve", lambda e, gcol=gcol: e.tensor_copy(
                    p.sb["KA"][0:64, gcol:gcol + 512], bank(p, 2, rows=64)),
                    wait=[("te", te)], inc="dve")
                if debug_stop == "projk1a":
                    xs_user = p.cnt["te"]
                    continue
                if debug_stop == "projk1bA":
                    p.sbuf("realsq", [128, 512], BF16)
                    p.emit("act", lambda e: e.activation(p.sb["realsq"][0:64, :],
                                                         bank(p, 2, rows=64), AF.Square),
                           wait=[("te", te), ("dve", dvka)], inc="act")
                    xs_user = p.cnt["te"]
                    continue
                if debug_stop == "projk1bB":
                    p.emit("act", lambda e: e.activation(p.sb["lnsq"][0:64, :],
                                                         bank(p, 2, rows=64), AF.Copy),
                           wait=[("te", te), ("dve", dvka)], inc="act")
                    xs_user = p.cnt["te"]
                    continue
                ac = p.emit("act", lambda e: e.activation(p.sb["lnsq"][0:64, :],
                                                          bank(p, 2, rows=64), AF.Square),
                            wait=[("te", te), ("dve", dvka)], inc="act")
                if debug_stop == "projk1b0":
                    xs_user = p.cnt["te"]
                    continue
                te2 = p.emit("te", lambda e: e.matmul(bank(p, 5, rows=1), p.sb["ones_bf"][0:64, 0:1],
                                                      p.sb["lnsq"][0:64, :], start=True, stop=True),
                             wait=[("act", ac)], inc="te")

                if debug_stop == "projk1b":
                    xs_user = p.cnt["te"]
                    continue

                p.emit("dve", lambda e: e.reduce_max(p.sb["rden"][0:1, 0:1], bank(p, 5, rows=1),
                                                     axis=mybir.AxisListType.X),
                       wait=[("te", te2)], inc="dve")
                p.emit("dve", lambda e: e.tensor_max(p.sb["kmx"][0:1, 0:1], p.sb["kmx"][0:1, 0:1],
                                                     p.sb["rden"][0:1, 0:1]), inc="dve")
                if debug_stop == "projk1c":
                    xs_user = p.cnt["te"]
                    continue

                if debug_stop == "projk":
                    xs_user = p.cnt["te"]
                    continue
                def mmq(e, hsl=hsl):
                    for dcc in range(DC):
                        last = e.matmul(bank(p, 3, rows=64), p.sb["wq_sb"][:, dcc, hsl],
                                        p.sb["xs"][:, dcc, :], start=(dcc == 0), stop=(dcc == DC - 1))
                    return last
                te = p.emit("te", mmq, wait=[("act", p.cnt["act"]), ("dve", p.cnt["dve"])], inc="te")
                dvqa = p.emit("dve", lambda e, gcol=gcol: e.tensor_scalar(
                    p.sb["QA"][0:64, gcol:gcol + 512], bank(p, 3, rows=64),
                    p.sb["qsc"][0:64, 0:1], None, ALU.mult), wait=[("te", te)], inc="dve")
                ac = p.emit("act", lambda e: e.activation(p.sb["lnsq"][0:64, :],
                                                          bank(p, 3, rows=64), AF.Square),
                            wait=[("te", te), ("dve", dvqa)], inc="act")
                te2 = p.emit("te", lambda e: e.matmul(bank(p, 6, rows=1), p.sb["ones_bf"][0:64, 0:1],
                                                      p.sb["lnsq"][0:64, :], start=True, stop=True),
                             wait=[("act", ac)], inc="te")
                p.emit("dve", lambda e, gcol=gcol: e.tensor_copy(
                    p.sb["QA"][64:65, gcol:gcol + 512], bank(p, 6, rows=1)),
                    wait=[("te", te2)], inc="dve")

                if debug_stop == "projq":
                    xs_user = p.cnt["te"]
                    continue
                def mmv(e, hsl=hsl):
                    if kind != "l1":
                        for dcc in range(DC):
                            last = e.matmul(bank(p, 4, rows=64), p.sb["wv_sb"][:, dcc, hsl],
                                            p.sb["xs"][:, dcc, :], start=(dcc == 0), stop=(dcc == DC - 1))
                    else:
                        for lc in range(2):
                            last = e.matmul(bank(p, 4, rows=64), p.sb["wv_sb"][:, lc, hsl],
                                            p.sb["lat_sb"][:, lc, :], start=(lc == 0), stop=(lc == 1))
                    return last
                te = p.emit("te", mmv, wait=[("act", p.cnt["act"]), ("dve", p.cnt["dve"])], inc="te")
                ac = p.emit("act", lambda e: e.activation(p.sb["vfm"][0:64, :], bank(p, 4, rows=64),
                                                          AF.Copy), wait=[("te", te)], inc="act")
                jj = o if half == 0 else 15 - o
                for sub in range(4):
                    tt = p.emit("te", lambda e, sub=sub: e.transpose(
                        p.pt[:, 0:64], p.sb["vfm"][0:64, sub * 128:(sub + 1) * 128],
                        p.sb["idb"][0:64, 0:64]), wait=[("act", ac), ("dve", p.cnt["dve"])], inc="te")
                    slot = va_slot(jj, sub)
                    p.emit("dve", lambda e, slot=slot: e.tensor_copy(
                        p.sb["VA"][:, slot, 0:64], p.pt[:, 0:64]), wait=[("te", tt)], inc="dve")
                xs_user = p.cnt["te"]

            if debug_stop is not None and debug_stop.startswith("proj"):
                continue
            hcq = HC_Q2L0 if kind == "l0" else HC_Q2
            p.emit("dve", lambda e, hcq=hcq: e.tensor_scalar(p.sb["QA"][64:65, :], p.sb["QA"][64:65, :],
                                                    p.sb["hc"][0:1, hcq:hcq + 1], None,
                                                    ALU.mult), inc="dve")
            dvm = p.emit("dve", lambda e: e.tensor_scalar(p.sb["QA"][64:65, :], p.sb["QA"][64:65, :],
                                                          p.sb["kmx"][0:1, 0:1],
                                                          p.sb["hc"][0:1, HC_HALF:HC_HALF + 1],
                                                          ALU.add, ALU.mult), inc="dve")

            prev_pair_user = {0: 0, 1: 0}
            av_user = {0: 0, 1: 0}
            for j in range(NCH):
                if kind == "l0":
                    jks = ([j] if j == 0 else [j - 1, j])
                else:
                    jks = list(range(j + 1))
                nkb = 4 * len(jks)
                qg = bcol(j)
                o_j, ro_j = stagecol(b, j)
                dv_j0 = p.cnt["dve"]
                ac_j0 = p.cnt["act"]
                for kb in range(nkb):
                    jk, sub = jks[kb // 4], kb % 4
                    kc = bcol(jk) + sub * 128
                    pair = (kb // 2) % 2
                    bk = (2 if pair == 0 else 4) + (kb % 2)
                    diag = (jk == j)
                    winedge = (kind == "l0" and jk == j - 1)

                    def qk(e, qg=qg, kc=kc, bk=bk, diag=diag, sub=sub, winedge=winedge):
                        last = e.matmul(bank(p, bk), p.sb["KA"][:, kc:kc + 128],
                                        p.sb["QA"][:, qg:qg + 512], start=True,
                                        stop=not (diag or winedge))
                        if diag:
                            last = e.matmul(bank(p, bk), p.sb["idb"][:, :], p.sb["diagm"][:, sub, :],
                                            start=False, stop=(kind != "l0"))
                        elif winedge:
                            last = e.matmul(bank(p, bk), p.sb["idb"][:, :], p.sb["winm"][:, sub, :],
                                            start=False, stop=False)
                        if kind == "l0":
                            mi = (1 if diag else 0) * 4 + sub
                            e.matmul(bank(p, bk), p.sb["sidb"][:, :], p.sb["relh"][:, mi, :],
                                     start=False, stop=False)
                            last = e.matmul(bank(p, bk), p.sb["sidb"][:, :], p.sb["rell"][:, mi, :],
                                            start=False, stop=True)
                        return last
                    p.emit("te", qk, wait=[("dve", dvm), ("act", max(prev_pair_user[pair], ac_j0))],
                           inc="te")
                    if kb % 2 == 1:
                        teqk = p.cnt["te"]
                        slot = (kb // 2) % 2

                        def expf(e, pair=pair, slot=slot):
                            c0 = (2 if pair == 0 else 4) * 512
                            return e.activation(p.sb["ptile"][:, slot, :], p.ps[:, c0:c0 + 1024],
                                                AF.Exp)
                        ac = p.emit("act", expf, wait=[("te", teqk), ("te", av_user[slot])],
                                    inc="act")
                        prev_pair_user[pair] = ac
                        for q2 in range(2):
                            kb2 = kb - 1 + q2
                            vs = va_slot(jks[kb2 // 4], kb2 % 4)

                            def av(e, vs=vs, slot=slot, q2=q2, kb2=kb2, nkb=nkb):
                                return e.matmul(bank(p, 0, rows=65), p.sb["VA"][:, vs, :],
                                                p.sb["ptile"][:, slot, q2 * 512:(q2 + 1) * 512],
                                                start=(kb2 == 0), stop=(kb2 == nkb - 1))
                            p.emit("te", av, wait=[("act", ac), ("dve", dv_j0)], inc="te")
                        av_user[slot] = p.cnt["te"]
                teav = p.cnt["te"]
                dv = p.emit("dve", lambda e: e.reciprocal(p.sb["rden"][0:1, 0:512],
                                                          p.ps[64:65, 0:512]),
                            wait=[("te", teav)], inc="dve")
                te6 = p.emit("te", lambda e: e.matmul(bank(p, 6, rows=64), p.sb["onesr_f"][0:1, 0:64],
                                                      p.sb["rden"][0:1, 0:512], start=True, stop=True),
                             wait=[("dve", dv)], inc="te")
                ac = p.emit("act", lambda e: e.activation(p.sb["ofsb"][0:64, :], bank(p, 0, rows=65)[0:64, :],
                                                          AF.Copy), wait=[("te", teav), ("dve", dv)], inc="act")

                def norm(e, hl=hl, o_j=o_j, ro_j=ro_j):
                    dst = p.sb["xln"][hl * 64:hl * 64 + 64, o_j, ro_j * 512:(ro_j + 1) * 512]
                    return e.tensor_tensor(dst, p.sb["ofsb"][0:64, :], bank(p, 6, rows=64), ALU.mult)
                p.emit("dve", norm, wait=[("te", te6), ("act", ac)], inc="dve")

    if debug_stop in ("proj", "attn"):
        return
    st = dma_sp(p, a2a_in.rearrange("(o pp) t -> pp o t", pp=128),
                p.sb["xln"][:, :, :], wait=[("dve", p.cnt["dve"])])

    def colla(e):
        with allow_ici_all_to_all():
            return e.collective_compute(
                "AllToAll", ALU.bypass, replica_groups=[list(range(NCORE))],
                ins=[a2a_in.opt()], outs=[a2a_out.opt()])
    cc = p.emit("pool", colla, wait=[("xld", st)], inc="cc")
    ld = dma_sp(p, p.sb["xln"][:, :, :], a2a_out.rearrange("(dc pp) t -> pp dc t", pp=128),
                wait=[("cc", cc)])

    if kind == "l0":
        ld2 = ld
        for tg in range(NRUN):
            sl = slice(tg * 512, tg * 512 + 512)
            for oc in range(DC):
                p.emit("dve", lambda e, oc=oc, sl=sl: e.tensor_add(
                    xres[:, oc, sl], xres[:, oc, sl], p.sb["xln"][:, oc, sl]),
                    wait=[("xld", ld2)], inc="dve")
        return
    wtile = p.sb["wtile"]
    slot_user = {0: 0, 1: 0}
    for tg in range(NRUN):
        sl = slice(tg * 512, tg * 512 + 512)
        for oc in range(DC):
            slot = oc % 2
            wl = dma_pool(p, wtile[:, slot],
                          WoT.rearrange("(dcc pp) o -> pp dcc o", pp=128)[:, :, oc * 128:(oc + 1) * 128],
                          wait=[("te", slot_user[slot])])

            def mmo(e, slot=slot, sl=sl):
                for dcc in range(DC):
                    last = e.matmul(bank(p, 2), wtile[:, slot, dcc, :], p.sb["xln"][:, dcc, sl],
                                    start=(dcc == 0), stop=(dcc == DC - 1))
                return last
            te = p.emit("te", mmo, wait=[("wld", wl), ("xld", ld), ("dve", p.cnt["dve"]),
                                         ("act", p.cnt["act"])], inc="te")
            slot_user[slot] = te
            p.emit("dve", lambda e, oc=oc, sl=sl: e.scalar_tensor_tensor(
                xres[:, oc, sl], bank(p, 2), p.sb["bo_sb"][:, oc:oc + 1], xres[:, oc, sl],
                ALU.add, ALU.add), wait=[("te", te)], inc="dve")


def emit_l3a(p, lw, debug_stop=None):
    wslow_in = p.din("l3a_wslow", [D, 3088])
    wout_in = p.din("l3a_wout", [D, D])
    lnw_in = p.din("l3a_lnw", [128, DC])
    lnb_in = p.din("l3a_lnb", [128, DC])
    xres = p.sb["xres"]

    p.carve("lnw", [128, DC], F32)
    p.carve("lnb", [128, DC], F32)
    p.carve("stato", [1, 2 * TLOC], BF16)
    p.carve("hh", [128, DC, TLOC], BF16)
    p.carve("outall", [128, DC, TLOC], BF16)
    p.carve("wsl", [128, 2, DC, 208], BF16)
    p.carve("zero1", [128, 1], F32)
    p.carve("vtmp", [128, CH], F32)
    mk3 = p.carve_mark()
    p.carve("xlf3", [128, DC, CH], F32)

    dma_sp(p, p.sb["lnw"][:, :], lnw_in[:, :])
    ldb = dma_sp(p, p.sb["lnb"][:, :], lnb_in[:, :])
    p.emit("dve", lambda e: e.memset(p.sb["zero1"][:, :], 0.0), wait=[("xld", ldb)], inc="dve")

    emit_ln(p, keep_xln=False, keep_bc=False)
    p.emit("dve", lambda e: e.tensor_copy(p.sb["stato"][0:1, 0:TLOC], p.sb["stat_m"][0:1, :]),
           inc="dve")
    p.emit("dve", lambda e: e.tensor_copy(p.sb["stato"][0:1, TLOC:2 * TLOC], p.sb["stat_r"][0:1, :]),
           inc="dve")
    hc = p.sb["hc"]
    p.emit("dve", lambda e: e.tensor_mul(p.sb["lnst"][0:1, :], p.sb["stat_r"][0:1, :],
                                         p.sb["stat_r"][0:1, :]), inc="dve")
    p.emit("dve", lambda e: e.tensor_scalar(p.sb["lnst"][0:1, :], p.sb["lnst"][0:1, :],
                                            hc[0:1, HC_NEPS:HC_NEPS + 1],
                                            hc[0:1, HC_ONE:HC_ONE + 1], ALU.mult, ALU.add),
           inc="dve")
    ac = p.emit("act", lambda e: e.activation(p.sb["lnst"][0:1, :], p.sb["lnst"][0:1, :], AF.Sqrt,
                                              bias=hc[0:1, HC_EPS:HC_EPS + 1]),
                wait=[("dve", p.cnt["dve"])], inc="act")
    p.emit("dve", lambda e: e.reciprocal(p.sb["lnst"][0:1, :], p.sb["lnst"][0:1, :]),
           wait=[("act", ac)], inc="dve")
    p.emit("dve", lambda e: e.tensor_mul(p.sb["stat_r"][0:1, :], p.sb["stat_r"][0:1, :],
                                         p.sb["lnst"][0:1, :]), inc="dve")

    for tg in range(NRUN):
        sl = slice(tg * 512, tg * 512 + 512)

        def bc4(e, sl=sl):
            e.matmul(bank(p, 3), p.sb["onesr_f"][0:1, :], p.sb["stat_m"][0:1, sl], start=True, stop=True)
            return e.matmul(bank(p, 4), p.sb["onesr_f"][0:1, :], p.sb["stat_r"][0:1, sl],
                            start=True, stop=True)
        te = p.emit("te", bc4, wait=[("dve", p.cnt["dve"]), ("act", p.cnt["act"])], inc="te")

        def xl1(e, sl=sl):
            mb = bank(p, 3).unsqueeze(1).broadcast_to([128, DC, 512])
            return e.tensor_tensor(p.sb["xlf3"][:, :, :], xres[:, :, sl], mb, ALU.subtract)
        p.emit("dve", xl1, wait=[("te", te)], inc="dve")

        def xl2(e):
            rb = bank(p, 4).unsqueeze(1).broadcast_to([128, DC, 512])
            return e.tensor_tensor(p.sb["xlf3"][:, :, :], p.sb["xlf3"][:, :, :], rb, ALU.mult)
        dv = p.emit("dve", xl2, inc="dve")
        for dc in range(DC):
            p.emit("dve", lambda e, dc=dc, sl=sl: e.tensor_scalar(
                p.sb["hh"][:, dc, sl], p.sb["xlf3"][:, dc, :],
                p.sb["lnw"][:, dc:dc + 1], p.sb["lnb"][:, dc:dc + 1], ALU.mult, ALU.add),
                wait=[("dve", dv)], inc="dve")
    ac_h = p.cnt["act"]

    if debug_stop == "l3prep":
        return
    p.carve_rewind(mk3, ["xlf3", "stat_m", "stat_r", "lnst", "lnsq", "lnsq2"])
    p.carve("qn", [64, TLOC], BF16)
    p.carve("kn", [64, TLOC], BF16)
    p.carve("vv", [64, TLOC], BF16)
    p.carve("bet", [1, TLOC], BF16)
    p.carve("q1", [64, CH], F32)
    p.carve("q1b", [64, CH], BF16)
    p.carve("dtmp", [64, 1024], BF16)
    p.carve("dtmp2", [64, 1024], BF16)
    p.carve("dtmp3", [64, 1024], BF16)
    p.carve("dtmp4", [64, 1024], BF16)
    p.carve("dcol", [1, 4096], BF16)
    p.carve("del0", [64, 1024], F32)
    p.carve("del1", [64, 1024], F32)
    p.carve("ot", [64, 512], F32)
    p.carve("rden3", [1, CH], F32)
    slot_user = {0: 0, 1: 0}
    for h in range(16):
        slot = h % 2
        wl = dma_pool(p, p.sb["wsl"][:, slot, :, 0:193],
                      wslow_in.rearrange("(dcc pp) o -> pp dcc o", pp=128)[:, :, 193 * h:193 * h + 193],
                      wait=[("te", slot_user[slot])])
        for tg in range(NRUN):
            sl = slice(tg * 512, tg * 512 + 512)

            def mma(e, slot=slot, sl=sl):
                for dcc in range(DC):
                    e.matmul(bank(p, 0), p.sb["wsl"][:, slot, dcc, 0:128], p.sb["hh"][:, dcc, sl],
                             start=(dcc == 0), stop=(dcc == DC - 1))
                for dcc in range(DC):
                    last = e.matmul(bank(p, 1, rows=65), p.sb["wsl"][:, slot, dcc, 128:193],
                                    p.sb["hh"][:, dcc, sl], start=(dcc == 0), stop=(dcc == DC - 1))
                return last
            te = p.emit("te", mma, wait=[("wld", wl), ("act", max(ac_h, p.cnt["act"])),
                                         ("dve", p.cnt["dve"])], inc="te")
            slot_user[slot] = te
            dvv = p.emit("dve", lambda e, sl=sl: e.tensor_copy(p.sb["vv"][0:64, sl],
                                                               bank(p, 1, rows=64)),
                         wait=[("te", te)], inc="dve")
            acb = p.emit("act", lambda e, sl=sl: e.activation(p.sb["bet"][0:1, sl],
                                                              p.ps[64:65, 512:1024], AF.Sigmoid),
                         wait=[("te", te), ("dve", dvv)], inc="act")
            for qi, rows0 in ((0, 0), (1, 64)):
                dst = p.sb["qn"] if qi == 0 else p.sb["kn"]
                bsl = bank(p, 0)[rows0:rows0 + 64, :]

                def relu_(e, bsl=bsl):
                    return e.tensor_scalar_max(p.sb["q1"][0:64, :], bsl, p.sb["zero1"][0:64, 0:1])
                dv1 = p.emit("dve", relu_, wait=[("te", te), ("act", p.cnt["act"])], inc="dve")

                def min_(e, bsl=bsl):
                    return e.tensor_scalar_min(p.sb["vtmp"][0:64, :], bsl, p.sb["zero1"][0:64, 0:1])
                dv2 = p.emit("dve", min_, inc="dve")
                ace = p.emit("act", lambda e: e.activation(p.sb["vtmp"][0:64, :], p.sb["vtmp"][0:64, :],
                                                           AF.Exp), wait=[("dve", dv2)], inc="act")
                dv3 = p.emit("dve", lambda e: e.tensor_add(p.sb["q1"][0:64, :], p.sb["q1"][0:64, :],
                                                           p.sb["vtmp"][0:64, :]),
                             wait=[("ace" if False else "act", ace)], inc="dve")
                dv4 = p.emit("dve", lambda e: e.tensor_copy(p.sb["q1b"][0:64, :], p.sb["q1"][0:64, :]),
                             inc="dve")
                te2 = p.emit("te", lambda e: e.matmul(bank(p, 2, rows=1), p.sb["ones_bf"][0:64, 0:1],
                                                      p.sb["q1b"][0:64, :], start=True, stop=True),
                             wait=[("dve", dv4)], inc="te")
                dv5 = p.emit("dve", lambda e: e.reciprocal(p.sb["rden3"][0:1, :], bank(p, 2, rows=1)),
                             wait=[("te", te2)], inc="dve")
                te3 = p.emit("te", lambda e: e.matmul(bank(p, 2, rows=64), p.sb["onesr_f"][0:1, 0:64],
                                                      p.sb["rden3"][0:1, :], start=True, stop=True),
                             wait=[("dve", dv5)], inc="te")
                p.emit("dve", lambda e, dst=dst, sl=sl: e.tensor_tensor(
                    dst[0:64, sl], p.sb["q1"][0:64, :], bank(p, 2, rows=64), ALU.mult),
                    wait=[("te", te3)], inc="dve")
        B0 = slice(0, 1024)
        B1 = slice(1024, 2048)

        def dots(e):
            e.tensor_mul(p.sb["dtmp"][0:64, :], p.sb["kn"][0:64, B0], p.sb["qn"][0:64, B0])
            return e.tensor_mul(p.sb["dtmp2"][0:64, :], p.sb["kn"][0:64, B0], p.sb["qn"][0:64, B1])
        dv = p.emit("dve", dots, inc="dve")

        def dots2(e):
            e.tensor_mul(p.sb["dtmp3"][0:64, :], p.sb["kn"][0:64, B0], p.sb["kn"][0:64, B1])
            return e.tensor_mul(p.sb["dtmp4"][0:64, :], p.sb["kn"][0:64, B1], p.sb["qn"][0:64, B1])
        dv = p.emit("dve", dots2, inc="dve")

        dots_names = ("dtmp", "dtmp2", "dtmp3", "dtmp4")
        for i, t in enumerate(dots_names):
            for q2 in range(2):
                te2 = p.emit("te", lambda e, t=t, q2=q2: e.matmul(
                    bank(p, 3, rows=1), p.sb["ones_bf"][0:64, 0:1],
                    p.sb[t][0:64, q2 * 512:(q2 + 1) * 512], start=True, stop=True),
                    wait=[("dve", p.cnt["dve"])], inc="te")
                p.emit("dve", lambda e, i=i, q2=q2: e.tensor_copy(
                    p.sb["dcol"][0:1, i * 1024 + q2 * 512:i * 1024 + (q2 + 1) * 512],
                    bank(p, 3, rows=1)), wait=[("te", te2)], inc="dve")
        te2 = p.emit("te", lambda e: e.matmul(bank(p, 4, rows=64), p.sb["onesr_b"][0:1, 0:64],
                                              p.sb["bet"][0:1, 0:512], start=True, stop=True),
                     wait=[("dve", p.cnt["dve"])], inc="te")
        te3 = p.emit("te", lambda e: e.matmul(bank(p, 5, rows=64), p.sb["onesr_b"][0:1, 0:64],
                                              p.sb["bet"][0:1, 512:1024], start=True, stop=True),
                     inc="te")
        p.emit("dve", lambda e: e.tensor_tensor(p.sb["del0"][0:64, 0:512], p.sb["vv"][0:64, 0:512],
                                                bank(p, 4, rows=64), ALU.mult),
               wait=[("te", te2)], inc="dve")
        dv = p.emit("dve", lambda e: e.tensor_tensor(p.sb["del0"][0:64, 512:1024],
                                                     p.sb["vv"][0:64, 512:1024],
                                                     bank(p, 5, rows=64), ALU.mult),
                    wait=[("te", te3)], inc="dve")
        for q2 in range(2):
            te2 = p.emit("te", lambda e, q2=q2: e.matmul(
                bank(p, 4, rows=64), p.sb["onesr_b"][0:1, 0:64],
                p.sb["dcol"][0:1, 2048 + q2 * 512:2048 + (q2 + 1) * 512], start=True, stop=True),
                wait=[("dve", p.cnt["dve"])], inc="te")
            te3 = p.emit("te", lambda e, q2=q2: e.matmul(
                bank(p, 5, rows=64), p.sb["onesr_b"][0:1, 0:64],
                p.sb["bet"][0:1, 1024 + q2 * 512:1024 + (q2 + 1) * 512], start=True, stop=True),
                inc="te")
            s5 = slice(q2 * 512, (q2 + 1) * 512)

            def d1a(e, s5=s5):
                return e.tensor_tensor(p.sb["ot"][0:64, 0:512], p.sb["del0"][0:64, s5],
                                       bank(p, 4, rows=64), ALU.mult)
            p.emit("dve", d1a, wait=[("te", te2)], inc="dve")

            def d1b(e, q2=q2):
                return e.tensor_sub(p.sb["ot"][0:64, 0:512],
                                    p.sb["vv"][0:64, 1024 + q2 * 512:1024 + (q2 + 1) * 512],
                                    p.sb["ot"][0:64, 0:512])
            p.emit("dve", d1b, inc="dve")

            def d1c(e, s5=s5):
                return e.tensor_tensor(p.sb["del1"][0:64, s5], p.sb["ot"][0:64, 0:512],
                                       bank(p, 5, rows=64), ALU.mult)
            p.emit("dve", d1c, wait=[("te", te3)], inc="dve")
        hp = 64 * (h % 2)
        hdc = h // 2
        for q2 in range(2):
            s5 = slice(q2 * 512, (q2 + 1) * 512)
            te2 = p.emit("te", lambda e, q2=q2: e.matmul(
                bank(p, 4, rows=64), p.sb["onesr_b"][0:1, 0:64],
                p.sb["dcol"][0:1, q2 * 512:(q2 + 1) * 512], start=True, stop=True),
                wait=[("dve", p.cnt["dve"])], inc="te")
            p.emit("dve", lambda e, hp=hp, hdc=hdc, q2=q2, s5=s5: e.tensor_tensor(
                p.sb["outall"][hp:hp + 64, hdc, q2 * 512:(q2 + 1) * 512],
                p.sb["del0"][0:64, s5], bank(p, 4, rows=64), ALU.mult),
                wait=[("te", te2)], inc="dve")
            te2 = p.emit("te", lambda e, q2=q2: e.matmul(
                bank(p, 4, rows=64), p.sb["onesr_b"][0:1, 0:64],
                p.sb["dcol"][0:1, 1024 + q2 * 512:1024 + (q2 + 1) * 512], start=True, stop=True),
                wait=[("dve", p.cnt["dve"])], inc="te")
            te3 = p.emit("te", lambda e, q2=q2: e.matmul(
                bank(p, 5, rows=64), p.sb["onesr_b"][0:1, 0:64],
                p.sb["dcol"][0:1, 3072 + q2 * 512:3072 + (q2 + 1) * 512], start=True, stop=True),
                inc="te")
            p.emit("dve", lambda e, s5=s5: e.tensor_tensor(
                p.sb["ot"][0:64, 0:512], p.sb["del0"][0:64, s5], bank(p, 4, rows=64), ALU.mult),
                wait=[("te", te2)], inc="dve")
            p.emit("dve", lambda e, s5=s5: e.tensor_tensor(
                p.sb["q1"][0:64, :], p.sb["del1"][0:64, s5], bank(p, 5, rows=64), ALU.mult),
                wait=[("te", te3)], inc="dve")
            p.emit("dve", lambda e, hp=hp, hdc=hdc, q2=q2: e.tensor_add(
                p.sb["outall"][hp:hp + 64, hdc, 1024 + q2 * 512:1024 + (q2 + 1) * 512],
                p.sb["ot"][0:64, 0:512], p.sb["q1"][0:64, :]), inc="dve")

    if debug_stop == "l3heads":
        return
    p.carve_rewind(mk3, ["qn", "kn", "vv", "bet", "q1", "q1b", "dtmp", "dtmp2", "dtmp3",
                         "dtmp4", "dcol", "del0", "del1", "ot", "rden3"])
    p.carve("xlf3", [128, DC, CH], F32)
    p.carve("wtile", [128, 2, DC, 128], BF16)
    slot_user = {0: p.cnt["te"], 1: p.cnt["te"]}
    dv_tail0 = p.cnt["dve"]
    for tg in range(NRUN):
        sl = slice(tg * 512, tg * 512 + 512)

        def bco(e, sl=sl, tg=tg):
            e.matmul(bank(p, 3), p.sb["onesr_b"][0:1, :], p.sb["stato"][0:1, sl], start=True, stop=True)
            return e.matmul(bank(p, 4), p.sb["onesr_b"][0:1, :],
                            p.sb["stato"][0:1, TLOC + tg * 512:TLOC + tg * 512 + 512],
                            start=True, stop=True)
        te = p.emit("te", bco, wait=[("dve", p.cnt["dve"]), ("act", p.cnt["act"])], inc="te")

        def xl1(e, sl=sl):
            mb = bank(p, 3).unsqueeze(1).broadcast_to([128, DC, 512])
            return e.tensor_tensor(p.sb["xlf3"][:, :, :], xres[:, :, sl], mb, ALU.subtract)
        p.emit("dve", xl1, wait=[("te", te)], inc="dve")

        def xl2(e):
            rb = bank(p, 4).unsqueeze(1).broadcast_to([128, DC, 512])
            return e.tensor_tensor(p.sb["xlf3"][:, :, :], p.sb["xlf3"][:, :, :], rb, ALU.mult)
        dvx = p.emit("dve", xl2, inc="dve")
        if debug_stop == "l3tail1":
            continue
        for oc in range(DC):
            slot = oc % 2
            wl = dma_pool(p, p.sb["wtile"][:, slot],
                          wout_in.rearrange("(dcc pp) o -> pp dcc o", pp=128)[:, :, oc * 128:(oc + 1) * 128],
                          wait=[("te", slot_user[slot]), ("dve", dv_tail0)])
            if debug_stop == "l3tail2":
                slot_user[slot] = p.cnt["te"]
                continue

            def mmo(e, slot=slot, sl=sl):
                for dcc in range(DC):
                    last = e.matmul(bank(p, 2), p.sb["wtile"][:, slot, dcc, :],
                                    p.sb["outall"][:, dcc, sl], start=(dcc == 0), stop=(dcc == DC - 1))
                return last
            te2 = p.emit("te", mmo, wait=[("wld", wl), ("dve", dvx), ("act", p.cnt["act"]),
                                          ("dve", p.cnt["dve"])], inc="te")
            slot_user[slot] = te2
            p.emit("dve", lambda e, oc=oc, sl=sl: e.tensor_add(
                p.sb["vtmp"][:, :], bank(p, 2), p.sb["xlf3"][:, oc, :]),
                wait=[("te", te2)], inc="dve")
            p.emit("dve", lambda e, oc=oc, sl=sl: e.tensor_add(
                xres[:, oc, sl], xres[:, oc, sl], p.sb["vtmp"][:, :]), inc="dve")


CAP1 = 2560
BIGM = 1.0e9


def emit_moe(p, lw, debug_stop=None):
    kind = lw["kind"]
    gwT = p.din(lw["gwT"], [D, 8])
    W1 = p.din(lw["W1"], [8, D, 4096])
    W2 = p.din(lw["W2"], [8, 2048 if kind == "l0f" else 4096, D])
    if kind == "l1f":
        gb = p.din("l1f_gb", [1, 8])
        b1f = p.din("l1f_b1", [8, 128, 32])
        b2f = p.din("l1f_b2", [8, 128, 8])
    xres = p.sb["xres"]
    INTER2 = 4096

    p.carve("gw_sb", [128, DC, 8], BF16)
    p.carve("zeroc", [128, 1], F32)
    p.carve("combT", [8, TLOC], F32)
    p.carve("gb_sb", [1, 8], BF16)
    p.carve("capc", [8, 1], F32)
    p.carve("nbig", [128, 1], F32)
    p.carve("xln", [128, DC, TLOC], BF16)
    mkA = p.carve_mark()
    emit_ln(p, keep_xln=True)
    p.carve("gprob", [128, 16, 8], F32)
    p.carve("w12c", [128, 16, 2], F32)
    p.carve("tkm", [128, 12], F32)
    p.carve("tk2", [128, 12], F32)
    p.carve("tk3", [128, 8], F32)
    p.carve("ohT", [8, 2, TLOC], BF16)
    p.carve("scanT", [8, TLOC], F32)
    p.carve("keepT", [8, TLOC], BF16)
    p.carve("wrow", [1, 2 * TLOC], F32)
    p.carve("thr", [8, 8], F32)
    p.carve("cnt8", [8, 8], F32)
    p.carve("pub", [4, 16], F32)
    p.carve("cagg", [32, 16], F32)
    p.carve("moff", [4, 16], F32)
    p.carve("scan2", [8, 2, TLOC], F32)

    ldg = dma_pool(p, p.sb["gw_sb"][:, :, :], gwT.rearrange("(dcc pp) o -> pp dcc o", pp=128))
    if kind == "l1f":
        ldg = dma_pool(p, p.sb["gb_sb"][:, :], gb[:, :])
    dvz = p.emit("dve", lambda e: e.memset(p.sb["zeroc"][:, :], 0.0), inc="dve")
    p.emit("dve", lambda e: e.memset(p.sb["combT"][:, :], 0.0), inc="dve")
    p.emit("dve", lambda e: e.memset(p.sb["capc"][0:8, 0:1], float(CAP1 + 1)), inc="dve")
    p.emit("dve", lambda e: e.memset(p.sb["nbig"][:, :], -BIGM), inc="dve")

    for tt in range(16):
        ts_ = slice(tt * 128, (tt + 1) * 128)

        if kind == "l1f":
            def gmm(e, ts_=ts_):
                for dcc in range(DC):
                    e.matmul(bank(p, 0, rows=128, width=8), p.sb["xln"][:, dcc, ts_],
                             p.sb["gw_sb"][:, dcc, :], start=(dcc == 0), stop=False)
                return e.matmul(bank(p, 0, rows=128, width=8), p.sb["onesr_b"][0:1, 0:128],
                                p.sb["gb_sb"][0:1, :], start=False, stop=True)
        else:
            def gmm(e, ts_=ts_):
                for dcc in range(DC):
                    last = e.matmul(bank(p, 0, rows=128, width=8), p.sb["xln"][:, dcc, ts_],
                                    p.sb["gw_sb"][:, dcc, :], start=(dcc == 0), stop=(dcc == DC - 1))
                return last
        te = p.emit("te", gmm, wait=[("wld", ldg), ("dve", p.cnt["dve"]),
                                     ("act", p.cnt["act"])], inc="te")
        dv = p.emit("dve", lambda e: e.reduce_max(p.sb["tkm"][:, 0:1], bank(p, 0, rows=128, width=8),
                                                  axis=mybir.AxisListType.X), wait=[("te", te)], inc="dve")
        dv = p.emit("dve", lambda e: e.tensor_sub(p.sb["tk2"][:, 0:1], p.sb["zeroc"][:, 0:1],
                                                  p.sb["tkm"][:, 0:1]), inc="dve")
        ac = p.emit("act", lambda e, tt=tt: e.activation(p.sb["gprob"][:, tt, :],
                                                         bank(p, 0, rows=128, width=8), AF.Exp,
                                                         bias=p.sb["tk2"][:, 0:1]),
                    wait=[("dve", dv)], inc="act")
        dv = p.emit("dve", lambda e, tt=tt: e.reduce_sum(p.sb["tkm"][:, 0:1], p.sb["gprob"][:, tt, :],
                                                         axis=mybir.AxisListType.X),
                    wait=[("act", ac)], inc="dve")
        dv = p.emit("dve", lambda e: e.reciprocal(p.sb["tk2"][:, 0:1], p.sb["tkm"][:, 0:1]), inc="dve")
        dv = p.emit("dve", lambda e, tt=tt: e.tensor_scalar(p.sb["gprob"][:, tt, :],
                                                            p.sb["gprob"][:, tt, :],
                                                            p.sb["tk2"][:, 0:1], None, ALU.mult),
                    inc="dve")
        gp = p.sb["gprob"]
        if kind == "l0f":
            p.emit("dve", lambda e, tt=tt: e.tensor_tensor(
                p.sb["tkm"][:, 0:4], gp[:, tt, 0::2], gp[:, tt, 1::2], ALU.max), inc="dve")
            p.emit("dve", lambda e: e.reduce_max(p.sb["tk2"][:, 0:1], p.sb["tkm"][:, 0:4],
                                                 axis=mybir.AxisListType.X), inc="dve")
            p.emit("dve", lambda e: e.tensor_scalar(p.sb["tk3"][:, 0:4], p.sb["tkm"][:, 0:4],
                                                    p.sb["tk2"][:, 0:1], None, ALU.is_ge), inc="dve")
            p.emit("dve", lambda e: e.scalar_tensor_tensor(p.sb["tk2"][:, 1:5], p.sb["tk3"][:, 0:4],
                                                           p.sb["nbig"][:, 0:1], p.sb["tkm"][:, 0:4],
                                                           ALU.mult, ALU.add), inc="dve")
            p.emit("dve", lambda e: e.reduce_max(p.sb["tk2"][:, 0:1], p.sb["tk2"][:, 1:5],
                                                 axis=mybir.AxisListType.X), inc="dve")
            p.emit("dve", lambda e: e.tensor_scalar(p.sb["tk3"][:, 0:4], p.sb["tkm"][:, 0:4],
                                                    p.sb["tk2"][:, 0:1], None, ALU.is_ge), inc="dve")
            p.emit("dve", lambda e: e.tensor_copy(p.sb["tkm"][:, 0:8][:, 0::2], p.sb["tk3"][:, 0:4]),
                   inc="dve")
            p.emit("dve", lambda e: e.tensor_copy(p.sb["tkm"][:, 0:8][:, 1::2], p.sb["tk3"][:, 0:4]),
                   inc="dve")
            p.emit("dve", lambda e, tt=tt: e.tensor_mul(p.sb["tkm"][:, 0:8], p.sb["tkm"][:, 0:8],
                                                        gp[:, tt, :]), inc="dve")
            masked = lambda: p.sb["tkm"][:, 0:8]
        else:
            p.emit("dve", lambda e, tt=tt: e.tensor_copy(p.sb["tkm"][:, 0:8], gp[:, tt, :]), inc="dve")
            masked = lambda: p.sb["tkm"][:, 0:8]
        p.emit("dve", lambda e: e.reduce_max(p.sb["tk2"][:, 0:1], masked(), axis=mybir.AxisListType.X),
               inc="dve")
        p.emit("dve", lambda e: e.tensor_scalar(p.sb["tk3"][:, 0:8], masked(), p.sb["tk2"][:, 0:1],
                                                None, ALU.is_ge), inc="dve")
        p.emit("dve", lambda e: e.scalar_tensor_tensor(p.sb["tk2"][:, 0:8], p.sb["tk3"][:, 0:8],
                                                       p.sb["nbig"][:, 0:1], masked(),
                                                       ALU.mult, ALU.add), inc="dve")
        p.emit("dve", lambda e: e.reduce_max(p.sb["tk2"][:, 8:9], p.sb["tk2"][:, 0:8],
                                             axis=mybir.AxisListType.X), inc="dve")
        p.emit("dve", lambda e: e.tensor_scalar(p.sb["tk2"][:, 0:8], masked(), p.sb["tk2"][:, 8:9],
                                                None, ALU.is_ge), inc="dve")
        p.emit("dve", lambda e: e.tensor_sub(p.sb["tk2"][:, 0:8], p.sb["tk2"][:, 0:8],
                                             p.sb["tk3"][:, 0:8]), inc="dve")
        p.emit("dve", lambda e: e.reduce_max(p.sb["tkm"][:, 8:9], masked(),
                                             axis=mybir.AxisListType.X), inc="dve")
        if kind == "l0f":
            p.emit("dve", lambda e: e.tensor_add(p.sb["tkm"][:, 9:10], p.sb["tkm"][:, 8:9],
                                                 p.sb["tk2"][:, 8:9]), inc="dve")
            p.emit("dve", lambda e: e.reciprocal(p.sb["tkm"][:, 10:11], p.sb["tkm"][:, 9:10]),
                   inc="dve")
            p.emit("dve", lambda e, tt=tt: e.tensor_tensor(p.sb["w12c"][:, tt, 0:1],
                                                           p.sb["tkm"][:, 8:9],
                                                           p.sb["tkm"][:, 10:11], ALU.mult), inc="dve")
            p.emit("dve", lambda e, tt=tt: e.tensor_tensor(p.sb["w12c"][:, tt, 1:2],
                                                           p.sb["tk2"][:, 8:9],
                                                           p.sb["tkm"][:, 10:11], ALU.mult), inc="dve")
        else:
            p.emit("dve", lambda e, tt=tt: e.tensor_copy(p.sb["w12c"][:, tt, 0:1],
                                                         p.sb["tkm"][:, 8:9]), inc="dve")
            p.emit("dve", lambda e, tt=tt: e.tensor_copy(p.sb["w12c"][:, tt, 1:2],
                                                         p.sb["tk2"][:, 8:9]), inc="dve")
        for ci, eqt in ((0, "tk3"), (1, "tk2")):
            dvq = p.emit("dve", lambda e, eqt=eqt: e.tensor_copy(p.sb["lnsq"][0:128, 0:8],
                                                                 p.sb[eqt][:, 0:8]), inc="dve")
            tt2 = p.emit("te", lambda e: e.transpose(p.pt[0:8, 0:128], p.sb["lnsq"][0:128, 0:8],
                                                     p.sb["idb"][:, :]),
                         wait=[("dve", dvq), ("dve", p.cnt["dve"])], inc="te")
            p.emit("dve", lambda e, ci=ci, tt=tt: e.tensor_copy(
                p.sb["ohT"][0:8, ci, tt * 128:(tt + 1) * 128], p.pt[0:8, 0:128]),
                wait=[("te", tt2)], inc="dve")
        for wi in range(2):
            dvq = p.emit("dve", lambda e, tt=tt, wi=wi: e.tensor_copy(
                p.sb["lnsq"][0:128, 0:1], p.sb["w12c"][:, tt, wi:wi + 1]), inc="dve")
            tt2 = p.emit("te", lambda e: e.transpose(p.pt[0:1, 0:128], p.sb["lnsq"][0:128, 0:1],
                                                     p.sb["idb"][:, :]),
                         wait=[("dve", dvq), ("dve", p.cnt["dve"])], inc="te")
            p.emit("dve", lambda e, tt=tt, wi=wi: e.tensor_copy(
                p.sb["wrow"][0:1, wi * TLOC + tt * 128:wi * TLOC + (tt + 1) * 128],
                p.pt[0:1, 0:128]), wait=[("te", tt2)], inc="dve")
    dv_rout = p.cnt["dve"]

    for ci in range(2):
        if kind == "l1f":
            if ci == 0:
                p.emit("dve", lambda e: e.memset(p.sb["keepT"][0:8, :], 1.0), inc="dve")
            p.emit("dve", lambda e, ci=ci: e.tensor_tensor_scan(
                p.sb["scanT"][0:8, :], p.sb["keepT"][0:8, :], p.sb["ohT"][0:8, ci, :],
                0.0, ALU.mult, ALU.add), wait=[("dve", dv_rout)], inc="dve")
            for r in range(NRUN):
                if r == 0:
                    p.emit("dve", lambda e, ci=ci: e.tensor_copy(
                        p.sb["cnt8"][0:8, ci * 4:ci * 4 + 1], p.sb["scanT"][0:8, 511:512]), inc="dve")
                else:
                    p.emit("dve", lambda e, ci=ci, r=r: e.tensor_sub(
                        p.sb["cnt8"][0:8, ci * 4 + r:ci * 4 + r + 1],
                        p.sb["scanT"][0:8, r * 512 + 511:r * 512 + 512],
                        p.sb["scanT"][0:8, r * 512 - 1:r * 512]), inc="dve")
            p.emit("dve", lambda e, ci=ci: e.memset(p.sb["thr"][0:8, ci * 4:ci * 4 + 1], 0.0),
                   inc="dve")
            for r in range(1, NRUN):
                p.emit("dve", lambda e, ci=ci, r=r: e.tensor_copy(
                    p.sb["thr"][0:8, ci * 4 + r:ci * 4 + r + 1],
                    p.sb["scanT"][0:8, r * 512 - 1:r * 512]), inc="dve")
            p.emit("dve", lambda e, ci=ci: e.tensor_copy(p.sb["scan2"][0:8, ci, :],
                                                         p.sb["scanT"][0:8, :]), inc="dve")
    if kind == "l1f":
        tt2 = p.emit("te", lambda e: e.transpose(bank(p, 3, rows=8, width=8),
                                                 p.sb["cnt8"][0:8, 0:8], p.sb["idf"][0:8, 0:8]),
                     wait=[("dve", p.cnt["dve"]), ("act", p.cnt["act"])], inc="te")
        p.carve("cnt8T", [8, 8], F32)
        dvp = p.emit("dve", lambda e: e.tensor_copy(p.sb["cnt8T"][0:8, 0:8],
                                                    bank(p, 3, rows=8, width=8)),
                     wait=[("te", tt2)], inc="dve")
        cnt_tmp = p.dint("l1f_cnt_tmp", [8, 8])
        st = dma_sp(p, cnt_tmp[:, :], p.sb["cnt8T"][0:8, :], wait=[("dve", dvp)])
        for cc2 in range(2):
            ld = dma_sp(p, p.sb["pub"][0:4, cc2 * 8:(cc2 + 1) * 8],
                        cnt_tmp[cc2 * 4:(cc2 + 1) * 4, :], wait=[("xld", st)])
        mo_in = p.dint("l1f_cnt_in", [4, 16])
        mo_all = p.dint("l1f_cnt_all", [32, 16])
        st = dma_sp(p, mo_in[:, :], p.sb["pub"][0:4, :], wait=[("dve", p.cnt["dve"])])
        cc = p.emit("pool", lambda e: e.collective_compute(
            "AllGather", ALU.bypass, replica_groups=[list(range(NCORE))],
            ins=[mo_in.opt()], outs=[mo_all.opt()]), wait=[("xld", st)], inc="cc")
        ld = dma_sp(p, p.sb["cagg"][0:32, :], mo_all[:, :], wait=[("cc", cc)])
        prec = p.din("c_moeprec", [32, 4])
        p.carve("prec32", [32, 4], F32)
        ld = dma_sp(p, p.sb["prec32"][:, :], prec[:, :])
        te = p.emit("te", lambda e: e.matmul(bank(p, 1, rows=4, width=16), p.sb["prec32"][:, :],
                                             p.sb["cagg"][0:32, :], start=True, stop=True),
                    wait=[("xld", ld)], inc="te")
        dv = p.emit("dve", lambda e: e.tensor_copy(p.sb["moff"][0:4, :], bank(p, 1, rows=4, width=16)),
                    wait=[("te", te)], inc="dve")
        tt2 = p.emit("te", lambda e: e.transpose(bank(p, 3, rows=16, width=4),
                                                 p.sb["moff"][0:4, :], p.sb["idf"][0:4, 0:4]),
                     wait=[("dve", p.cnt["dve"]), ("act", p.cnt["act"])], inc="te")
        p.carve("mof16", [16, 4], F32)
        dv = p.emit("dve", lambda e: e.tensor_copy(p.sb["mof16"][0:16, 0:4],
                                                   bank(p, 3, rows=16, width=4)),
                    wait=[("te", tt2)], inc="dve")
        mo_tmp = p.dint("l1f_mof_tmp", [16, 4])
        p.carve("mof8", [8, 8], F32)
        st = dma_sp(p, mo_tmp[:, :], p.sb["mof16"][0:16, :], wait=[("dve", dv)])
        for cc2 in range(2):
            ld = dma_sp(p, p.sb["mof8"][0:8, cc2 * 4:(cc2 + 1) * 4],
                        mo_tmp[cc2 * 8:(cc2 + 1) * 8, :], wait=[("xld", st)])
        dv = p.emit("dve", lambda e: e.tensor_sub(p.sb["thr"][0:8, :], p.sb["thr"][0:8, :],
                                                  p.sb["mof8"][0:8, :]),
                    wait=[("xld", ld)], inc="dve")
        dv = p.emit("dve", lambda e: e.tensor_scalar(p.sb["thr"][0:8, :], p.sb["thr"][0:8, :],
                                                     p.sb["capc"][0:8, 0:1], None, ALU.add),
                    inc="dve")
    dv_cap = p.cnt["dve"]

    for ci in range(2):
        if kind == "l1f":
            for r in range(NRUN):
                p.emit("dve", lambda e, ci=ci, r=r: e.tensor_scalar(
                    p.sb["keepT"][0:8, r * 512:(r + 1) * 512],
                    p.sb["scan2"][0:8, ci, r * 512:(r + 1) * 512],
                    p.sb["thr"][0:8, ci * 4 + r:ci * 4 + r + 1], None, ALU.is_lt),
                    wait=[("dve", dv_cap)], inc="dve")
        else:
            p.emit("dve", lambda e: e.memset(p.sb["keepT"][0:8, :], 1.0), inc="dve")
        for q4 in range(NRUN):
            s5 = slice(q4 * 512, (q4 + 1) * 512)
            te = p.emit("te", lambda e, ci=ci, q4=q4: e.matmul(
                bank(p, 2, rows=8), p.sb["onesr_f"][0:1, 0:8],
                p.sb["wrow"][0:1, ci * TLOC + q4 * 512:ci * TLOC + (q4 + 1) * 512],
                start=True, stop=True), wait=[("dve", p.cnt["dve"]), ("dve", dv_rout)], inc="te")
            p.emit("dve", lambda e, ci=ci, s5=s5: e.tensor_tensor(
                p.sb["scanT"][0:8, s5], p.sb["ohT"][0:8, ci, s5], p.sb["keepT"][0:8, s5],
                ALU.mult), inc="dve")
            p.emit("dve", lambda e, s5=s5: e.tensor_tensor(
                p.sb["scanT"][0:8, s5], p.sb["scanT"][0:8, s5], bank(p, 2, rows=8), ALU.mult),
                wait=[("te", te)], inc="dve")
            p.emit("dve", lambda e, s5=s5: e.tensor_add(
                p.sb["combT"][0:8, s5], p.sb["combT"][0:8, s5], p.sb["scanT"][0:8, s5]), inc="dve")
    dv_comb = p.cnt["dve"]

    if debug_stop == "route":
        return
    p.carve_rewind(mkA, ["stat_m", "stat_r", "lnst", "lnsq", "lnsq2", "gprob", "w12c",
                         "tkm", "tk2", "tk3", "ohT", "scanT", "keepT", "wrow", "thr", "cnt8",
                         "pub", "cagg", "moff", "scan2", "cnt8T", "prec32", "mof16", "mof8"])
    p.carve("mid", [128, 32, 512], BF16)
    p.carve("acts", [128, 16, 512], BF16)
    p.carve("wtile", [128, 2, DC, 128], BF16)
    p.carve("w2tile", [128, 2, 32, 128], BF16)
    p.carve("cb", [128, 512], F32)
    p.carve("crow", [1, TLOC], F32)
    p.carve("b1_sb", [128, 32], F32)
    p.carve("b2_sb", [128, 8], F32)
    p.carve("vt", [128, 512], F32)

    slot_user = {0: p.cnt["te"], 1: p.cnt["te"]}
    nfi = 32
    cr_d = p.dint(f"{lw['tag']}_crow", [8, TLOC])
    for ei in range(8):
        if ei == 0:
            dma_sp(p, cr_d[:, :], p.sb["combT"][0:8, :], wait=[("dve", dv_comb)])
        crl = dma_sp(p, p.sb["crow"][0:1, :], cr_d[ei:ei + 1, :],
                     wait=[("te", p.cnt["te"])])
        if kind == "l1f":
            dma_sp(p, p.sb["b1_sb"][:, :], b1f[ei], wait=[("dve", p.cnt["dve"])])
            ldb = dma_sp(p, p.sb["b2_sb"][:, :], b2f[ei], wait=[("dve", p.cnt["dve"])])
        for tg in range(NRUN):
            sl = slice(tg * 512, tg * 512 + 512)
            for fi in range(nfi):
                slot = fi % 2
                wl = dma_pool(p, p.sb["wtile"][:, slot],
                              W1[ei].rearrange("(dcc pp) o -> pp dcc o", pp=128)[:, :, fi * 128:(fi + 1) * 128],
                              wait=[("te", slot_user[slot]), ("dve", dv_comb)])

                def mm1(e, slot=slot, sl=sl):
                    for dcc in range(DC):
                        last = e.matmul(bank(p, 0), p.sb["wtile"][:, slot, dcc, :],
                                        p.sb["xln"][:, dcc, sl], start=(dcc == 0), stop=(dcc == DC - 1))
                    return last
                te = p.emit("te", mm1, wait=[("wld", wl), ("xld", p.cnt["xld"]),
                                             ("dve", p.cnt["dve"]), ("act", p.cnt["act"])], inc="te")
                slot_user[slot] = te
                if kind == "l1f":
                    p.emit("dve", lambda e, fi=fi: e.tensor_scalar(
                        p.sb["mid"][:, fi, :], bank(p, 0), p.sb["b1_sb"][:, fi:fi + 1],
                        p.sb["zeroc"][:, 0:1], ALU.add, ALU.max), wait=[("te", te)], inc="dve")
                else:
                    if fi < 16:
                        p.emit("act", lambda e, fi=fi: e.activation(p.sb["mid"][:, fi, :],
                                                                    bank(p, 0), AF.Silu),
                               wait=[("te", te)], inc="act")
                    else:
                        p.emit("dve", lambda e, fi=fi: e.tensor_copy(p.sb["mid"][:, fi, :],
                                                                     bank(p, 0)),
                               wait=[("te", te)], inc="dve")
            if kind == "l0f":
                for fi in range(16):
                    p.emit("dve", lambda e, fi=fi: e.tensor_mul(
                        p.sb["acts"][:, fi, :], p.sb["mid"][:, fi, :], p.sb["mid"][:, 16 + fi, :]),
                        wait=[("act", p.cnt["act"])], inc="dve")
                msrc, nf2 = "acts", 16
            else:
                msrc, nf2 = "mid", 32
            te = p.emit("te", lambda e, sl=sl: e.matmul(bank(p, 6), p.sb["onesr_f"][0:1, :],
                                                        p.sb["crow"][0:1, sl], start=True, stop=True),
                        wait=[("xld", crl), ("dve", p.cnt["dve"]), ("act", p.cnt["act"])], inc="te")
            dvc = p.emit("dve", lambda e: e.tensor_copy(p.sb["cb"][:, :], bank(p, 6)),
                         wait=[("te", te)], inc="dve")
            for oc in range(DC):
                slot = oc % 2
                wl = dma_pool(p, p.sb["w2tile"][:, slot, 0:nf2, :],
                              W2[ei].rearrange("(ii pp) o -> pp ii o", pp=128)[:, 0:nf2, oc * 128:(oc + 1) * 128],
                              wait=[("te", slot_user[slot]), ("dve", dv_comb)])

                def mm2(e, slot=slot, nf2=nf2, msrc=msrc):
                    for ii in range(nf2):
                        last = e.matmul(bank(p, 1), p.sb["w2tile"][:, slot, ii, :],
                                        p.sb[msrc][:, ii, :], start=(ii == 0), stop=(ii == nf2 - 1))
                    return last
                te = p.emit("te", mm2, wait=[("wld", wl), ("dve", p.cnt["dve"]),
                                             ("act", p.cnt["act"])], inc="te")
                slot_user[slot] = te
                if kind == "l1f":
                    p.emit("dve", lambda e, oc=oc: e.tensor_scalar(
                        p.sb["vt"][:, :], bank(p, 1), p.sb["b2_sb"][:, oc:oc + 1], None, ALU.add),
                        wait=[("te", te)], inc="dve")
                else:
                    p.emit("dve", lambda e: e.tensor_copy(p.sb["vt"][:, :], bank(p, 1)),
                           wait=[("te", te)], inc="dve")
                p.emit("dve", lambda e: e.tensor_mul(p.sb["vt"][:, :], p.sb["vt"][:, :],
                                                     p.sb["cb"][:, :]), inc="dve")
                p.emit("dve", lambda e, oc=oc, sl=sl: e.tensor_add(
                    xres[:, oc, sl], xres[:, oc, sl], p.sb["vt"][:, :]), inc="dve")


def fm_vec(v):
    return np.ascontiguousarray(np.asarray(v).reshape(DC, 128).T.astype(np.float32))


def prep_host(inputs):
    g = {}
    pc = [dict() for _ in range(NCORE)]

    g["c_ident"] = np.eye(128, dtype=np.float32)
    hc = np.zeros((1, 16), dtype=np.float32)
    hc[0, HC_INV_D] = 1.0 / D
    hc[0, HC_EPS] = LN_EPS
    hc[0, HC_HALF] = 0.5
    hc[0, HC_Q2] = 0.125 * 0.125
    hc[0, HC_Q2L0] = (1.0 / 32) ** 2
    hc[0, HC_ZERO] = 0.0
    hc[0, HC_NEPS] = -LN_EPS
    hc[0, HC_ONE] = 1.0
    g["c_hc"] = hc

    x = np.asarray(inputs["L0_attn_x"])
    for c in range(NCORE):
        xr = np.empty((128, DC, TLOC), dtype=np.float32)
        for r, (b, j) in enumerate(runs_of_core(c)):
            seg = x[b, j * CH:(j + 1) * CH, :]
            xr[:, :, r * CH:(r + 1) * CH] = seg.T.reshape(DC, 128, CH).transpose(1, 0, 2)
        pc[c]["xres_in"] = np.ascontiguousarray(xr)

    for tag, pre in (("l2f", "L2_ffn"), ("l3f", "L3_ffn")):
        g[f"{tag}_WrT"] = np.ascontiguousarray(np.asarray(inputs[f"{pre}_W_r"]).T)
        g[f"{tag}_WkT"] = np.ascontiguousarray(np.asarray(inputs[f"{pre}_W_k"]).T)
        g[f"{tag}_WvT"] = np.ascontiguousarray(np.asarray(inputs[f"{pre}_W_v"]).T)
        g[f"{tag}_mur"] = fm_vec(inputs[f"{pre}_mu_r"])
        g[f"{tag}_muk"] = fm_vec(inputs[f"{pre}_mu_k"])

    WqT = np.asarray(inputs["L2_attn_W_q"]).T
    WkT = np.asarray(inputs["L2_attn_W_k"]).T
    WvT = np.asarray(inputs["L2_attn_W_v"]).T
    g["l2a_WoT"] = np.ascontiguousarray(np.asarray(inputs["L2_attn_W_o"]).T)
    g["l2a_bo"] = fm_vec(inputs["L2_attn_b_o"])
    for c in range(NCORE):
        hsl = slice(2 * c * 64, (2 * c + 2) * 64)
        pc[c]["l2a_WqT"] = np.ascontiguousarray(WqT[:, hsl])
        pc[c]["l2a_WkT"] = np.ascontiguousarray(WkT[:, hsl])
        pc[c]["l2a_WvT"] = np.ascontiguousarray(WvT[:, hsl])

    wkqv = np.asarray(inputs["L0_attn_w_kqv"])
    for c in range(NCORE):
        hsl = slice(2 * c * 64, (2 * c + 2) * 64)
        pc[c]["l0a_WkT"] = np.ascontiguousarray(wkqv[:, 0:1024][:, hsl])
        pc[c]["l0a_WqT"] = np.ascontiguousarray(wkqv[:, 1024:2048][:, hsl])
        pc[c]["l0a_WvT"] = np.ascontiguousarray(wkqv[:, 2048:3072][:, hsl])
        xx = (2.0 ** 8) ** (1.0 / 16)
        slopes = np.asarray([1.0 / xx ** (i + 1) for i in range(16)], dtype=np.float32)
        srow = np.zeros((128, 16), dtype=np.float32)
        srow[:, 0] = slopes[2 * c]
        srow[:, 1] = slopes[2 * c + 1]
        pc[c]["c_slopes"] = srow
    relh = np.zeros((128, 8, 512), dtype=np.float32)
    rell = np.zeros((128, 8, 512), dtype=np.float32)
    jq = np.arange(512)[None, :]
    for off in range(2):
        for sub in range(4):
            i = np.arange(128)[:, None]
            rel = (off - 1) * 512 + sub * 128 + i - jq
            lo = np.mod(rel, 128)
            hi = rel - lo
            relh[:, off * 4 + sub, :] = hi
            rell[:, off * 4 + sub, :] = lo
    g["c_relh"] = relh
    g["c_rell"] = rell
    wm = np.zeros((4, 128, 512), dtype=np.float32)
    for sub in range(4):
        for i in range(128):
            wm[sub, i, sub * 128 + i + 1:] = -30000.0
    g["c_winmask"] = wm

    Wq1 = np.asarray(inputs["L1_attn_Wq"]).reshape(D, 16 * 64)
    Wku = np.asarray(inputs["L1_attn_Wk_up"]).reshape(256, 16 * 64)
    Wvu = np.asarray(inputs["L1_attn_Wv_up"]).reshape(256, 16 * 64)
    g["l1a_Wkvd"] = np.ascontiguousarray(np.asarray(inputs["L1_attn_Wkv_down"]))
    g["l1a_WoT"] = np.ascontiguousarray(np.asarray(inputs["L1_attn_Wout"]).reshape(16 * 64, D))
    for c in range(NCORE):
        hsl = slice(2 * c * 64, (2 * c + 2) * 64)
        pc[c]["l1a_WqT"] = np.ascontiguousarray(Wq1[:, hsl])
        pc[c]["l1a_WkuT"] = np.ascontiguousarray(Wku[:, hsl])
        pc[c]["l1a_WvuT"] = np.ascontiguousarray(Wvu[:, hsl])

    dm = np.zeros((4, 128, 512), dtype=np.float32)
    for sub in range(4):
        for i in range(128):
            dm[sub, i, : min(512, sub * 128 + i)] = -30000.0
    g["c_diagmask"] = dm

    g["l0f_gwT"] = np.ascontiguousarray(np.asarray(inputs["L0_ffn_gate_w"]).T)
    g["l0f_W1"] = np.ascontiguousarray(np.asarray(inputs["L0_ffn_w13_weight"]).transpose(0, 2, 1))
    g["l0f_W2"] = np.ascontiguousarray(np.asarray(inputs["L0_ffn_w2_weight"]).transpose(0, 2, 1))
    g["l1f_gwT"] = np.ascontiguousarray(np.asarray(inputs["L1_ffn_gate_w"]).T)
    g["l1f_gb"] = np.ascontiguousarray(np.asarray(inputs["L1_ffn_gate_b"]).reshape(1, 8))
    g["l1f_W1"] = np.ascontiguousarray(np.asarray(inputs["L1_ffn_W1"]))
    g["l1f_W2"] = np.ascontiguousarray(np.asarray(inputs["L1_ffn_W2"]))
    g["l1f_b1"] = np.ascontiguousarray(np.asarray(inputs["L1_ffn_b1"]).reshape(8, 32, 128).transpose(0, 2, 1))
    g["l1f_b2"] = np.ascontiguousarray(np.asarray(inputs["L1_ffn_b2"]).reshape(8, 8, 128).transpose(0, 2, 1))
    for c in range(NCORE):
        own = runs_of_core(c)
        prec = np.zeros((32, 4), dtype=np.float32)
        for g_o in range(NCORE):
            for g_r, (gb_, gj) in enumerate(runs_of_core(g_o)):
                for r, (b_, j_) in enumerate(own):
                    if gb_ * 16 + gj < b_ * 16 + j_:
                        prec[g_o * 4 + g_r, r] = 1.0
        pc[c]["c_moeprec"] = prec

    g["l3a_wslow"] = np.ascontiguousarray(np.asarray(inputs["L3_attn_w_slow"]))
    g["l3a_wout"] = np.ascontiguousarray(np.asarray(inputs["L3_attn_w_out"]).reshape(D, D))
    g["l3a_lnw"] = fm_vec(inputs["L3_attn_ln_w"])
    g["l3a_lnb"] = fm_vec(inputs["L3_attn_ln_b"])

    for c in range(NCORE):
        sel = np.zeros((33, 4), dtype=np.float32)
        for r, (b, j) in enumerate(runs_of_core(c)):
            if j == 0:
                sel[32, r] = 1.0
            else:
                o = owner_of_chunk(j - 1)
                sel[o * 4 + run_index(o, b, j - 1), r] = 1.0
        pc[c]["c_bndsel"] = sel

    return g, pc


LAYER_RWKV = {
    "l2f": dict(tag="l2f", WrT="l2f_WrT", WkT="l2f_WkT", WvT="l2f_WvT", mur="l2f_mur", muk="l2f_muk"),
    "l3f": dict(tag="l3f", WrT="l3f_WrT", WkT="l3f_WkT", WvT="l3f_WvT", mur="l3f_mur", muk="l3f_muk"),
}


def build_program(block_ids, debug_stop=None):
    stack = ExitStack()
    nc = bass.Bass("TRN2", target_bir_lowering=False, debug=False)
    p = Prog(nc, stack)
    with stack:
        for s in ("wld", "xld", "te", "act", "dve", "cc"):
            p.sem(s)
        xin = p.din("xres_in", [128, DC, TLOC])
        yout = p.dout("xres_out", [128, DC, TLOC])
        alloc_common(p)
        ld = dma_sp(p, p.sb["xres"][:, :, :], xin[:, :, :])
        for e in ("te", "act", "dve", "pool"):
            p.waitpt(e, "xld", "wld", "dve")

        for bid in block_ids:
            p.block_fence()
            p.carve_reset()
            if bid in LAYER_RWKV:
                emit_rwkv(p, LAYER_RWKV[bid])
            elif bid == "l0a":
                emit_attn(p, dict(tag="l0a", kind="l0", WqT="l0a_WqT", WkT="l0a_WkT",
                                  WvT="l0a_WvT"), debug_stop=debug_stop)
            elif bid == "l2a":
                emit_attn(p, dict(tag="l2a", kind="l2", WqT="l2a_WqT", WkT="l2a_WkT",
                                  WvT="l2a_WvT", WoT="l2a_WoT", bo="l2a_bo"), debug_stop=debug_stop)
            elif bid == "l1a":
                emit_attn(p, dict(tag="l1a", kind="l1", WqT="l1a_WqT", Wkvd="l1a_Wkvd",
                                  WkuT="l1a_WkuT", WvuT="l1a_WvuT", WoT="l1a_WoT"),
                          debug_stop=debug_stop)
            elif bid == "l3a":
                emit_l3a(p, {}, debug_stop=debug_stop)
            elif bid == "l0f":
                emit_moe(p, dict(tag="l0f", kind="l0f", gwT="l0f_gwT", W1="l0f_W1", W2="l0f_W2"),
                         debug_stop=debug_stop)
            elif bid == "l1f":
                emit_moe(p, dict(tag="l1f", kind="l1f", gwT="l1f_gwT", W1="l1f_W1", W2="l1f_W2"),
                         debug_stop=debug_stop)
            else:
                raise NotImplementedError(bid)

        dma_sp(p, yout[:, :, :], p.sb["xres"][:, :, :],
               wait=[("dve", p.cnt["dve"]), ("te", p.cnt["te"]), ("act", p.cnt["act"])])
        p.waitpt("sp", "xld")
        p.build()
    return p


_GROUPS = [("l0a",), ("l0f",), ("l1a",), ("l1f",), ("l2a",), ("l2f",), ("l3a",), ("l3f",)]


def kernel(**inputs):
    glob, pc = prep_host(inputs)
    xcur = [pc[c]["xres_in"] for c in range(NCORE)]
    for grp in _GROUPS:
        p = build_program(grp)
        in_maps = []
        for c in range(NCORE):
            m = {}
            for name in p.inputs:
                if name == "xres_in":
                    m[name] = xcur[c]
                elif name in pc[c]:
                    m[name] = pc[c][name]
                else:
                    m[name] = glob[name]
            in_maps.append(m)
        res = run_bass_kernel_spmd(p.nc, in_maps, core_ids=list(range(NCORE)))
        xcur = [res.results[c]["xres_out"] for c in range(NCORE)]
    out = np.empty((B, S, D), dtype=np.float32)
    for c in range(NCORE):
        for r, (b, j) in enumerate(runs_of_core(c)):
            seg = xcur[c][:, :, r * CH:(r + 1) * CH]
            out[b, j * CH:(j + 1) * CH, :] = seg.transpose(1, 0, 2).reshape(D, CH).T
    return out


# version 03
# speedup vs baseline: 1.0099x; 1.0099x over previous
import numpy as np
import ml_dtypes
from contextlib import ExitStack

import concourse.bass as bass
import concourse.mybir as mybir
from concourse.bass_utils import run_bass_kernel_spmd
from concourse.aot_env import allow_ici_all_to_all

F32 = mybir.dt.float32
BF16 = mybir.dt.bfloat16
AF = mybir.ActivationFunctionType
ALU = mybir.AluOpType
BF = ml_dtypes.bfloat16

NCORE = 8
B, S, D = 2, 8192, 1024
DC = 8
CH = 512
NCH = 16
TLOC = 4 * CH
NRUN = 4
LN_EPS = 1e-5

BLOCKS = ["l0a", "l0f", "l1a", "l1f", "l2a", "l2f", "l3a", "l3f"]


def runs_of_core(c):
    return [(0, c), (0, 15 - c), (1, c), (1, 15 - c)]


def owner_of_chunk(j):
    return min(j, 15 - j)


def run_index(c, b, j):
    return runs_of_core(c).index((b, j))


class Prog:
    ENGINES = ("te", "act", "dve", "pool", "sp")

    def __init__(self, nc, stack):
        self.nc = nc
        self.stack = stack
        self.steps = {e: [] for e in self.ENGINES}
        self.semh = {}
        self.cnt = {}
        self.sb = {}
        self.dram = {}
        self.inputs = []
        self.ps = None
        self.pt = None

    def sem(self, name):
        if name not in self.semh:
            self.semh[name] = self.stack.enter_context(self.nc.semaphore(name))
            self.cnt[name] = 0
        return self.semh[name]

    def sbuf(self, name, shape, dtype):
        if name not in self.sb:
            self.sb[name] = self.stack.enter_context(self.nc.sbuf_tensor(name, shape, dtype))
        return self.sb[name]

    UNI_BYTES = 131008

    def alloc_uni(self):
        self.uni = self.stack.enter_context(
            self.nc.sbuf_tensor("UNI", [128, self.UNI_BYTES // 2], BF16))
        self.carve_off = 0
        self.carve_hw = 0
        self.carved = set()

    def carve_reset(self):
        for n in self.carved:
            self.sb.pop(n, None)
        self.carved = set()
        self.carve_off = 0

    def carve_mark(self):
        return self.carve_off

    def carve_rewind(self, mark, names):
        for n in names:
            if n in self.carved:
                self.carved.discard(n)
                self.sb.pop(n, None)
        self.carve_off = mark

    def carve(self, name, shape, dtype):
        if name in self.sb:
            return self.sb[name]
        import math as _m
        nb = int(np.prod(shape[1:])) * mybir.dt.size(dtype)
        nb_al = (nb + 63) & ~63
        off = self.carve_off
        assert off + nb_al <= self.UNI_BYTES, \
            f"UNI overflow carving {name}: off={off} need={nb_al} cap={self.UNI_BYTES}"
        view = self.uni[0:shape[0], off // 2:(off + nb) // 2]
        if dtype != BF16:
            view = view.bitcast(dtype)
        if len(shape) > 2:
            dims = " ".join(f"d{i}" for i in range(1, len(shape)))
            kw = {f"d{i}": shape[i] for i in range(1, len(shape))}
            view = view.rearrange(f"p ({dims}) -> p {dims}", **kw)
        self.sb[name] = view
        self.carved.add(name)
        self.carve_off = off + nb_al
        self.carve_hw = max(self.carve_hw, self.carve_off)
        return view

    def block_fence(self):
        sems = [s for s in ("te", "act", "dve", "xld", "wld", "cc") if s in self.cnt]
        for e in self.ENGINES:
            self.waitpt(e, *sems)

    def din(self, name, shape, dtype=F32):
        if name not in self.dram:
            self.dram[name] = self.nc.dram_tensor(name, shape, dtype, kind="ExternalInput")
            self.inputs.append(name)
        return self.dram[name].ap()

    def dout(self, name, shape, dtype=F32):
        if name not in self.dram:
            self.dram[name] = self.nc.dram_tensor(name, shape, dtype, kind="ExternalOutput")
        return self.dram[name].ap()

    def dint(self, name, shape, dtype=F32):
        if name not in self.dram:
            self.dram[name] = self.nc.dram_tensor(name, shape, dtype, kind="Internal")
        return self.dram[name].ap()

    def emit(self, eng, fn, wait=(), inc=None, inc_by=1):
        for s, _ in wait:
            self.sem(s)
        if inc:
            self.sem(inc)
        if eng in ("act", "dve"):
            wait = [(eng, self.cnt.get(eng, 0))] + list(wait)
        wait = tuple((s, v) for s, v in wait if v > 0)

        snap = dict(self.sb)

        def run(e, _fn=fn, _wait=wait, _inc=inc, _by=inc_by, _snap=snap):
            saved = self.sb
            self.sb = _snap
            try:
                for s, v in _wait:
                    e.wait_ge(self.semh[s], v)
                last = _fn(e)
                if _inc is not None:
                    assert last is not None, f"step on {eng} with inc={_inc} returned no instruction"
                    last.then_inc(self.semh[_inc], _by)
            finally:
                self.sb = saved
        self.steps[eng].append(run)
        if inc:
            self.cnt[inc] += inc_by
            return self.cnt[inc]
        return None

    def waitpt(self, eng, *sems):
        w = [(s, self.cnt.get(s, 0)) for s in sems]
        self.emit(eng, lambda e: None, wait=w)

    def build(self):
        eng_map = dict(te="tensor", act="scalar", dve="vector", pool="gpsimd", sp="sync")
        with self.nc.Block() as block:
            for eng in self.ENGINES:
                steps = self.steps[eng]
                getattr(block, eng_map[eng])(lambda e, _s=steps: [f(e) for f in _s])


def dma_pool(p, out_ap, in_ap, wait=(), sem="wld"):
    w = [(sem, p.cnt.get(sem, 0))] + list(wait)
    return p.emit("pool", lambda e: e.dma_start(out=out_ap, in_=in_ap),
                  wait=w, inc=sem, inc_by=16)


def dma_sp(p, out_ap, in_ap, wait=(), sem="xld"):
    w = [(sem, p.cnt.get(sem, 0))] + list(wait)
    return p.emit("sp", lambda e: e.dma_start(out=out_ap, in_=in_ap),
                  wait=w, inc=sem, inc_by=16)


def bank(p, i, rows=128, width=512):
    return p.ps[0:rows, i * 512:i * 512 + width]


HC_INV_D = 0
HC_EPS = 1
HC_HALF = 2
HC_Q2 = 3
HC_Q2L0 = 4
HC_ZERO = 5
HC_NEPS = 6
HC_ONE = 7


def alloc_common(p):
    p.sbuf("xres", [128, DC, TLOC], F32)
    p.sbuf("ones_bf", [128, 2], BF16)
    p.sbuf("ones_f", [128, 2], F32)
    p.sbuf("onesr_f", [1, 128], F32)
    p.sbuf("onesr_b", [1, 128], BF16)
    p.sbuf("hc", [1, 16], F32)
    p.sbuf("idf", [128, 128], F32)
    p.sbuf("idb", [128, 128], BF16)
    p.alloc_uni()
    p.ps = p.stack.enter_context(p.nc.psum_tensor("PS", [128, 7 * 512], F32))
    p.pt = p.stack.enter_context(p.nc.psum_tensor("PT", [128, 1024], BF16))

    idf = p.din("c_ident", [128, 128])
    hc = p.din("c_hc", [1, 16])
    dma_sp(p, p.sb["idf"][:, :], idf[:, :])
    dma_pool(p, p.sb["idb"][:, :], idf[:, :])
    ld = dma_sp(p, p.sb["hc"][:, :], hc[:, :])
    lw = p.cnt["wld"]
    p.emit("dve", lambda e: e.memset(p.sb["ones_bf"][:, :], 1.0),
           wait=[("wld", lw), ("xld", ld)], inc="dve")
    p.emit("dve", lambda e: e.memset(p.sb["ones_f"][:, :], 1.0), inc="dve")
    p.emit("dve", lambda e: e.memset(p.sb["onesr_f"][:, :], 1.0), inc="dve")
    p.emit("dve", lambda e: e.memset(p.sb["onesr_b"][:, :], 1.0), inc="dve")


def emit_ln(p, keep_xln=True, keep_bc=False, src="xres"):
    p.carve("stat_m", [1, TLOC], F32)
    p.carve("stat_r", [1, TLOC], F32)
    p.carve("lnst", [1, TLOC], F32)
    p.carve("lnsq", [128, 1024], BF16)
    p.carve("lnsq2", [128, 1024], BF16)
    xres, lnsq, lnsq2, hc = p.sb["xres"], p.sb["lnsq"], p.sb["lnsq2"], p.sb["hc"]
    if keep_bc:
        p.carve("mbc", [128, TLOC], F32)
        p.carve("rbc", [128, TLOC], F32)

    for half in range(2):
        c0 = half * 1024
        for dc in range(DC):
            if src == "xres":
                dv1 = p.emit("dve", lambda e, dc=dc, c0=c0: e.tensor_copy(
                    lnsq[:, :], xres[:, dc, c0:c0 + 1024]),
                    wait=[("te", p.cnt["te"])], inc="dve")
            else:
                dv1 = p.emit("dve", lambda e, dc=dc, c0=c0: e.tensor_copy(
                    lnsq[:, :], p.sb[src][:, dc, c0:c0 + 1024]),
                    wait=[("te", p.cnt["te"])], inc="dve")
            ac2 = p.emit("act", lambda e: e.activation(lnsq2[:, :], lnsq[:, :], AF.Square),
                         wait=[("dve", dv1)], inc="act")

            def mm(e, dc=dc):
                for tg2 in range(2):
                    sl = slice(tg2 * 512, tg2 * 512 + 512)
                    e.matmul(bank(p, tg2, rows=1), p.sb["ones_bf"][:, 0:1],
                             lnsq[:, sl], start=(dc == 0), stop=(dc == DC - 1))
                    last = e.matmul(bank(p, 2 + tg2, rows=1), p.sb["ones_bf"][:, 0:1],
                                    lnsq2[:, sl], start=(dc == 0), stop=(dc == DC - 1))
                return last
            p.emit("te", mm, wait=[("act", ac2), ("dve", dv1)], inc="te")
        te_done = p.cnt["te"]

        def evac(e, half=half):
            e.tensor_copy(p.sb["stat_m"][0:1, half * 1024:half * 1024 + 1024],
                          p.ps[0:1, 0:1024])
            return e.tensor_copy(p.sb["stat_r"][0:1, half * 1024:half * 1024 + 1024],
                                 p.ps[0:1, 1024:2048])
        p.emit("dve", evac, wait=[("te", te_done)], inc="dve")

    p.emit("dve", lambda e: e.tensor_scalar(p.sb["stat_m"][0:1, :], p.sb["stat_m"][0:1, :],
                                            hc[0:1, HC_INV_D:HC_INV_D + 1], None, ALU.mult), inc="dve")
    p.emit("dve", lambda e: e.tensor_scalar(p.sb["stat_r"][0:1, :], p.sb["stat_r"][0:1, :],
                                            hc[0:1, HC_INV_D:HC_INV_D + 1], None, ALU.mult), inc="dve")
    p.emit("dve", lambda e: e.tensor_mul(p.sb["lnst"][0:1, :], p.sb["stat_m"][0:1, :],
                                         p.sb["stat_m"][0:1, :]), inc="dve")
    dv = p.emit("dve", lambda e: e.tensor_sub(p.sb["stat_r"][0:1, :], p.sb["stat_r"][0:1, :],
                                              p.sb["lnst"][0:1, :]), inc="dve")
    ac = p.emit("act", lambda e: e.activation(p.sb["lnst"][0:1, :], p.sb["stat_r"][0:1, :],
                                              AF.Sqrt, bias=hc[0:1, HC_EPS:HC_EPS + 1]),
                wait=[("dve", dv)], inc="act")
    dv = p.emit("dve", lambda e: e.reciprocal(p.sb["stat_r"][0:1, :], p.sb["lnst"][0:1, :]),
                wait=[("act", ac)], inc="dve")

    if keep_xln or keep_bc:
        for tg in range(NRUN):
            sl = slice(tg * 512, tg * 512 + 512)

            def bc(e, sl=sl):
                e.matmul(bank(p, 4), p.sb["onesr_f"][0:1, :], p.sb["stat_m"][0:1, sl],
                         start=True, stop=True)
                return e.matmul(bank(p, 5), p.sb["onesr_f"][0:1, :], p.sb["stat_r"][0:1, sl],
                                start=True, stop=True)
            te = p.emit("te", bc, wait=[("dve", dv), ("dve", p.cnt["dve"]),
                                        ("act", p.cnt["act"])], inc="te")
            ac_sv = 0
            if keep_bc:
                def sv(e, sl=sl):
                    e.activation(p.sb["mbc"][:, sl], bank(p, 4), AF.Copy)
                    return e.activation(p.sb["rbc"][:, sl], bank(p, 5), AF.Copy)
                ac_sv = p.emit("act", sv, wait=[("te", te)], inc="act")
            if keep_xln:
                p.carve("xln", [128, DC, TLOC], BF16)
                xln = p.sb["xln"]

                def ap1(e, sl=sl):
                    mb = bank(p, 4).unsqueeze(1).broadcast_to([128, DC, 512])
                    return e.tensor_tensor(xln[:, :, sl], xres[:, :, sl], mb, ALU.subtract)
                p.emit("dve", ap1, wait=[("te", te), ("act", ac_sv)], inc="dve")

                def ap2(e, sl=sl):
                    rb = bank(p, 5).unsqueeze(1).broadcast_to([128, DC, 512])
                    return e.tensor_tensor(xln[:, :, sl], xln[:, :, sl], rb, ALU.mult)
                p.emit("dve", ap2, inc="dve")


def emit_rwkv(p, lw):
    tag = lw["tag"]
    WrT = p.din(lw["WrT"], [D, D])
    WkT = p.din(lw["WkT"], [D, D])
    WvT = p.din(lw["WvT"], [D, D])
    mur = p.din(lw["mur"], [128, DC])
    muk = p.din(lw["muk"], [128, DC])
    bndsel = p.din("c_bndsel", [33, 4])

    xres = p.sb["xres"]
    p.carve("mu2", [128, 2, DC], F32)
    p.carve("bsel33", [33, 4], F32)
    p.carve("xlf", [128, DC, CH], F32)
    p.carve("scr_f1", [128, DC, CH], F32)
    p.carve("sbbnd", [128, NRUN, DC], F32)
    p.carve("x0col", [128, DC, NRUN], F32)
    p.carve("bndt", [33, 1024], F32)
    p.carve("bndev", [4, 1024], F32)
    p.carve("xprev0", [128, DC, NRUN], F32)
    p.carve("xrmix", [128, DC, CH], BF16)
    p.carve("xkmix", [128, DC, CH], BF16)
    p.carve("wtile", [128, 2, DC, 128], BF16)
    p.carve("rgate", [128, DC, CH], F32)
    p.carve("kact", [128, DC, CH], BF16)
    p.carve("vtmp", [128, CH], F32)
    p.carve("vtmp2", [128, CH], F32)
    p.carve("d0scr", [128, DC], F32)

    bnd_in = p.dint(f"{tag}_bnd_in", [NRUN, 1024])
    bnd_all = p.dint(f"{tag}_bnd_all", [4 * NCORE, 1024])

    dma_sp(p, p.sb["mu2"][:, 0, :], mur[:, :])
    dma_sp(p, p.sb["mu2"][:, 1, :], muk[:, :])
    dma_sp(p, p.sb["bsel33"][:, :], bndsel[:, :])
    p.waitpt("dve", "xld")

    emit_ln(p, keep_xln=False, keep_bc=False)

    def bcast_stats(r):
        sl = slice(r * 512, r * 512 + 512)

        def bc(e, sl=sl):
            e.matmul(bank(p, 4), p.sb["onesr_f"][0:1, :], p.sb["stat_m"][0:1, sl],
                     start=True, stop=True)
            return e.matmul(bank(p, 5), p.sb["onesr_f"][0:1, :], p.sb["stat_r"][0:1, sl],
                            start=True, stop=True)
        return p.emit("te", bc, wait=[("dve", p.cnt["dve"]), ("act", p.cnt["act"])], inc="te")

    def mk_xlf(r, te):
        sl = slice(r * 512, r * 512 + 512)

        def f1(e, sl=sl):
            mb = bank(p, 4).unsqueeze(1).broadcast_to([128, DC, 512])
            return e.tensor_tensor(p.sb["xlf"][:, :, :], xres[:, :, sl], mb, ALU.subtract)

        def f2(e, sl=sl):
            rb = bank(p, 5).unsqueeze(1).broadcast_to([128, DC, 512])
            return e.tensor_tensor(p.sb["xlf"][:, :, :], p.sb["xlf"][:, :, :], rb, ALU.mult)
        p.emit("dve", f1, wait=[("te", te)], inc="dve")
        return p.emit("dve", f2, inc="dve")

    for r in range(NRUN):
        te = bcast_stats(r)
        mk_xlf(r, te)
        p.emit("dve", lambda e, r=r: e.tensor_copy(p.sb["sbbnd"][:, r, :], p.sb["xlf"][:, :, 511]),
               inc="dve")
        p.emit("dve", lambda e, r=r: e.tensor_copy(p.sb["x0col"][:, :, r], p.sb["xlf"][:, :, 0]),
               inc="dve")
    dv_pub = p.cnt["dve"]

    st = p.cnt["xld"]
    for r in range(NRUN):
        st = dma_sp(p, bnd_in[r].rearrange("(pp dcc) -> pp dcc", pp=128),
                    p.sb["sbbnd"][:, r, :],
                    wait=[("dve", dv_pub)] if r == 0 else ())

    def coll(e):
        return e.collective_compute(
            "AllGather", ALU.bypass, replica_groups=[list(range(NCORE))],
            ins=[bnd_in.opt()], outs=[bnd_all.opt()])
    cc = p.emit("pool", coll, wait=[("xld", st)], inc="cc")
    ld = dma_sp(p, p.sb["bndt"][0:32, :], bnd_all[:, :], wait=[("cc", cc)])
    dvz = p.emit("dve", lambda e: e.memset(p.sb["bndt"][32:33, :], 0.0),
                 wait=[("xld", ld)], inc="dve")

    def gmm(e):
        e.matmul(bank(p, 4, rows=4), p.sb["bsel33"][:, :], p.sb["bndt"][:, 0:512],
                 start=True, stop=True)
        return e.matmul(bank(p, 5, rows=4), p.sb["bsel33"][:, :], p.sb["bndt"][:, 512:1024],
                        start=True, stop=True)
    te = p.emit("te", gmm, wait=[("dve", dvz)], inc="te")

    def gev(e):
        e.tensor_copy(p.sb["bndev"][0:4, 0:512], bank(p, 4, rows=4))
        return e.tensor_copy(p.sb["bndev"][0:4, 512:1024], bank(p, 5, rows=4))
    dv = p.emit("dve", gev, wait=[("te", te)], inc="dve")

    ac_xp = p.cnt["act"]
    for dc in range(DC):
        def tr(e, dc=dc):
            src = p.sb["bndev"][0:4, :].rearrange("r (pp dcc) -> r pp dcc", pp=128)[:, :, dc]
            return e.transpose(bank(p, 6, rows=128, width=4), src, p.sb["idf"][0:4, 0:4])
        te = p.emit("te", tr, wait=[("dve", dv), ("act", ac_xp)], inc="te")
        ac_xp = p.emit("act", lambda e, dc=dc: e.activation(
            p.sb["xprev0"][:, dc, :], bank(p, 6, rows=128, width=4), AF.Copy),
            wait=[("te", te)], inc="act")

    wtile = p.sb["wtile"]
    slot_user = {0: 0, 1: 0}
    for tg in range(NRUN):
        sl = slice(tg * 512, tg * 512 + 512)
        te = bcast_stats(tg)
        p.waitpt("dve", "act")
        mk_xlf(tg, te)
        def mixsub(e, tg=tg):
            xv = p.sb["xlf"]
            e.tensor_tensor(p.sb["scr_f1"][:, :, 1:512], xv[:, :, 1:512], xv[:, :, 0:511],
                            ALU.subtract)
            return e.tensor_sub(p.sb["d0scr"][:, :], p.sb["x0col"][:, :, tg], p.sb["xprev0"][:, :, tg])
        p.emit("dve", mixsub, inc="dve")
        for mi, out in ((0, "xrmix"), (1, "xkmix")):
            def mix(e, tg=tg, mi=mi, out=out):
                xv = p.sb["xlf"]
                ov = p.sb[out]
                for dc in range(DC):
                    e.scalar_tensor_tensor(
                        ov[:, dc, 1:512], p.sb["scr_f1"][:, dc, 1:512],
                        p.sb["mu2"][:, mi, dc:dc + 1], xv[:, dc, 0:511], ALU.mult, ALU.add)
                    last = e.scalar_tensor_tensor(
                        ov[:, dc, 0:1], p.sb["d0scr"][:, dc:dc + 1],
                        p.sb["mu2"][:, mi, dc:dc + 1], p.sb["xprev0"][:, dc, tg:tg + 1],
                        ALU.mult, ALU.add)
                return last
            p.emit("dve", mix, inc="dve")
        dv_mix = p.cnt["dve"]

        for pi, (Wap, src) in enumerate(((WrT, "xrmix"), (WkT, "xkmix"))):
            for oc in range(DC):
                slot = (pi * 8 + oc) % 2
                wl = dma_pool(p, wtile[:, slot],
                              Wap.rearrange("(dcc pp) o -> pp dcc o", pp=128)[:, :, oc * 128:(oc + 1) * 128],
                              wait=[("te", slot_user[slot])])

                def mm(e, pi=pi, slot=slot, src=src):
                    for dcc in range(DC):
                        last = e.matmul(bank(p, pi), wtile[:, slot, dcc, :],
                                        p.sb[src][:, dcc, :],
                                        start=(dcc == 0), stop=(dcc == DC - 1))
                    return last
                te = p.emit("te", mm, wait=[("wld", wl), ("dve", dv_mix),
                                            ("act", p.cnt["act"])], inc="te")
                slot_user[slot] = te
                if pi == 0:
                    p.emit("act", lambda e, oc=oc: e.activation(
                        p.sb["rgate"][:, oc, :], bank(p, 0), AF.Sigmoid),
                        wait=[("te", te)], inc="act")
                else:
                    dvr = p.emit("dve", lambda e: e.tensor_scalar_max(
                        p.sb["vtmp"][:, :], bank(p, 1), 0.0), wait=[("te", te)], inc="dve")
                    p.emit("act", lambda e, oc=oc: e.activation(
                        p.sb["kact"][:, oc, :], p.sb["vtmp"][:, :], AF.Square),
                        wait=[("dve", dvr)], inc="act")
        ac_k = p.cnt["act"]
        for oc in range(DC):
            slot = oc % 2
            wl = dma_pool(p, wtile[:, slot],
                          WvT.rearrange("(dcc pp) o -> pp dcc o", pp=128)[:, :, oc * 128:(oc + 1) * 128],
                          wait=[("te", slot_user[slot])])

            def mmv(e, slot=slot):
                for dcc in range(DC):
                    last = e.matmul(bank(p, 2), wtile[:, slot, dcc, :], p.sb["kact"][:, dcc, :],
                                    start=(dcc == 0), stop=(dcc == DC - 1))
                return last
            te = p.emit("te", mmv, wait=[("wld", wl), ("act", ac_k),
                                         ("dve", p.cnt["dve"])], inc="te")
            slot_user[slot] = te

            p.emit("dve", lambda e, oc=oc: e.tensor_tensor(
                p.sb["vtmp2"][:, :], p.sb["rgate"][:, oc, :], bank(p, 2), ALU.mult),
                wait=[("te", te)], inc="dve")
            p.emit("dve", lambda e, oc=oc, sl=sl: e.tensor_add(
                xres[:, oc, sl], xres[:, oc, sl], p.sb["vtmp2"][:, :]), inc="dve")


def bcol(j):
    o = owner_of_chunk(j)
    return o * 1024 + (0 if j == o else 512)


def stagecol(b, j):
    o = owner_of_chunk(j)
    ro = 2 * b + (0 if j == o else 1)
    return o, ro


def va_slot(j, sub):
    return (bcol(j) // 128) + sub


def emit_attn(p, lw, debug_stop=None):
    tag = lw["tag"]
    kind = lw["kind"]
    xres = p.sb["xres"]

    NAUG = 65
    WoT = p.din(lw["WoT"], [D, D]) if kind != "l0" else None
    WqT = p.din(lw["WqT"], [D, 128])
    if kind == "l0":
        WkT = p.din(lw["WkT"], [D, 128])
        WvT = p.din(lw["WvT"], [D, 128])
        slopes_in = p.din("c_slopes", [128, 16])
        relh_in = p.din("c_relh", [128, 8, 512])
        rell_in = p.din("c_rell", [128, 8, 512])
        winm_in = p.din("c_winmask", [4, 128, 512])
    elif kind == "l2":
        WkT = p.din(lw["WkT"], [D, 128])
        WvT = p.din(lw["WvT"], [D, 128])
        bo = p.din(lw["bo"], [128, DC])
    else:
        Wkvd = p.din(lw["Wkvd"], [D, 256])
        WkuT = p.din(lw["WkuT"], [256, 128])
        WvuT = p.din(lw["WvuT"], [256, 128])
    diagm_in = p.din("c_diagmask", [4, 128, 512])

    p.carve("xln", [128, DC, TLOC], BF16)
    p.carve("xs", [128, DC, 512], BF16)
    if kind == "l0":
        p.carve("winm", [128, 4, 512], BF16)
        p.carve("slp", [128, 16], F32)
        p.carve("relh", [128, 8, 512], BF16)
        p.carve("rell", [128, 8, 512], BF16)
        p.carve("sidb", [128, 128], BF16)
    p.carve("wq_sb", [128, DC, 128], BF16)
    p.carve("wk_sb", [128, DC, 128], BF16)
    p.carve("wv_sb", [128, DC, 128], BF16)
    p.carve("diagm", [128, 4, 512], BF16)
    p.carve("qsc", [128, 1], F32)
    p.carve("kmx", [2, 4], F32)
    if kind != "l0":
        p.carve("wtile", [128, 2, DC, 128], BF16)
        p.carve("vtmp", [128, CH], F32)
    p.carve("bo_sb", [128, DC], F32)
    if kind == "l1":
        p.carve("lat_sb", [128, 2, 512], BF16)
        p.carve("wkvd_sb", [128, DC, 256], BF16)

    ag_in = p.dint(f"{tag}_ag_in", [128, DC * TLOC], BF16)
    xag = p.dint(f"{tag}_xag", [NCORE, 128, DC, TLOC], BF16)
    a2a_in = p.dint(f"{tag}_a2a_in", [NCORE * 128, TLOC], BF16)
    a2a_out = p.dint(f"{tag}_a2a_out", [NCORE * 128, TLOC], BF16)
    if kind == "l1":
        lat_in = p.dint(f"{tag}_lat_in", [128, 2 * TLOC], BF16)
        lat_all = p.dint(f"{tag}_lat_all", [NCORE, 128, 2, TLOC], BF16)

    dma_pool(p, p.sb["wq_sb"][:, :, :], WqT.rearrange("(dcc pp) o -> pp dcc o", pp=128))
    if kind == "l0":
        dma_pool(p, p.sb["wk_sb"][:, :, :], WkT.rearrange("(dcc pp) o -> pp dcc o", pp=128))
        dma_pool(p, p.sb["wv_sb"][:, :, :], WvT.rearrange("(dcc pp) o -> pp dcc o", pp=128))
        dma_pool(p, p.sb["winm"][:, :, :], winm_in.rearrange("m pp q -> pp m q"))
        dma_sp(p, p.sb["slp"][:, :], slopes_in[:, :])
        dma_pool(p, p.sb["relh"][:, :, :], relh_in[:, :, :])
        dma_pool(p, p.sb["rell"][:, :, :], rell_in[:, :, :])
        p.emit("dve", lambda e: e.memset(p.sb["bo_sb"][:, :], 0.0), inc="dve")
    elif kind == "l2":
        dma_pool(p, p.sb["wk_sb"][:, :, :], WkT.rearrange("(dcc pp) o -> pp dcc o", pp=128))
        dma_pool(p, p.sb["wv_sb"][:, :, :], WvT.rearrange("(dcc pp) o -> pp dcc o", pp=128))
        dma_pool(p, p.sb["bo_sb"][:, :], bo[:, :])
    else:
        dma_pool(p, p.sb["wk_sb"][:, 0:2, :], WkuT.rearrange("(lc pp) o -> pp lc o", pp=128))
        dma_pool(p, p.sb["wv_sb"][:, 0:2, :], WvuT.rearrange("(lc pp) o -> pp lc o", pp=128))
        p.emit("dve", lambda e: e.memset(p.sb["bo_sb"][:, :], 0.0), inc="dve")
    dma_pool(p, p.sb["diagm"][:, :, :], diagm_in.rearrange("m pp q -> pp m q"))
    p.emit("dve", lambda e: e.memset(p.sb["qsc"][:, :], 0.125 if kind != "l0" else 0.03125),
           inc="dve")

    mark = p.carve_mark()
    emit_ln(p, keep_xln=True)
    st = dma_sp(p, ag_in[:, :], p.sb["xln"].rearrange("p dc t -> p (dc t)"),
                wait=[("dve", p.cnt["dve"])])
    cc = p.emit("pool", lambda e: e.collective_compute(
        "AllGather", ALU.bypass, replica_groups=[list(range(NCORE))],
        ins=[ag_in.opt()], outs=[xag.rearrange("o pp f1 f2 -> (o pp) (f1 f2)").opt()]),
        wait=[("xld", st), ("wld", p.cnt["wld"])], inc="cc")

    if debug_stop == "ag":
        return
    if kind == "l1":
        dma_pool(p, p.sb["wkvd_sb"][:, :, :], Wkvd.rearrange("(dcc pp) o -> pp dcc o", pp=128))
        wl = p.cnt["wld"]
        for tg in range(NRUN):
            sl = slice(tg * 512, tg * 512 + 512)
            for lc in range(2):
                def mml(e, sl=sl, lc=lc):
                    for dcc in range(DC):
                        last = e.matmul(bank(p, 2), p.sb["wkvd_sb"][:, dcc, lc * 128:(lc + 1) * 128],
                                        p.sb["xln"][:, dcc, sl], start=(dcc == 0), stop=(dcc == DC - 1))
                    return last
                te = p.emit("te", mml, wait=[("wld", wl), ("dve", p.cnt["dve"]),
                                             ("act", p.cnt["act"]), ("xld", p.cnt["xld"])], inc="te")
                p.emit("act", lambda e, lc=lc: e.activation(
                    p.sb["lat_sb"][:, lc, :], bank(p, 2), AF.Copy),
                    wait=[("te", te), ("xld", p.cnt["xld"])], inc="act")
            dma_sp(p, lat_in.rearrange("p (l t) -> p l t", l=2)[:, :, sl],
                   p.sb["lat_sb"][:, :, :], wait=[("act", p.cnt["act"])])
        cc = p.emit("pool", lambda e: e.collective_compute(
            "AllGather", ALU.bypass, replica_groups=[list(range(NCORE))],
            ins=[lat_in.opt()], outs=[lat_all.rearrange("o pp f1 f2 -> (o pp) (f1 f2)").opt()]),
            wait=[("xld", p.cnt["xld"])], inc="cc")

    p.carve_rewind(mark, ["stat_m", "stat_r", "lnst", "lnsq", "lnsq2"])
    p.carve("lnsq", [128, 512], BF16)
    p.carve("KA", [NAUG, 8192], BF16)
    p.carve("QA", [NAUG, 8192], BF16)
    p.carve("VA", [128, 64, 65], BF16)
    p.carve("ptile", [128, 2, 1024], BF16)
    p.carve("vfm", [64, 512], BF16)
    p.carve("ofsb", [64, 512], F32)
    p.carve("rden", [1, 512], F32)

    def initk(e):
        e.memset(p.sb["KA"][64:65, :], -1.0)
        return e.memset(p.sb["VA"][:, :, 64:65], 1.0)
    p.emit("dve", initk, wait=[("xld", p.cnt["xld"]), ("te", p.cnt["te"]),
                               ("act", p.cnt["act"])], inc="dve")

    if debug_stop == "init":
        return
    if debug_stop == "xsdma":
        dma_sp(p, p.sb["xs"][:, :, :], xag[3][:, :, 512:1024], wait=[("cc", p.cnt["cc"])])
        dma_sp(p, p.sb["lnsq"][0:64, :], xag[5][0:64, 2, 1024:1536], wait=[("cc", p.cnt["cc"])])
        return

    for b in range(2):
        for hl in range(2):
            hsl = slice(hl * 64, hl * 64 + 64)
            dvk = p.emit("dve", lambda e: e.memset(p.sb["kmx"][0:1, 0:4], 0.0), inc="dve")
            if kind == "l0":
                p.emit("dve", lambda e, hl=hl: e.tensor_scalar(
                    p.sb["sidb"][:, :], p.sb["idb"][:, :], p.sb["slp"][:, hl:hl + 1],
                    None, ALU.mult), inc="dve")
            xs_user = p.cnt["te"]
            ntg16 = 1 if (debug_stop or "").startswith("projk1") else 16
            if (debug_stop or "").startswith("projk1") and (b > 0 or hl > 0):
                ntg16 = 0
            for tg16 in range(ntg16):
                o, half = tg16 // 2, tg16 % 2
                lcol = 2 * b * 512 + half * 512
                gcol = o * 1024 + half * 512
                xsl = dma_sp(p, p.sb["xs"][:, :, :],
                             xag[o][:, :, lcol:lcol + 512],
                             wait=[("te", xs_user), ("cc", p.cnt["cc"])])
                if kind == "l1":
                    xsl = dma_sp(p, p.sb["lat_sb"][:, :, :],
                                 lat_all[o][:, :, lcol:lcol + 512],
                                 wait=[("te", xs_user), ("cc", p.cnt["cc"])])

                def mmk(e, hsl=hsl):
                    if kind != "l1":
                        for dcc in range(DC):
                            last = e.matmul(bank(p, 2, rows=64), p.sb["wk_sb"][:, dcc, hsl],
                                            p.sb["xs"][:, dcc, :], start=(dcc == 0), stop=(dcc == DC - 1))
                    else:
                        for lc in range(2):
                            last = e.matmul(bank(p, 2, rows=64), p.sb["wk_sb"][:, lc, hsl],
                                            p.sb["lat_sb"][:, lc, :], start=(lc == 0), stop=(lc == 1))
                    return last
                te = p.emit("te", mmk, wait=[("xld", xsl), ("wld", p.cnt["wld"]),
                                             ("dve", p.cnt["dve"]), ("act", p.cnt["act"])], inc="te")
                dvka = p.emit("dve", lambda e, gcol=gcol: e.tensor_copy(
                    p.sb["KA"][0:64, gcol:gcol + 512], bank(p, 2, rows=64)),
                    wait=[("te", te)], inc="dve")
                if debug_stop == "projk1a":
                    xs_user = p.cnt["te"]
                    continue
                if debug_stop == "projk1bA":
                    p.sbuf("realsq", [128, 512], BF16)
                    p.emit("act", lambda e: e.activation(p.sb["realsq"][0:64, :],
                                                         bank(p, 2, rows=64), AF.Square),
                           wait=[("te", te), ("dve", dvka)], inc="act")
                    xs_user = p.cnt["te"]
                    continue
                if debug_stop == "projk1bB":
                    p.emit("act", lambda e: e.activation(p.sb["lnsq"][0:64, :],
                                                         bank(p, 2, rows=64), AF.Copy),
                           wait=[("te", te), ("dve", dvka)], inc="act")
                    xs_user = p.cnt["te"]
                    continue
                ac = p.emit("act", lambda e: e.activation(p.sb["lnsq"][0:64, :],
                                                          bank(p, 2, rows=64), AF.Square),
                            wait=[("te", te), ("dve", dvka)], inc="act")
                if debug_stop == "projk1b0":
                    xs_user = p.cnt["te"]
                    continue
                te2 = p.emit("te", lambda e: e.matmul(bank(p, 5, rows=1), p.sb["ones_bf"][0:64, 0:1],
                                                      p.sb["lnsq"][0:64, :], start=True, stop=True),
                             wait=[("act", ac)], inc="te")

                if debug_stop == "projk1b":
                    xs_user = p.cnt["te"]
                    continue

                p.emit("dve", lambda e: e.reduce_max(p.sb["rden"][0:1, 0:1], bank(p, 5, rows=1),
                                                     axis=mybir.AxisListType.X),
                       wait=[("te", te2)], inc="dve")
                p.emit("dve", lambda e: e.tensor_max(p.sb["kmx"][0:1, 0:1], p.sb["kmx"][0:1, 0:1],
                                                     p.sb["rden"][0:1, 0:1]), inc="dve")
                if debug_stop == "projk1c":
                    xs_user = p.cnt["te"]
                    continue

                if debug_stop == "projk":
                    xs_user = p.cnt["te"]
                    continue
                def mmq(e, hsl=hsl):
                    for dcc in range(DC):
                        last = e.matmul(bank(p, 3, rows=64), p.sb["wq_sb"][:, dcc, hsl],
                                        p.sb["xs"][:, dcc, :], start=(dcc == 0), stop=(dcc == DC - 1))
                    return last
                te = p.emit("te", mmq, wait=[("act", p.cnt["act"]), ("dve", p.cnt["dve"])], inc="te")
                dvqa = p.emit("dve", lambda e, gcol=gcol: e.tensor_scalar(
                    p.sb["QA"][0:64, gcol:gcol + 512], bank(p, 3, rows=64),
                    p.sb["qsc"][0:64, 0:1], None, ALU.mult), wait=[("te", te)], inc="dve")
                ac = p.emit("act", lambda e: e.activation(p.sb["lnsq"][0:64, :],
                                                          bank(p, 3, rows=64), AF.Square),
                            wait=[("te", te), ("dve", dvqa)], inc="act")
                te2 = p.emit("te", lambda e: e.matmul(bank(p, 6, rows=1), p.sb["ones_bf"][0:64, 0:1],
                                                      p.sb["lnsq"][0:64, :], start=True, stop=True),
                             wait=[("act", ac)], inc="te")
                p.emit("dve", lambda e, gcol=gcol: e.tensor_copy(
                    p.sb["QA"][64:65, gcol:gcol + 512], bank(p, 6, rows=1)),
                    wait=[("te", te2)], inc="dve")

                if debug_stop == "projq":
                    xs_user = p.cnt["te"]
                    continue
                def mmv(e, hsl=hsl):
                    if kind != "l1":
                        for dcc in range(DC):
                            last = e.matmul(bank(p, 4, rows=64), p.sb["wv_sb"][:, dcc, hsl],
                                            p.sb["xs"][:, dcc, :], start=(dcc == 0), stop=(dcc == DC - 1))
                    else:
                        for lc in range(2):
                            last = e.matmul(bank(p, 4, rows=64), p.sb["wv_sb"][:, lc, hsl],
                                            p.sb["lat_sb"][:, lc, :], start=(lc == 0), stop=(lc == 1))
                    return last
                te = p.emit("te", mmv, wait=[("act", p.cnt["act"]), ("dve", p.cnt["dve"])], inc="te")
                ac = p.emit("act", lambda e: e.activation(p.sb["vfm"][0:64, :], bank(p, 4, rows=64),
                                                          AF.Copy), wait=[("te", te)], inc="act")
                jj = o if half == 0 else 15 - o
                for sub in range(4):
                    tt = p.emit("te", lambda e, sub=sub: e.transpose(
                        p.pt[:, 0:64], p.sb["vfm"][0:64, sub * 128:(sub + 1) * 128],
                        p.sb["idb"][0:64, 0:64]), wait=[("act", ac), ("dve", p.cnt["dve"])], inc="te")
                    slot = va_slot(jj, sub)
                    p.emit("dve", lambda e, slot=slot: e.tensor_copy(
                        p.sb["VA"][:, slot, 0:64], p.pt[:, 0:64]), wait=[("te", tt)], inc="dve")
                xs_user = p.cnt["te"]

            if debug_stop is not None and debug_stop.startswith("proj"):
                continue
            hcq = HC_Q2L0 if kind == "l0" else HC_Q2
            p.emit("dve", lambda e, hcq=hcq: e.tensor_scalar(p.sb["QA"][64:65, :], p.sb["QA"][64:65, :],
                                                    p.sb["hc"][0:1, hcq:hcq + 1], None,
                                                    ALU.mult), inc="dve")
            dvm = p.emit("dve", lambda e: e.tensor_scalar(p.sb["QA"][64:65, :], p.sb["QA"][64:65, :],
                                                          p.sb["kmx"][0:1, 0:1],
                                                          p.sb["hc"][0:1, HC_HALF:HC_HALF + 1],
                                                          ALU.add, ALU.mult), inc="dve")

            prev_pair_user = {0: 0, 1: 0}
            av_user = {0: 0, 1: 0}
            for j in range(NCH):
                if kind == "l0":
                    jks = ([j] if j == 0 else [j - 1, j])
                else:
                    jks = list(range(j + 1))
                nkb = 4 * len(jks)
                qg = bcol(j)
                o_j, ro_j = stagecol(b, j)
                dv_j0 = p.cnt["dve"]
                ac_j0 = p.cnt["act"]
                for kb in range(nkb):
                    jk, sub = jks[kb // 4], kb % 4
                    kc = bcol(jk) + sub * 128
                    pair = (kb // 2) % 2
                    bk = (2 if pair == 0 else 4) + (kb % 2)
                    diag = (jk == j)
                    winedge = (kind == "l0" and jk == j - 1)

                    def qk(e, qg=qg, kc=kc, bk=bk, diag=diag, sub=sub, winedge=winedge):
                        last = e.matmul(bank(p, bk), p.sb["KA"][:, kc:kc + 128],
                                        p.sb["QA"][:, qg:qg + 512], start=True,
                                        stop=not (diag or winedge))
                        if diag:
                            last = e.matmul(bank(p, bk), p.sb["idb"][:, :], p.sb["diagm"][:, sub, :],
                                            start=False, stop=(kind != "l0"))
                        elif winedge:
                            last = e.matmul(bank(p, bk), p.sb["idb"][:, :], p.sb["winm"][:, sub, :],
                                            start=False, stop=False)
                        if kind == "l0":
                            mi = (1 if diag else 0) * 4 + sub
                            e.matmul(bank(p, bk), p.sb["sidb"][:, :], p.sb["relh"][:, mi, :],
                                     start=False, stop=False)
                            last = e.matmul(bank(p, bk), p.sb["sidb"][:, :], p.sb["rell"][:, mi, :],
                                            start=False, stop=True)
                        return last
                    p.emit("te", qk, wait=[("dve", dvm), ("act", max(prev_pair_user[pair], ac_j0))],
                           inc="te")
                    if kb % 2 == 1:
                        teqk = p.cnt["te"]
                        slot = (kb // 2) % 2

                        def expf(e, pair=pair, slot=slot):
                            c0 = (2 if pair == 0 else 4) * 512
                            return e.activation(p.sb["ptile"][:, slot, :], p.ps[:, c0:c0 + 1024],
                                                AF.Exp)
                        ac = p.emit("act", expf, wait=[("te", teqk), ("te", av_user[slot])],
                                    inc="act")
                        prev_pair_user[pair] = ac
                        for q2 in range(2):
                            kb2 = kb - 1 + q2
                            vs = va_slot(jks[kb2 // 4], kb2 % 4)

                            def av(e, vs=vs, slot=slot, q2=q2, kb2=kb2, nkb=nkb):
                                return e.matmul(bank(p, 0, rows=65), p.sb["VA"][:, vs, :],
                                                p.sb["ptile"][:, slot, q2 * 512:(q2 + 1) * 512],
                                                start=(kb2 == 0), stop=(kb2 == nkb - 1))
                            p.emit("te", av, wait=[("act", ac), ("dve", dv_j0)], inc="te")
                        av_user[slot] = p.cnt["te"]
                teav = p.cnt["te"]
                dv = p.emit("dve", lambda e: e.reciprocal(p.sb["rden"][0:1, 0:512],
                                                          p.ps[64:65, 0:512]),
                            wait=[("te", teav)], inc="dve")
                te6 = p.emit("te", lambda e: e.matmul(bank(p, 6, rows=64), p.sb["onesr_f"][0:1, 0:64],
                                                      p.sb["rden"][0:1, 0:512], start=True, stop=True),
                             wait=[("dve", dv)], inc="te")
                ac = p.emit("act", lambda e: e.activation(p.sb["ofsb"][0:64, :], bank(p, 0, rows=65)[0:64, :],
                                                          AF.Copy), wait=[("te", teav), ("dve", dv)], inc="act")

                def norm(e, hl=hl, o_j=o_j, ro_j=ro_j):
                    dst = p.sb["xln"][hl * 64:hl * 64 + 64, o_j, ro_j * 512:(ro_j + 1) * 512]
                    return e.tensor_tensor(dst, p.sb["ofsb"][0:64, :], bank(p, 6, rows=64), ALU.mult)
                p.emit("dve", norm, wait=[("te", te6), ("act", ac)], inc="dve")

    if debug_stop in ("proj", "attn"):
        return
    st = dma_sp(p, a2a_in.rearrange("(o pp) t -> pp o t", pp=128),
                p.sb["xln"][:, :, :], wait=[("dve", p.cnt["dve"])])

    def colla(e):
        with allow_ici_all_to_all():
            return e.collective_compute(
                "AllToAll", ALU.bypass, replica_groups=[list(range(NCORE))],
                ins=[a2a_in.opt()], outs=[a2a_out.opt()])
    cc = p.emit("pool", colla, wait=[("xld", st)], inc="cc")
    ld = dma_sp(p, p.sb["xln"][:, :, :], a2a_out.rearrange("(dc pp) t -> pp dc t", pp=128),
                wait=[("cc", cc)])

    if kind == "l0":
        ld2 = ld
        for tg in range(NRUN):
            sl = slice(tg * 512, tg * 512 + 512)
            for oc in range(DC):
                p.emit("dve", lambda e, oc=oc, sl=sl: e.tensor_add(
                    xres[:, oc, sl], xres[:, oc, sl], p.sb["xln"][:, oc, sl]),
                    wait=[("xld", ld2)], inc="dve")
        return
    wtile = p.sb["wtile"]
    slot_user = {0: 0, 1: 0}
    for tg in range(NRUN):
        sl = slice(tg * 512, tg * 512 + 512)
        for oc in range(DC):
            slot = oc % 2
            wl = dma_pool(p, wtile[:, slot],
                          WoT.rearrange("(dcc pp) o -> pp dcc o", pp=128)[:, :, oc * 128:(oc + 1) * 128],
                          wait=[("te", slot_user[slot])])

            def mmo(e, slot=slot, sl=sl):
                for dcc in range(DC):
                    last = e.matmul(bank(p, 2), wtile[:, slot, dcc, :], p.sb["xln"][:, dcc, sl],
                                    start=(dcc == 0), stop=(dcc == DC - 1))
                return last
            te = p.emit("te", mmo, wait=[("wld", wl), ("xld", ld), ("dve", p.cnt["dve"]),
                                         ("act", p.cnt["act"])], inc="te")
            slot_user[slot] = te
            p.emit("dve", lambda e, oc=oc, sl=sl: e.scalar_tensor_tensor(
                xres[:, oc, sl], bank(p, 2), p.sb["bo_sb"][:, oc:oc + 1], xres[:, oc, sl],
                ALU.add, ALU.add), wait=[("te", te)], inc="dve")


def emit_l3a(p, lw, debug_stop=None):
    wslow_in = p.din("l3a_wslow", [D, 3088])
    wout_in = p.din("l3a_wout", [D, D])
    lnw_in = p.din("l3a_lnw", [128, DC])
    lnb_in = p.din("l3a_lnb", [128, DC])
    xres = p.sb["xres"]

    p.carve("lnw", [128, DC], F32)
    p.carve("lnb", [128, DC], F32)
    p.carve("stato", [1, 2 * TLOC], BF16)
    p.carve("hh", [128, DC, TLOC], BF16)
    p.carve("outall", [128, DC, TLOC], BF16)
    p.carve("wsl", [128, 2, DC, 208], BF16)
    p.carve("zero1", [128, 1], F32)
    p.carve("vtmp", [128, CH], F32)
    mk3 = p.carve_mark()
    p.carve("xlf3", [128, DC, CH], F32)

    dma_sp(p, p.sb["lnw"][:, :], lnw_in[:, :])
    ldb = dma_sp(p, p.sb["lnb"][:, :], lnb_in[:, :])
    p.emit("dve", lambda e: e.memset(p.sb["zero1"][:, :], 0.0), wait=[("xld", ldb)], inc="dve")

    emit_ln(p, keep_xln=False, keep_bc=False)
    p.emit("dve", lambda e: e.tensor_copy(p.sb["stato"][0:1, 0:TLOC], p.sb["stat_m"][0:1, :]),
           inc="dve")
    p.emit("dve", lambda e: e.tensor_copy(p.sb["stato"][0:1, TLOC:2 * TLOC], p.sb["stat_r"][0:1, :]),
           inc="dve")
    hc = p.sb["hc"]
    p.emit("dve", lambda e: e.tensor_mul(p.sb["lnst"][0:1, :], p.sb["stat_r"][0:1, :],
                                         p.sb["stat_r"][0:1, :]), inc="dve")
    p.emit("dve", lambda e: e.tensor_scalar(p.sb["lnst"][0:1, :], p.sb["lnst"][0:1, :],
                                            hc[0:1, HC_NEPS:HC_NEPS + 1],
                                            hc[0:1, HC_ONE:HC_ONE + 1], ALU.mult, ALU.add),
           inc="dve")
    ac = p.emit("act", lambda e: e.activation(p.sb["lnst"][0:1, :], p.sb["lnst"][0:1, :], AF.Sqrt,
                                              bias=hc[0:1, HC_EPS:HC_EPS + 1]),
                wait=[("dve", p.cnt["dve"])], inc="act")
    p.emit("dve", lambda e: e.reciprocal(p.sb["lnst"][0:1, :], p.sb["lnst"][0:1, :]),
           wait=[("act", ac)], inc="dve")
    p.emit("dve", lambda e: e.tensor_mul(p.sb["stat_r"][0:1, :], p.sb["stat_r"][0:1, :],
                                         p.sb["lnst"][0:1, :]), inc="dve")

    for tg in range(NRUN):
        sl = slice(tg * 512, tg * 512 + 512)

        def bc4(e, sl=sl):
            e.matmul(bank(p, 3), p.sb["onesr_f"][0:1, :], p.sb["stat_m"][0:1, sl], start=True, stop=True)
            return e.matmul(bank(p, 4), p.sb["onesr_f"][0:1, :], p.sb["stat_r"][0:1, sl],
                            start=True, stop=True)
        te = p.emit("te", bc4, wait=[("dve", p.cnt["dve"]), ("act", p.cnt["act"])], inc="te")

        def xl1(e, sl=sl):
            mb = bank(p, 3).unsqueeze(1).broadcast_to([128, DC, 512])
            return e.tensor_tensor(p.sb["xlf3"][:, :, :], xres[:, :, sl], mb, ALU.subtract)
        p.emit("dve", xl1, wait=[("te", te)], inc="dve")

        def xl2(e):
            rb = bank(p, 4).unsqueeze(1).broadcast_to([128, DC, 512])
            return e.tensor_tensor(p.sb["xlf3"][:, :, :], p.sb["xlf3"][:, :, :], rb, ALU.mult)
        dv = p.emit("dve", xl2, inc="dve")
        for dc in range(DC):
            p.emit("dve", lambda e, dc=dc, sl=sl: e.tensor_scalar(
                p.sb["hh"][:, dc, sl], p.sb["xlf3"][:, dc, :],
                p.sb["lnw"][:, dc:dc + 1], p.sb["lnb"][:, dc:dc + 1], ALU.mult, ALU.add),
                wait=[("dve", dv)], inc="dve")
    ac_h = p.cnt["act"]

    if debug_stop == "l3prep":
        return
    p.carve_rewind(mk3, ["xlf3", "stat_m", "stat_r", "lnst", "lnsq", "lnsq2"])
    p.carve("qn", [64, TLOC], BF16)
    p.carve("kn", [64, TLOC], BF16)
    p.carve("vv", [64, TLOC], BF16)
    p.carve("bet", [1, TLOC], BF16)
    p.carve("q1", [64, CH], F32)
    p.carve("q1b", [64, CH], BF16)
    p.carve("dtmp", [64, 1024], BF16)
    p.carve("dtmp2", [64, 1024], BF16)
    p.carve("dtmp3", [64, 1024], BF16)
    p.carve("dtmp4", [64, 1024], BF16)
    p.carve("dcol", [1, 4096], BF16)
    p.carve("del0", [64, 1024], F32)
    p.carve("del1", [64, 1024], F32)
    p.carve("ot", [64, 512], F32)
    p.carve("rden3", [1, CH], F32)
    slot_user = {0: 0, 1: 0}
    for h in range(16):
        slot = h % 2
        wl = dma_pool(p, p.sb["wsl"][:, slot, :, 0:193],
                      wslow_in.rearrange("(dcc pp) o -> pp dcc o", pp=128)[:, :, 193 * h:193 * h + 193],
                      wait=[("te", slot_user[slot])])
        for tg in range(NRUN):
            sl = slice(tg * 512, tg * 512 + 512)

            def mma(e, slot=slot, sl=sl):
                for dcc in range(DC):
                    e.matmul(bank(p, 0), p.sb["wsl"][:, slot, dcc, 0:128], p.sb["hh"][:, dcc, sl],
                             start=(dcc == 0), stop=(dcc == DC - 1))
                for dcc in range(DC):
                    last = e.matmul(bank(p, 1, rows=65), p.sb["wsl"][:, slot, dcc, 128:193],
                                    p.sb["hh"][:, dcc, sl], start=(dcc == 0), stop=(dcc == DC - 1))
                return last
            te = p.emit("te", mma, wait=[("wld", wl), ("act", max(ac_h, p.cnt["act"])),
                                         ("dve", p.cnt["dve"])], inc="te")
            slot_user[slot] = te
            dvv = p.emit("dve", lambda e, sl=sl: e.tensor_copy(p.sb["vv"][0:64, sl],
                                                               bank(p, 1, rows=64)),
                         wait=[("te", te)], inc="dve")
            acb = p.emit("act", lambda e, sl=sl: e.activation(p.sb["bet"][0:1, sl],
                                                              p.ps[64:65, 512:1024], AF.Sigmoid),
                         wait=[("te", te), ("dve", dvv)], inc="act")
            for qi, rows0 in ((0, 0), (1, 64)):
                dst = p.sb["qn"] if qi == 0 else p.sb["kn"]
                bsl = bank(p, 0)[rows0:rows0 + 64, :]

                def relu_(e, bsl=bsl):
                    return e.tensor_scalar_max(p.sb["q1"][0:64, :], bsl, p.sb["zero1"][0:64, 0:1])
                dv1 = p.emit("dve", relu_, wait=[("te", te), ("act", p.cnt["act"])], inc="dve")

                def min_(e, bsl=bsl):
                    return e.tensor_scalar_min(p.sb["vtmp"][0:64, :], bsl, p.sb["zero1"][0:64, 0:1])
                dv2 = p.emit("dve", min_, inc="dve")
                ace = p.emit("act", lambda e: e.activation(p.sb["vtmp"][0:64, :], p.sb["vtmp"][0:64, :],
                                                           AF.Exp), wait=[("dve", dv2)], inc="act")
                dv3 = p.emit("dve", lambda e: e.tensor_add(p.sb["q1"][0:64, :], p.sb["q1"][0:64, :],
                                                           p.sb["vtmp"][0:64, :]),
                             wait=[("ace" if False else "act", ace)], inc="dve")
                dv4 = p.emit("dve", lambda e: e.tensor_copy(p.sb["q1b"][0:64, :], p.sb["q1"][0:64, :]),
                             inc="dve")
                te2 = p.emit("te", lambda e: e.matmul(bank(p, 2, rows=1), p.sb["ones_bf"][0:64, 0:1],
                                                      p.sb["q1b"][0:64, :], start=True, stop=True),
                             wait=[("dve", dv4)], inc="te")
                dv5 = p.emit("dve", lambda e: e.reciprocal(p.sb["rden3"][0:1, :], bank(p, 2, rows=1)),
                             wait=[("te", te2)], inc="dve")
                te3 = p.emit("te", lambda e: e.matmul(bank(p, 2, rows=64), p.sb["onesr_f"][0:1, 0:64],
                                                      p.sb["rden3"][0:1, :], start=True, stop=True),
                             wait=[("dve", dv5)], inc="te")
                p.emit("dve", lambda e, dst=dst, sl=sl: e.tensor_tensor(
                    dst[0:64, sl], p.sb["q1"][0:64, :], bank(p, 2, rows=64), ALU.mult),
                    wait=[("te", te3)], inc="dve")
        B0 = slice(0, 1024)
        B1 = slice(1024, 2048)

        def dots(e):
            e.tensor_mul(p.sb["dtmp"][0:64, :], p.sb["kn"][0:64, B0], p.sb["qn"][0:64, B0])
            return e.tensor_mul(p.sb["dtmp2"][0:64, :], p.sb["kn"][0:64, B0], p.sb["qn"][0:64, B1])
        dv = p.emit("dve", dots, inc="dve")

        def dots2(e):
            e.tensor_mul(p.sb["dtmp3"][0:64, :], p.sb["kn"][0:64, B0], p.sb["kn"][0:64, B1])
            return e.tensor_mul(p.sb["dtmp4"][0:64, :], p.sb["kn"][0:64, B1], p.sb["qn"][0:64, B1])
        dv = p.emit("dve", dots2, inc="dve")

        dots_names = ("dtmp", "dtmp2", "dtmp3", "dtmp4")
        for i, t in enumerate(dots_names):
            for q2 in range(2):
                te2 = p.emit("te", lambda e, t=t, q2=q2: e.matmul(
                    bank(p, 3, rows=1), p.sb["ones_bf"][0:64, 0:1],
                    p.sb[t][0:64, q2 * 512:(q2 + 1) * 512], start=True, stop=True),
                    wait=[("dve", p.cnt["dve"])], inc="te")
                p.emit("dve", lambda e, i=i, q2=q2: e.tensor_copy(
                    p.sb["dcol"][0:1, i * 1024 + q2 * 512:i * 1024 + (q2 + 1) * 512],
                    bank(p, 3, rows=1)), wait=[("te", te2)], inc="dve")
        te2 = p.emit("te", lambda e: e.matmul(bank(p, 4, rows=64), p.sb["onesr_b"][0:1, 0:64],
                                              p.sb["bet"][0:1, 0:512], start=True, stop=True),
                     wait=[("dve", p.cnt["dve"])], inc="te")
        te3 = p.emit("te", lambda e: e.matmul(bank(p, 5, rows=64), p.sb["onesr_b"][0:1, 0:64],
                                              p.sb["bet"][0:1, 512:1024], start=True, stop=True),
                     inc="te")
        p.emit("dve", lambda e: e.tensor_tensor(p.sb["del0"][0:64, 0:512], p.sb["vv"][0:64, 0:512],
                                                bank(p, 4, rows=64), ALU.mult),
               wait=[("te", te2)], inc="dve")
        dv = p.emit("dve", lambda e: e.tensor_tensor(p.sb["del0"][0:64, 512:1024],
                                                     p.sb["vv"][0:64, 512:1024],
                                                     bank(p, 5, rows=64), ALU.mult),
                    wait=[("te", te3)], inc="dve")
        for q2 in range(2):
            te2 = p.emit("te", lambda e, q2=q2: e.matmul(
                bank(p, 4, rows=64), p.sb["onesr_b"][0:1, 0:64],
                p.sb["dcol"][0:1, 2048 + q2 * 512:2048 + (q2 + 1) * 512], start=True, stop=True),
                wait=[("dve", p.cnt["dve"])], inc="te")
            te3 = p.emit("te", lambda e, q2=q2: e.matmul(
                bank(p, 5, rows=64), p.sb["onesr_b"][0:1, 0:64],
                p.sb["bet"][0:1, 1024 + q2 * 512:1024 + (q2 + 1) * 512], start=True, stop=True),
                inc="te")
            s5 = slice(q2 * 512, (q2 + 1) * 512)

            def d1a(e, s5=s5):
                return e.tensor_tensor(p.sb["ot"][0:64, 0:512], p.sb["del0"][0:64, s5],
                                       bank(p, 4, rows=64), ALU.mult)
            p.emit("dve", d1a, wait=[("te", te2)], inc="dve")

            def d1b(e, q2=q2):
                return e.tensor_sub(p.sb["ot"][0:64, 0:512],
                                    p.sb["vv"][0:64, 1024 + q2 * 512:1024 + (q2 + 1) * 512],
                                    p.sb["ot"][0:64, 0:512])
            p.emit("dve", d1b, inc="dve")

            def d1c(e, s5=s5):
                return e.tensor_tensor(p.sb["del1"][0:64, s5], p.sb["ot"][0:64, 0:512],
                                       bank(p, 5, rows=64), ALU.mult)
            p.emit("dve", d1c, wait=[("te", te3)], inc="dve")
        hp = 64 * (h % 2)
        hdc = h // 2
        for q2 in range(2):
            s5 = slice(q2 * 512, (q2 + 1) * 512)
            te2 = p.emit("te", lambda e, q2=q2: e.matmul(
                bank(p, 4, rows=64), p.sb["onesr_b"][0:1, 0:64],
                p.sb["dcol"][0:1, q2 * 512:(q2 + 1) * 512], start=True, stop=True),
                wait=[("dve", p.cnt["dve"])], inc="te")
            p.emit("dve", lambda e, hp=hp, hdc=hdc, q2=q2, s5=s5: e.tensor_tensor(
                p.sb["outall"][hp:hp + 64, hdc, q2 * 512:(q2 + 1) * 512],
                p.sb["del0"][0:64, s5], bank(p, 4, rows=64), ALU.mult),
                wait=[("te", te2)], inc="dve")
            te2 = p.emit("te", lambda e, q2=q2: e.matmul(
                bank(p, 4, rows=64), p.sb["onesr_b"][0:1, 0:64],
                p.sb["dcol"][0:1, 1024 + q2 * 512:1024 + (q2 + 1) * 512], start=True, stop=True),
                wait=[("dve", p.cnt["dve"])], inc="te")
            te3 = p.emit("te", lambda e, q2=q2: e.matmul(
                bank(p, 5, rows=64), p.sb["onesr_b"][0:1, 0:64],
                p.sb["dcol"][0:1, 3072 + q2 * 512:3072 + (q2 + 1) * 512], start=True, stop=True),
                inc="te")
            p.emit("dve", lambda e, s5=s5: e.tensor_tensor(
                p.sb["ot"][0:64, 0:512], p.sb["del0"][0:64, s5], bank(p, 4, rows=64), ALU.mult),
                wait=[("te", te2)], inc="dve")
            p.emit("dve", lambda e, s5=s5: e.tensor_tensor(
                p.sb["q1"][0:64, :], p.sb["del1"][0:64, s5], bank(p, 5, rows=64), ALU.mult),
                wait=[("te", te3)], inc="dve")
            p.emit("dve", lambda e, hp=hp, hdc=hdc, q2=q2: e.tensor_add(
                p.sb["outall"][hp:hp + 64, hdc, 1024 + q2 * 512:1024 + (q2 + 1) * 512],
                p.sb["ot"][0:64, 0:512], p.sb["q1"][0:64, :]), inc="dve")

    if debug_stop == "l3heads":
        return
    p.carve_rewind(mk3, ["qn", "kn", "vv", "bet", "q1", "q1b", "dtmp", "dtmp2", "dtmp3",
                         "dtmp4", "dcol", "del0", "del1", "ot", "rden3"])
    p.carve("xlf3", [128, DC, CH], F32)
    p.carve("wtile", [128, 2, DC, 128], BF16)
    slot_user = {0: p.cnt["te"], 1: p.cnt["te"]}
    dv_tail0 = p.cnt["dve"]
    for tg in range(NRUN):
        sl = slice(tg * 512, tg * 512 + 512)

        def bco(e, sl=sl, tg=tg):
            e.matmul(bank(p, 3), p.sb["onesr_b"][0:1, :], p.sb["stato"][0:1, sl], start=True, stop=True)
            return e.matmul(bank(p, 4), p.sb["onesr_b"][0:1, :],
                            p.sb["stato"][0:1, TLOC + tg * 512:TLOC + tg * 512 + 512],
                            start=True, stop=True)
        te = p.emit("te", bco, wait=[("dve", p.cnt["dve"]), ("act", p.cnt["act"])], inc="te")

        def xl1(e, sl=sl):
            mb = bank(p, 3).unsqueeze(1).broadcast_to([128, DC, 512])
            return e.tensor_tensor(p.sb["xlf3"][:, :, :], xres[:, :, sl], mb, ALU.subtract)
        p.emit("dve", xl1, wait=[("te", te)], inc="dve")

        def xl2(e):
            rb = bank(p, 4).unsqueeze(1).broadcast_to([128, DC, 512])
            return e.tensor_tensor(p.sb["xlf3"][:, :, :], p.sb["xlf3"][:, :, :], rb, ALU.mult)
        dvx = p.emit("dve", xl2, inc="dve")
        if debug_stop == "l3tail1":
            continue
        for oc in range(DC):
            slot = oc % 2
            wl = dma_pool(p, p.sb["wtile"][:, slot],
                          wout_in.rearrange("(dcc pp) o -> pp dcc o", pp=128)[:, :, oc * 128:(oc + 1) * 128],
                          wait=[("te", slot_user[slot]), ("dve", dv_tail0)])
            if debug_stop == "l3tail2":
                slot_user[slot] = p.cnt["te"]
                continue

            def mmo(e, slot=slot, sl=sl):
                for dcc in range(DC):
                    last = e.matmul(bank(p, 2), p.sb["wtile"][:, slot, dcc, :],
                                    p.sb["outall"][:, dcc, sl], start=(dcc == 0), stop=(dcc == DC - 1))
                return last
            te2 = p.emit("te", mmo, wait=[("wld", wl), ("dve", dvx), ("act", p.cnt["act"]),
                                          ("dve", p.cnt["dve"])], inc="te")
            slot_user[slot] = te2
            p.emit("dve", lambda e, oc=oc, sl=sl: e.tensor_add(
                p.sb["vtmp"][:, :], bank(p, 2), p.sb["xlf3"][:, oc, :]),
                wait=[("te", te2)], inc="dve")
            p.emit("dve", lambda e, oc=oc, sl=sl: e.tensor_add(
                xres[:, oc, sl], xres[:, oc, sl], p.sb["vtmp"][:, :]), inc="dve")


CAP1 = 2560
BIGM = 1.0e9


def emit_moe(p, lw, debug_stop=None):
    kind = lw["kind"]
    gwT = p.din(lw["gwT"], [D, 8])
    W1 = p.din(lw["W1"], [8, D, 4096])
    W2 = p.din(lw["W2"], [8, 2048 if kind == "l0f" else 4096, D])
    if kind == "l1f":
        gb = p.din("l1f_gb", [1, 8])
        b1f = p.din("l1f_b1", [8, 128, 32])
        b2f = p.din("l1f_b2", [8, 128, 8])
    xres = p.sb["xres"]
    INTER2 = 4096

    p.carve("gw_sb", [128, DC, 8], BF16)
    p.carve("zeroc", [128, 1], F32)
    p.carve("combT", [8, TLOC], F32)
    p.carve("gb_sb", [1, 8], BF16)
    p.carve("capc", [8, 1], F32)
    p.carve("nbig", [128, 1], F32)
    p.carve("xln", [128, DC, TLOC], BF16)
    mkA = p.carve_mark()
    emit_ln(p, keep_xln=True)
    p.carve("gprob", [128, 16, 8], F32)
    p.carve("w12c", [128, 16, 2], F32)
    p.carve("tkm", [128, 12], F32)
    p.carve("tk2", [128, 12], F32)
    p.carve("tk3", [128, 8], F32)
    p.carve("ohT", [8, 2, TLOC], BF16)
    p.carve("scanT", [8, TLOC], F32)
    p.carve("keepT", [8, TLOC], BF16)
    p.carve("wrow", [1, 2 * TLOC], F32)
    p.carve("thr", [8, 8], F32)
    p.carve("cnt8", [8, 8], F32)
    p.carve("pub", [4, 16], F32)
    p.carve("cagg", [32, 16], F32)
    p.carve("moff", [4, 16], F32)
    p.carve("scan2", [8, 2, TLOC], F32)

    ldg = dma_pool(p, p.sb["gw_sb"][:, :, :], gwT.rearrange("(dcc pp) o -> pp dcc o", pp=128))
    if kind == "l1f":
        ldg = dma_pool(p, p.sb["gb_sb"][:, :], gb[:, :])
    dvz = p.emit("dve", lambda e: e.memset(p.sb["zeroc"][:, :], 0.0), inc="dve")
    p.emit("dve", lambda e: e.memset(p.sb["combT"][:, :], 0.0), inc="dve")
    p.emit("dve", lambda e: e.memset(p.sb["capc"][0:8, 0:1], float(CAP1 + 1)), inc="dve")
    p.emit("dve", lambda e: e.memset(p.sb["nbig"][:, :], -BIGM), inc="dve")

    for tt in range(16):
        ts_ = slice(tt * 128, (tt + 1) * 128)

        if kind == "l1f":
            def gmm(e, ts_=ts_):
                for dcc in range(DC):
                    e.matmul(bank(p, 0, rows=128, width=8), p.sb["xln"][:, dcc, ts_],
                             p.sb["gw_sb"][:, dcc, :], start=(dcc == 0), stop=False)
                return e.matmul(bank(p, 0, rows=128, width=8), p.sb["onesr_b"][0:1, 0:128],
                                p.sb["gb_sb"][0:1, :], start=False, stop=True)
        else:
            def gmm(e, ts_=ts_):
                for dcc in range(DC):
                    last = e.matmul(bank(p, 0, rows=128, width=8), p.sb["xln"][:, dcc, ts_],
                                    p.sb["gw_sb"][:, dcc, :], start=(dcc == 0), stop=(dcc == DC - 1))
                return last
        te = p.emit("te", gmm, wait=[("wld", ldg), ("dve", p.cnt["dve"]),
                                     ("act", p.cnt["act"])], inc="te")
        dv = p.emit("dve", lambda e: e.reduce_max(p.sb["tkm"][:, 0:1], bank(p, 0, rows=128, width=8),
                                                  axis=mybir.AxisListType.X), wait=[("te", te)], inc="dve")
        dv = p.emit("dve", lambda e: e.tensor_sub(p.sb["tk2"][:, 0:1], p.sb["zeroc"][:, 0:1],
                                                  p.sb["tkm"][:, 0:1]), inc="dve")
        ac = p.emit("act", lambda e, tt=tt: e.activation(p.sb["gprob"][:, tt, :],
                                                         bank(p, 0, rows=128, width=8), AF.Exp,
                                                         bias=p.sb["tk2"][:, 0:1]),
                    wait=[("dve", dv)], inc="act")
        dv = p.emit("dve", lambda e, tt=tt: e.reduce_sum(p.sb["tkm"][:, 0:1], p.sb["gprob"][:, tt, :],
                                                         axis=mybir.AxisListType.X),
                    wait=[("act", ac)], inc="dve")
        dv = p.emit("dve", lambda e: e.reciprocal(p.sb["tk2"][:, 0:1], p.sb["tkm"][:, 0:1]), inc="dve")
        dv = p.emit("dve", lambda e, tt=tt: e.tensor_scalar(p.sb["gprob"][:, tt, :],
                                                            p.sb["gprob"][:, tt, :],
                                                            p.sb["tk2"][:, 0:1], None, ALU.mult),
                    inc="dve")
        gp = p.sb["gprob"]
        if kind == "l0f":
            p.emit("dve", lambda e, tt=tt: e.tensor_tensor(
                p.sb["tkm"][:, 0:4], gp[:, tt, 0::2], gp[:, tt, 1::2], ALU.max), inc="dve")
            p.emit("dve", lambda e: e.reduce_max(p.sb["tk2"][:, 0:1], p.sb["tkm"][:, 0:4],
                                                 axis=mybir.AxisListType.X), inc="dve")
            p.emit("dve", lambda e: e.tensor_scalar(p.sb["tk3"][:, 0:4], p.sb["tkm"][:, 0:4],
                                                    p.sb["tk2"][:, 0:1], None, ALU.is_ge), inc="dve")
            p.emit("dve", lambda e: e.scalar_tensor_tensor(p.sb["tk2"][:, 1:5], p.sb["tk3"][:, 0:4],
                                                           p.sb["nbig"][:, 0:1], p.sb["tkm"][:, 0:4],
                                                           ALU.mult, ALU.add), inc="dve")
            p.emit("dve", lambda e: e.reduce_max(p.sb["tk2"][:, 0:1], p.sb["tk2"][:, 1:5],
                                                 axis=mybir.AxisListType.X), inc="dve")
            p.emit("dve", lambda e: e.tensor_scalar(p.sb["tk3"][:, 0:4], p.sb["tkm"][:, 0:4],
                                                    p.sb["tk2"][:, 0:1], None, ALU.is_ge), inc="dve")
            p.emit("dve", lambda e: e.tensor_copy(p.sb["tkm"][:, 0:8][:, 0::2], p.sb["tk3"][:, 0:4]),
                   inc="dve")
            p.emit("dve", lambda e: e.tensor_copy(p.sb["tkm"][:, 0:8][:, 1::2], p.sb["tk3"][:, 0:4]),
                   inc="dve")
            p.emit("dve", lambda e, tt=tt: e.tensor_mul(p.sb["tkm"][:, 0:8], p.sb["tkm"][:, 0:8],
                                                        gp[:, tt, :]), inc="dve")
            masked = lambda: p.sb["tkm"][:, 0:8]
        else:
            p.emit("dve", lambda e, tt=tt: e.tensor_copy(p.sb["tkm"][:, 0:8], gp[:, tt, :]), inc="dve")
            masked = lambda: p.sb["tkm"][:, 0:8]
        p.emit("dve", lambda e: e.reduce_max(p.sb["tk2"][:, 0:1], masked(), axis=mybir.AxisListType.X),
               inc="dve")
        p.emit("dve", lambda e: e.tensor_scalar(p.sb["tk3"][:, 0:8], masked(), p.sb["tk2"][:, 0:1],
                                                None, ALU.is_ge), inc="dve")
        p.emit("dve", lambda e: e.scalar_tensor_tensor(p.sb["tk2"][:, 0:8], p.sb["tk3"][:, 0:8],
                                                       p.sb["nbig"][:, 0:1], masked(),
                                                       ALU.mult, ALU.add), inc="dve")
        p.emit("dve", lambda e: e.reduce_max(p.sb["tk2"][:, 8:9], p.sb["tk2"][:, 0:8],
                                             axis=mybir.AxisListType.X), inc="dve")
        p.emit("dve", lambda e: e.tensor_scalar(p.sb["tk2"][:, 0:8], masked(), p.sb["tk2"][:, 8:9],
                                                None, ALU.is_ge), inc="dve")
        p.emit("dve", lambda e: e.tensor_sub(p.sb["tk2"][:, 0:8], p.sb["tk2"][:, 0:8],
                                             p.sb["tk3"][:, 0:8]), inc="dve")
        p.emit("dve", lambda e: e.reduce_max(p.sb["tkm"][:, 8:9], masked(),
                                             axis=mybir.AxisListType.X), inc="dve")
        if kind == "l0f":
            p.emit("dve", lambda e: e.tensor_add(p.sb["tkm"][:, 9:10], p.sb["tkm"][:, 8:9],
                                                 p.sb["tk2"][:, 8:9]), inc="dve")
            p.emit("dve", lambda e: e.reciprocal(p.sb["tkm"][:, 10:11], p.sb["tkm"][:, 9:10]),
                   inc="dve")
            p.emit("dve", lambda e, tt=tt: e.tensor_tensor(p.sb["w12c"][:, tt, 0:1],
                                                           p.sb["tkm"][:, 8:9],
                                                           p.sb["tkm"][:, 10:11], ALU.mult), inc="dve")
            p.emit("dve", lambda e, tt=tt: e.tensor_tensor(p.sb["w12c"][:, tt, 1:2],
                                                           p.sb["tk2"][:, 8:9],
                                                           p.sb["tkm"][:, 10:11], ALU.mult), inc="dve")
        else:
            p.emit("dve", lambda e, tt=tt: e.tensor_copy(p.sb["w12c"][:, tt, 0:1],
                                                         p.sb["tkm"][:, 8:9]), inc="dve")
            p.emit("dve", lambda e, tt=tt: e.tensor_copy(p.sb["w12c"][:, tt, 1:2],
                                                         p.sb["tk2"][:, 8:9]), inc="dve")
        for ci, eqt in ((0, "tk3"), (1, "tk2")):
            dvq = p.emit("dve", lambda e, eqt=eqt: e.tensor_copy(p.sb["lnsq"][0:128, 0:8],
                                                                 p.sb[eqt][:, 0:8]), inc="dve")
            tt2 = p.emit("te", lambda e: e.transpose(p.pt[0:8, 0:128], p.sb["lnsq"][0:128, 0:8],
                                                     p.sb["idb"][:, :]),
                         wait=[("dve", dvq), ("dve", p.cnt["dve"])], inc="te")
            p.emit("dve", lambda e, ci=ci, tt=tt: e.tensor_copy(
                p.sb["ohT"][0:8, ci, tt * 128:(tt + 1) * 128], p.pt[0:8, 0:128]),
                wait=[("te", tt2)], inc="dve")
        for wi in range(2):
            dvq = p.emit("dve", lambda e, tt=tt, wi=wi: e.tensor_copy(
                p.sb["lnsq"][0:128, 0:1], p.sb["w12c"][:, tt, wi:wi + 1]), inc="dve")
            tt2 = p.emit("te", lambda e: e.transpose(p.pt[0:1, 0:128], p.sb["lnsq"][0:128, 0:1],
                                                     p.sb["idb"][:, :]),
                         wait=[("dve", dvq), ("dve", p.cnt["dve"])], inc="te")
            p.emit("dve", lambda e, tt=tt, wi=wi: e.tensor_copy(
                p.sb["wrow"][0:1, wi * TLOC + tt * 128:wi * TLOC + (tt + 1) * 128],
                p.pt[0:1, 0:128]), wait=[("te", tt2)], inc="dve")
    dv_rout = p.cnt["dve"]

    for ci in range(2):
        if kind == "l1f":
            if ci == 0:
                p.emit("dve", lambda e: e.memset(p.sb["keepT"][0:8, :], 1.0), inc="dve")
            p.emit("dve", lambda e, ci=ci: e.tensor_tensor_scan(
                p.sb["scanT"][0:8, :], p.sb["keepT"][0:8, :], p.sb["ohT"][0:8, ci, :],
                0.0, ALU.mult, ALU.add), wait=[("dve", dv_rout)], inc="dve")
            for r in range(NRUN):
                if r == 0:
                    p.emit("dve", lambda e, ci=ci: e.tensor_copy(
                        p.sb["cnt8"][0:8, ci * 4:ci * 4 + 1], p.sb["scanT"][0:8, 511:512]), inc="dve")
                else:
                    p.emit("dve", lambda e, ci=ci, r=r: e.tensor_sub(
                        p.sb["cnt8"][0:8, ci * 4 + r:ci * 4 + r + 1],
                        p.sb["scanT"][0:8, r * 512 + 511:r * 512 + 512],
                        p.sb["scanT"][0:8, r * 512 - 1:r * 512]), inc="dve")
            p.emit("dve", lambda e, ci=ci: e.memset(p.sb["thr"][0:8, ci * 4:ci * 4 + 1], 0.0),
                   inc="dve")
            for r in range(1, NRUN):
                p.emit("dve", lambda e, ci=ci, r=r: e.tensor_copy(
                    p.sb["thr"][0:8, ci * 4 + r:ci * 4 + r + 1],
                    p.sb["scanT"][0:8, r * 512 - 1:r * 512]), inc="dve")
            p.emit("dve", lambda e, ci=ci: e.tensor_copy(p.sb["scan2"][0:8, ci, :],
                                                         p.sb["scanT"][0:8, :]), inc="dve")
    if kind == "l1f":
        tt2 = p.emit("te", lambda e: e.transpose(bank(p, 3, rows=8, width=8),
                                                 p.sb["cnt8"][0:8, 0:8], p.sb["idf"][0:8, 0:8]),
                     wait=[("dve", p.cnt["dve"]), ("act", p.cnt["act"])], inc="te")
        p.carve("cnt8T", [8, 8], F32)
        dvp = p.emit("dve", lambda e: e.tensor_copy(p.sb["cnt8T"][0:8, 0:8],
                                                    bank(p, 3, rows=8, width=8)),
                     wait=[("te", tt2)], inc="dve")
        cnt_tmp = p.dint("l1f_cnt_tmp", [8, 8])
        st = dma_sp(p, cnt_tmp[:, :], p.sb["cnt8T"][0:8, :], wait=[("dve", dvp)])
        for cc2 in range(2):
            ld = dma_sp(p, p.sb["pub"][0:4, cc2 * 8:(cc2 + 1) * 8],
                        cnt_tmp[cc2 * 4:(cc2 + 1) * 4, :], wait=[("xld", st)])
        mo_in = p.dint("l1f_cnt_in", [4, 16])
        mo_all = p.dint("l1f_cnt_all", [32, 16])
        st = dma_sp(p, mo_in[:, :], p.sb["pub"][0:4, :], wait=[("dve", p.cnt["dve"])])
        cc = p.emit("pool", lambda e: e.collective_compute(
            "AllGather", ALU.bypass, replica_groups=[list(range(NCORE))],
            ins=[mo_in.opt()], outs=[mo_all.opt()]), wait=[("xld", st)], inc="cc")
        ld = dma_sp(p, p.sb["cagg"][0:32, :], mo_all[:, :], wait=[("cc", cc)])
        prec = p.din("c_moeprec", [32, 4])
        p.carve("prec32", [32, 4], F32)
        ld = dma_sp(p, p.sb["prec32"][:, :], prec[:, :])
        te = p.emit("te", lambda e: e.matmul(bank(p, 1, rows=4, width=16), p.sb["prec32"][:, :],
                                             p.sb["cagg"][0:32, :], start=True, stop=True),
                    wait=[("xld", ld)], inc="te")
        dv = p.emit("dve", lambda e: e.tensor_copy(p.sb["moff"][0:4, :], bank(p, 1, rows=4, width=16)),
                    wait=[("te", te)], inc="dve")
        tt2 = p.emit("te", lambda e: e.transpose(bank(p, 3, rows=16, width=4),
                                                 p.sb["moff"][0:4, :], p.sb["idf"][0:4, 0:4]),
                     wait=[("dve", p.cnt["dve"]), ("act", p.cnt["act"])], inc="te")
        p.carve("mof16", [16, 4], F32)
        dv = p.emit("dve", lambda e: e.tensor_copy(p.sb["mof16"][0:16, 0:4],
                                                   bank(p, 3, rows=16, width=4)),
                    wait=[("te", tt2)], inc="dve")
        mo_tmp = p.dint("l1f_mof_tmp", [16, 4])
        p.carve("mof8", [8, 8], F32)
        st = dma_sp(p, mo_tmp[:, :], p.sb["mof16"][0:16, :], wait=[("dve", dv)])
        for cc2 in range(2):
            ld = dma_sp(p, p.sb["mof8"][0:8, cc2 * 4:(cc2 + 1) * 4],
                        mo_tmp[cc2 * 8:(cc2 + 1) * 8, :], wait=[("xld", st)])
        dv = p.emit("dve", lambda e: e.tensor_sub(p.sb["thr"][0:8, :], p.sb["thr"][0:8, :],
                                                  p.sb["mof8"][0:8, :]),
                    wait=[("xld", ld)], inc="dve")
        dv = p.emit("dve", lambda e: e.tensor_scalar(p.sb["thr"][0:8, :], p.sb["thr"][0:8, :],
                                                     p.sb["capc"][0:8, 0:1], None, ALU.add),
                    inc="dve")
    dv_cap = p.cnt["dve"]

    for ci in range(2):
        if kind == "l1f":
            for r in range(NRUN):
                p.emit("dve", lambda e, ci=ci, r=r: e.tensor_scalar(
                    p.sb["keepT"][0:8, r * 512:(r + 1) * 512],
                    p.sb["scan2"][0:8, ci, r * 512:(r + 1) * 512],
                    p.sb["thr"][0:8, ci * 4 + r:ci * 4 + r + 1], None, ALU.is_lt),
                    wait=[("dve", dv_cap)], inc="dve")
        else:
            p.emit("dve", lambda e: e.memset(p.sb["keepT"][0:8, :], 1.0), inc="dve")
        for q4 in range(NRUN):
            s5 = slice(q4 * 512, (q4 + 1) * 512)
            te = p.emit("te", lambda e, ci=ci, q4=q4: e.matmul(
                bank(p, 2, rows=8), p.sb["onesr_f"][0:1, 0:8],
                p.sb["wrow"][0:1, ci * TLOC + q4 * 512:ci * TLOC + (q4 + 1) * 512],
                start=True, stop=True), wait=[("dve", p.cnt["dve"]), ("dve", dv_rout)], inc="te")
            p.emit("dve", lambda e, ci=ci, s5=s5: e.tensor_tensor(
                p.sb["scanT"][0:8, s5], p.sb["ohT"][0:8, ci, s5], p.sb["keepT"][0:8, s5],
                ALU.mult), inc="dve")
            p.emit("dve", lambda e, s5=s5: e.tensor_tensor(
                p.sb["scanT"][0:8, s5], p.sb["scanT"][0:8, s5], bank(p, 2, rows=8), ALU.mult),
                wait=[("te", te)], inc="dve")
            p.emit("dve", lambda e, s5=s5: e.tensor_add(
                p.sb["combT"][0:8, s5], p.sb["combT"][0:8, s5], p.sb["scanT"][0:8, s5]), inc="dve")
    dv_comb = p.cnt["dve"]

    if debug_stop == "route":
        return
    p.carve_rewind(mkA, ["stat_m", "stat_r", "lnst", "lnsq", "lnsq2", "gprob", "w12c",
                         "tkm", "tk2", "tk3", "ohT", "scanT", "keepT", "wrow", "thr", "cnt8",
                         "pub", "cagg", "moff", "scan2", "cnt8T", "prec32", "mof16", "mof8"])
    p.carve("mid", [128, 32, 512], BF16)
    p.carve("acts", [128, 16, 512], BF16)
    p.carve("wtile", [128, 2, DC, 128], BF16)
    p.carve("w2tile", [128, 2, 32, 128], BF16)
    p.carve("cb", [128, 512], F32)
    p.carve("crow", [1, TLOC], F32)
    p.carve("b1_sb", [128, 32], F32)
    p.carve("b2_sb", [128, 8], F32)
    p.carve("vt", [128, 512], F32)

    slot_user = {0: p.cnt["te"], 1: p.cnt["te"]}
    nfi = 32
    cr_d = p.dint(f"{lw['tag']}_crow", [8, TLOC])
    for ei in range(8):
        if ei == 0:
            dma_sp(p, cr_d[:, :], p.sb["combT"][0:8, :], wait=[("dve", dv_comb)])
        crl = dma_sp(p, p.sb["crow"][0:1, :], cr_d[ei:ei + 1, :],
                     wait=[("te", p.cnt["te"])])
        if kind == "l1f":
            dma_sp(p, p.sb["b1_sb"][:, :], b1f[ei], wait=[("dve", p.cnt["dve"])])
            ldb = dma_sp(p, p.sb["b2_sb"][:, :], b2f[ei], wait=[("dve", p.cnt["dve"])])
        for tg in range(NRUN):
            sl = slice(tg * 512, tg * 512 + 512)
            for fi in range(nfi):
                slot = fi % 2
                wl = dma_pool(p, p.sb["wtile"][:, slot],
                              W1[ei].rearrange("(dcc pp) o -> pp dcc o", pp=128)[:, :, fi * 128:(fi + 1) * 128],
                              wait=[("te", slot_user[slot]), ("dve", dv_comb)])

                def mm1(e, slot=slot, sl=sl):
                    for dcc in range(DC):
                        last = e.matmul(bank(p, 0), p.sb["wtile"][:, slot, dcc, :],
                                        p.sb["xln"][:, dcc, sl], start=(dcc == 0), stop=(dcc == DC - 1))
                    return last
                te = p.emit("te", mm1, wait=[("wld", wl), ("xld", p.cnt["xld"]),
                                             ("dve", p.cnt["dve"]), ("act", p.cnt["act"])], inc="te")
                slot_user[slot] = te
                if kind == "l1f":
                    p.emit("dve", lambda e, fi=fi: e.tensor_scalar(
                        p.sb["mid"][:, fi, :], bank(p, 0), p.sb["b1_sb"][:, fi:fi + 1],
                        p.sb["zeroc"][:, 0:1], ALU.add, ALU.max), wait=[("te", te)], inc="dve")
                else:
                    if fi < 16:
                        p.emit("act", lambda e, fi=fi: e.activation(p.sb["mid"][:, fi, :],
                                                                    bank(p, 0), AF.Silu),
                               wait=[("te", te)], inc="act")
                    else:
                        p.emit("dve", lambda e, fi=fi: e.tensor_copy(p.sb["mid"][:, fi, :],
                                                                     bank(p, 0)),
                               wait=[("te", te)], inc="dve")
            if kind == "l0f":
                for fi in range(16):
                    p.emit("dve", lambda e, fi=fi: e.tensor_mul(
                        p.sb["acts"][:, fi, :], p.sb["mid"][:, fi, :], p.sb["mid"][:, 16 + fi, :]),
                        wait=[("act", p.cnt["act"])], inc="dve")
                msrc, nf2 = "acts", 16
            else:
                msrc, nf2 = "mid", 32
            te = p.emit("te", lambda e, sl=sl: e.matmul(bank(p, 6), p.sb["onesr_f"][0:1, :],
                                                        p.sb["crow"][0:1, sl], start=True, stop=True),
                        wait=[("xld", crl), ("dve", p.cnt["dve"]), ("act", p.cnt["act"])], inc="te")
            dvc = p.emit("dve", lambda e: e.tensor_copy(p.sb["cb"][:, :], bank(p, 6)),
                         wait=[("te", te)], inc="dve")
            for oc in range(DC):
                slot = oc % 2
                wl = dma_pool(p, p.sb["w2tile"][:, slot, 0:nf2, :],
                              W2[ei].rearrange("(ii pp) o -> pp ii o", pp=128)[:, 0:nf2, oc * 128:(oc + 1) * 128],
                              wait=[("te", slot_user[slot]), ("dve", dv_comb)])

                def mm2(e, slot=slot, nf2=nf2, msrc=msrc):
                    for ii in range(nf2):
                        last = e.matmul(bank(p, 1), p.sb["w2tile"][:, slot, ii, :],
                                        p.sb[msrc][:, ii, :], start=(ii == 0), stop=(ii == nf2 - 1))
                    return last
                te = p.emit("te", mm2, wait=[("wld", wl), ("dve", p.cnt["dve"]),
                                             ("act", p.cnt["act"])], inc="te")
                slot_user[slot] = te
                if kind == "l1f":
                    p.emit("dve", lambda e, oc=oc: e.tensor_scalar(
                        p.sb["vt"][:, :], bank(p, 1), p.sb["b2_sb"][:, oc:oc + 1], None, ALU.add),
                        wait=[("te", te)], inc="dve")
                else:
                    p.emit("dve", lambda e: e.tensor_copy(p.sb["vt"][:, :], bank(p, 1)),
                           wait=[("te", te)], inc="dve")
                p.emit("dve", lambda e: e.tensor_mul(p.sb["vt"][:, :], p.sb["vt"][:, :],
                                                     p.sb["cb"][:, :]), inc="dve")
                p.emit("dve", lambda e, oc=oc, sl=sl: e.tensor_add(
                    xres[:, oc, sl], xres[:, oc, sl], p.sb["vt"][:, :]), inc="dve")


def fm_vec(v):
    return np.ascontiguousarray(np.asarray(v).reshape(DC, 128).T.astype(np.float32))


def prep_host(inputs):
    g = {}
    pc = [dict() for _ in range(NCORE)]

    g["c_ident"] = np.eye(128, dtype=np.float32)
    hc = np.zeros((1, 16), dtype=np.float32)
    hc[0, HC_INV_D] = 1.0 / D
    hc[0, HC_EPS] = LN_EPS
    hc[0, HC_HALF] = 0.5
    hc[0, HC_Q2] = 0.125 * 0.125
    hc[0, HC_Q2L0] = (1.0 / 32) ** 2
    hc[0, HC_ZERO] = 0.0
    hc[0, HC_NEPS] = -LN_EPS
    hc[0, HC_ONE] = 1.0
    g["c_hc"] = hc

    x = np.asarray(inputs["L0_attn_x"])
    for c in range(NCORE):
        xr = np.empty((128, DC, TLOC), dtype=np.float32)
        for r, (b, j) in enumerate(runs_of_core(c)):
            seg = x[b, j * CH:(j + 1) * CH, :]
            xr[:, :, r * CH:(r + 1) * CH] = seg.T.reshape(DC, 128, CH).transpose(1, 0, 2)
        pc[c]["xres_in"] = np.ascontiguousarray(xr)

    for tag, pre in (("l2f", "L2_ffn"), ("l3f", "L3_ffn")):
        g[f"{tag}_WrT"] = np.ascontiguousarray(np.asarray(inputs[f"{pre}_W_r"]).T)
        g[f"{tag}_WkT"] = np.ascontiguousarray(np.asarray(inputs[f"{pre}_W_k"]).T)
        g[f"{tag}_WvT"] = np.ascontiguousarray(np.asarray(inputs[f"{pre}_W_v"]).T)
        g[f"{tag}_mur"] = fm_vec(inputs[f"{pre}_mu_r"])
        g[f"{tag}_muk"] = fm_vec(inputs[f"{pre}_mu_k"])

    WqT = np.asarray(inputs["L2_attn_W_q"]).T
    WkT = np.asarray(inputs["L2_attn_W_k"]).T
    WvT = np.asarray(inputs["L2_attn_W_v"]).T
    g["l2a_WoT"] = np.ascontiguousarray(np.asarray(inputs["L2_attn_W_o"]).T)
    g["l2a_bo"] = fm_vec(inputs["L2_attn_b_o"])
    for c in range(NCORE):
        hsl = slice(2 * c * 64, (2 * c + 2) * 64)
        pc[c]["l2a_WqT"] = np.ascontiguousarray(WqT[:, hsl])
        pc[c]["l2a_WkT"] = np.ascontiguousarray(WkT[:, hsl])
        pc[c]["l2a_WvT"] = np.ascontiguousarray(WvT[:, hsl])

    wkqv = np.asarray(inputs["L0_attn_w_kqv"])
    for c in range(NCORE):
        hsl = slice(2 * c * 64, (2 * c + 2) * 64)
        pc[c]["l0a_WkT"] = np.ascontiguousarray(wkqv[:, 0:1024][:, hsl])
        pc[c]["l0a_WqT"] = np.ascontiguousarray(wkqv[:, 1024:2048][:, hsl])
        pc[c]["l0a_WvT"] = np.ascontiguousarray(wkqv[:, 2048:3072][:, hsl])
        xx = (2.0 ** 8) ** (1.0 / 16)
        slopes = np.asarray([1.0 / xx ** (i + 1) for i in range(16)], dtype=np.float32)
        srow = np.zeros((128, 16), dtype=np.float32)
        srow[:, 0] = slopes[2 * c]
        srow[:, 1] = slopes[2 * c + 1]
        pc[c]["c_slopes"] = srow
    relh = np.zeros((128, 8, 512), dtype=np.float32)
    rell = np.zeros((128, 8, 512), dtype=np.float32)
    jq = np.arange(512)[None, :]
    for off in range(2):
        for sub in range(4):
            i = np.arange(128)[:, None]
            rel = (off - 1) * 512 + sub * 128 + i - jq
            lo = np.mod(rel, 128)
            hi = rel - lo
            relh[:, off * 4 + sub, :] = hi
            rell[:, off * 4 + sub, :] = lo
    g["c_relh"] = relh
    g["c_rell"] = rell
    wm = np.zeros((4, 128, 512), dtype=np.float32)
    for sub in range(4):
        for i in range(128):
            wm[sub, i, sub * 128 + i + 1:] = -30000.0
    g["c_winmask"] = wm

    Wq1 = np.asarray(inputs["L1_attn_Wq"]).reshape(D, 16 * 64)
    Wku = np.asarray(inputs["L1_attn_Wk_up"]).reshape(256, 16 * 64)
    Wvu = np.asarray(inputs["L1_attn_Wv_up"]).reshape(256, 16 * 64)
    g["l1a_Wkvd"] = np.ascontiguousarray(np.asarray(inputs["L1_attn_Wkv_down"]))
    g["l1a_WoT"] = np.ascontiguousarray(np.asarray(inputs["L1_attn_Wout"]).reshape(16 * 64, D))
    for c in range(NCORE):
        hsl = slice(2 * c * 64, (2 * c + 2) * 64)
        pc[c]["l1a_WqT"] = np.ascontiguousarray(Wq1[:, hsl])
        pc[c]["l1a_WkuT"] = np.ascontiguousarray(Wku[:, hsl])
        pc[c]["l1a_WvuT"] = np.ascontiguousarray(Wvu[:, hsl])

    dm = np.zeros((4, 128, 512), dtype=np.float32)
    for sub in range(4):
        for i in range(128):
            dm[sub, i, : min(512, sub * 128 + i)] = -30000.0
    g["c_diagmask"] = dm

    g["l0f_gwT"] = np.ascontiguousarray(np.asarray(inputs["L0_ffn_gate_w"]).T)
    g["l0f_W1"] = np.ascontiguousarray(np.asarray(inputs["L0_ffn_w13_weight"]).transpose(0, 2, 1))
    g["l0f_W2"] = np.ascontiguousarray(np.asarray(inputs["L0_ffn_w2_weight"]).transpose(0, 2, 1))
    g["l1f_gwT"] = np.ascontiguousarray(np.asarray(inputs["L1_ffn_gate_w"]).T)
    g["l1f_gb"] = np.ascontiguousarray(np.asarray(inputs["L1_ffn_gate_b"]).reshape(1, 8))
    g["l1f_W1"] = np.ascontiguousarray(np.asarray(inputs["L1_ffn_W1"]))
    g["l1f_W2"] = np.ascontiguousarray(np.asarray(inputs["L1_ffn_W2"]))
    g["l1f_b1"] = np.ascontiguousarray(np.asarray(inputs["L1_ffn_b1"]).reshape(8, 32, 128).transpose(0, 2, 1))
    g["l1f_b2"] = np.ascontiguousarray(np.asarray(inputs["L1_ffn_b2"]).reshape(8, 8, 128).transpose(0, 2, 1))
    for c in range(NCORE):
        own = runs_of_core(c)
        prec = np.zeros((32, 4), dtype=np.float32)
        for g_o in range(NCORE):
            for g_r, (gb_, gj) in enumerate(runs_of_core(g_o)):
                for r, (b_, j_) in enumerate(own):
                    if gb_ * 16 + gj < b_ * 16 + j_:
                        prec[g_o * 4 + g_r, r] = 1.0
        pc[c]["c_moeprec"] = prec

    g["l3a_wslow"] = np.ascontiguousarray(np.asarray(inputs["L3_attn_w_slow"]))
    g["l3a_wout"] = np.ascontiguousarray(np.asarray(inputs["L3_attn_w_out"]).reshape(D, D))
    g["l3a_lnw"] = fm_vec(inputs["L3_attn_ln_w"])
    g["l3a_lnb"] = fm_vec(inputs["L3_attn_ln_b"])

    for c in range(NCORE):
        sel = np.zeros((33, 4), dtype=np.float32)
        for r, (b, j) in enumerate(runs_of_core(c)):
            if j == 0:
                sel[32, r] = 1.0
            else:
                o = owner_of_chunk(j - 1)
                sel[o * 4 + run_index(o, b, j - 1), r] = 1.0
        pc[c]["c_bndsel"] = sel

    return g, pc


LAYER_RWKV = {
    "l2f": dict(tag="l2f", WrT="l2f_WrT", WkT="l2f_WkT", WvT="l2f_WvT", mur="l2f_mur", muk="l2f_muk"),
    "l3f": dict(tag="l3f", WrT="l3f_WrT", WkT="l3f_WkT", WvT="l3f_WvT", mur="l3f_mur", muk="l3f_muk"),
}


def build_program(block_ids, debug_stop=None):
    stack = ExitStack()
    nc = bass.Bass("TRN2", target_bir_lowering=False, debug=False)
    p = Prog(nc, stack)
    with stack:
        for s in ("wld", "xld", "te", "act", "dve", "cc"):
            p.sem(s)
        xin = p.din("xres_in", [128, DC, TLOC])
        yout = p.dout("xres_out", [128, DC, TLOC])
        alloc_common(p)
        ld = dma_sp(p, p.sb["xres"][:, :, :], xin[:, :, :])
        for e in ("te", "act", "dve", "pool"):
            p.waitpt(e, "xld", "wld", "dve")

        for bid in block_ids:
            p.block_fence()
            p.carve_reset()
            if bid in LAYER_RWKV:
                emit_rwkv(p, LAYER_RWKV[bid])
            elif bid == "l0a":
                emit_attn(p, dict(tag="l0a", kind="l0", WqT="l0a_WqT", WkT="l0a_WkT",
                                  WvT="l0a_WvT"), debug_stop=debug_stop)
            elif bid == "l2a":
                emit_attn(p, dict(tag="l2a", kind="l2", WqT="l2a_WqT", WkT="l2a_WkT",
                                  WvT="l2a_WvT", WoT="l2a_WoT", bo="l2a_bo"), debug_stop=debug_stop)
            elif bid == "l1a":
                emit_attn(p, dict(tag="l1a", kind="l1", WqT="l1a_WqT", Wkvd="l1a_Wkvd",
                                  WkuT="l1a_WkuT", WvuT="l1a_WvuT", WoT="l1a_WoT"),
                          debug_stop=debug_stop)
            elif bid == "l3a":
                emit_l3a(p, {}, debug_stop=debug_stop)
            elif bid == "l0f":
                emit_moe(p, dict(tag="l0f", kind="l0f", gwT="l0f_gwT", W1="l0f_W1", W2="l0f_W2"),
                         debug_stop=debug_stop)
            elif bid == "l1f":
                emit_moe(p, dict(tag="l1f", kind="l1f", gwT="l1f_gwT", W1="l1f_W1", W2="l1f_W2"),
                         debug_stop=debug_stop)
            else:
                raise NotImplementedError(bid)

        dma_sp(p, yout[:, :, :], p.sb["xres"][:, :, :],
               wait=[("dve", p.cnt["dve"]), ("te", p.cnt["te"]), ("act", p.cnt["act"])])
        p.waitpt("sp", "xld")
        p.build()
    return p


_GROUPS = [("l0a", "l0f", "l1a", "l1f", "l2a", "l2f", "l3a", "l3f")]


def kernel(**inputs):
    glob, pc = prep_host(inputs)
    xcur = [pc[c]["xres_in"] for c in range(NCORE)]
    for grp in _GROUPS:
        p = build_program(grp)
        in_maps = []
        for c in range(NCORE):
            m = {}
            for name in p.inputs:
                if name == "xres_in":
                    m[name] = xcur[c]
                elif name in pc[c]:
                    m[name] = pc[c][name]
                else:
                    m[name] = glob[name]
            in_maps.append(m)
        res = run_bass_kernel_spmd(p.nc, in_maps, core_ids=list(range(NCORE)))
        xcur = [res.results[c]["xres_out"] for c in range(NCORE)]
    out = np.empty((B, S, D), dtype=np.float32)
    for c in range(NCORE):
        for r, (b, j) in enumerate(runs_of_core(c)):
            seg = xcur[c][:, :, r * CH:(r + 1) * CH]
            out[b, j * CH:(j + 1) * CH, :] = seg.transpose(1, 0, 2).reshape(D, CH).T
    return out


# version 04
# speedup vs baseline: 1.0565x; 1.0462x over previous
import numpy as np
import ml_dtypes
from contextlib import ExitStack

import concourse.bass as bass
import concourse.mybir as mybir
from concourse.bass_utils import run_bass_kernel_spmd
from concourse.aot_env import allow_ici_all_to_all

F32 = mybir.dt.float32
BF16 = mybir.dt.bfloat16
AF = mybir.ActivationFunctionType
ALU = mybir.AluOpType
BF = ml_dtypes.bfloat16

NCORE = 8
B, S, D = 2, 8192, 1024
DC = 8
CH = 512
NCH = 16
TLOC = 4 * CH
NRUN = 4
LN_EPS = 1e-5

BLOCKS = ["l0a", "l0f", "l1a", "l1f", "l2a", "l2f", "l3a", "l3f"]


def runs_of_core(c):
    return [(0, c), (0, 15 - c), (1, c), (1, 15 - c)]


def owner_of_chunk(j):
    return min(j, 15 - j)


def run_index(c, b, j):
    return runs_of_core(c).index((b, j))


class Prog:
    ENGINES = ("te", "act", "dve", "pool", "sp")

    def __init__(self, nc, stack):
        self.nc = nc
        self.stack = stack
        self.steps = {e: [] for e in self.ENGINES}
        self.semh = {}
        self.cnt = {}
        self.sb = {}
        self.dram = {}
        self.inputs = []
        self.ps = None
        self.pt = None

    def sem(self, name):
        if name not in self.semh:
            self.semh[name] = self.stack.enter_context(self.nc.semaphore(name))
            self.cnt[name] = 0
        return self.semh[name]

    def sbuf(self, name, shape, dtype):
        if name not in self.sb:
            self.sb[name] = self.stack.enter_context(self.nc.sbuf_tensor(name, shape, dtype))
        return self.sb[name]

    UNI_BYTES = 131008

    def alloc_uni(self):
        self.uni = self.stack.enter_context(
            self.nc.sbuf_tensor("UNI", [128, self.UNI_BYTES // 2], BF16))
        self.carve_off = 0
        self.carve_hw = 0
        self.carved = set()

    def carve_reset(self):
        for n in self.carved:
            self.sb.pop(n, None)
        self.carved = set()
        self.carve_off = 0

    def carve_mark(self):
        return self.carve_off

    def carve_rewind(self, mark, names):
        for n in names:
            if n in self.carved:
                self.carved.discard(n)
                self.sb.pop(n, None)
        self.carve_off = mark

    def carve(self, name, shape, dtype):
        if name in self.sb:
            return self.sb[name]
        import math as _m
        nb = int(np.prod(shape[1:])) * mybir.dt.size(dtype)
        nb_al = (nb + 63) & ~63
        off = self.carve_off
        assert off + nb_al <= self.UNI_BYTES, \
            f"UNI overflow carving {name}: off={off} need={nb_al} cap={self.UNI_BYTES}"
        view = self.uni[0:shape[0], off // 2:(off + nb) // 2]
        if dtype != BF16:
            view = view.bitcast(dtype)
        if len(shape) > 2:
            dims = " ".join(f"d{i}" for i in range(1, len(shape)))
            kw = {f"d{i}": shape[i] for i in range(1, len(shape))}
            view = view.rearrange(f"p ({dims}) -> p {dims}", **kw)
        self.sb[name] = view
        self.carved.add(name)
        self.carve_off = off + nb_al
        self.carve_hw = max(self.carve_hw, self.carve_off)
        return view

    def block_fence(self):
        sems = [s for s in ("te", "act", "dve", "xld", "wld0", "wld1", "wld2", "wld3", "cc")
                if s in self.cnt]
        for e in self.ENGINES:
            self.waitpt(e, *sems)

    def din(self, name, shape, dtype=F32):
        if name not in self.dram:
            self.dram[name] = self.nc.dram_tensor(name, shape, dtype, kind="ExternalInput")
            self.inputs.append(name)
        return self.dram[name].ap()

    def dout(self, name, shape, dtype=F32):
        if name not in self.dram:
            self.dram[name] = self.nc.dram_tensor(name, shape, dtype, kind="ExternalOutput")
        return self.dram[name].ap()

    def dint(self, name, shape, dtype=F32):
        if name not in self.dram:
            self.dram[name] = self.nc.dram_tensor(name, shape, dtype, kind="Internal")
        return self.dram[name].ap()

    def emit(self, eng, fn, wait=(), inc=None, inc_by=1):
        for s, _ in wait:
            self.sem(s)
        if inc:
            self.sem(inc)
        if eng in ("act", "dve"):
            wait = [(eng, self.cnt.get(eng, 0))] + list(wait)
        wait = tuple((s, v) for s, v in wait if v > 0)

        snap = dict(self.sb)

        def run(e, _fn=fn, _wait=wait, _inc=inc, _by=inc_by, _snap=snap):
            saved = self.sb
            self.sb = _snap
            try:
                for s, v in _wait:
                    e.wait_ge(self.semh[s], v)
                last = _fn(e)
                if _inc is not None:
                    assert last is not None, f"step on {eng} with inc={_inc} returned no instruction"
                    last.then_inc(self.semh[_inc], _by)
            finally:
                self.sb = saved
        self.steps[eng].append(run)
        if inc:
            self.cnt[inc] += inc_by
            return self.cnt[inc]
        return None

    def waitpt(self, eng, *sems):
        w = [(s, self.cnt.get(s, 0)) for s in sems]
        self.emit(eng, lambda e: None, wait=w)

    def build(self):
        eng_map = dict(te="tensor", act="scalar", dve="vector", pool="gpsimd", sp="sync")
        with self.nc.Block() as block:
            for eng in self.ENGINES:
                steps = self.steps[eng]
                getattr(block, eng_map[eng])(lambda e, _s=steps: [f(e) for f in _s])


NWLD = 4


def dma_pool(p, out_ap, in_ap, wait=(), sem=None):
    if sem is None:
        i = getattr(p, "_wldi", 0)
        p._wldi = (i + 1) % NWLD
        sem = f"wld{i}"
    w = [(sem, p.cnt.get(sem, 0))] + list(wait)
    cnt = p.emit("pool", lambda e: e.dma_start(out=out_ap, in_=in_ap),
                 wait=w, inc=sem, inc_by=16)
    return (sem, cnt)


def wld_all(p):
    return [(f"wld{i}", p.cnt.get(f"wld{i}", 0)) for i in range(NWLD)]


def dma_sp(p, out_ap, in_ap, wait=(), sem="xld"):
    w = [(sem, p.cnt.get(sem, 0))] + list(wait)
    return p.emit("sp", lambda e: e.dma_start(out=out_ap, in_=in_ap),
                  wait=w, inc=sem, inc_by=16)


def bank(p, i, rows=128, width=512):
    return p.ps[0:rows, i * 512:i * 512 + width]


HC_INV_D = 0
HC_EPS = 1
HC_HALF = 2
HC_Q2 = 3
HC_Q2L0 = 4
HC_ZERO = 5
HC_NEPS = 6
HC_ONE = 7


def alloc_common(p):
    p.sbuf("xres", [128, DC, TLOC], F32)
    p.sbuf("ones_bf", [128, 2], BF16)
    p.sbuf("ones_f", [128, 2], F32)
    p.sbuf("onesr_f", [1, 128], F32)
    p.sbuf("onesr_b", [1, 128], BF16)
    p.sbuf("hc", [1, 16], F32)
    p.sbuf("idf", [128, 128], F32)
    p.sbuf("idb", [128, 128], BF16)
    p.alloc_uni()
    p.ps = p.stack.enter_context(p.nc.psum_tensor("PS", [128, 7 * 512], F32))
    p.pt = p.stack.enter_context(p.nc.psum_tensor("PT", [128, 1024], BF16))

    idf = p.din("c_ident", [128, 128])
    hc = p.din("c_hc", [1, 16])
    dma_sp(p, p.sb["idf"][:, :], idf[:, :])
    dma_pool(p, p.sb["idb"][:, :], idf[:, :])
    ld = dma_sp(p, p.sb["hc"][:, :], hc[:, :])
    p.emit("dve", lambda e: e.memset(p.sb["ones_bf"][:, :], 1.0),
           wait=wld_all(p) + [("xld", ld)], inc="dve")
    p.emit("dve", lambda e: e.memset(p.sb["ones_f"][:, :], 1.0), inc="dve")
    p.emit("dve", lambda e: e.memset(p.sb["onesr_f"][:, :], 1.0), inc="dve")
    p.emit("dve", lambda e: e.memset(p.sb["onesr_b"][:, :], 1.0), inc="dve")


def emit_ln(p, keep_xln=True, keep_bc=False, src="xres"):
    p.carve("stat_m", [1, TLOC], F32)
    p.carve("stat_r", [1, TLOC], F32)
    p.carve("lnst", [1, TLOC], F32)
    p.carve("lnsq", [128, 1024], BF16)
    p.carve("lnsq2", [128, 1024], BF16)
    xres, lnsq, lnsq2, hc = p.sb["xres"], p.sb["lnsq"], p.sb["lnsq2"], p.sb["hc"]
    if keep_bc:
        p.carve("mbc", [128, TLOC], F32)
        p.carve("rbc", [128, TLOC], F32)

    for half in range(2):
        c0 = half * 1024
        for dc in range(DC):
            if src == "xres":
                dv1 = p.emit("dve", lambda e, dc=dc, c0=c0: e.tensor_copy(
                    lnsq[:, :], xres[:, dc, c0:c0 + 1024]),
                    wait=[("te", p.cnt["te"])], inc="dve")
            else:
                dv1 = p.emit("dve", lambda e, dc=dc, c0=c0: e.tensor_copy(
                    lnsq[:, :], p.sb[src][:, dc, c0:c0 + 1024]),
                    wait=[("te", p.cnt["te"])], inc="dve")
            ac2 = p.emit("act", lambda e: e.activation(lnsq2[:, :], lnsq[:, :], AF.Square),
                         wait=[("dve", dv1)], inc="act")

            def mm(e, dc=dc):
                for tg2 in range(2):
                    sl = slice(tg2 * 512, tg2 * 512 + 512)
                    e.matmul(bank(p, tg2, rows=1), p.sb["ones_bf"][:, 0:1],
                             lnsq[:, sl], start=(dc == 0), stop=(dc == DC - 1))
                    last = e.matmul(bank(p, 2 + tg2, rows=1), p.sb["ones_bf"][:, 0:1],
                                    lnsq2[:, sl], start=(dc == 0), stop=(dc == DC - 1))
                return last
            p.emit("te", mm, wait=[("act", ac2), ("dve", dv1)], inc="te")
        te_done = p.cnt["te"]

        def evac(e, half=half):
            e.tensor_copy(p.sb["stat_m"][0:1, half * 1024:half * 1024 + 1024],
                          p.ps[0:1, 0:1024])
            return e.tensor_copy(p.sb["stat_r"][0:1, half * 1024:half * 1024 + 1024],
                                 p.ps[0:1, 1024:2048])
        p.emit("dve", evac, wait=[("te", te_done)], inc="dve")

    p.emit("dve", lambda e: e.tensor_scalar(p.sb["stat_m"][0:1, :], p.sb["stat_m"][0:1, :],
                                            hc[0:1, HC_INV_D:HC_INV_D + 1], None, ALU.mult), inc="dve")
    p.emit("dve", lambda e: e.tensor_scalar(p.sb["stat_r"][0:1, :], p.sb["stat_r"][0:1, :],
                                            hc[0:1, HC_INV_D:HC_INV_D + 1], None, ALU.mult), inc="dve")
    p.emit("dve", lambda e: e.tensor_mul(p.sb["lnst"][0:1, :], p.sb["stat_m"][0:1, :],
                                         p.sb["stat_m"][0:1, :]), inc="dve")
    dv = p.emit("dve", lambda e: e.tensor_sub(p.sb["stat_r"][0:1, :], p.sb["stat_r"][0:1, :],
                                              p.sb["lnst"][0:1, :]), inc="dve")
    ac = p.emit("act", lambda e: e.activation(p.sb["lnst"][0:1, :], p.sb["stat_r"][0:1, :],
                                              AF.Sqrt, bias=hc[0:1, HC_EPS:HC_EPS + 1]),
                wait=[("dve", dv)], inc="act")
    dv = p.emit("dve", lambda e: e.reciprocal(p.sb["stat_r"][0:1, :], p.sb["lnst"][0:1, :]),
                wait=[("act", ac)], inc="dve")

    if keep_xln or keep_bc:
        for tg in range(NRUN):
            sl = slice(tg * 512, tg * 512 + 512)

            def bc(e, sl=sl):
                e.matmul(bank(p, 4), p.sb["onesr_f"][0:1, :], p.sb["stat_m"][0:1, sl],
                         start=True, stop=True)
                return e.matmul(bank(p, 5), p.sb["onesr_f"][0:1, :], p.sb["stat_r"][0:1, sl],
                                start=True, stop=True)
            te = p.emit("te", bc, wait=[("dve", dv), ("dve", p.cnt["dve"]),
                                        ("act", p.cnt["act"])], inc="te")
            ac_sv = 0
            if keep_bc:
                def sv(e, sl=sl):
                    e.activation(p.sb["mbc"][:, sl], bank(p, 4), AF.Copy)
                    return e.activation(p.sb["rbc"][:, sl], bank(p, 5), AF.Copy)
                ac_sv = p.emit("act", sv, wait=[("te", te)], inc="act")
            if keep_xln:
                p.carve("xln", [128, DC, TLOC], BF16)
                xln = p.sb["xln"]

                def ap1(e, sl=sl):
                    mb = bank(p, 4).unsqueeze(1).broadcast_to([128, DC, 512])
                    return e.tensor_tensor(xln[:, :, sl], xres[:, :, sl], mb, ALU.subtract)
                p.emit("dve", ap1, wait=[("te", te), ("act", ac_sv)], inc="dve")

                def ap2(e, sl=sl):
                    rb = bank(p, 5).unsqueeze(1).broadcast_to([128, DC, 512])
                    return e.tensor_tensor(xln[:, :, sl], xln[:, :, sl], rb, ALU.mult)
                p.emit("dve", ap2, inc="dve")


def emit_rwkv(p, lw):
    tag = lw["tag"]
    WrT = p.din(lw["WrT"], [D, D])
    WkT = p.din(lw["WkT"], [D, D])
    WvT = p.din(lw["WvT"], [D, D])
    mur = p.din(lw["mur"], [128, DC])
    muk = p.din(lw["muk"], [128, DC])
    bndsel = p.din("c_bndsel", [33, 4])

    xres = p.sb["xres"]
    p.carve("mu2", [128, 2, DC], F32)
    p.carve("bsel33", [33, 4], F32)
    p.carve("xlf", [128, DC, CH], F32)
    p.carve("scr_f1", [128, DC, CH], F32)
    p.carve("sbbnd", [128, NRUN, DC], F32)
    p.carve("x0col", [128, DC, NRUN], F32)
    p.carve("bndt", [33, 1024], F32)
    p.carve("bndev", [4, 1024], F32)
    p.carve("xprev0", [128, DC, NRUN], F32)
    p.carve("xrmix", [128, DC, CH], BF16)
    p.carve("xkmix", [128, DC, CH], BF16)
    p.carve("wtile", [128, 2, DC, 128], BF16)
    p.carve("rgate", [128, DC, CH], F32)
    p.carve("kact", [128, DC, CH], BF16)
    p.carve("vtmp", [128, CH], F32)
    p.carve("vtmp2", [128, CH], F32)
    p.carve("d0scr", [128, DC], F32)

    bnd_in = p.dint(f"{tag}_bnd_in", [NRUN, 1024])
    bnd_all = p.dint(f"{tag}_bnd_all", [4 * NCORE, 1024])

    dma_sp(p, p.sb["mu2"][:, 0, :], mur[:, :])
    dma_sp(p, p.sb["mu2"][:, 1, :], muk[:, :])
    dma_sp(p, p.sb["bsel33"][:, :], bndsel[:, :])
    p.waitpt("dve", "xld")

    emit_ln(p, keep_xln=False, keep_bc=False)

    def bcast_stats(r):
        sl = slice(r * 512, r * 512 + 512)

        def bc(e, sl=sl):
            e.matmul(bank(p, 4), p.sb["onesr_f"][0:1, :], p.sb["stat_m"][0:1, sl],
                     start=True, stop=True)
            return e.matmul(bank(p, 5), p.sb["onesr_f"][0:1, :], p.sb["stat_r"][0:1, sl],
                            start=True, stop=True)
        return p.emit("te", bc, wait=[("dve", p.cnt["dve"]), ("act", p.cnt["act"])], inc="te")

    def mk_xlf(r, te):
        sl = slice(r * 512, r * 512 + 512)

        def f1(e, sl=sl):
            mb = bank(p, 4).unsqueeze(1).broadcast_to([128, DC, 512])
            return e.tensor_tensor(p.sb["xlf"][:, :, :], xres[:, :, sl], mb, ALU.subtract)

        def f2(e, sl=sl):
            rb = bank(p, 5).unsqueeze(1).broadcast_to([128, DC, 512])
            return e.tensor_tensor(p.sb["xlf"][:, :, :], p.sb["xlf"][:, :, :], rb, ALU.mult)
        p.emit("dve", f1, wait=[("te", te)], inc="dve")
        return p.emit("dve", f2, inc="dve")

    for r in range(NRUN):
        te = bcast_stats(r)
        mk_xlf(r, te)
        p.emit("dve", lambda e, r=r: e.tensor_copy(p.sb["sbbnd"][:, r, :], p.sb["xlf"][:, :, 511]),
               inc="dve")
        p.emit("dve", lambda e, r=r: e.tensor_copy(p.sb["x0col"][:, :, r], p.sb["xlf"][:, :, 0]),
               inc="dve")
    dv_pub = p.cnt["dve"]

    st = p.cnt["xld"]
    for r in range(NRUN):
        st = dma_sp(p, bnd_in[r].rearrange("(pp dcc) -> pp dcc", pp=128),
                    p.sb["sbbnd"][:, r, :],
                    wait=[("dve", dv_pub)] if r == 0 else ())

    def coll(e):
        return e.collective_compute(
            "AllGather", ALU.bypass, replica_groups=[list(range(NCORE))],
            ins=[bnd_in.opt()], outs=[bnd_all.opt()])
    cc = p.emit("pool", coll, wait=[("xld", st)], inc="cc")
    ld = dma_sp(p, p.sb["bndt"][0:32, :], bnd_all[:, :], wait=[("cc", cc)])
    dvz = p.emit("dve", lambda e: e.memset(p.sb["bndt"][32:33, :], 0.0),
                 wait=[("xld", ld)], inc="dve")

    def gmm(e):
        e.matmul(bank(p, 4, rows=4), p.sb["bsel33"][:, :], p.sb["bndt"][:, 0:512],
                 start=True, stop=True)
        return e.matmul(bank(p, 5, rows=4), p.sb["bsel33"][:, :], p.sb["bndt"][:, 512:1024],
                        start=True, stop=True)
    te = p.emit("te", gmm, wait=[("dve", dvz)], inc="te")

    def gev(e):
        e.tensor_copy(p.sb["bndev"][0:4, 0:512], bank(p, 4, rows=4))
        return e.tensor_copy(p.sb["bndev"][0:4, 512:1024], bank(p, 5, rows=4))
    dv = p.emit("dve", gev, wait=[("te", te)], inc="dve")

    ac_xp = p.cnt["act"]
    for dc in range(DC):
        def tr(e, dc=dc):
            src = p.sb["bndev"][0:4, :].rearrange("r (pp dcc) -> r pp dcc", pp=128)[:, :, dc]
            return e.transpose(bank(p, 6, rows=128, width=4), src, p.sb["idf"][0:4, 0:4])
        te = p.emit("te", tr, wait=[("dve", dv), ("act", ac_xp)], inc="te")
        ac_xp = p.emit("act", lambda e, dc=dc: e.activation(
            p.sb["xprev0"][:, dc, :], bank(p, 6, rows=128, width=4), AF.Copy),
            wait=[("te", te)], inc="act")

    wtile = p.sb["wtile"]
    slot_user = {0: 0, 1: 0}
    for tg in range(NRUN):
        sl = slice(tg * 512, tg * 512 + 512)
        te = bcast_stats(tg)
        p.waitpt("dve", "act")
        mk_xlf(tg, te)
        def mixsub(e, tg=tg):
            xv = p.sb["xlf"]
            e.tensor_tensor(p.sb["scr_f1"][:, :, 1:512], xv[:, :, 1:512], xv[:, :, 0:511],
                            ALU.subtract)
            return e.tensor_sub(p.sb["d0scr"][:, :], p.sb["x0col"][:, :, tg], p.sb["xprev0"][:, :, tg])
        p.emit("dve", mixsub, inc="dve")
        for mi, out in ((0, "xrmix"), (1, "xkmix")):
            def mix(e, tg=tg, mi=mi, out=out):
                xv = p.sb["xlf"]
                ov = p.sb[out]
                for dc in range(DC):
                    e.scalar_tensor_tensor(
                        ov[:, dc, 1:512], p.sb["scr_f1"][:, dc, 1:512],
                        p.sb["mu2"][:, mi, dc:dc + 1], xv[:, dc, 0:511], ALU.mult, ALU.add)
                    last = e.scalar_tensor_tensor(
                        ov[:, dc, 0:1], p.sb["d0scr"][:, dc:dc + 1],
                        p.sb["mu2"][:, mi, dc:dc + 1], p.sb["xprev0"][:, dc, tg:tg + 1],
                        ALU.mult, ALU.add)
                return last
            p.emit("dve", mix, inc="dve")
        dv_mix = p.cnt["dve"]

        for pi, (Wap, src) in enumerate(((WrT, "xrmix"), (WkT, "xkmix"))):
            for oc in range(DC):
                slot = (pi * 8 + oc) % 2
                wl = dma_pool(p, wtile[:, slot],
                              Wap.rearrange("(dcc pp) o -> pp dcc o", pp=128)[:, :, oc * 128:(oc + 1) * 128],
                              wait=[("te", slot_user[slot])])

                def mm(e, pi=pi, slot=slot, src=src):
                    for dcc in range(DC):
                        last = e.matmul(bank(p, pi), wtile[:, slot, dcc, :],
                                        p.sb[src][:, dcc, :],
                                        start=(dcc == 0), stop=(dcc == DC - 1))
                    return last
                te = p.emit("te", mm, wait=[wl, ("dve", dv_mix),
                                            ("act", p.cnt["act"])], inc="te")
                slot_user[slot] = te
                if pi == 0:
                    p.emit("act", lambda e, oc=oc: e.activation(
                        p.sb["rgate"][:, oc, :], bank(p, 0), AF.Sigmoid),
                        wait=[("te", te)], inc="act")
                else:
                    dvr = p.emit("dve", lambda e: e.tensor_scalar_max(
                        p.sb["vtmp"][:, :], bank(p, 1), 0.0), wait=[("te", te)], inc="dve")
                    p.emit("act", lambda e, oc=oc: e.activation(
                        p.sb["kact"][:, oc, :], p.sb["vtmp"][:, :], AF.Square),
                        wait=[("dve", dvr)], inc="act")
        ac_k = p.cnt["act"]
        for oc in range(DC):
            slot = oc % 2
            wl = dma_pool(p, wtile[:, slot],
                          WvT.rearrange("(dcc pp) o -> pp dcc o", pp=128)[:, :, oc * 128:(oc + 1) * 128],
                          wait=[("te", slot_user[slot])])

            def mmv(e, slot=slot):
                for dcc in range(DC):
                    last = e.matmul(bank(p, 2), wtile[:, slot, dcc, :], p.sb["kact"][:, dcc, :],
                                    start=(dcc == 0), stop=(dcc == DC - 1))
                return last
            te = p.emit("te", mmv, wait=[wl, ("act", ac_k),
                                         ("dve", p.cnt["dve"])], inc="te")
            slot_user[slot] = te

            p.emit("dve", lambda e, oc=oc: e.tensor_tensor(
                p.sb["vtmp2"][:, :], p.sb["rgate"][:, oc, :], bank(p, 2), ALU.mult),
                wait=[("te", te)], inc="dve")
            p.emit("dve", lambda e, oc=oc, sl=sl: e.tensor_add(
                xres[:, oc, sl], xres[:, oc, sl], p.sb["vtmp2"][:, :]), inc="dve")


def bcol(j):
    o = owner_of_chunk(j)
    return o * 1024 + (0 if j == o else 512)


def stagecol(b, j):
    o = owner_of_chunk(j)
    ro = 2 * b + (0 if j == o else 1)
    return o, ro


def va_slot(j, sub):
    return (bcol(j) // 128) + sub


def emit_attn(p, lw, debug_stop=None):
    tag = lw["tag"]
    kind = lw["kind"]
    xres = p.sb["xres"]

    NAUG = 65
    WoT = p.din(lw["WoT"], [D, D]) if kind != "l0" else None
    WqT = p.din(lw["WqT"], [D, 128])
    if kind == "l0":
        WkT = p.din(lw["WkT"], [D, 128])
        WvT = p.din(lw["WvT"], [D, 128])
        slopes_in = p.din("c_slopes", [128, 16])
        relh_in = p.din("c_relh", [128, 8, 512])
        rell_in = p.din("c_rell", [128, 8, 512])
        winm_in = p.din("c_winmask", [4, 128, 512])
    elif kind == "l2":
        WkT = p.din(lw["WkT"], [D, 128])
        WvT = p.din(lw["WvT"], [D, 128])
        bo = p.din(lw["bo"], [128, DC])
    else:
        Wkvd = p.din(lw["Wkvd"], [D, 256])
        WkuT = p.din(lw["WkuT"], [256, 128])
        WvuT = p.din(lw["WvuT"], [256, 128])
    diagm_in = p.din("c_diagmask", [4, 128, 512])

    p.carve("xln", [128, DC, TLOC], BF16)
    p.carve("xs", [128, DC, 512], BF16)
    if kind == "l0":
        p.carve("winm", [128, 4, 512], BF16)
        p.carve("slp", [128, 16], F32)
        p.carve("relh", [128, 8, 512], BF16)
        p.carve("rell", [128, 8, 512], BF16)
        p.carve("sidb", [128, 128], BF16)
    p.carve("wq_sb", [128, DC, 128], BF16)
    p.carve("wk_sb", [128, DC, 128], BF16)
    p.carve("wv_sb", [128, DC, 128], BF16)
    p.carve("diagm", [128, 4, 512], BF16)
    p.carve("qsc", [128, 1], F32)
    p.carve("kmx", [2, 4], F32)
    if kind != "l0":
        p.carve("wtile", [128, 2, DC, 128], BF16)
        p.carve("vtmp", [128, CH], F32)
    p.carve("bo_sb", [128, DC], F32)
    if kind == "l1":
        p.carve("lat_sb", [128, 2, 512], BF16)
        p.carve("wkvd_sb", [128, DC, 256], BF16)

    ag_in = p.dint(f"{tag}_ag_in", [128, DC * TLOC], BF16)
    xag = p.dint(f"{tag}_xag", [NCORE, 128, DC, TLOC], BF16)
    a2a_in = p.dint(f"{tag}_a2a_in", [NCORE * 128, TLOC], BF16)
    a2a_out = p.dint(f"{tag}_a2a_out", [NCORE * 128, TLOC], BF16)
    if kind == "l1":
        lat_in = p.dint(f"{tag}_lat_in", [128, 2 * TLOC], BF16)
        lat_all = p.dint(f"{tag}_lat_all", [NCORE, 128, 2, TLOC], BF16)

    dma_pool(p, p.sb["wq_sb"][:, :, :], WqT.rearrange("(dcc pp) o -> pp dcc o", pp=128))
    if kind == "l0":
        dma_pool(p, p.sb["wk_sb"][:, :, :], WkT.rearrange("(dcc pp) o -> pp dcc o", pp=128))
        dma_pool(p, p.sb["wv_sb"][:, :, :], WvT.rearrange("(dcc pp) o -> pp dcc o", pp=128))
        dma_pool(p, p.sb["winm"][:, :, :], winm_in.rearrange("m pp q -> pp m q"))
        dma_sp(p, p.sb["slp"][:, :], slopes_in[:, :])
        dma_pool(p, p.sb["relh"][:, :, :], relh_in[:, :, :])
        dma_pool(p, p.sb["rell"][:, :, :], rell_in[:, :, :])
        p.emit("dve", lambda e: e.memset(p.sb["bo_sb"][:, :], 0.0), inc="dve")
    elif kind == "l2":
        dma_pool(p, p.sb["wk_sb"][:, :, :], WkT.rearrange("(dcc pp) o -> pp dcc o", pp=128))
        dma_pool(p, p.sb["wv_sb"][:, :, :], WvT.rearrange("(dcc pp) o -> pp dcc o", pp=128))
        dma_pool(p, p.sb["bo_sb"][:, :], bo[:, :])
    else:
        dma_pool(p, p.sb["wk_sb"][:, 0:2, :], WkuT.rearrange("(lc pp) o -> pp lc o", pp=128))
        dma_pool(p, p.sb["wv_sb"][:, 0:2, :], WvuT.rearrange("(lc pp) o -> pp lc o", pp=128))
        p.emit("dve", lambda e: e.memset(p.sb["bo_sb"][:, :], 0.0), inc="dve")
    dma_pool(p, p.sb["diagm"][:, :, :], diagm_in.rearrange("m pp q -> pp m q"))
    p.emit("dve", lambda e: e.memset(p.sb["qsc"][:, :], 0.125 if kind != "l0" else 0.03125),
           inc="dve")

    mark = p.carve_mark()
    emit_ln(p, keep_xln=True)
    st = dma_sp(p, ag_in[:, :], p.sb["xln"].rearrange("p dc t -> p (dc t)"),
                wait=[("dve", p.cnt["dve"])])
    cc = p.emit("pool", lambda e: e.collective_compute(
        "AllGather", ALU.bypass, replica_groups=[list(range(NCORE))],
        ins=[ag_in.opt()], outs=[xag.rearrange("o pp f1 f2 -> (o pp) (f1 f2)").opt()]),
        wait=[("xld", st)] + wld_all(p), inc="cc")

    if debug_stop == "ag":
        return
    if kind == "l1":
        dma_pool(p, p.sb["wkvd_sb"][:, :, :], Wkvd.rearrange("(dcc pp) o -> pp dcc o", pp=128))
        wlats = wld_all(p)
        for tg in range(NRUN):
            sl = slice(tg * 512, tg * 512 + 512)
            for lc in range(2):
                def mml(e, sl=sl, lc=lc):
                    for dcc in range(DC):
                        last = e.matmul(bank(p, 2), p.sb["wkvd_sb"][:, dcc, lc * 128:(lc + 1) * 128],
                                        p.sb["xln"][:, dcc, sl], start=(dcc == 0), stop=(dcc == DC - 1))
                    return last
                te = p.emit("te", mml, wait=wlats + [("dve", p.cnt["dve"]),
                                             ("act", p.cnt["act"]), ("xld", p.cnt["xld"])], inc="te")
                p.emit("act", lambda e, lc=lc: e.activation(
                    p.sb["lat_sb"][:, lc, :], bank(p, 2), AF.Copy),
                    wait=[("te", te), ("xld", p.cnt["xld"])], inc="act")
            dma_sp(p, lat_in.rearrange("p (l t) -> p l t", l=2)[:, :, sl],
                   p.sb["lat_sb"][:, :, :], wait=[("act", p.cnt["act"])])
        cc = p.emit("pool", lambda e: e.collective_compute(
            "AllGather", ALU.bypass, replica_groups=[list(range(NCORE))],
            ins=[lat_in.opt()], outs=[lat_all.rearrange("o pp f1 f2 -> (o pp) (f1 f2)").opt()]),
            wait=[("xld", p.cnt["xld"])], inc="cc")

    p.carve_rewind(mark, ["stat_m", "stat_r", "lnst", "lnsq", "lnsq2"])
    p.carve("lnsq", [128, 512], BF16)
    p.carve("KA", [NAUG, 8192], BF16)
    p.carve("QA", [NAUG, 8192], BF16)
    p.carve("VA", [128, 64, 65], BF16)
    p.carve("ptile", [128, 2, 1024], BF16)
    p.carve("vfm", [64, 512], BF16)
    p.carve("ofsb", [64, 512], F32)
    p.carve("rden", [1, 512], F32)

    def initk(e):
        e.memset(p.sb["KA"][64:65, :], -1.0)
        return e.memset(p.sb["VA"][:, :, 64:65], 1.0)
    p.emit("dve", initk, wait=[("xld", p.cnt["xld"]), ("te", p.cnt["te"]),
                               ("act", p.cnt["act"])], inc="dve")

    if debug_stop == "init":
        return
    if debug_stop == "xsdma":
        dma_sp(p, p.sb["xs"][:, :, :], xag[3][:, :, 512:1024], wait=[("cc", p.cnt["cc"])])
        dma_sp(p, p.sb["lnsq"][0:64, :], xag[5][0:64, 2, 1024:1536], wait=[("cc", p.cnt["cc"])])
        return

    for b in range(2):
        for hl in range(2):
            hsl = slice(hl * 64, hl * 64 + 64)
            dvk = p.emit("dve", lambda e: e.memset(p.sb["kmx"][0:1, 0:4], 0.0), inc="dve")
            if kind == "l0":
                p.emit("dve", lambda e, hl=hl: e.tensor_scalar(
                    p.sb["sidb"][:, :], p.sb["idb"][:, :], p.sb["slp"][:, hl:hl + 1],
                    None, ALU.mult), inc="dve")
            xs_user = p.cnt["te"]
            ntg16 = 1 if (debug_stop or "").startswith("projk1") else 16
            if (debug_stop or "").startswith("projk1") and (b > 0 or hl > 0):
                ntg16 = 0
            for tg16 in range(ntg16):
                o, half = tg16 // 2, tg16 % 2
                lcol = 2 * b * 512 + half * 512
                gcol = o * 1024 + half * 512
                xsl = dma_sp(p, p.sb["xs"][:, :, :],
                             xag[o][:, :, lcol:lcol + 512],
                             wait=[("te", xs_user), ("cc", p.cnt["cc"])])
                if kind == "l1":
                    xsl = dma_sp(p, p.sb["lat_sb"][:, :, :],
                                 lat_all[o][:, :, lcol:lcol + 512],
                                 wait=[("te", xs_user), ("cc", p.cnt["cc"])])

                def mmk(e, hsl=hsl):
                    if kind != "l1":
                        for dcc in range(DC):
                            last = e.matmul(bank(p, 2, rows=64), p.sb["wk_sb"][:, dcc, hsl],
                                            p.sb["xs"][:, dcc, :], start=(dcc == 0), stop=(dcc == DC - 1))
                    else:
                        for lc in range(2):
                            last = e.matmul(bank(p, 2, rows=64), p.sb["wk_sb"][:, lc, hsl],
                                            p.sb["lat_sb"][:, lc, :], start=(lc == 0), stop=(lc == 1))
                    return last
                te = p.emit("te", mmk, wait=[("xld", xsl)] + wld_all(p) + [
                                             ("dve", p.cnt["dve"]), ("act", p.cnt["act"])], inc="te")
                dvka = p.emit("dve", lambda e, gcol=gcol: e.tensor_copy(
                    p.sb["KA"][0:64, gcol:gcol + 512], bank(p, 2, rows=64)),
                    wait=[("te", te)], inc="dve")
                if debug_stop == "projk1a":
                    xs_user = p.cnt["te"]
                    continue
                if debug_stop == "projk1bA":
                    p.sbuf("realsq", [128, 512], BF16)
                    p.emit("act", lambda e: e.activation(p.sb["realsq"][0:64, :],
                                                         bank(p, 2, rows=64), AF.Square),
                           wait=[("te", te), ("dve", dvka)], inc="act")
                    xs_user = p.cnt["te"]
                    continue
                if debug_stop == "projk1bB":
                    p.emit("act", lambda e: e.activation(p.sb["lnsq"][0:64, :],
                                                         bank(p, 2, rows=64), AF.Copy),
                           wait=[("te", te), ("dve", dvka)], inc="act")
                    xs_user = p.cnt["te"]
                    continue
                ac = p.emit("act", lambda e: e.activation(p.sb["lnsq"][0:64, :],
                                                          bank(p, 2, rows=64), AF.Square),
                            wait=[("te", te), ("dve", dvka)], inc="act")
                if debug_stop == "projk1b0":
                    xs_user = p.cnt["te"]
                    continue
                te2 = p.emit("te", lambda e: e.matmul(bank(p, 5, rows=1), p.sb["ones_bf"][0:64, 0:1],
                                                      p.sb["lnsq"][0:64, :], start=True, stop=True),
                             wait=[("act", ac)], inc="te")

                if debug_stop == "projk1b":
                    xs_user = p.cnt["te"]
                    continue

                p.emit("dve", lambda e: e.reduce_max(p.sb["rden"][0:1, 0:1], bank(p, 5, rows=1),
                                                     axis=mybir.AxisListType.X),
                       wait=[("te", te2)], inc="dve")
                p.emit("dve", lambda e: e.tensor_max(p.sb["kmx"][0:1, 0:1], p.sb["kmx"][0:1, 0:1],
                                                     p.sb["rden"][0:1, 0:1]), inc="dve")
                if debug_stop == "projk1c":
                    xs_user = p.cnt["te"]
                    continue

                if debug_stop == "projk":
                    xs_user = p.cnt["te"]
                    continue
                def mmq(e, hsl=hsl):
                    for dcc in range(DC):
                        last = e.matmul(bank(p, 3, rows=64), p.sb["wq_sb"][:, dcc, hsl],
                                        p.sb["xs"][:, dcc, :], start=(dcc == 0), stop=(dcc == DC - 1))
                    return last
                te = p.emit("te", mmq, wait=[("act", p.cnt["act"]), ("dve", p.cnt["dve"])], inc="te")
                dvqa = p.emit("dve", lambda e, gcol=gcol: e.tensor_scalar(
                    p.sb["QA"][0:64, gcol:gcol + 512], bank(p, 3, rows=64),
                    p.sb["qsc"][0:64, 0:1], None, ALU.mult), wait=[("te", te)], inc="dve")
                ac = p.emit("act", lambda e: e.activation(p.sb["lnsq"][0:64, :],
                                                          bank(p, 3, rows=64), AF.Square),
                            wait=[("te", te), ("dve", dvqa)], inc="act")
                te2 = p.emit("te", lambda e: e.matmul(bank(p, 6, rows=1), p.sb["ones_bf"][0:64, 0:1],
                                                      p.sb["lnsq"][0:64, :], start=True, stop=True),
                             wait=[("act", ac)], inc="te")
                p.emit("dve", lambda e, gcol=gcol: e.tensor_copy(
                    p.sb["QA"][64:65, gcol:gcol + 512], bank(p, 6, rows=1)),
                    wait=[("te", te2)], inc="dve")

                if debug_stop == "projq":
                    xs_user = p.cnt["te"]
                    continue
                def mmv(e, hsl=hsl):
                    if kind != "l1":
                        for dcc in range(DC):
                            last = e.matmul(bank(p, 4, rows=64), p.sb["wv_sb"][:, dcc, hsl],
                                            p.sb["xs"][:, dcc, :], start=(dcc == 0), stop=(dcc == DC - 1))
                    else:
                        for lc in range(2):
                            last = e.matmul(bank(p, 4, rows=64), p.sb["wv_sb"][:, lc, hsl],
                                            p.sb["lat_sb"][:, lc, :], start=(lc == 0), stop=(lc == 1))
                    return last
                te = p.emit("te", mmv, wait=[("act", p.cnt["act"]), ("dve", p.cnt["dve"])], inc="te")
                ac = p.emit("act", lambda e: e.activation(p.sb["vfm"][0:64, :], bank(p, 4, rows=64),
                                                          AF.Copy), wait=[("te", te)], inc="act")
                jj = o if half == 0 else 15 - o
                for sub in range(4):
                    tt = p.emit("te", lambda e, sub=sub: e.transpose(
                        p.pt[:, 0:64], p.sb["vfm"][0:64, sub * 128:(sub + 1) * 128],
                        p.sb["idb"][0:64, 0:64]), wait=[("act", ac), ("dve", p.cnt["dve"])], inc="te")
                    slot = va_slot(jj, sub)
                    p.emit("dve", lambda e, slot=slot: e.tensor_copy(
                        p.sb["VA"][:, slot, 0:64], p.pt[:, 0:64]), wait=[("te", tt)], inc="dve")
                xs_user = p.cnt["te"]

            if debug_stop is not None and debug_stop.startswith("proj"):
                continue
            hcq = HC_Q2L0 if kind == "l0" else HC_Q2
            p.emit("dve", lambda e, hcq=hcq: e.tensor_scalar(p.sb["QA"][64:65, :], p.sb["QA"][64:65, :],
                                                    p.sb["hc"][0:1, hcq:hcq + 1], None,
                                                    ALU.mult), inc="dve")
            dvm = p.emit("dve", lambda e: e.tensor_scalar(p.sb["QA"][64:65, :], p.sb["QA"][64:65, :],
                                                          p.sb["kmx"][0:1, 0:1],
                                                          p.sb["hc"][0:1, HC_HALF:HC_HALF + 1],
                                                          ALU.add, ALU.mult), inc="dve")

            prev_pair_user = {0: 0, 1: 0}
            av_user = {0: 0, 1: 0}
            for j in range(NCH):
                if kind == "l0":
                    jks = ([j] if j == 0 else [j - 1, j])
                else:
                    jks = list(range(j + 1))
                nkb = 4 * len(jks)
                qg = bcol(j)
                o_j, ro_j = stagecol(b, j)
                dv_j0 = p.cnt["dve"]
                ac_j0 = p.cnt["act"]
                for kb in range(nkb):
                    jk, sub = jks[kb // 4], kb % 4
                    kc = bcol(jk) + sub * 128
                    pair = (kb // 2) % 2
                    bk = (2 if pair == 0 else 4) + (kb % 2)
                    diag = (jk == j)
                    winedge = (kind == "l0" and jk == j - 1)

                    def qk(e, qg=qg, kc=kc, bk=bk, diag=diag, sub=sub, winedge=winedge):
                        last = e.matmul(bank(p, bk), p.sb["KA"][:, kc:kc + 128],
                                        p.sb["QA"][:, qg:qg + 512], start=True,
                                        stop=not (diag or winedge))
                        if diag:
                            last = e.matmul(bank(p, bk), p.sb["idb"][:, :], p.sb["diagm"][:, sub, :],
                                            start=False, stop=(kind != "l0"))
                        elif winedge:
                            last = e.matmul(bank(p, bk), p.sb["idb"][:, :], p.sb["winm"][:, sub, :],
                                            start=False, stop=False)
                        if kind == "l0":
                            mi = (1 if diag else 0) * 4 + sub
                            e.matmul(bank(p, bk), p.sb["sidb"][:, :], p.sb["relh"][:, mi, :],
                                     start=False, stop=False)
                            last = e.matmul(bank(p, bk), p.sb["sidb"][:, :], p.sb["rell"][:, mi, :],
                                            start=False, stop=True)
                        return last
                    p.emit("te", qk, wait=[("dve", dvm), ("act", max(prev_pair_user[pair], ac_j0))],
                           inc="te")
                    if kb % 2 == 1:
                        teqk = p.cnt["te"]
                        slot = (kb // 2) % 2

                        def expf(e, pair=pair, slot=slot):
                            c0 = (2 if pair == 0 else 4) * 512
                            return e.activation(p.sb["ptile"][:, slot, :], p.ps[:, c0:c0 + 1024],
                                                AF.Exp)
                        ac = p.emit("act", expf, wait=[("te", teqk), ("te", av_user[slot])],
                                    inc="act")
                        prev_pair_user[pair] = ac
                        for q2 in range(2):
                            kb2 = kb - 1 + q2
                            vs = va_slot(jks[kb2 // 4], kb2 % 4)

                            def av(e, vs=vs, slot=slot, q2=q2, kb2=kb2, nkb=nkb):
                                return e.matmul(bank(p, 0, rows=65), p.sb["VA"][:, vs, :],
                                                p.sb["ptile"][:, slot, q2 * 512:(q2 + 1) * 512],
                                                start=(kb2 == 0), stop=(kb2 == nkb - 1))
                            p.emit("te", av, wait=[("act", ac), ("dve", dv_j0)], inc="te")
                        av_user[slot] = p.cnt["te"]
                teav = p.cnt["te"]
                dv = p.emit("dve", lambda e: e.reciprocal(p.sb["rden"][0:1, 0:512],
                                                          p.ps[64:65, 0:512]),
                            wait=[("te", teav)], inc="dve")
                te6 = p.emit("te", lambda e: e.matmul(bank(p, 6, rows=64), p.sb["onesr_f"][0:1, 0:64],
                                                      p.sb["rden"][0:1, 0:512], start=True, stop=True),
                             wait=[("dve", dv)], inc="te")
                ac = p.emit("act", lambda e: e.activation(p.sb["ofsb"][0:64, :], bank(p, 0, rows=65)[0:64, :],
                                                          AF.Copy), wait=[("te", teav), ("dve", dv)], inc="act")

                def norm(e, hl=hl, o_j=o_j, ro_j=ro_j):
                    dst = p.sb["xln"][hl * 64:hl * 64 + 64, o_j, ro_j * 512:(ro_j + 1) * 512]
                    return e.tensor_tensor(dst, p.sb["ofsb"][0:64, :], bank(p, 6, rows=64), ALU.mult)
                p.emit("dve", norm, wait=[("te", te6), ("act", ac)], inc="dve")

    if debug_stop in ("proj", "attn"):
        return
    st = dma_sp(p, a2a_in.rearrange("(o pp) t -> pp o t", pp=128),
                p.sb["xln"][:, :, :], wait=[("dve", p.cnt["dve"])])

    def colla(e):
        with allow_ici_all_to_all():
            return e.collective_compute(
                "AllToAll", ALU.bypass, replica_groups=[list(range(NCORE))],
                ins=[a2a_in.opt()], outs=[a2a_out.opt()])
    cc = p.emit("pool", colla, wait=[("xld", st)], inc="cc")
    ld = dma_sp(p, p.sb["xln"][:, :, :], a2a_out.rearrange("(dc pp) t -> pp dc t", pp=128),
                wait=[("cc", cc)])

    if kind == "l0":
        ld2 = ld
        for tg in range(NRUN):
            sl = slice(tg * 512, tg * 512 + 512)
            for oc in range(DC):
                p.emit("dve", lambda e, oc=oc, sl=sl: e.tensor_add(
                    xres[:, oc, sl], xres[:, oc, sl], p.sb["xln"][:, oc, sl]),
                    wait=[("xld", ld2)], inc="dve")
        return
    wtile = p.sb["wtile"]
    slot_user = {0: 0, 1: 0}
    for tg in range(NRUN):
        sl = slice(tg * 512, tg * 512 + 512)
        for oc in range(DC):
            slot = oc % 2
            wl = dma_pool(p, wtile[:, slot],
                          WoT.rearrange("(dcc pp) o -> pp dcc o", pp=128)[:, :, oc * 128:(oc + 1) * 128],
                          wait=[("te", slot_user[slot])])

            def mmo(e, slot=slot, sl=sl):
                for dcc in range(DC):
                    last = e.matmul(bank(p, 2), wtile[:, slot, dcc, :], p.sb["xln"][:, dcc, sl],
                                    start=(dcc == 0), stop=(dcc == DC - 1))
                return last
            te = p.emit("te", mmo, wait=[wl, ("xld", ld), ("dve", p.cnt["dve"]),
                                         ("act", p.cnt["act"])], inc="te")
            slot_user[slot] = te
            p.emit("dve", lambda e, oc=oc, sl=sl: e.scalar_tensor_tensor(
                xres[:, oc, sl], bank(p, 2), p.sb["bo_sb"][:, oc:oc + 1], xres[:, oc, sl],
                ALU.add, ALU.add), wait=[("te", te)], inc="dve")


def emit_l3a(p, lw, debug_stop=None):
    wslow_in = p.din("l3a_wslow", [D, 3088])
    wout_in = p.din("l3a_wout", [D, D])
    lnw_in = p.din("l3a_lnw", [128, DC])
    lnb_in = p.din("l3a_lnb", [128, DC])
    xres = p.sb["xres"]

    p.carve("lnw", [128, DC], F32)
    p.carve("lnb", [128, DC], F32)
    p.carve("stato", [1, 2 * TLOC], BF16)
    p.carve("hh", [128, DC, TLOC], BF16)
    p.carve("outall", [128, DC, TLOC], BF16)
    p.carve("wsl", [128, 2, DC, 208], BF16)
    p.carve("zero1", [128, 1], F32)
    p.carve("vtmp", [128, CH], F32)
    mk3 = p.carve_mark()
    p.carve("xlf3", [128, DC, CH], F32)

    dma_sp(p, p.sb["lnw"][:, :], lnw_in[:, :])
    ldb = dma_sp(p, p.sb["lnb"][:, :], lnb_in[:, :])
    p.emit("dve", lambda e: e.memset(p.sb["zero1"][:, :], 0.0), wait=[("xld", ldb)], inc="dve")

    emit_ln(p, keep_xln=False, keep_bc=False)
    p.emit("dve", lambda e: e.tensor_copy(p.sb["stato"][0:1, 0:TLOC], p.sb["stat_m"][0:1, :]),
           inc="dve")
    p.emit("dve", lambda e: e.tensor_copy(p.sb["stato"][0:1, TLOC:2 * TLOC], p.sb["stat_r"][0:1, :]),
           inc="dve")
    hc = p.sb["hc"]
    p.emit("dve", lambda e: e.tensor_mul(p.sb["lnst"][0:1, :], p.sb["stat_r"][0:1, :],
                                         p.sb["stat_r"][0:1, :]), inc="dve")
    p.emit("dve", lambda e: e.tensor_scalar(p.sb["lnst"][0:1, :], p.sb["lnst"][0:1, :],
                                            hc[0:1, HC_NEPS:HC_NEPS + 1],
                                            hc[0:1, HC_ONE:HC_ONE + 1], ALU.mult, ALU.add),
           inc="dve")
    ac = p.emit("act", lambda e: e.activation(p.sb["lnst"][0:1, :], p.sb["lnst"][0:1, :], AF.Sqrt,
                                              bias=hc[0:1, HC_EPS:HC_EPS + 1]),
                wait=[("dve", p.cnt["dve"])], inc="act")
    p.emit("dve", lambda e: e.reciprocal(p.sb["lnst"][0:1, :], p.sb["lnst"][0:1, :]),
           wait=[("act", ac)], inc="dve")
    p.emit("dve", lambda e: e.tensor_mul(p.sb["stat_r"][0:1, :], p.sb["stat_r"][0:1, :],
                                         p.sb["lnst"][0:1, :]), inc="dve")

    for tg in range(NRUN):
        sl = slice(tg * 512, tg * 512 + 512)

        def bc4(e, sl=sl):
            e.matmul(bank(p, 3), p.sb["onesr_f"][0:1, :], p.sb["stat_m"][0:1, sl], start=True, stop=True)
            return e.matmul(bank(p, 4), p.sb["onesr_f"][0:1, :], p.sb["stat_r"][0:1, sl],
                            start=True, stop=True)
        te = p.emit("te", bc4, wait=[("dve", p.cnt["dve"]), ("act", p.cnt["act"])], inc="te")

        def xl1(e, sl=sl):
            mb = bank(p, 3).unsqueeze(1).broadcast_to([128, DC, 512])
            return e.tensor_tensor(p.sb["xlf3"][:, :, :], xres[:, :, sl], mb, ALU.subtract)
        p.emit("dve", xl1, wait=[("te", te)], inc="dve")

        def xl2(e):
            rb = bank(p, 4).unsqueeze(1).broadcast_to([128, DC, 512])
            return e.tensor_tensor(p.sb["xlf3"][:, :, :], p.sb["xlf3"][:, :, :], rb, ALU.mult)
        dv = p.emit("dve", xl2, inc="dve")
        for dc in range(DC):
            p.emit("dve", lambda e, dc=dc, sl=sl: e.tensor_scalar(
                p.sb["hh"][:, dc, sl], p.sb["xlf3"][:, dc, :],
                p.sb["lnw"][:, dc:dc + 1], p.sb["lnb"][:, dc:dc + 1], ALU.mult, ALU.add),
                wait=[("dve", dv)], inc="dve")
    ac_h = p.cnt["act"]

    if debug_stop == "l3prep":
        return
    p.carve_rewind(mk3, ["xlf3", "stat_m", "stat_r", "lnst", "lnsq", "lnsq2"])
    p.carve("qn", [64, TLOC], BF16)
    p.carve("kn", [64, TLOC], BF16)
    p.carve("vv", [64, TLOC], BF16)
    p.carve("bet", [1, TLOC], BF16)
    p.carve("q1", [64, CH], F32)
    p.carve("q1b", [64, CH], BF16)
    p.carve("dtmp", [64, 1024], BF16)
    p.carve("dtmp2", [64, 1024], BF16)
    p.carve("dtmp3", [64, 1024], BF16)
    p.carve("dtmp4", [64, 1024], BF16)
    p.carve("dcol", [1, 4096], BF16)
    p.carve("del0", [64, 1024], F32)
    p.carve("del1", [64, 1024], F32)
    p.carve("ot", [64, 512], F32)
    p.carve("rden3", [1, CH], F32)
    slot_user = {0: 0, 1: 0}
    for h in range(16):
        slot = h % 2
        wl = dma_pool(p, p.sb["wsl"][:, slot, :, 0:193],
                      wslow_in.rearrange("(dcc pp) o -> pp dcc o", pp=128)[:, :, 193 * h:193 * h + 193],
                      wait=[("te", slot_user[slot])])
        for tg in range(NRUN):
            sl = slice(tg * 512, tg * 512 + 512)

            def mma(e, slot=slot, sl=sl):
                for dcc in range(DC):
                    e.matmul(bank(p, 0), p.sb["wsl"][:, slot, dcc, 0:128], p.sb["hh"][:, dcc, sl],
                             start=(dcc == 0), stop=(dcc == DC - 1))
                for dcc in range(DC):
                    last = e.matmul(bank(p, 1, rows=65), p.sb["wsl"][:, slot, dcc, 128:193],
                                    p.sb["hh"][:, dcc, sl], start=(dcc == 0), stop=(dcc == DC - 1))
                return last
            te = p.emit("te", mma, wait=[wl, ("act", max(ac_h, p.cnt["act"])),
                                         ("dve", p.cnt["dve"])], inc="te")
            slot_user[slot] = te
            dvv = p.emit("dve", lambda e, sl=sl: e.tensor_copy(p.sb["vv"][0:64, sl],
                                                               bank(p, 1, rows=64)),
                         wait=[("te", te)], inc="dve")
            acb = p.emit("act", lambda e, sl=sl: e.activation(p.sb["bet"][0:1, sl],
                                                              p.ps[64:65, 512:1024], AF.Sigmoid),
                         wait=[("te", te), ("dve", dvv)], inc="act")
            for qi, rows0 in ((0, 0), (1, 64)):
                dst = p.sb["qn"] if qi == 0 else p.sb["kn"]
                bsl = bank(p, 0)[rows0:rows0 + 64, :]

                def relu_(e, bsl=bsl):
                    return e.tensor_scalar_max(p.sb["q1"][0:64, :], bsl, p.sb["zero1"][0:64, 0:1])
                dv1 = p.emit("dve", relu_, wait=[("te", te), ("act", p.cnt["act"])], inc="dve")

                def min_(e, bsl=bsl):
                    return e.tensor_scalar_min(p.sb["vtmp"][0:64, :], bsl, p.sb["zero1"][0:64, 0:1])
                dv2 = p.emit("dve", min_, inc="dve")
                ace = p.emit("act", lambda e: e.activation(p.sb["vtmp"][0:64, :], p.sb["vtmp"][0:64, :],
                                                           AF.Exp), wait=[("dve", dv2)], inc="act")
                dv3 = p.emit("dve", lambda e: e.tensor_add(p.sb["q1"][0:64, :], p.sb["q1"][0:64, :],
                                                           p.sb["vtmp"][0:64, :]),
                             wait=[("ace" if False else "act", ace)], inc="dve")
                dv4 = p.emit("dve", lambda e: e.tensor_copy(p.sb["q1b"][0:64, :], p.sb["q1"][0:64, :]),
                             inc="dve")
                te2 = p.emit("te", lambda e: e.matmul(bank(p, 2, rows=1), p.sb["ones_bf"][0:64, 0:1],
                                                      p.sb["q1b"][0:64, :], start=True, stop=True),
                             wait=[("dve", dv4)], inc="te")
                dv5 = p.emit("dve", lambda e: e.reciprocal(p.sb["rden3"][0:1, :], bank(p, 2, rows=1)),
                             wait=[("te", te2)], inc="dve")
                te3 = p.emit("te", lambda e: e.matmul(bank(p, 2, rows=64), p.sb["onesr_f"][0:1, 0:64],
                                                      p.sb["rden3"][0:1, :], start=True, stop=True),
                             wait=[("dve", dv5)], inc="te")
                p.emit("dve", lambda e, dst=dst, sl=sl: e.tensor_tensor(
                    dst[0:64, sl], p.sb["q1"][0:64, :], bank(p, 2, rows=64), ALU.mult),
                    wait=[("te", te3)], inc="dve")
        B0 = slice(0, 1024)
        B1 = slice(1024, 2048)

        def dots(e):
            e.tensor_mul(p.sb["dtmp"][0:64, :], p.sb["kn"][0:64, B0], p.sb["qn"][0:64, B0])
            return e.tensor_mul(p.sb["dtmp2"][0:64, :], p.sb["kn"][0:64, B0], p.sb["qn"][0:64, B1])
        dv = p.emit("dve", dots, inc="dve")

        def dots2(e):
            e.tensor_mul(p.sb["dtmp3"][0:64, :], p.sb["kn"][0:64, B0], p.sb["kn"][0:64, B1])
            return e.tensor_mul(p.sb["dtmp4"][0:64, :], p.sb["kn"][0:64, B1], p.sb["qn"][0:64, B1])
        dv = p.emit("dve", dots2, inc="dve")

        dots_names = ("dtmp", "dtmp2", "dtmp3", "dtmp4")
        for i, t in enumerate(dots_names):
            for q2 in range(2):
                te2 = p.emit("te", lambda e, t=t, q2=q2: e.matmul(
                    bank(p, 3, rows=1), p.sb["ones_bf"][0:64, 0:1],
                    p.sb[t][0:64, q2 * 512:(q2 + 1) * 512], start=True, stop=True),
                    wait=[("dve", p.cnt["dve"])], inc="te")
                p.emit("dve", lambda e, i=i, q2=q2: e.tensor_copy(
                    p.sb["dcol"][0:1, i * 1024 + q2 * 512:i * 1024 + (q2 + 1) * 512],
                    bank(p, 3, rows=1)), wait=[("te", te2)], inc="dve")
        te2 = p.emit("te", lambda e: e.matmul(bank(p, 4, rows=64), p.sb["onesr_b"][0:1, 0:64],
                                              p.sb["bet"][0:1, 0:512], start=True, stop=True),
                     wait=[("dve", p.cnt["dve"])], inc="te")
        te3 = p.emit("te", lambda e: e.matmul(bank(p, 5, rows=64), p.sb["onesr_b"][0:1, 0:64],
                                              p.sb["bet"][0:1, 512:1024], start=True, stop=True),
                     inc="te")
        p.emit("dve", lambda e: e.tensor_tensor(p.sb["del0"][0:64, 0:512], p.sb["vv"][0:64, 0:512],
                                                bank(p, 4, rows=64), ALU.mult),
               wait=[("te", te2)], inc="dve")
        dv = p.emit("dve", lambda e: e.tensor_tensor(p.sb["del0"][0:64, 512:1024],
                                                     p.sb["vv"][0:64, 512:1024],
                                                     bank(p, 5, rows=64), ALU.mult),
                    wait=[("te", te3)], inc="dve")
        for q2 in range(2):
            te2 = p.emit("te", lambda e, q2=q2: e.matmul(
                bank(p, 4, rows=64), p.sb["onesr_b"][0:1, 0:64],
                p.sb["dcol"][0:1, 2048 + q2 * 512:2048 + (q2 + 1) * 512], start=True, stop=True),
                wait=[("dve", p.cnt["dve"])], inc="te")
            te3 = p.emit("te", lambda e, q2=q2: e.matmul(
                bank(p, 5, rows=64), p.sb["onesr_b"][0:1, 0:64],
                p.sb["bet"][0:1, 1024 + q2 * 512:1024 + (q2 + 1) * 512], start=True, stop=True),
                inc="te")
            s5 = slice(q2 * 512, (q2 + 1) * 512)

            def d1a(e, s5=s5):
                return e.tensor_tensor(p.sb["ot"][0:64, 0:512], p.sb["del0"][0:64, s5],
                                       bank(p, 4, rows=64), ALU.mult)
            p.emit("dve", d1a, wait=[("te", te2)], inc="dve")

            def d1b(e, q2=q2):
                return e.tensor_sub(p.sb["ot"][0:64, 0:512],
                                    p.sb["vv"][0:64, 1024 + q2 * 512:1024 + (q2 + 1) * 512],
                                    p.sb["ot"][0:64, 0:512])
            p.emit("dve", d1b, inc="dve")

            def d1c(e, s5=s5):
                return e.tensor_tensor(p.sb["del1"][0:64, s5], p.sb["ot"][0:64, 0:512],
                                       bank(p, 5, rows=64), ALU.mult)
            p.emit("dve", d1c, wait=[("te", te3)], inc="dve")
        hp = 64 * (h % 2)
        hdc = h // 2
        for q2 in range(2):
            s5 = slice(q2 * 512, (q2 + 1) * 512)
            te2 = p.emit("te", lambda e, q2=q2: e.matmul(
                bank(p, 4, rows=64), p.sb["onesr_b"][0:1, 0:64],
                p.sb["dcol"][0:1, q2 * 512:(q2 + 1) * 512], start=True, stop=True),
                wait=[("dve", p.cnt["dve"])], inc="te")
            p.emit("dve", lambda e, hp=hp, hdc=hdc, q2=q2, s5=s5: e.tensor_tensor(
                p.sb["outall"][hp:hp + 64, hdc, q2 * 512:(q2 + 1) * 512],
                p.sb["del0"][0:64, s5], bank(p, 4, rows=64), ALU.mult),
                wait=[("te", te2)], inc="dve")
            te2 = p.emit("te", lambda e, q2=q2: e.matmul(
                bank(p, 4, rows=64), p.sb["onesr_b"][0:1, 0:64],
                p.sb["dcol"][0:1, 1024 + q2 * 512:1024 + (q2 + 1) * 512], start=True, stop=True),
                wait=[("dve", p.cnt["dve"])], inc="te")
            te3 = p.emit("te", lambda e, q2=q2: e.matmul(
                bank(p, 5, rows=64), p.sb["onesr_b"][0:1, 0:64],
                p.sb["dcol"][0:1, 3072 + q2 * 512:3072 + (q2 + 1) * 512], start=True, stop=True),
                inc="te")
            p.emit("dve", lambda e, s5=s5: e.tensor_tensor(
                p.sb["ot"][0:64, 0:512], p.sb["del0"][0:64, s5], bank(p, 4, rows=64), ALU.mult),
                wait=[("te", te2)], inc="dve")
            p.emit("dve", lambda e, s5=s5: e.tensor_tensor(
                p.sb["q1"][0:64, :], p.sb["del1"][0:64, s5], bank(p, 5, rows=64), ALU.mult),
                wait=[("te", te3)], inc="dve")
            p.emit("dve", lambda e, hp=hp, hdc=hdc, q2=q2: e.tensor_add(
                p.sb["outall"][hp:hp + 64, hdc, 1024 + q2 * 512:1024 + (q2 + 1) * 512],
                p.sb["ot"][0:64, 0:512], p.sb["q1"][0:64, :]), inc="dve")

    if debug_stop == "l3heads":
        return
    p.carve_rewind(mk3, ["qn", "kn", "vv", "bet", "q1", "q1b", "dtmp", "dtmp2", "dtmp3",
                         "dtmp4", "dcol", "del0", "del1", "ot", "rden3"])
    p.carve("xlf3", [128, DC, CH], F32)
    p.carve("wtile", [128, 2, DC, 128], BF16)
    slot_user = {0: p.cnt["te"], 1: p.cnt["te"]}
    dv_tail0 = p.cnt["dve"]
    for tg in range(NRUN):
        sl = slice(tg * 512, tg * 512 + 512)

        def bco(e, sl=sl, tg=tg):
            e.matmul(bank(p, 3), p.sb["onesr_b"][0:1, :], p.sb["stato"][0:1, sl], start=True, stop=True)
            return e.matmul(bank(p, 4), p.sb["onesr_b"][0:1, :],
                            p.sb["stato"][0:1, TLOC + tg * 512:TLOC + tg * 512 + 512],
                            start=True, stop=True)
        te = p.emit("te", bco, wait=[("dve", p.cnt["dve"]), ("act", p.cnt["act"])], inc="te")

        def xl1(e, sl=sl):
            mb = bank(p, 3).unsqueeze(1).broadcast_to([128, DC, 512])
            return e.tensor_tensor(p.sb["xlf3"][:, :, :], xres[:, :, sl], mb, ALU.subtract)
        p.emit("dve", xl1, wait=[("te", te)], inc="dve")

        def xl2(e):
            rb = bank(p, 4).unsqueeze(1).broadcast_to([128, DC, 512])
            return e.tensor_tensor(p.sb["xlf3"][:, :, :], p.sb["xlf3"][:, :, :], rb, ALU.mult)
        dvx = p.emit("dve", xl2, inc="dve")
        if debug_stop == "l3tail1":
            continue
        for oc in range(DC):
            slot = oc % 2
            wl = dma_pool(p, p.sb["wtile"][:, slot],
                          wout_in.rearrange("(dcc pp) o -> pp dcc o", pp=128)[:, :, oc * 128:(oc + 1) * 128],
                          wait=[("te", slot_user[slot]), ("dve", dv_tail0)])
            if debug_stop == "l3tail2":
                slot_user[slot] = p.cnt["te"]
                continue

            def mmo(e, slot=slot, sl=sl):
                for dcc in range(DC):
                    last = e.matmul(bank(p, 2), p.sb["wtile"][:, slot, dcc, :],
                                    p.sb["outall"][:, dcc, sl], start=(dcc == 0), stop=(dcc == DC - 1))
                return last
            te2 = p.emit("te", mmo, wait=[wl, ("dve", dvx), ("act", p.cnt["act"]),
                                          ("dve", p.cnt["dve"])], inc="te")
            slot_user[slot] = te2
            p.emit("dve", lambda e, oc=oc, sl=sl: e.tensor_add(
                p.sb["vtmp"][:, :], bank(p, 2), p.sb["xlf3"][:, oc, :]),
                wait=[("te", te2)], inc="dve")
            p.emit("dve", lambda e, oc=oc, sl=sl: e.tensor_add(
                xres[:, oc, sl], xres[:, oc, sl], p.sb["vtmp"][:, :]), inc="dve")


CAP1 = 2560
BIGM = 1.0e9


def emit_moe(p, lw, debug_stop=None):
    kind = lw["kind"]
    gwT = p.din(lw["gwT"], [D, 8])
    W1 = p.din(lw["W1"], [8, D, 4096])
    W2 = p.din(lw["W2"], [8, 2048 if kind == "l0f" else 4096, D])
    if kind == "l1f":
        gb = p.din("l1f_gb", [1, 8])
        b1f = p.din("l1f_b1", [8, 128, 32])
        b2f = p.din("l1f_b2", [8, 128, 8])
    xres = p.sb["xres"]
    INTER2 = 4096

    p.carve("gw_sb", [128, DC, 8], BF16)
    p.carve("zeroc", [128, 1], F32)
    p.carve("combT", [8, TLOC], F32)
    p.carve("gb_sb", [1, 8], BF16)
    p.carve("capc", [8, 1], F32)
    p.carve("nbig", [128, 1], F32)
    p.carve("xln", [128, DC, TLOC], BF16)
    mkA = p.carve_mark()
    emit_ln(p, keep_xln=True)
    p.carve("gprob", [128, 16, 8], F32)
    p.carve("w12c", [128, 16, 2], F32)
    p.carve("tkm", [128, 12], F32)
    p.carve("tk2", [128, 12], F32)
    p.carve("tk3", [128, 8], F32)
    p.carve("ohT", [8, 2, TLOC], BF16)
    p.carve("scanT", [8, TLOC], F32)
    p.carve("keepT", [8, TLOC], BF16)
    p.carve("wrow", [1, 2 * TLOC], F32)
    p.carve("thr", [8, 8], F32)
    p.carve("cnt8", [8, 8], F32)
    p.carve("pub", [4, 16], F32)
    p.carve("cagg", [32, 16], F32)
    p.carve("moff", [4, 16], F32)
    p.carve("scan2", [8, 2, TLOC], F32)

    ldg = [dma_pool(p, p.sb["gw_sb"][:, :, :], gwT.rearrange("(dcc pp) o -> pp dcc o", pp=128))]
    if kind == "l1f":
        ldg.append(dma_pool(p, p.sb["gb_sb"][:, :], gb[:, :]))
    dvz = p.emit("dve", lambda e: e.memset(p.sb["zeroc"][:, :], 0.0), inc="dve")
    p.emit("dve", lambda e: e.memset(p.sb["combT"][:, :], 0.0), inc="dve")
    p.emit("dve", lambda e: e.memset(p.sb["capc"][0:8, 0:1], float(CAP1 + 1)), inc="dve")
    p.emit("dve", lambda e: e.memset(p.sb["nbig"][:, :], -BIGM), inc="dve")

    for tt in range(16):
        ts_ = slice(tt * 128, (tt + 1) * 128)

        if kind == "l1f":
            def gmm(e, ts_=ts_):
                for dcc in range(DC):
                    e.matmul(bank(p, 0, rows=128, width=8), p.sb["xln"][:, dcc, ts_],
                             p.sb["gw_sb"][:, dcc, :], start=(dcc == 0), stop=False)
                return e.matmul(bank(p, 0, rows=128, width=8), p.sb["onesr_b"][0:1, 0:128],
                                p.sb["gb_sb"][0:1, :], start=False, stop=True)
        else:
            def gmm(e, ts_=ts_):
                for dcc in range(DC):
                    last = e.matmul(bank(p, 0, rows=128, width=8), p.sb["xln"][:, dcc, ts_],
                                    p.sb["gw_sb"][:, dcc, :], start=(dcc == 0), stop=(dcc == DC - 1))
                return last
        te = p.emit("te", gmm, wait=ldg + [("dve", p.cnt["dve"]),
                                           ("act", p.cnt["act"])], inc="te")
        dv = p.emit("dve", lambda e: e.reduce_max(p.sb["tkm"][:, 0:1], bank(p, 0, rows=128, width=8),
                                                  axis=mybir.AxisListType.X), wait=[("te", te)], inc="dve")
        dv = p.emit("dve", lambda e: e.tensor_sub(p.sb["tk2"][:, 0:1], p.sb["zeroc"][:, 0:1],
                                                  p.sb["tkm"][:, 0:1]), inc="dve")
        ac = p.emit("act", lambda e, tt=tt: e.activation(p.sb["gprob"][:, tt, :],
                                                         bank(p, 0, rows=128, width=8), AF.Exp,
                                                         bias=p.sb["tk2"][:, 0:1]),
                    wait=[("dve", dv)], inc="act")
        dv = p.emit("dve", lambda e, tt=tt: e.reduce_sum(p.sb["tkm"][:, 0:1], p.sb["gprob"][:, tt, :],
                                                         axis=mybir.AxisListType.X),
                    wait=[("act", ac)], inc="dve")
        dv = p.emit("dve", lambda e: e.reciprocal(p.sb["tk2"][:, 0:1], p.sb["tkm"][:, 0:1]), inc="dve")
        dv = p.emit("dve", lambda e, tt=tt: e.tensor_scalar(p.sb["gprob"][:, tt, :],
                                                            p.sb["gprob"][:, tt, :],
                                                            p.sb["tk2"][:, 0:1], None, ALU.mult),
                    inc="dve")
        gp = p.sb["gprob"]
        if kind == "l0f":
            p.emit("dve", lambda e, tt=tt: e.tensor_tensor(
                p.sb["tkm"][:, 0:4], gp[:, tt, 0::2], gp[:, tt, 1::2], ALU.max), inc="dve")
            p.emit("dve", lambda e: e.reduce_max(p.sb["tk2"][:, 0:1], p.sb["tkm"][:, 0:4],
                                                 axis=mybir.AxisListType.X), inc="dve")
            p.emit("dve", lambda e: e.tensor_scalar(p.sb["tk3"][:, 0:4], p.sb["tkm"][:, 0:4],
                                                    p.sb["tk2"][:, 0:1], None, ALU.is_ge), inc="dve")
            p.emit("dve", lambda e: e.scalar_tensor_tensor(p.sb["tk2"][:, 1:5], p.sb["tk3"][:, 0:4],
                                                           p.sb["nbig"][:, 0:1], p.sb["tkm"][:, 0:4],
                                                           ALU.mult, ALU.add), inc="dve")
            p.emit("dve", lambda e: e.reduce_max(p.sb["tk2"][:, 0:1], p.sb["tk2"][:, 1:5],
                                                 axis=mybir.AxisListType.X), inc="dve")
            p.emit("dve", lambda e: e.tensor_scalar(p.sb["tk3"][:, 0:4], p.sb["tkm"][:, 0:4],
                                                    p.sb["tk2"][:, 0:1], None, ALU.is_ge), inc="dve")
            p.emit("dve", lambda e: e.tensor_copy(p.sb["tkm"][:, 0:8][:, 0::2], p.sb["tk3"][:, 0:4]),
                   inc="dve")
            p.emit("dve", lambda e: e.tensor_copy(p.sb["tkm"][:, 0:8][:, 1::2], p.sb["tk3"][:, 0:4]),
                   inc="dve")
            p.emit("dve", lambda e, tt=tt: e.tensor_mul(p.sb["tkm"][:, 0:8], p.sb["tkm"][:, 0:8],
                                                        gp[:, tt, :]), inc="dve")
            masked = lambda: p.sb["tkm"][:, 0:8]
        else:
            p.emit("dve", lambda e, tt=tt: e.tensor_copy(p.sb["tkm"][:, 0:8], gp[:, tt, :]), inc="dve")
            masked = lambda: p.sb["tkm"][:, 0:8]
        p.emit("dve", lambda e: e.reduce_max(p.sb["tk2"][:, 0:1], masked(), axis=mybir.AxisListType.X),
               inc="dve")
        p.emit("dve", lambda e: e.tensor_scalar(p.sb["tk3"][:, 0:8], masked(), p.sb["tk2"][:, 0:1],
                                                None, ALU.is_ge), inc="dve")
        p.emit("dve", lambda e: e.scalar_tensor_tensor(p.sb["tk2"][:, 0:8], p.sb["tk3"][:, 0:8],
                                                       p.sb["nbig"][:, 0:1], masked(),
                                                       ALU.mult, ALU.add), inc="dve")
        p.emit("dve", lambda e: e.reduce_max(p.sb["tk2"][:, 8:9], p.sb["tk2"][:, 0:8],
                                             axis=mybir.AxisListType.X), inc="dve")
        p.emit("dve", lambda e: e.tensor_scalar(p.sb["tk2"][:, 0:8], masked(), p.sb["tk2"][:, 8:9],
                                                None, ALU.is_ge), inc="dve")
        p.emit("dve", lambda e: e.tensor_sub(p.sb["tk2"][:, 0:8], p.sb["tk2"][:, 0:8],
                                             p.sb["tk3"][:, 0:8]), inc="dve")
        p.emit("dve", lambda e: e.reduce_max(p.sb["tkm"][:, 8:9], masked(),
                                             axis=mybir.AxisListType.X), inc="dve")
        if kind == "l0f":
            p.emit("dve", lambda e: e.tensor_add(p.sb["tkm"][:, 9:10], p.sb["tkm"][:, 8:9],
                                                 p.sb["tk2"][:, 8:9]), inc="dve")
            p.emit("dve", lambda e: e.reciprocal(p.sb["tkm"][:, 10:11], p.sb["tkm"][:, 9:10]),
                   inc="dve")
            p.emit("dve", lambda e, tt=tt: e.tensor_tensor(p.sb["w12c"][:, tt, 0:1],
                                                           p.sb["tkm"][:, 8:9],
                                                           p.sb["tkm"][:, 10:11], ALU.mult), inc="dve")
            p.emit("dve", lambda e, tt=tt: e.tensor_tensor(p.sb["w12c"][:, tt, 1:2],
                                                           p.sb["tk2"][:, 8:9],
                                                           p.sb["tkm"][:, 10:11], ALU.mult), inc="dve")
        else:
            p.emit("dve", lambda e, tt=tt: e.tensor_copy(p.sb["w12c"][:, tt, 0:1],
                                                         p.sb["tkm"][:, 8:9]), inc="dve")
            p.emit("dve", lambda e, tt=tt: e.tensor_copy(p.sb["w12c"][:, tt, 1:2],
                                                         p.sb["tk2"][:, 8:9]), inc="dve")
        for ci, eqt in ((0, "tk3"), (1, "tk2")):
            dvq = p.emit("dve", lambda e, eqt=eqt: e.tensor_copy(p.sb["lnsq"][0:128, 0:8],
                                                                 p.sb[eqt][:, 0:8]), inc="dve")
            tt2 = p.emit("te", lambda e: e.transpose(p.pt[0:8, 0:128], p.sb["lnsq"][0:128, 0:8],
                                                     p.sb["idb"][:, :]),
                         wait=[("dve", dvq), ("dve", p.cnt["dve"])], inc="te")
            p.emit("dve", lambda e, ci=ci, tt=tt: e.tensor_copy(
                p.sb["ohT"][0:8, ci, tt * 128:(tt + 1) * 128], p.pt[0:8, 0:128]),
                wait=[("te", tt2)], inc="dve")
        for wi in range(2):
            dvq = p.emit("dve", lambda e, tt=tt, wi=wi: e.tensor_copy(
                p.sb["lnsq"][0:128, 0:1], p.sb["w12c"][:, tt, wi:wi + 1]), inc="dve")
            tt2 = p.emit("te", lambda e: e.transpose(p.pt[0:1, 0:128], p.sb["lnsq"][0:128, 0:1],
                                                     p.sb["idb"][:, :]),
                         wait=[("dve", dvq), ("dve", p.cnt["dve"])], inc="te")
            p.emit("dve", lambda e, tt=tt, wi=wi: e.tensor_copy(
                p.sb["wrow"][0:1, wi * TLOC + tt * 128:wi * TLOC + (tt + 1) * 128],
                p.pt[0:1, 0:128]), wait=[("te", tt2)], inc="dve")
    dv_rout = p.cnt["dve"]

    for ci in range(2):
        if kind == "l1f":
            if ci == 0:
                p.emit("dve", lambda e: e.memset(p.sb["keepT"][0:8, :], 1.0), inc="dve")
            p.emit("dve", lambda e, ci=ci: e.tensor_tensor_scan(
                p.sb["scanT"][0:8, :], p.sb["keepT"][0:8, :], p.sb["ohT"][0:8, ci, :],
                0.0, ALU.mult, ALU.add), wait=[("dve", dv_rout)], inc="dve")
            for r in range(NRUN):
                if r == 0:
                    p.emit("dve", lambda e, ci=ci: e.tensor_copy(
                        p.sb["cnt8"][0:8, ci * 4:ci * 4 + 1], p.sb["scanT"][0:8, 511:512]), inc="dve")
                else:
                    p.emit("dve", lambda e, ci=ci, r=r: e.tensor_sub(
                        p.sb["cnt8"][0:8, ci * 4 + r:ci * 4 + r + 1],
                        p.sb["scanT"][0:8, r * 512 + 511:r * 512 + 512],
                        p.sb["scanT"][0:8, r * 512 - 1:r * 512]), inc="dve")
            p.emit("dve", lambda e, ci=ci: e.memset(p.sb["thr"][0:8, ci * 4:ci * 4 + 1], 0.0),
                   inc="dve")
            for r in range(1, NRUN):
                p.emit("dve", lambda e, ci=ci, r=r: e.tensor_copy(
                    p.sb["thr"][0:8, ci * 4 + r:ci * 4 + r + 1],
                    p.sb["scanT"][0:8, r * 512 - 1:r * 512]), inc="dve")
            p.emit("dve", lambda e, ci=ci: e.tensor_copy(p.sb["scan2"][0:8, ci, :],
                                                         p.sb["scanT"][0:8, :]), inc="dve")
    if kind == "l1f":
        tt2 = p.emit("te", lambda e: e.transpose(bank(p, 3, rows=8, width=8),
                                                 p.sb["cnt8"][0:8, 0:8], p.sb["idf"][0:8, 0:8]),
                     wait=[("dve", p.cnt["dve"]), ("act", p.cnt["act"])], inc="te")
        p.carve("cnt8T", [8, 8], F32)
        dvp = p.emit("dve", lambda e: e.tensor_copy(p.sb["cnt8T"][0:8, 0:8],
                                                    bank(p, 3, rows=8, width=8)),
                     wait=[("te", tt2)], inc="dve")
        cnt_tmp = p.dint("l1f_cnt_tmp", [8, 8])
        st = dma_sp(p, cnt_tmp[:, :], p.sb["cnt8T"][0:8, :], wait=[("dve", dvp)])
        for cc2 in range(2):
            ld = dma_sp(p, p.sb["pub"][0:4, cc2 * 8:(cc2 + 1) * 8],
                        cnt_tmp[cc2 * 4:(cc2 + 1) * 4, :], wait=[("xld", st)])
        mo_in = p.dint("l1f_cnt_in", [4, 16])
        mo_all = p.dint("l1f_cnt_all", [32, 16])
        st = dma_sp(p, mo_in[:, :], p.sb["pub"][0:4, :], wait=[("dve", p.cnt["dve"])])
        cc = p.emit("pool", lambda e: e.collective_compute(
            "AllGather", ALU.bypass, replica_groups=[list(range(NCORE))],
            ins=[mo_in.opt()], outs=[mo_all.opt()]), wait=[("xld", st)], inc="cc")
        ld = dma_sp(p, p.sb["cagg"][0:32, :], mo_all[:, :], wait=[("cc", cc)])
        prec = p.din("c_moeprec", [32, 4])
        p.carve("prec32", [32, 4], F32)
        ld = dma_sp(p, p.sb["prec32"][:, :], prec[:, :])
        te = p.emit("te", lambda e: e.matmul(bank(p, 1, rows=4, width=16), p.sb["prec32"][:, :],
                                             p.sb["cagg"][0:32, :], start=True, stop=True),
                    wait=[("xld", ld)], inc="te")
        dv = p.emit("dve", lambda e: e.tensor_copy(p.sb["moff"][0:4, :], bank(p, 1, rows=4, width=16)),
                    wait=[("te", te)], inc="dve")
        tt2 = p.emit("te", lambda e: e.transpose(bank(p, 3, rows=16, width=4),
                                                 p.sb["moff"][0:4, :], p.sb["idf"][0:4, 0:4]),
                     wait=[("dve", p.cnt["dve"]), ("act", p.cnt["act"])], inc="te")
        p.carve("mof16", [16, 4], F32)
        dv = p.emit("dve", lambda e: e.tensor_copy(p.sb["mof16"][0:16, 0:4],
                                                   bank(p, 3, rows=16, width=4)),
                    wait=[("te", tt2)], inc="dve")
        mo_tmp = p.dint("l1f_mof_tmp", [16, 4])
        p.carve("mof8", [8, 8], F32)
        st = dma_sp(p, mo_tmp[:, :], p.sb["mof16"][0:16, :], wait=[("dve", dv)])
        for cc2 in range(2):
            ld = dma_sp(p, p.sb["mof8"][0:8, cc2 * 4:(cc2 + 1) * 4],
                        mo_tmp[cc2 * 8:(cc2 + 1) * 8, :], wait=[("xld", st)])
        dv = p.emit("dve", lambda e: e.tensor_sub(p.sb["thr"][0:8, :], p.sb["thr"][0:8, :],
                                                  p.sb["mof8"][0:8, :]),
                    wait=[("xld", ld)], inc="dve")
        dv = p.emit("dve", lambda e: e.tensor_scalar(p.sb["thr"][0:8, :], p.sb["thr"][0:8, :],
                                                     p.sb["capc"][0:8, 0:1], None, ALU.add),
                    inc="dve")
    dv_cap = p.cnt["dve"]

    for ci in range(2):
        if kind == "l1f":
            for r in range(NRUN):
                p.emit("dve", lambda e, ci=ci, r=r: e.tensor_scalar(
                    p.sb["keepT"][0:8, r * 512:(r + 1) * 512],
                    p.sb["scan2"][0:8, ci, r * 512:(r + 1) * 512],
                    p.sb["thr"][0:8, ci * 4 + r:ci * 4 + r + 1], None, ALU.is_lt),
                    wait=[("dve", dv_cap)], inc="dve")
        else:
            p.emit("dve", lambda e: e.memset(p.sb["keepT"][0:8, :], 1.0), inc="dve")
        for q4 in range(NRUN):
            s5 = slice(q4 * 512, (q4 + 1) * 512)
            te = p.emit("te", lambda e, ci=ci, q4=q4: e.matmul(
                bank(p, 2, rows=8), p.sb["onesr_f"][0:1, 0:8],
                p.sb["wrow"][0:1, ci * TLOC + q4 * 512:ci * TLOC + (q4 + 1) * 512],
                start=True, stop=True), wait=[("dve", p.cnt["dve"]), ("dve", dv_rout)], inc="te")
            p.emit("dve", lambda e, ci=ci, s5=s5: e.tensor_tensor(
                p.sb["scanT"][0:8, s5], p.sb["ohT"][0:8, ci, s5], p.sb["keepT"][0:8, s5],
                ALU.mult), inc="dve")
            p.emit("dve", lambda e, s5=s5: e.tensor_tensor(
                p.sb["scanT"][0:8, s5], p.sb["scanT"][0:8, s5], bank(p, 2, rows=8), ALU.mult),
                wait=[("te", te)], inc="dve")
            p.emit("dve", lambda e, s5=s5: e.tensor_add(
                p.sb["combT"][0:8, s5], p.sb["combT"][0:8, s5], p.sb["scanT"][0:8, s5]), inc="dve")
    dv_comb = p.cnt["dve"]

    if debug_stop == "route":
        return
    p.carve_rewind(mkA, ["stat_m", "stat_r", "lnst", "lnsq", "lnsq2", "gprob", "w12c",
                         "tkm", "tk2", "tk3", "ohT", "scanT", "keepT", "wrow", "thr", "cnt8",
                         "pub", "cagg", "moff", "scan2", "cnt8T", "prec32", "mof16", "mof8"])
    p.carve("mid", [128, 32, 512], BF16)
    p.carve("acts", [128, 16, 512], BF16)
    p.carve("wtile", [128, 2, DC, 128], BF16)
    p.carve("w2tile", [128, 2, 32, 128], BF16)
    p.carve("cb", [128, 512], F32)
    p.carve("crow", [1, TLOC], F32)
    p.carve("b1_sb", [128, 32], F32)
    p.carve("b2_sb", [128, 8], F32)
    p.carve("vt", [128, 512], F32)

    slot_user = {0: p.cnt["te"], 1: p.cnt["te"]}
    nfi = 32
    cr_d = p.dint(f"{lw['tag']}_crow", [8, TLOC])
    for ei in range(8):
        if ei == 0:
            dma_sp(p, cr_d[:, :], p.sb["combT"][0:8, :], wait=[("dve", dv_comb)])
        crl = dma_sp(p, p.sb["crow"][0:1, :], cr_d[ei:ei + 1, :],
                     wait=[("te", p.cnt["te"])])
        if kind == "l1f":
            dma_sp(p, p.sb["b1_sb"][:, :], b1f[ei], wait=[("dve", p.cnt["dve"])])
            ldb = dma_sp(p, p.sb["b2_sb"][:, :], b2f[ei], wait=[("dve", p.cnt["dve"])])
        for tg in range(NRUN):
            sl = slice(tg * 512, tg * 512 + 512)
            for fi in range(nfi):
                slot = fi % 2
                wl = dma_pool(p, p.sb["wtile"][:, slot],
                              W1[ei].rearrange("(dcc pp) o -> pp dcc o", pp=128)[:, :, fi * 128:(fi + 1) * 128],
                              wait=[("te", slot_user[slot]), ("dve", dv_comb)])

                def mm1(e, slot=slot, sl=sl):
                    for dcc in range(DC):
                        last = e.matmul(bank(p, 0), p.sb["wtile"][:, slot, dcc, :],
                                        p.sb["xln"][:, dcc, sl], start=(dcc == 0), stop=(dcc == DC - 1))
                    return last
                te = p.emit("te", mm1, wait=[wl, ("xld", p.cnt["xld"]),
                                             ("dve", p.cnt["dve"]), ("act", p.cnt["act"])], inc="te")
                slot_user[slot] = te
                if kind == "l1f":
                    p.emit("dve", lambda e, fi=fi: e.tensor_scalar(
                        p.sb["mid"][:, fi, :], bank(p, 0), p.sb["b1_sb"][:, fi:fi + 1],
                        p.sb["zeroc"][:, 0:1], ALU.add, ALU.max), wait=[("te", te)], inc="dve")
                else:
                    if fi < 16:
                        p.emit("act", lambda e, fi=fi: e.activation(p.sb["mid"][:, fi, :],
                                                                    bank(p, 0), AF.Silu),
                               wait=[("te", te)], inc="act")
                    else:
                        p.emit("dve", lambda e, fi=fi: e.tensor_copy(p.sb["mid"][:, fi, :],
                                                                     bank(p, 0)),
                               wait=[("te", te)], inc="dve")
            if kind == "l0f":
                for fi in range(16):
                    p.emit("dve", lambda e, fi=fi: e.tensor_mul(
                        p.sb["acts"][:, fi, :], p.sb["mid"][:, fi, :], p.sb["mid"][:, 16 + fi, :]),
                        wait=[("act", p.cnt["act"])], inc="dve")
                msrc, nf2 = "acts", 16
            else:
                msrc, nf2 = "mid", 32
            te = p.emit("te", lambda e, sl=sl: e.matmul(bank(p, 6), p.sb["onesr_f"][0:1, :],
                                                        p.sb["crow"][0:1, sl], start=True, stop=True),
                        wait=[("xld", crl), ("dve", p.cnt["dve"]), ("act", p.cnt["act"])], inc="te")
            dvc = p.emit("dve", lambda e: e.tensor_copy(p.sb["cb"][:, :], bank(p, 6)),
                         wait=[("te", te)], inc="dve")
            for oc in range(DC):
                slot = oc % 2
                wl = dma_pool(p, p.sb["w2tile"][:, slot, 0:nf2, :],
                              W2[ei].rearrange("(ii pp) o -> pp ii o", pp=128)[:, 0:nf2, oc * 128:(oc + 1) * 128],
                              wait=[("te", slot_user[slot]), ("dve", dv_comb)])

                def mm2(e, slot=slot, nf2=nf2, msrc=msrc):
                    for ii in range(nf2):
                        last = e.matmul(bank(p, 1), p.sb["w2tile"][:, slot, ii, :],
                                        p.sb[msrc][:, ii, :], start=(ii == 0), stop=(ii == nf2 - 1))
                    return last
                te = p.emit("te", mm2, wait=[wl, ("dve", p.cnt["dve"]),
                                             ("act", p.cnt["act"])], inc="te")
                slot_user[slot] = te
                if kind == "l1f":
                    p.emit("dve", lambda e, oc=oc: e.tensor_scalar(
                        p.sb["vt"][:, :], bank(p, 1), p.sb["b2_sb"][:, oc:oc + 1], None, ALU.add),
                        wait=[("te", te)], inc="dve")
                else:
                    p.emit("dve", lambda e: e.tensor_copy(p.sb["vt"][:, :], bank(p, 1)),
                           wait=[("te", te)], inc="dve")
                p.emit("dve", lambda e: e.tensor_mul(p.sb["vt"][:, :], p.sb["vt"][:, :],
                                                     p.sb["cb"][:, :]), inc="dve")
                p.emit("dve", lambda e, oc=oc, sl=sl: e.tensor_add(
                    xres[:, oc, sl], xres[:, oc, sl], p.sb["vt"][:, :]), inc="dve")


def fm_vec(v):
    return np.ascontiguousarray(np.asarray(v).reshape(DC, 128).T.astype(np.float32))


def prep_host(inputs):
    g = {}
    pc = [dict() for _ in range(NCORE)]

    g["c_ident"] = np.eye(128, dtype=np.float32)
    hc = np.zeros((1, 16), dtype=np.float32)
    hc[0, HC_INV_D] = 1.0 / D
    hc[0, HC_EPS] = LN_EPS
    hc[0, HC_HALF] = 0.5
    hc[0, HC_Q2] = 0.125 * 0.125
    hc[0, HC_Q2L0] = (1.0 / 32) ** 2
    hc[0, HC_ZERO] = 0.0
    hc[0, HC_NEPS] = -LN_EPS
    hc[0, HC_ONE] = 1.0
    g["c_hc"] = hc

    x = np.asarray(inputs["L0_attn_x"])
    for c in range(NCORE):
        xr = np.empty((128, DC, TLOC), dtype=np.float32)
        for r, (b, j) in enumerate(runs_of_core(c)):
            seg = x[b, j * CH:(j + 1) * CH, :]
            xr[:, :, r * CH:(r + 1) * CH] = seg.T.reshape(DC, 128, CH).transpose(1, 0, 2)
        pc[c]["xres_in"] = np.ascontiguousarray(xr)

    for tag, pre in (("l2f", "L2_ffn"), ("l3f", "L3_ffn")):
        g[f"{tag}_WrT"] = np.ascontiguousarray(np.asarray(inputs[f"{pre}_W_r"]).T)
        g[f"{tag}_WkT"] = np.ascontiguousarray(np.asarray(inputs[f"{pre}_W_k"]).T)
        g[f"{tag}_WvT"] = np.ascontiguousarray(np.asarray(inputs[f"{pre}_W_v"]).T)
        g[f"{tag}_mur"] = fm_vec(inputs[f"{pre}_mu_r"])
        g[f"{tag}_muk"] = fm_vec(inputs[f"{pre}_mu_k"])

    WqT = np.asarray(inputs["L2_attn_W_q"]).T
    WkT = np.asarray(inputs["L2_attn_W_k"]).T
    WvT = np.asarray(inputs["L2_attn_W_v"]).T
    g["l2a_WoT"] = np.ascontiguousarray(np.asarray(inputs["L2_attn_W_o"]).T)
    g["l2a_bo"] = fm_vec(inputs["L2_attn_b_o"])
    for c in range(NCORE):
        hsl = slice(2 * c * 64, (2 * c + 2) * 64)
        pc[c]["l2a_WqT"] = np.ascontiguousarray(WqT[:, hsl])
        pc[c]["l2a_WkT"] = np.ascontiguousarray(WkT[:, hsl])
        pc[c]["l2a_WvT"] = np.ascontiguousarray(WvT[:, hsl])

    wkqv = np.asarray(inputs["L0_attn_w_kqv"])
    for c in range(NCORE):
        hsl = slice(2 * c * 64, (2 * c + 2) * 64)
        pc[c]["l0a_WkT"] = np.ascontiguousarray(wkqv[:, 0:1024][:, hsl])
        pc[c]["l0a_WqT"] = np.ascontiguousarray(wkqv[:, 1024:2048][:, hsl])
        pc[c]["l0a_WvT"] = np.ascontiguousarray(wkqv[:, 2048:3072][:, hsl])
        xx = (2.0 ** 8) ** (1.0 / 16)
        slopes = np.asarray([1.0 / xx ** (i + 1) for i in range(16)], dtype=np.float32)
        srow = np.zeros((128, 16), dtype=np.float32)
        srow[:, 0] = slopes[2 * c]
        srow[:, 1] = slopes[2 * c + 1]
        pc[c]["c_slopes"] = srow
    relh = np.zeros((128, 8, 512), dtype=np.float32)
    rell = np.zeros((128, 8, 512), dtype=np.float32)
    jq = np.arange(512)[None, :]
    for off in range(2):
        for sub in range(4):
            i = np.arange(128)[:, None]
            rel = (off - 1) * 512 + sub * 128 + i - jq
            lo = np.mod(rel, 128)
            hi = rel - lo
            relh[:, off * 4 + sub, :] = hi
            rell[:, off * 4 + sub, :] = lo
    g["c_relh"] = relh
    g["c_rell"] = rell
    wm = np.zeros((4, 128, 512), dtype=np.float32)
    for sub in range(4):
        for i in range(128):
            wm[sub, i, sub * 128 + i + 1:] = -30000.0
    g["c_winmask"] = wm

    Wq1 = np.asarray(inputs["L1_attn_Wq"]).reshape(D, 16 * 64)
    Wku = np.asarray(inputs["L1_attn_Wk_up"]).reshape(256, 16 * 64)
    Wvu = np.asarray(inputs["L1_attn_Wv_up"]).reshape(256, 16 * 64)
    g["l1a_Wkvd"] = np.ascontiguousarray(np.asarray(inputs["L1_attn_Wkv_down"]))
    g["l1a_WoT"] = np.ascontiguousarray(np.asarray(inputs["L1_attn_Wout"]).reshape(16 * 64, D))
    for c in range(NCORE):
        hsl = slice(2 * c * 64, (2 * c + 2) * 64)
        pc[c]["l1a_WqT"] = np.ascontiguousarray(Wq1[:, hsl])
        pc[c]["l1a_WkuT"] = np.ascontiguousarray(Wku[:, hsl])
        pc[c]["l1a_WvuT"] = np.ascontiguousarray(Wvu[:, hsl])

    dm = np.zeros((4, 128, 512), dtype=np.float32)
    for sub in range(4):
        for i in range(128):
            dm[sub, i, : min(512, sub * 128 + i)] = -30000.0
    g["c_diagmask"] = dm

    g["l0f_gwT"] = np.ascontiguousarray(np.asarray(inputs["L0_ffn_gate_w"]).T)
    g["l0f_W1"] = np.ascontiguousarray(np.asarray(inputs["L0_ffn_w13_weight"]).transpose(0, 2, 1))
    g["l0f_W2"] = np.ascontiguousarray(np.asarray(inputs["L0_ffn_w2_weight"]).transpose(0, 2, 1))
    g["l1f_gwT"] = np.ascontiguousarray(np.asarray(inputs["L1_ffn_gate_w"]).T)
    g["l1f_gb"] = np.ascontiguousarray(np.asarray(inputs["L1_ffn_gate_b"]).reshape(1, 8))
    g["l1f_W1"] = np.ascontiguousarray(np.asarray(inputs["L1_ffn_W1"]))
    g["l1f_W2"] = np.ascontiguousarray(np.asarray(inputs["L1_ffn_W2"]))
    g["l1f_b1"] = np.ascontiguousarray(np.asarray(inputs["L1_ffn_b1"]).reshape(8, 32, 128).transpose(0, 2, 1))
    g["l1f_b2"] = np.ascontiguousarray(np.asarray(inputs["L1_ffn_b2"]).reshape(8, 8, 128).transpose(0, 2, 1))
    for c in range(NCORE):
        own = runs_of_core(c)
        prec = np.zeros((32, 4), dtype=np.float32)
        for g_o in range(NCORE):
            for g_r, (gb_, gj) in enumerate(runs_of_core(g_o)):
                for r, (b_, j_) in enumerate(own):
                    if gb_ * 16 + gj < b_ * 16 + j_:
                        prec[g_o * 4 + g_r, r] = 1.0
        pc[c]["c_moeprec"] = prec

    g["l3a_wslow"] = np.ascontiguousarray(np.asarray(inputs["L3_attn_w_slow"]))
    g["l3a_wout"] = np.ascontiguousarray(np.asarray(inputs["L3_attn_w_out"]).reshape(D, D))
    g["l3a_lnw"] = fm_vec(inputs["L3_attn_ln_w"])
    g["l3a_lnb"] = fm_vec(inputs["L3_attn_ln_b"])

    for c in range(NCORE):
        sel = np.zeros((33, 4), dtype=np.float32)
        for r, (b, j) in enumerate(runs_of_core(c)):
            if j == 0:
                sel[32, r] = 1.0
            else:
                o = owner_of_chunk(j - 1)
                sel[o * 4 + run_index(o, b, j - 1), r] = 1.0
        pc[c]["c_bndsel"] = sel

    return g, pc


LAYER_RWKV = {
    "l2f": dict(tag="l2f", WrT="l2f_WrT", WkT="l2f_WkT", WvT="l2f_WvT", mur="l2f_mur", muk="l2f_muk"),
    "l3f": dict(tag="l3f", WrT="l3f_WrT", WkT="l3f_WkT", WvT="l3f_WvT", mur="l3f_mur", muk="l3f_muk"),
}


def build_program(block_ids, debug_stop=None):
    stack = ExitStack()
    nc = bass.Bass("TRN2", target_bir_lowering=False, debug=False)
    p = Prog(nc, stack)
    with stack:
        for s in ("wld0", "wld1", "wld2", "wld3", "xld", "te", "act", "dve", "cc"):
            p.sem(s)
        xin = p.din("xres_in", [128, DC, TLOC])
        yout = p.dout("xres_out", [128, DC, TLOC])
        alloc_common(p)
        ld = dma_sp(p, p.sb["xres"][:, :, :], xin[:, :, :])
        for e in ("te", "act", "dve", "pool"):
            p.waitpt(e, "xld", "wld0", "wld1", "wld2", "wld3", "dve")

        for bid in block_ids:
            p.block_fence()
            p.carve_reset()
            if bid in LAYER_RWKV:
                emit_rwkv(p, LAYER_RWKV[bid])
            elif bid == "l0a":
                emit_attn(p, dict(tag="l0a", kind="l0", WqT="l0a_WqT", WkT="l0a_WkT",
                                  WvT="l0a_WvT"), debug_stop=debug_stop)
            elif bid == "l2a":
                emit_attn(p, dict(tag="l2a", kind="l2", WqT="l2a_WqT", WkT="l2a_WkT",
                                  WvT="l2a_WvT", WoT="l2a_WoT", bo="l2a_bo"), debug_stop=debug_stop)
            elif bid == "l1a":
                emit_attn(p, dict(tag="l1a", kind="l1", WqT="l1a_WqT", Wkvd="l1a_Wkvd",
                                  WkuT="l1a_WkuT", WvuT="l1a_WvuT", WoT="l1a_WoT"),
                          debug_stop=debug_stop)
            elif bid == "l3a":
                emit_l3a(p, {}, debug_stop=debug_stop)
            elif bid == "l0f":
                emit_moe(p, dict(tag="l0f", kind="l0f", gwT="l0f_gwT", W1="l0f_W1", W2="l0f_W2"),
                         debug_stop=debug_stop)
            elif bid == "l1f":
                emit_moe(p, dict(tag="l1f", kind="l1f", gwT="l1f_gwT", W1="l1f_W1", W2="l1f_W2"),
                         debug_stop=debug_stop)
            else:
                raise NotImplementedError(bid)

        dma_sp(p, yout[:, :, :], p.sb["xres"][:, :, :],
               wait=[("dve", p.cnt["dve"]), ("te", p.cnt["te"]), ("act", p.cnt["act"])])
        p.waitpt("sp", "xld")
        p.build()
    return p


_GROUPS = [("l0a", "l0f", "l1a", "l1f", "l2a", "l2f", "l3a", "l3f")]


def kernel(**inputs):
    glob, pc = prep_host(inputs)
    xcur = [pc[c]["xres_in"] for c in range(NCORE)]
    for grp in _GROUPS:
        p = build_program(grp)
        in_maps = []
        for c in range(NCORE):
            m = {}
            for name in p.inputs:
                if name == "xres_in":
                    m[name] = xcur[c]
                elif name in pc[c]:
                    m[name] = pc[c][name]
                else:
                    m[name] = glob[name]
            in_maps.append(m)
        res = run_bass_kernel_spmd(p.nc, in_maps, core_ids=list(range(NCORE)))
        xcur = [res.results[c]["xres_out"] for c in range(NCORE)]
    out = np.empty((B, S, D), dtype=np.float32)
    for c in range(NCORE):
        for r, (b, j) in enumerate(runs_of_core(c)):
            seg = xcur[c][:, :, r * CH:(r + 1) * CH]
            out[b, j * CH:(j + 1) * CH, :] = seg.transpose(1, 0, 2).reshape(D, CH).T
    return out


# version 05 (final)
# speedup vs baseline: 1.0657x; 1.0087x over previous
import numpy as np
import ml_dtypes
from contextlib import ExitStack

import concourse.bass as bass
import concourse.mybir as mybir
from concourse.bass_utils import run_bass_kernel_spmd
from concourse.aot_env import allow_ici_all_to_all

F32 = mybir.dt.float32
BF16 = mybir.dt.bfloat16
AF = mybir.ActivationFunctionType
ALU = mybir.AluOpType
BF = ml_dtypes.bfloat16

NCORE = 8
B, S, D = 2, 8192, 1024
DC = 8
CH = 512
NCH = 16
TLOC = 4 * CH
NRUN = 4
LN_EPS = 1e-5

BLOCKS = ["l0a", "l0f", "l1a", "l1f", "l2a", "l2f", "l3a", "l3f"]


def runs_of_core(c):
    return [(0, c), (0, 15 - c), (1, c), (1, 15 - c)]


def owner_of_chunk(j):
    return min(j, 15 - j)


def run_index(c, b, j):
    return runs_of_core(c).index((b, j))


class Prog:
    ENGINES = ("te", "act", "dve", "pool", "sp")

    def __init__(self, nc, stack):
        self.nc = nc
        self.stack = stack
        self.steps = {e: [] for e in self.ENGINES}
        self.semh = {}
        self.cnt = {}
        self.sb = {}
        self.dram = {}
        self.inputs = []
        self.ps = None
        self.pt = None

    def sem(self, name):
        if name not in self.semh:
            self.semh[name] = self.stack.enter_context(self.nc.semaphore(name))
            self.cnt[name] = 0
        return self.semh[name]

    def sbuf(self, name, shape, dtype):
        if name not in self.sb:
            self.sb[name] = self.stack.enter_context(self.nc.sbuf_tensor(name, shape, dtype))
        return self.sb[name]

    UNI_BYTES = 131008

    def alloc_uni(self):
        self.uni = self.stack.enter_context(
            self.nc.sbuf_tensor("UNI", [128, self.UNI_BYTES // 2], BF16))
        self.carve_off = 0
        self.carve_hw = 0
        self.carved = set()

    def carve_reset(self):
        for n in self.carved:
            self.sb.pop(n, None)
        self.carved = set()
        self.carve_off = 0

    def carve_mark(self):
        return self.carve_off

    def carve_rewind(self, mark, names):
        for n in names:
            if n in self.carved:
                self.carved.discard(n)
                self.sb.pop(n, None)
        self.carve_off = mark

    def carve(self, name, shape, dtype):
        if name in self.sb:
            return self.sb[name]
        import math as _m
        nb = int(np.prod(shape[1:])) * mybir.dt.size(dtype)
        nb_al = (nb + 63) & ~63
        off = self.carve_off
        assert off + nb_al <= self.UNI_BYTES, \
            f"UNI overflow carving {name}: off={off} need={nb_al} cap={self.UNI_BYTES}"
        view = self.uni[0:shape[0], off // 2:(off + nb) // 2]
        if dtype != BF16:
            view = view.bitcast(dtype)
        if len(shape) > 2:
            dims = " ".join(f"d{i}" for i in range(1, len(shape)))
            kw = {f"d{i}": shape[i] for i in range(1, len(shape))}
            view = view.rearrange(f"p ({dims}) -> p {dims}", **kw)
        self.sb[name] = view
        self.carved.add(name)
        self.carve_off = off + nb_al
        self.carve_hw = max(self.carve_hw, self.carve_off)
        return view

    def block_fence(self):
        sems = [s for s in ("te", "act", "dve", "xld", "wld", "cc") if s in self.cnt]
        for e in self.ENGINES:
            self.waitpt(e, *sems)

    def din(self, name, shape, dtype=F32):
        if name not in self.dram:
            self.dram[name] = self.nc.dram_tensor(name, shape, dtype, kind="ExternalInput")
            self.inputs.append(name)
        return self.dram[name].ap()

    def dout(self, name, shape, dtype=F32):
        if name not in self.dram:
            self.dram[name] = self.nc.dram_tensor(name, shape, dtype, kind="ExternalOutput")
        return self.dram[name].ap()

    def dint(self, name, shape, dtype=F32):
        if name not in self.dram:
            self.dram[name] = self.nc.dram_tensor(name, shape, dtype, kind="Internal")
        return self.dram[name].ap()

    def emit(self, eng, fn, wait=(), inc=None, inc_by=1):
        for s, _ in wait:
            self.sem(s)
        if inc:
            self.sem(inc)
        if eng in ("act", "dve"):
            wait = [(eng, self.cnt.get(eng, 0))] + list(wait)
        wait = tuple((s, v) for s, v in wait if v > 0)

        snap = dict(self.sb)

        def run(e, _fn=fn, _wait=wait, _inc=inc, _by=inc_by, _snap=snap):
            saved = self.sb
            self.sb = _snap
            try:
                for s, v in _wait:
                    e.wait_ge(self.semh[s], v)
                last = _fn(e)
                if _inc is not None:
                    assert last is not None, f"step on {eng} with inc={_inc} returned no instruction"
                    last.then_inc(self.semh[_inc], _by)
            finally:
                self.sb = saved
        self.steps[eng].append(run)
        if inc:
            self.cnt[inc] += inc_by
            return self.cnt[inc]
        return None

    def waitpt(self, eng, *sems):
        w = [(s, self.cnt.get(s, 0)) for s in sems]
        self.emit(eng, lambda e: None, wait=w)

    def build(self):
        eng_map = dict(te="tensor", act="scalar", dve="vector", pool="gpsimd", sp="sync")
        with self.nc.Block() as block:
            for eng in self.ENGINES:
                steps = self.steps[eng]
                getattr(block, eng_map[eng])(lambda e, _s=steps: [f(e) for f in _s])


def dma_pool(p, out_ap, in_ap, wait=(), sem="wld"):
    w = [(sem, p.cnt.get(sem, 0))] + list(wait)
    return p.emit("pool", lambda e: e.dma_start(out=out_ap, in_=in_ap),
                  wait=w, inc=sem, inc_by=16)


def dma_sp(p, out_ap, in_ap, wait=(), sem="xld"):
    w = [(sem, p.cnt.get(sem, 0))] + list(wait)
    return p.emit("sp", lambda e: e.dma_start(out=out_ap, in_=in_ap),
                  wait=w, inc=sem, inc_by=16)


def bank(p, i, rows=128, width=512):
    return p.ps[0:rows, i * 512:i * 512 + width]


HC_INV_D = 0
HC_EPS = 1
HC_HALF = 2
HC_Q2 = 3
HC_Q2L0 = 4
HC_ZERO = 5
HC_NEPS = 6
HC_ONE = 7


def alloc_common(p):
    p.sbuf("xres", [128, DC, TLOC], F32)
    p.sbuf("ones_bf", [128, 2], BF16)
    p.sbuf("ones_f", [128, 2], F32)
    p.sbuf("onesr_f", [1, 128], F32)
    p.sbuf("onesr_b", [1, 128], BF16)
    p.sbuf("hc", [1, 16], F32)
    p.sbuf("idf", [128, 128], F32)
    p.sbuf("idb", [128, 128], BF16)
    p.alloc_uni()
    p.ps = p.stack.enter_context(p.nc.psum_tensor("PS", [128, 7 * 512], F32))
    p.pt = p.stack.enter_context(p.nc.psum_tensor("PT", [128, 1024], BF16))

    idf = p.din("c_ident", [128, 128])
    hc = p.din("c_hc", [1, 16])
    dma_sp(p, p.sb["idf"][:, :], idf[:, :])
    dma_pool(p, p.sb["idb"][:, :], idf[:, :])
    ld = dma_sp(p, p.sb["hc"][:, :], hc[:, :])
    lw = p.cnt["wld"]
    p.emit("dve", lambda e: e.memset(p.sb["ones_bf"][:, :], 1.0),
           wait=[("wld", lw), ("xld", ld)], inc="dve")
    p.emit("dve", lambda e: e.memset(p.sb["ones_f"][:, :], 1.0), inc="dve")
    p.emit("dve", lambda e: e.memset(p.sb["onesr_f"][:, :], 1.0), inc="dve")
    p.emit("dve", lambda e: e.memset(p.sb["onesr_b"][:, :], 1.0), inc="dve")


def emit_ln(p, keep_xln=True, keep_bc=False, src="xres"):
    p.carve("stat_m", [1, TLOC], F32)
    p.carve("stat_r", [1, TLOC], F32)
    p.carve("lnst", [1, TLOC], F32)
    p.carve("lnsq", [128, 1024], BF16)
    p.carve("lnsq2", [128, 1024], BF16)
    xres, lnsq, lnsq2, hc = p.sb["xres"], p.sb["lnsq"], p.sb["lnsq2"], p.sb["hc"]
    if keep_bc:
        p.carve("mbc", [128, TLOC], F32)
        p.carve("rbc", [128, TLOC], F32)

    for half in range(2):
        c0 = half * 1024
        for dc in range(DC):
            if src == "xres":
                dv1 = p.emit("dve", lambda e, dc=dc, c0=c0: e.tensor_copy(
                    lnsq[:, :], xres[:, dc, c0:c0 + 1024]),
                    wait=[("te", p.cnt["te"])], inc="dve")
            else:
                dv1 = p.emit("dve", lambda e, dc=dc, c0=c0: e.tensor_copy(
                    lnsq[:, :], p.sb[src][:, dc, c0:c0 + 1024]),
                    wait=[("te", p.cnt["te"])], inc="dve")
            ac2 = p.emit("act", lambda e: e.activation(lnsq2[:, :], lnsq[:, :], AF.Square),
                         wait=[("dve", dv1)], inc="act")

            def mm(e, dc=dc):
                for tg2 in range(2):
                    sl = slice(tg2 * 512, tg2 * 512 + 512)
                    e.matmul(bank(p, tg2, rows=1), p.sb["ones_bf"][:, 0:1],
                             lnsq[:, sl], start=(dc == 0), stop=(dc == DC - 1))
                    last = e.matmul(bank(p, 2 + tg2, rows=1), p.sb["ones_bf"][:, 0:1],
                                    lnsq2[:, sl], start=(dc == 0), stop=(dc == DC - 1))
                return last
            p.emit("te", mm, wait=[("act", ac2), ("dve", dv1)], inc="te")
        te_done = p.cnt["te"]

        def evac(e, half=half):
            e.tensor_copy(p.sb["stat_m"][0:1, half * 1024:half * 1024 + 1024],
                          p.ps[0:1, 0:1024])
            return e.tensor_copy(p.sb["stat_r"][0:1, half * 1024:half * 1024 + 1024],
                                 p.ps[0:1, 1024:2048])
        p.emit("dve", evac, wait=[("te", te_done)], inc="dve")

    p.emit("dve", lambda e: e.tensor_scalar(p.sb["stat_m"][0:1, :], p.sb["stat_m"][0:1, :],
                                            hc[0:1, HC_INV_D:HC_INV_D + 1], None, ALU.mult), inc="dve")
    p.emit("dve", lambda e: e.tensor_scalar(p.sb["stat_r"][0:1, :], p.sb["stat_r"][0:1, :],
                                            hc[0:1, HC_INV_D:HC_INV_D + 1], None, ALU.mult), inc="dve")
    p.emit("dve", lambda e: e.tensor_mul(p.sb["lnst"][0:1, :], p.sb["stat_m"][0:1, :],
                                         p.sb["stat_m"][0:1, :]), inc="dve")
    dv = p.emit("dve", lambda e: e.tensor_sub(p.sb["stat_r"][0:1, :], p.sb["stat_r"][0:1, :],
                                              p.sb["lnst"][0:1, :]), inc="dve")
    ac = p.emit("act", lambda e: e.activation(p.sb["lnst"][0:1, :], p.sb["stat_r"][0:1, :],
                                              AF.Sqrt, bias=hc[0:1, HC_EPS:HC_EPS + 1]),
                wait=[("dve", dv)], inc="act")
    dv = p.emit("dve", lambda e: e.reciprocal(p.sb["stat_r"][0:1, :], p.sb["lnst"][0:1, :]),
                wait=[("act", ac)], inc="dve")

    if keep_xln or keep_bc:
        for tg in range(NRUN):
            sl = slice(tg * 512, tg * 512 + 512)

            def bc(e, sl=sl):
                e.matmul(bank(p, 4), p.sb["onesr_f"][0:1, :], p.sb["stat_m"][0:1, sl],
                         start=True, stop=True)
                return e.matmul(bank(p, 5), p.sb["onesr_f"][0:1, :], p.sb["stat_r"][0:1, sl],
                                start=True, stop=True)
            te = p.emit("te", bc, wait=[("dve", dv), ("dve", p.cnt["dve"]),
                                        ("act", p.cnt["act"])], inc="te")
            ac_sv = 0
            if keep_bc:
                def sv(e, sl=sl):
                    e.activation(p.sb["mbc"][:, sl], bank(p, 4), AF.Copy)
                    return e.activation(p.sb["rbc"][:, sl], bank(p, 5), AF.Copy)
                ac_sv = p.emit("act", sv, wait=[("te", te)], inc="act")
            if keep_xln:
                p.carve("xln", [128, DC, TLOC], BF16)
                xln = p.sb["xln"]

                def ap1(e, sl=sl):
                    mb = bank(p, 4).unsqueeze(1).broadcast_to([128, DC, 512])
                    return e.tensor_tensor(xln[:, :, sl], xres[:, :, sl], mb, ALU.subtract)
                p.emit("dve", ap1, wait=[("te", te), ("act", ac_sv)], inc="dve")

                def ap2(e, sl=sl):
                    rb = bank(p, 5).unsqueeze(1).broadcast_to([128, DC, 512])
                    return e.tensor_tensor(xln[:, :, sl], xln[:, :, sl], rb, ALU.mult)
                p.emit("dve", ap2, inc="dve")


def emit_rwkv(p, lw):
    tag = lw["tag"]
    WrT = p.din(lw["WrT"], [D, D])
    WkT = p.din(lw["WkT"], [D, D])
    WvT = p.din(lw["WvT"], [D, D])
    mur = p.din(lw["mur"], [128, DC])
    muk = p.din(lw["muk"], [128, DC])
    bndsel = p.din("c_bndsel", [33, 4])

    xres = p.sb["xres"]
    p.carve("mu2", [128, 2, DC], F32)
    p.carve("bsel33", [33, 4], F32)
    p.carve("xlf", [128, DC, CH], F32)
    p.carve("scr_f1", [128, DC, CH], F32)
    p.carve("sbbnd", [128, NRUN, DC], F32)
    p.carve("x0col", [128, DC, NRUN], F32)
    p.carve("bndt", [33, 1024], F32)
    p.carve("bndev", [4, 1024], F32)
    p.carve("xprev0", [128, DC, NRUN], F32)
    p.carve("xrmix", [128, DC, CH], BF16)
    p.carve("xkmix", [128, DC, CH], BF16)
    p.carve("wtile", [128, 2, DC, 128], BF16)
    p.carve("rgate", [128, DC, CH], F32)
    p.carve("kact", [128, DC, CH], BF16)
    p.carve("vtmp", [128, CH], F32)
    p.carve("vtmp2", [128, CH], F32)
    p.carve("d0scr", [128, DC], F32)

    bnd_in = p.dint(f"{tag}_bnd_in", [NRUN, 1024])
    bnd_all = p.dint(f"{tag}_bnd_all", [4 * NCORE, 1024])

    dma_sp(p, p.sb["mu2"][:, 0, :], mur[:, :])
    dma_sp(p, p.sb["mu2"][:, 1, :], muk[:, :])
    dma_sp(p, p.sb["bsel33"][:, :], bndsel[:, :])
    p.waitpt("dve", "xld")

    emit_ln(p, keep_xln=False, keep_bc=False)

    def bcast_stats(r):
        sl = slice(r * 512, r * 512 + 512)

        def bc(e, sl=sl):
            e.matmul(bank(p, 4), p.sb["onesr_f"][0:1, :], p.sb["stat_m"][0:1, sl],
                     start=True, stop=True)
            return e.matmul(bank(p, 5), p.sb["onesr_f"][0:1, :], p.sb["stat_r"][0:1, sl],
                            start=True, stop=True)
        return p.emit("te", bc, wait=[("dve", p.cnt["dve"]), ("act", p.cnt["act"])], inc="te")

    def mk_xlf(r, te):
        sl = slice(r * 512, r * 512 + 512)

        def f1(e, sl=sl):
            mb = bank(p, 4).unsqueeze(1).broadcast_to([128, DC, 512])
            return e.tensor_tensor(p.sb["xlf"][:, :, :], xres[:, :, sl], mb, ALU.subtract)

        def f2(e, sl=sl):
            rb = bank(p, 5).unsqueeze(1).broadcast_to([128, DC, 512])
            return e.tensor_tensor(p.sb["xlf"][:, :, :], p.sb["xlf"][:, :, :], rb, ALU.mult)
        p.emit("dve", f1, wait=[("te", te)], inc="dve")
        return p.emit("dve", f2, inc="dve")

    for r in range(NRUN):
        te = bcast_stats(r)
        mk_xlf(r, te)
        p.emit("dve", lambda e, r=r: e.tensor_copy(p.sb["sbbnd"][:, r, :], p.sb["xlf"][:, :, 511]),
               inc="dve")
        p.emit("dve", lambda e, r=r: e.tensor_copy(p.sb["x0col"][:, :, r], p.sb["xlf"][:, :, 0]),
               inc="dve")
    dv_pub = p.cnt["dve"]

    st = p.cnt["xld"]
    for r in range(NRUN):
        st = dma_sp(p, bnd_in[r].rearrange("(pp dcc) -> pp dcc", pp=128),
                    p.sb["sbbnd"][:, r, :],
                    wait=[("dve", dv_pub)] if r == 0 else ())

    def coll(e):
        return e.collective_compute(
            "AllGather", ALU.bypass, replica_groups=[list(range(NCORE))],
            ins=[bnd_in.opt()], outs=[bnd_all.opt()])
    cc = p.emit("pool", coll, wait=[("xld", st)], inc="cc")
    ld = dma_sp(p, p.sb["bndt"][0:32, :], bnd_all[:, :], wait=[("cc", cc)])
    dvz = p.emit("dve", lambda e: e.memset(p.sb["bndt"][32:33, :], 0.0),
                 wait=[("xld", ld)], inc="dve")

    def gmm(e):
        e.matmul(bank(p, 4, rows=4), p.sb["bsel33"][:, :], p.sb["bndt"][:, 0:512],
                 start=True, stop=True)
        return e.matmul(bank(p, 5, rows=4), p.sb["bsel33"][:, :], p.sb["bndt"][:, 512:1024],
                        start=True, stop=True)
    te = p.emit("te", gmm, wait=[("dve", dvz)], inc="te")

    def gev(e):
        e.tensor_copy(p.sb["bndev"][0:4, 0:512], bank(p, 4, rows=4))
        return e.tensor_copy(p.sb["bndev"][0:4, 512:1024], bank(p, 5, rows=4))
    dv = p.emit("dve", gev, wait=[("te", te)], inc="dve")

    ac_xp = p.cnt["act"]
    for dc in range(DC):
        def tr(e, dc=dc):
            src = p.sb["bndev"][0:4, :].rearrange("r (pp dcc) -> r pp dcc", pp=128)[:, :, dc]
            return e.transpose(bank(p, 6, rows=128, width=4), src, p.sb["idf"][0:4, 0:4])
        te = p.emit("te", tr, wait=[("dve", dv), ("act", ac_xp)], inc="te")
        ac_xp = p.emit("act", lambda e, dc=dc: e.activation(
            p.sb["xprev0"][:, dc, :], bank(p, 6, rows=128, width=4), AF.Copy),
            wait=[("te", te)], inc="act")

    wtile = p.sb["wtile"]
    slot_user = {0: 0, 1: 0}
    for tg in range(NRUN):
        sl = slice(tg * 512, tg * 512 + 512)
        te = bcast_stats(tg)
        p.waitpt("dve", "act")
        mk_xlf(tg, te)
        def mixsub(e, tg=tg):
            xv = p.sb["xlf"]
            e.tensor_tensor(p.sb["scr_f1"][:, :, 1:512], xv[:, :, 1:512], xv[:, :, 0:511],
                            ALU.subtract)
            return e.tensor_sub(p.sb["d0scr"][:, :], p.sb["x0col"][:, :, tg], p.sb["xprev0"][:, :, tg])
        p.emit("dve", mixsub, inc="dve")
        for mi, out in ((0, "xrmix"), (1, "xkmix")):
            def mix(e, tg=tg, mi=mi, out=out):
                xv = p.sb["xlf"]
                ov = p.sb[out]
                for dc in range(DC):
                    e.scalar_tensor_tensor(
                        ov[:, dc, 1:512], p.sb["scr_f1"][:, dc, 1:512],
                        p.sb["mu2"][:, mi, dc:dc + 1], xv[:, dc, 0:511], ALU.mult, ALU.add)
                    last = e.scalar_tensor_tensor(
                        ov[:, dc, 0:1], p.sb["d0scr"][:, dc:dc + 1],
                        p.sb["mu2"][:, mi, dc:dc + 1], p.sb["xprev0"][:, dc, tg:tg + 1],
                        ALU.mult, ALU.add)
                return last
            p.emit("dve", mix, inc="dve")
        dv_mix = p.cnt["dve"]

        for pi, (Wap, src) in enumerate(((WrT, "xrmix"), (WkT, "xkmix"))):
            for oc in range(DC):
                slot = (pi * 8 + oc) % 2
                wl = dma_pool(p, wtile[:, slot],
                              Wap.rearrange("(dcc pp) o -> pp dcc o", pp=128)[:, :, oc * 128:(oc + 1) * 128],
                              wait=[("te", slot_user[slot])])

                def mm(e, pi=pi, slot=slot, src=src):
                    for dcc in range(DC):
                        last = e.matmul(bank(p, pi), wtile[:, slot, dcc, :],
                                        p.sb[src][:, dcc, :],
                                        start=(dcc == 0), stop=(dcc == DC - 1))
                    return last
                te = p.emit("te", mm, wait=[("wld", wl), ("dve", dv_mix),
                                            ("act", p.cnt["act"])], inc="te")
                slot_user[slot] = te
                if pi == 0:
                    p.emit("act", lambda e, oc=oc: e.activation(
                        p.sb["rgate"][:, oc, :], bank(p, 0), AF.Sigmoid),
                        wait=[("te", te)], inc="act")
                else:
                    dvr = p.emit("dve", lambda e: e.tensor_scalar_max(
                        p.sb["vtmp"][:, :], bank(p, 1), 0.0), wait=[("te", te)], inc="dve")
                    p.emit("act", lambda e, oc=oc: e.activation(
                        p.sb["kact"][:, oc, :], p.sb["vtmp"][:, :], AF.Square),
                        wait=[("dve", dvr)], inc="act")
        ac_k = p.cnt["act"]
        for oc in range(DC):
            slot = oc % 2
            wl = dma_pool(p, wtile[:, slot],
                          WvT.rearrange("(dcc pp) o -> pp dcc o", pp=128)[:, :, oc * 128:(oc + 1) * 128],
                          wait=[("te", slot_user[slot])])

            def mmv(e, slot=slot):
                for dcc in range(DC):
                    last = e.matmul(bank(p, 2), wtile[:, slot, dcc, :], p.sb["kact"][:, dcc, :],
                                    start=(dcc == 0), stop=(dcc == DC - 1))
                return last
            te = p.emit("te", mmv, wait=[("wld", wl), ("act", ac_k),
                                         ("dve", p.cnt["dve"])], inc="te")
            slot_user[slot] = te

            p.emit("dve", lambda e, oc=oc: e.tensor_tensor(
                p.sb["vtmp2"][:, :], p.sb["rgate"][:, oc, :], bank(p, 2), ALU.mult),
                wait=[("te", te)], inc="dve")
            p.emit("dve", lambda e, oc=oc, sl=sl: e.tensor_add(
                xres[:, oc, sl], xres[:, oc, sl], p.sb["vtmp2"][:, :]), inc="dve")


def bcol(j):
    o = owner_of_chunk(j)
    return o * 1024 + (0 if j == o else 512)


def stagecol(b, j):
    o = owner_of_chunk(j)
    ro = 2 * b + (0 if j == o else 1)
    return o, ro


def va_slot(j, sub):
    return (bcol(j) // 128) + sub


def emit_attn(p, lw, debug_stop=None):
    tag = lw["tag"]
    kind = lw["kind"]
    xres = p.sb["xres"]

    NAUG = 65
    WoT = p.din(lw["WoT"], [D, D]) if kind != "l0" else None
    WqT = p.din(lw["WqT"], [D, 128])
    if kind == "l0":
        WkT = p.din(lw["WkT"], [D, 128])
        WvT = p.din(lw["WvT"], [D, 128])
        slopes_in = p.din("c_slopes", [128, 16])
        relh_in = p.din("c_relh", [128, 8, 512])
        rell_in = p.din("c_rell", [128, 8, 512])
        winm_in = p.din("c_winmask", [4, 128, 512])
    elif kind == "l2":
        WkT = p.din(lw["WkT"], [D, 128])
        WvT = p.din(lw["WvT"], [D, 128])
        bo = p.din(lw["bo"], [128, DC])
    else:
        Wkvd = p.din(lw["Wkvd"], [D, 256])
        WkuT = p.din(lw["WkuT"], [256, 128])
        WvuT = p.din(lw["WvuT"], [256, 128])
    diagm_in = p.din("c_diagmask", [4, 128, 512])

    p.carve("xln", [128, DC, TLOC], BF16)
    p.carve("xs", [128, DC, 512], BF16)
    if kind == "l0":
        p.carve("winm", [128, 4, 512], BF16)
        p.carve("slp", [128, 16], F32)
        p.carve("relh", [128, 8, 512], BF16)
        p.carve("rell", [128, 8, 512], BF16)
        p.carve("sidb", [128, 128], BF16)
    p.carve("wq_sb", [128, DC, 128], BF16)
    p.carve("wk_sb", [128, DC, 128], BF16)
    p.carve("wv_sb", [128, DC, 128], BF16)
    p.carve("diagm", [128, 4, 512], BF16)
    p.carve("qsc", [128, 1], F32)
    p.carve("kmx", [2, 4], F32)
    if kind != "l0":
        p.carve("wtile", [128, 2, DC, 128], BF16)
        p.carve("vtmp", [128, CH], F32)
    p.carve("bo_sb", [128, DC], F32)
    if kind == "l1":
        p.carve("lat_sb", [128, 2, 512], BF16)
        p.carve("wkvd_sb", [128, DC, 256], BF16)

    ag_in = p.dint(f"{tag}_ag_in", [128, DC * TLOC], BF16)
    xag = p.dint(f"{tag}_xag", [NCORE, 128, DC, TLOC], BF16)
    a2a_in = p.dint(f"{tag}_a2a_in", [NCORE * 128, TLOC], BF16)
    a2a_out = p.dint(f"{tag}_a2a_out", [NCORE * 128, TLOC], BF16)
    if kind == "l1":
        lat_in = p.dint(f"{tag}_lat_in", [128, 2 * TLOC], BF16)
        lat_all = p.dint(f"{tag}_lat_all", [NCORE, 128, 2, TLOC], BF16)

    dma_pool(p, p.sb["wq_sb"][:, :, :], WqT.rearrange("(dcc pp) o -> pp dcc o", pp=128))
    if kind == "l0":
        dma_pool(p, p.sb["wk_sb"][:, :, :], WkT.rearrange("(dcc pp) o -> pp dcc o", pp=128))
        dma_pool(p, p.sb["wv_sb"][:, :, :], WvT.rearrange("(dcc pp) o -> pp dcc o", pp=128))
        dma_pool(p, p.sb["winm"][:, :, :], winm_in.rearrange("m pp q -> pp m q"))
        dma_sp(p, p.sb["slp"][:, :], slopes_in[:, :])
        dma_pool(p, p.sb["relh"][:, :, :], relh_in[:, :, :])
        dma_pool(p, p.sb["rell"][:, :, :], rell_in[:, :, :])
        p.emit("dve", lambda e: e.memset(p.sb["bo_sb"][:, :], 0.0), inc="dve")
    elif kind == "l2":
        dma_pool(p, p.sb["wk_sb"][:, :, :], WkT.rearrange("(dcc pp) o -> pp dcc o", pp=128))
        dma_pool(p, p.sb["wv_sb"][:, :, :], WvT.rearrange("(dcc pp) o -> pp dcc o", pp=128))
        dma_pool(p, p.sb["bo_sb"][:, :], bo[:, :])
    else:
        dma_pool(p, p.sb["wk_sb"][:, 0:2, :], WkuT.rearrange("(lc pp) o -> pp lc o", pp=128))
        dma_pool(p, p.sb["wv_sb"][:, 0:2, :], WvuT.rearrange("(lc pp) o -> pp lc o", pp=128))
        p.emit("dve", lambda e: e.memset(p.sb["bo_sb"][:, :], 0.0), inc="dve")
    dma_pool(p, p.sb["diagm"][:, :, :], diagm_in.rearrange("m pp q -> pp m q"))
    p.emit("dve", lambda e: e.memset(p.sb["qsc"][:, :], 0.125 if kind != "l0" else 0.03125),
           inc="dve")

    mark = p.carve_mark()
    emit_ln(p, keep_xln=True)
    st = dma_sp(p, ag_in[:, :], p.sb["xln"].rearrange("p dc t -> p (dc t)"),
                wait=[("dve", p.cnt["dve"])])
    cc = p.emit("pool", lambda e: e.collective_compute(
        "AllGather", ALU.bypass, replica_groups=[list(range(NCORE))],
        ins=[ag_in.opt()], outs=[xag.rearrange("o pp f1 f2 -> (o pp) (f1 f2)").opt()]),
        wait=[("xld", st), ("wld", p.cnt["wld"])], inc="cc")

    if debug_stop == "ag":
        return
    if kind == "l1":
        dma_pool(p, p.sb["wkvd_sb"][:, :, :], Wkvd.rearrange("(dcc pp) o -> pp dcc o", pp=128))
        wl = p.cnt["wld"]
        for tg in range(NRUN):
            sl = slice(tg * 512, tg * 512 + 512)
            for lc in range(2):
                def mml(e, sl=sl, lc=lc):
                    for dcc in range(DC):
                        last = e.matmul(bank(p, 2), p.sb["wkvd_sb"][:, dcc, lc * 128:(lc + 1) * 128],
                                        p.sb["xln"][:, dcc, sl], start=(dcc == 0), stop=(dcc == DC - 1))
                    return last
                te = p.emit("te", mml, wait=[("wld", wl), ("dve", p.cnt["dve"]),
                                             ("act", p.cnt["act"]), ("xld", p.cnt["xld"])], inc="te")
                p.emit("act", lambda e, lc=lc: e.activation(
                    p.sb["lat_sb"][:, lc, :], bank(p, 2), AF.Copy),
                    wait=[("te", te), ("xld", p.cnt["xld"])], inc="act")
            dma_sp(p, lat_in.rearrange("p (l t) -> p l t", l=2)[:, :, sl],
                   p.sb["lat_sb"][:, :, :], wait=[("act", p.cnt["act"])])
        cc = p.emit("pool", lambda e: e.collective_compute(
            "AllGather", ALU.bypass, replica_groups=[list(range(NCORE))],
            ins=[lat_in.opt()], outs=[lat_all.rearrange("o pp f1 f2 -> (o pp) (f1 f2)").opt()]),
            wait=[("xld", p.cnt["xld"])], inc="cc")

    p.carve_rewind(mark, ["stat_m", "stat_r", "lnst", "lnsq", "lnsq2"])
    p.carve("lnsq", [128, 512], BF16)
    p.carve("KA", [NAUG, 8192], BF16)
    p.carve("QA", [NAUG, 8192], BF16)
    p.carve("VA", [128, 64, 65], BF16)
    p.carve("ptile", [128, 2, 1024], BF16)
    p.carve("vfm", [64, 512], BF16)
    p.carve("ofsb", [64, 512], F32)
    p.carve("rden", [1, 512], F32)

    def initk(e):
        e.memset(p.sb["KA"][64:65, :], -1.0)
        return e.memset(p.sb["VA"][:, :, 64:65], 1.0)
    p.emit("dve", initk, wait=[("xld", p.cnt["xld"]), ("te", p.cnt["te"]),
                               ("act", p.cnt["act"])], inc="dve")

    if debug_stop == "init":
        return
    if debug_stop == "xsdma":
        dma_sp(p, p.sb["xs"][:, :, :], xag[3][:, :, 512:1024], wait=[("cc", p.cnt["cc"])])
        dma_sp(p, p.sb["lnsq"][0:64, :], xag[5][0:64, 2, 1024:1536], wait=[("cc", p.cnt["cc"])])
        return

    for b in range(2):
        for hl in range(2):
            hsl = slice(hl * 64, hl * 64 + 64)
            dvk = p.emit("dve", lambda e: e.memset(p.sb["kmx"][0:1, 0:4], 0.0), inc="dve")
            if kind == "l0":
                p.emit("dve", lambda e, hl=hl: e.tensor_scalar(
                    p.sb["sidb"][:, :], p.sb["idb"][:, :], p.sb["slp"][:, hl:hl + 1],
                    None, ALU.mult), inc="dve")
            xs_user = p.cnt["te"]
            ntg16 = 1 if (debug_stop or "").startswith("projk1") else 16
            if (debug_stop or "").startswith("projk1") and (b > 0 or hl > 0):
                ntg16 = 0
            for tg16 in range(ntg16):
                o, half = tg16 // 2, tg16 % 2
                lcol = 2 * b * 512 + half * 512
                gcol = o * 1024 + half * 512
                xsl = dma_sp(p, p.sb["xs"][:, :, :],
                             xag[o][:, :, lcol:lcol + 512],
                             wait=[("te", xs_user), ("cc", p.cnt["cc"])])
                if kind == "l1":
                    xsl = dma_sp(p, p.sb["lat_sb"][:, :, :],
                                 lat_all[o][:, :, lcol:lcol + 512],
                                 wait=[("te", xs_user), ("cc", p.cnt["cc"])])

                def mmk(e, hsl=hsl):
                    if kind != "l1":
                        for dcc in range(DC):
                            last = e.matmul(bank(p, 2, rows=64), p.sb["wk_sb"][:, dcc, hsl],
                                            p.sb["xs"][:, dcc, :], start=(dcc == 0), stop=(dcc == DC - 1))
                    else:
                        for lc in range(2):
                            last = e.matmul(bank(p, 2, rows=64), p.sb["wk_sb"][:, lc, hsl],
                                            p.sb["lat_sb"][:, lc, :], start=(lc == 0), stop=(lc == 1))
                    return last
                te = p.emit("te", mmk, wait=[("xld", xsl), ("wld", p.cnt["wld"]),
                                             ("dve", p.cnt["dve"]), ("act", p.cnt["act"])], inc="te")
                dvka = p.emit("dve", lambda e, gcol=gcol: e.tensor_copy(
                    p.sb["KA"][0:64, gcol:gcol + 512], bank(p, 2, rows=64)),
                    wait=[("te", te)], inc="dve")
                if debug_stop == "projk1a":
                    xs_user = p.cnt["te"]
                    continue
                if debug_stop == "projk1bA":
                    p.sbuf("realsq", [128, 512], BF16)
                    p.emit("act", lambda e: e.activation(p.sb["realsq"][0:64, :],
                                                         bank(p, 2, rows=64), AF.Square),
                           wait=[("te", te), ("dve", dvka)], inc="act")
                    xs_user = p.cnt["te"]
                    continue
                if debug_stop == "projk1bB":
                    p.emit("act", lambda e: e.activation(p.sb["lnsq"][0:64, :],
                                                         bank(p, 2, rows=64), AF.Copy),
                           wait=[("te", te), ("dve", dvka)], inc="act")
                    xs_user = p.cnt["te"]
                    continue
                ac = p.emit("act", lambda e: e.activation(p.sb["lnsq"][0:64, :],
                                                          bank(p, 2, rows=64), AF.Square),
                            wait=[("te", te), ("dve", dvka)], inc="act")
                if debug_stop == "projk1b0":
                    xs_user = p.cnt["te"]
                    continue
                te2 = p.emit("te", lambda e: e.matmul(bank(p, 5, rows=1), p.sb["ones_bf"][0:64, 0:1],
                                                      p.sb["lnsq"][0:64, :], start=True, stop=True),
                             wait=[("act", ac)], inc="te")

                if debug_stop == "projk1b":
                    xs_user = p.cnt["te"]
                    continue

                p.emit("dve", lambda e: e.reduce_max(p.sb["rden"][0:1, 0:1], bank(p, 5, rows=1),
                                                     axis=mybir.AxisListType.X),
                       wait=[("te", te2)], inc="dve")
                p.emit("dve", lambda e: e.tensor_max(p.sb["kmx"][0:1, 0:1], p.sb["kmx"][0:1, 0:1],
                                                     p.sb["rden"][0:1, 0:1]), inc="dve")
                if debug_stop == "projk1c":
                    xs_user = p.cnt["te"]
                    continue

                if debug_stop == "projk":
                    xs_user = p.cnt["te"]
                    continue
                def mmq(e, hsl=hsl):
                    for dcc in range(DC):
                        last = e.matmul(bank(p, 3, rows=64), p.sb["wq_sb"][:, dcc, hsl],
                                        p.sb["xs"][:, dcc, :], start=(dcc == 0), stop=(dcc == DC - 1))
                    return last
                te = p.emit("te", mmq, wait=[("act", p.cnt["act"]), ("dve", p.cnt["dve"])], inc="te")
                dvqa = p.emit("dve", lambda e, gcol=gcol: e.tensor_scalar(
                    p.sb["QA"][0:64, gcol:gcol + 512], bank(p, 3, rows=64),
                    p.sb["qsc"][0:64, 0:1], None, ALU.mult), wait=[("te", te)], inc="dve")
                ac = p.emit("act", lambda e: e.activation(p.sb["lnsq"][0:64, :],
                                                          bank(p, 3, rows=64), AF.Square),
                            wait=[("te", te), ("dve", dvqa)], inc="act")
                te2 = p.emit("te", lambda e: e.matmul(bank(p, 6, rows=1), p.sb["ones_bf"][0:64, 0:1],
                                                      p.sb["lnsq"][0:64, :], start=True, stop=True),
                             wait=[("act", ac)], inc="te")
                p.emit("dve", lambda e, gcol=gcol: e.tensor_copy(
                    p.sb["QA"][64:65, gcol:gcol + 512], bank(p, 6, rows=1)),
                    wait=[("te", te2)], inc="dve")

                if debug_stop == "projq":
                    xs_user = p.cnt["te"]
                    continue
                def mmv(e, hsl=hsl):
                    if kind != "l1":
                        for dcc in range(DC):
                            last = e.matmul(bank(p, 4, rows=64), p.sb["wv_sb"][:, dcc, hsl],
                                            p.sb["xs"][:, dcc, :], start=(dcc == 0), stop=(dcc == DC - 1))
                    else:
                        for lc in range(2):
                            last = e.matmul(bank(p, 4, rows=64), p.sb["wv_sb"][:, lc, hsl],
                                            p.sb["lat_sb"][:, lc, :], start=(lc == 0), stop=(lc == 1))
                    return last
                te = p.emit("te", mmv, wait=[("act", p.cnt["act"]), ("dve", p.cnt["dve"])], inc="te")
                ac = p.emit("act", lambda e: e.activation(p.sb["vfm"][0:64, :], bank(p, 4, rows=64),
                                                          AF.Copy), wait=[("te", te)], inc="act")
                jj = o if half == 0 else 15 - o
                for sub in range(4):
                    tt = p.emit("te", lambda e, sub=sub: e.transpose(
                        p.pt[:, 0:64], p.sb["vfm"][0:64, sub * 128:(sub + 1) * 128],
                        p.sb["idb"][0:64, 0:64]), wait=[("act", ac), ("dve", p.cnt["dve"])], inc="te")
                    slot = va_slot(jj, sub)
                    p.emit("dve", lambda e, slot=slot: e.tensor_copy(
                        p.sb["VA"][:, slot, 0:64], p.pt[:, 0:64]), wait=[("te", tt)], inc="dve")
                xs_user = p.cnt["te"]

            if debug_stop is not None and debug_stop.startswith("proj"):
                continue
            hcq = HC_Q2L0 if kind == "l0" else HC_Q2
            p.emit("dve", lambda e, hcq=hcq: e.tensor_scalar(p.sb["QA"][64:65, :], p.sb["QA"][64:65, :],
                                                    p.sb["hc"][0:1, hcq:hcq + 1], None,
                                                    ALU.mult), inc="dve")
            dvm = p.emit("dve", lambda e: e.tensor_scalar(p.sb["QA"][64:65, :], p.sb["QA"][64:65, :],
                                                          p.sb["kmx"][0:1, 0:1],
                                                          p.sb["hc"][0:1, HC_HALF:HC_HALF + 1],
                                                          ALU.add, ALU.mult), inc="dve")

            prev_pair_user = {0: 0, 1: 0}
            av_user = {0: 0, 1: 0}
            for j in range(NCH):
                if kind == "l0":
                    jks = ([j] if j == 0 else [j - 1, j])
                else:
                    jks = list(range(j + 1))
                nkb = 4 * len(jks)
                qg = bcol(j)
                o_j, ro_j = stagecol(b, j)
                dv_j0 = p.cnt["dve"]
                ac_j0 = p.cnt["act"]
                for kb in range(nkb):
                    jk, sub = jks[kb // 4], kb % 4
                    kc = bcol(jk) + sub * 128
                    pair = (kb // 2) % 2
                    bk = (2 if pair == 0 else 4) + (kb % 2)
                    diag = (jk == j)
                    winedge = (kind == "l0" and jk == j - 1)

                    def qk(e, qg=qg, kc=kc, bk=bk, diag=diag, sub=sub, winedge=winedge):
                        last = e.matmul(bank(p, bk), p.sb["KA"][:, kc:kc + 128],
                                        p.sb["QA"][:, qg:qg + 512], start=True,
                                        stop=not (diag or winedge))
                        if diag:
                            last = e.matmul(bank(p, bk), p.sb["idb"][:, :], p.sb["diagm"][:, sub, :],
                                            start=False, stop=(kind != "l0"))
                        elif winedge:
                            last = e.matmul(bank(p, bk), p.sb["idb"][:, :], p.sb["winm"][:, sub, :],
                                            start=False, stop=False)
                        if kind == "l0":
                            mi = (1 if diag else 0) * 4 + sub
                            e.matmul(bank(p, bk), p.sb["sidb"][:, :], p.sb["relh"][:, mi, :],
                                     start=False, stop=False)
                            last = e.matmul(bank(p, bk), p.sb["sidb"][:, :], p.sb["rell"][:, mi, :],
                                            start=False, stop=True)
                        return last
                    p.emit("te", qk, wait=[("dve", dvm), ("act", max(prev_pair_user[pair], ac_j0))],
                           inc="te")
                    if kb % 2 == 1:
                        teqk = p.cnt["te"]
                        slot = (kb // 2) % 2

                        def expf(e, pair=pair, slot=slot):
                            c0 = (2 if pair == 0 else 4) * 512
                            return e.activation(p.sb["ptile"][:, slot, :], p.ps[:, c0:c0 + 1024],
                                                AF.Exp)
                        ac = p.emit("act", expf, wait=[("te", teqk), ("te", av_user[slot])],
                                    inc="act")
                        prev_pair_user[pair] = ac
                        for q2 in range(2):
                            kb2 = kb - 1 + q2
                            vs = va_slot(jks[kb2 // 4], kb2 % 4)

                            def av(e, vs=vs, slot=slot, q2=q2, kb2=kb2, nkb=nkb):
                                return e.matmul(bank(p, 0, rows=65), p.sb["VA"][:, vs, :],
                                                p.sb["ptile"][:, slot, q2 * 512:(q2 + 1) * 512],
                                                start=(kb2 == 0), stop=(kb2 == nkb - 1))
                            p.emit("te", av, wait=[("act", ac), ("dve", dv_j0)], inc="te")
                        av_user[slot] = p.cnt["te"]
                teav = p.cnt["te"]
                dv = p.emit("dve", lambda e: e.reciprocal(p.sb["rden"][0:1, 0:512],
                                                          p.ps[64:65, 0:512]),
                            wait=[("te", teav)], inc="dve")
                te6 = p.emit("te", lambda e: e.matmul(bank(p, 6, rows=64), p.sb["onesr_f"][0:1, 0:64],
                                                      p.sb["rden"][0:1, 0:512], start=True, stop=True),
                             wait=[("dve", dv)], inc="te")
                ac = p.emit("act", lambda e: e.activation(p.sb["ofsb"][0:64, :], bank(p, 0, rows=65)[0:64, :],
                                                          AF.Copy), wait=[("te", teav), ("dve", dv)], inc="act")

                def norm(e, hl=hl, o_j=o_j, ro_j=ro_j):
                    dst = p.sb["xln"][hl * 64:hl * 64 + 64, o_j, ro_j * 512:(ro_j + 1) * 512]
                    return e.tensor_tensor(dst, p.sb["ofsb"][0:64, :], bank(p, 6, rows=64), ALU.mult)
                p.emit("dve", norm, wait=[("te", te6), ("act", ac)], inc="dve")

    if debug_stop in ("proj", "attn"):
        return
    st = dma_sp(p, a2a_in.rearrange("(o pp) t -> pp o t", pp=128),
                p.sb["xln"][:, :, :], wait=[("dve", p.cnt["dve"])])

    def colla(e):
        with allow_ici_all_to_all():
            return e.collective_compute(
                "AllToAll", ALU.bypass, replica_groups=[list(range(NCORE))],
                ins=[a2a_in.opt()], outs=[a2a_out.opt()])
    cc = p.emit("pool", colla, wait=[("xld", st)], inc="cc")
    ld = dma_sp(p, p.sb["xln"][:, :, :], a2a_out.rearrange("(dc pp) t -> pp dc t", pp=128),
                wait=[("cc", cc)])

    if kind == "l0":
        ld2 = ld
        for tg in range(NRUN):
            sl = slice(tg * 512, tg * 512 + 512)
            for oc in range(DC):
                p.emit("dve", lambda e, oc=oc, sl=sl: e.tensor_add(
                    xres[:, oc, sl], xres[:, oc, sl], p.sb["xln"][:, oc, sl]),
                    wait=[("xld", ld2)], inc="dve")
        return
    wtile = p.sb["wtile"]
    slot_user = {0: 0, 1: 0}
    for tg in range(NRUN):
        sl = slice(tg * 512, tg * 512 + 512)
        for oc in range(DC):
            slot = oc % 2
            wl = dma_pool(p, wtile[:, slot],
                          WoT.rearrange("(dcc pp) o -> pp dcc o", pp=128)[:, :, oc * 128:(oc + 1) * 128],
                          wait=[("te", slot_user[slot])])

            def mmo(e, slot=slot, sl=sl):
                for dcc in range(DC):
                    last = e.matmul(bank(p, 2), wtile[:, slot, dcc, :], p.sb["xln"][:, dcc, sl],
                                    start=(dcc == 0), stop=(dcc == DC - 1))
                return last
            te = p.emit("te", mmo, wait=[("wld", wl), ("xld", ld), ("dve", p.cnt["dve"]),
                                         ("act", p.cnt["act"])], inc="te")
            slot_user[slot] = te
            p.emit("dve", lambda e, oc=oc, sl=sl: e.scalar_tensor_tensor(
                xres[:, oc, sl], bank(p, 2), p.sb["bo_sb"][:, oc:oc + 1], xres[:, oc, sl],
                ALU.add, ALU.add), wait=[("te", te)], inc="dve")


def emit_l3a(p, lw, debug_stop=None):
    wslow_in = p.din("l3a_wslow", [D, 3088])
    wout_in = p.din("l3a_wout", [D, D])
    lnw_in = p.din("l3a_lnw", [128, DC])
    lnb_in = p.din("l3a_lnb", [128, DC])
    xres = p.sb["xres"]

    p.carve("lnw", [128, DC], F32)
    p.carve("lnb", [128, DC], F32)
    p.carve("stato", [1, 2 * TLOC], BF16)
    p.carve("hh", [128, DC, TLOC], BF16)
    p.carve("outall", [128, DC, TLOC], BF16)
    p.carve("wsl", [128, 2, DC, 208], BF16)
    p.carve("zero1", [128, 1], F32)
    p.carve("vtmp", [128, CH], F32)
    mk3 = p.carve_mark()
    p.carve("xlf3", [128, DC, CH], F32)

    dma_sp(p, p.sb["lnw"][:, :], lnw_in[:, :])
    ldb = dma_sp(p, p.sb["lnb"][:, :], lnb_in[:, :])
    p.emit("dve", lambda e: e.memset(p.sb["zero1"][:, :], 0.0), wait=[("xld", ldb)], inc="dve")

    emit_ln(p, keep_xln=False, keep_bc=False)
    p.emit("dve", lambda e: e.tensor_copy(p.sb["stato"][0:1, 0:TLOC], p.sb["stat_m"][0:1, :]),
           inc="dve")
    p.emit("dve", lambda e: e.tensor_copy(p.sb["stato"][0:1, TLOC:2 * TLOC], p.sb["stat_r"][0:1, :]),
           inc="dve")
    hc = p.sb["hc"]
    p.emit("dve", lambda e: e.tensor_mul(p.sb["lnst"][0:1, :], p.sb["stat_r"][0:1, :],
                                         p.sb["stat_r"][0:1, :]), inc="dve")
    p.emit("dve", lambda e: e.tensor_scalar(p.sb["lnst"][0:1, :], p.sb["lnst"][0:1, :],
                                            hc[0:1, HC_NEPS:HC_NEPS + 1],
                                            hc[0:1, HC_ONE:HC_ONE + 1], ALU.mult, ALU.add),
           inc="dve")
    ac = p.emit("act", lambda e: e.activation(p.sb["lnst"][0:1, :], p.sb["lnst"][0:1, :], AF.Sqrt,
                                              bias=hc[0:1, HC_EPS:HC_EPS + 1]),
                wait=[("dve", p.cnt["dve"])], inc="act")
    p.emit("dve", lambda e: e.reciprocal(p.sb["lnst"][0:1, :], p.sb["lnst"][0:1, :]),
           wait=[("act", ac)], inc="dve")
    p.emit("dve", lambda e: e.tensor_mul(p.sb["stat_r"][0:1, :], p.sb["stat_r"][0:1, :],
                                         p.sb["lnst"][0:1, :]), inc="dve")

    for tg in range(NRUN):
        sl = slice(tg * 512, tg * 512 + 512)

        def bc4(e, sl=sl):
            e.matmul(bank(p, 3), p.sb["onesr_f"][0:1, :], p.sb["stat_m"][0:1, sl], start=True, stop=True)
            return e.matmul(bank(p, 4), p.sb["onesr_f"][0:1, :], p.sb["stat_r"][0:1, sl],
                            start=True, stop=True)
        te = p.emit("te", bc4, wait=[("dve", p.cnt["dve"]), ("act", p.cnt["act"])], inc="te")

        def xl1(e, sl=sl):
            mb = bank(p, 3).unsqueeze(1).broadcast_to([128, DC, 512])
            return e.tensor_tensor(p.sb["xlf3"][:, :, :], xres[:, :, sl], mb, ALU.subtract)
        p.emit("dve", xl1, wait=[("te", te)], inc="dve")

        def xl2(e):
            rb = bank(p, 4).unsqueeze(1).broadcast_to([128, DC, 512])
            return e.tensor_tensor(p.sb["xlf3"][:, :, :], p.sb["xlf3"][:, :, :], rb, ALU.mult)
        dv = p.emit("dve", xl2, inc="dve")
        for dc in range(DC):
            p.emit("dve", lambda e, dc=dc, sl=sl: e.tensor_scalar(
                p.sb["hh"][:, dc, sl], p.sb["xlf3"][:, dc, :],
                p.sb["lnw"][:, dc:dc + 1], p.sb["lnb"][:, dc:dc + 1], ALU.mult, ALU.add),
                wait=[("dve", dv)], inc="dve")
    ac_h = p.cnt["act"]

    if debug_stop == "l3prep":
        return
    p.carve_rewind(mk3, ["xlf3", "stat_m", "stat_r", "lnst", "lnsq", "lnsq2"])
    p.carve("qn", [64, TLOC], BF16)
    p.carve("kn", [64, TLOC], BF16)
    p.carve("vv", [64, TLOC], BF16)
    p.carve("bet", [1, TLOC], BF16)
    p.carve("q1", [64, CH], F32)
    p.carve("q1b", [64, CH], BF16)
    p.carve("dtmp", [64, 1024], BF16)
    p.carve("dtmp2", [64, 1024], BF16)
    p.carve("dtmp3", [64, 1024], BF16)
    p.carve("dtmp4", [64, 1024], BF16)
    p.carve("dcol", [1, 4096], BF16)
    p.carve("del0", [64, 1024], F32)
    p.carve("del1", [64, 1024], F32)
    p.carve("ot", [64, 512], F32)
    p.carve("rden3", [1, CH], F32)
    slot_user = {0: 0, 1: 0}
    for h in range(16):
        slot = h % 2
        wl = dma_pool(p, p.sb["wsl"][:, slot, :, 0:193],
                      wslow_in.rearrange("(dcc pp) o -> pp dcc o", pp=128)[:, :, 193 * h:193 * h + 193],
                      wait=[("te", slot_user[slot])])
        for tg in range(NRUN):
            sl = slice(tg * 512, tg * 512 + 512)

            def mma(e, slot=slot, sl=sl):
                for dcc in range(DC):
                    e.matmul(bank(p, 0), p.sb["wsl"][:, slot, dcc, 0:128], p.sb["hh"][:, dcc, sl],
                             start=(dcc == 0), stop=(dcc == DC - 1))
                for dcc in range(DC):
                    last = e.matmul(bank(p, 1, rows=65), p.sb["wsl"][:, slot, dcc, 128:193],
                                    p.sb["hh"][:, dcc, sl], start=(dcc == 0), stop=(dcc == DC - 1))
                return last
            te = p.emit("te", mma, wait=[("wld", wl), ("act", max(ac_h, p.cnt["act"])),
                                         ("dve", p.cnt["dve"])], inc="te")
            slot_user[slot] = te
            dvv = p.emit("dve", lambda e, sl=sl: e.tensor_copy(p.sb["vv"][0:64, sl],
                                                               bank(p, 1, rows=64)),
                         wait=[("te", te)], inc="dve")
            acb = p.emit("act", lambda e, sl=sl: e.activation(p.sb["bet"][0:1, sl],
                                                              p.ps[64:65, 512:1024], AF.Sigmoid),
                         wait=[("te", te), ("dve", dvv)], inc="act")
            for qi, rows0 in ((0, 0), (1, 64)):
                dst = p.sb["qn"] if qi == 0 else p.sb["kn"]
                bsl = bank(p, 0)[rows0:rows0 + 64, :]

                def relu_(e, bsl=bsl):
                    return e.tensor_scalar_max(p.sb["q1"][0:64, :], bsl, p.sb["zero1"][0:64, 0:1])
                dv1 = p.emit("dve", relu_, wait=[("te", te), ("act", p.cnt["act"])], inc="dve")

                def min_(e, bsl=bsl):
                    return e.tensor_scalar_min(p.sb["vtmp"][0:64, :], bsl, p.sb["zero1"][0:64, 0:1])
                dv2 = p.emit("dve", min_, inc="dve")
                ace = p.emit("act", lambda e: e.activation(p.sb["vtmp"][0:64, :], p.sb["vtmp"][0:64, :],
                                                           AF.Exp), wait=[("dve", dv2)], inc="act")
                dv3 = p.emit("dve", lambda e: e.tensor_add(p.sb["q1"][0:64, :], p.sb["q1"][0:64, :],
                                                           p.sb["vtmp"][0:64, :]),
                             wait=[("ace" if False else "act", ace)], inc="dve")
                dv4 = p.emit("dve", lambda e: e.tensor_copy(p.sb["q1b"][0:64, :], p.sb["q1"][0:64, :]),
                             inc="dve")
                te2 = p.emit("te", lambda e: e.matmul(bank(p, 2, rows=1), p.sb["ones_bf"][0:64, 0:1],
                                                      p.sb["q1b"][0:64, :], start=True, stop=True),
                             wait=[("dve", dv4)], inc="te")
                dv5 = p.emit("dve", lambda e: e.reciprocal(p.sb["rden3"][0:1, :], bank(p, 2, rows=1)),
                             wait=[("te", te2)], inc="dve")
                te3 = p.emit("te", lambda e: e.matmul(bank(p, 2, rows=64), p.sb["onesr_f"][0:1, 0:64],
                                                      p.sb["rden3"][0:1, :], start=True, stop=True),
                             wait=[("dve", dv5)], inc="te")
                p.emit("dve", lambda e, dst=dst, sl=sl: e.tensor_tensor(
                    dst[0:64, sl], p.sb["q1"][0:64, :], bank(p, 2, rows=64), ALU.mult),
                    wait=[("te", te3)], inc="dve")
        B0 = slice(0, 1024)
        B1 = slice(1024, 2048)

        def dots(e):
            e.tensor_mul(p.sb["dtmp"][0:64, :], p.sb["kn"][0:64, B0], p.sb["qn"][0:64, B0])
            return e.tensor_mul(p.sb["dtmp2"][0:64, :], p.sb["kn"][0:64, B0], p.sb["qn"][0:64, B1])
        dv = p.emit("dve", dots, inc="dve")

        def dots2(e):
            e.tensor_mul(p.sb["dtmp3"][0:64, :], p.sb["kn"][0:64, B0], p.sb["kn"][0:64, B1])
            return e.tensor_mul(p.sb["dtmp4"][0:64, :], p.sb["kn"][0:64, B1], p.sb["qn"][0:64, B1])
        dv = p.emit("dve", dots2, inc="dve")

        dots_names = ("dtmp", "dtmp2", "dtmp3", "dtmp4")
        for i, t in enumerate(dots_names):
            for q2 in range(2):
                te2 = p.emit("te", lambda e, t=t, q2=q2: e.matmul(
                    bank(p, 3, rows=1), p.sb["ones_bf"][0:64, 0:1],
                    p.sb[t][0:64, q2 * 512:(q2 + 1) * 512], start=True, stop=True),
                    wait=[("dve", p.cnt["dve"])], inc="te")
                p.emit("dve", lambda e, i=i, q2=q2: e.tensor_copy(
                    p.sb["dcol"][0:1, i * 1024 + q2 * 512:i * 1024 + (q2 + 1) * 512],
                    bank(p, 3, rows=1)), wait=[("te", te2)], inc="dve")
        te2 = p.emit("te", lambda e: e.matmul(bank(p, 4, rows=64), p.sb["onesr_b"][0:1, 0:64],
                                              p.sb["bet"][0:1, 0:512], start=True, stop=True),
                     wait=[("dve", p.cnt["dve"])], inc="te")
        te3 = p.emit("te", lambda e: e.matmul(bank(p, 5, rows=64), p.sb["onesr_b"][0:1, 0:64],
                                              p.sb["bet"][0:1, 512:1024], start=True, stop=True),
                     inc="te")
        p.emit("dve", lambda e: e.tensor_tensor(p.sb["del0"][0:64, 0:512], p.sb["vv"][0:64, 0:512],
                                                bank(p, 4, rows=64), ALU.mult),
               wait=[("te", te2)], inc="dve")
        dv = p.emit("dve", lambda e: e.tensor_tensor(p.sb["del0"][0:64, 512:1024],
                                                     p.sb["vv"][0:64, 512:1024],
                                                     bank(p, 5, rows=64), ALU.mult),
                    wait=[("te", te3)], inc="dve")
        for q2 in range(2):
            te2 = p.emit("te", lambda e, q2=q2: e.matmul(
                bank(p, 4, rows=64), p.sb["onesr_b"][0:1, 0:64],
                p.sb["dcol"][0:1, 2048 + q2 * 512:2048 + (q2 + 1) * 512], start=True, stop=True),
                wait=[("dve", p.cnt["dve"])], inc="te")
            te3 = p.emit("te", lambda e, q2=q2: e.matmul(
                bank(p, 5, rows=64), p.sb["onesr_b"][0:1, 0:64],
                p.sb["bet"][0:1, 1024 + q2 * 512:1024 + (q2 + 1) * 512], start=True, stop=True),
                inc="te")
            s5 = slice(q2 * 512, (q2 + 1) * 512)

            def d1a(e, s5=s5):
                return e.tensor_tensor(p.sb["ot"][0:64, 0:512], p.sb["del0"][0:64, s5],
                                       bank(p, 4, rows=64), ALU.mult)
            p.emit("dve", d1a, wait=[("te", te2)], inc="dve")

            def d1b(e, q2=q2):
                return e.tensor_sub(p.sb["ot"][0:64, 0:512],
                                    p.sb["vv"][0:64, 1024 + q2 * 512:1024 + (q2 + 1) * 512],
                                    p.sb["ot"][0:64, 0:512])
            p.emit("dve", d1b, inc="dve")

            def d1c(e, s5=s5):
                return e.tensor_tensor(p.sb["del1"][0:64, s5], p.sb["ot"][0:64, 0:512],
                                       bank(p, 5, rows=64), ALU.mult)
            p.emit("dve", d1c, wait=[("te", te3)], inc="dve")
        hp = 64 * (h % 2)
        hdc = h // 2
        for q2 in range(2):
            s5 = slice(q2 * 512, (q2 + 1) * 512)
            te2 = p.emit("te", lambda e, q2=q2: e.matmul(
                bank(p, 4, rows=64), p.sb["onesr_b"][0:1, 0:64],
                p.sb["dcol"][0:1, q2 * 512:(q2 + 1) * 512], start=True, stop=True),
                wait=[("dve", p.cnt["dve"])], inc="te")
            p.emit("dve", lambda e, hp=hp, hdc=hdc, q2=q2, s5=s5: e.tensor_tensor(
                p.sb["outall"][hp:hp + 64, hdc, q2 * 512:(q2 + 1) * 512],
                p.sb["del0"][0:64, s5], bank(p, 4, rows=64), ALU.mult),
                wait=[("te", te2)], inc="dve")
            te2 = p.emit("te", lambda e, q2=q2: e.matmul(
                bank(p, 4, rows=64), p.sb["onesr_b"][0:1, 0:64],
                p.sb["dcol"][0:1, 1024 + q2 * 512:1024 + (q2 + 1) * 512], start=True, stop=True),
                wait=[("dve", p.cnt["dve"])], inc="te")
            te3 = p.emit("te", lambda e, q2=q2: e.matmul(
                bank(p, 5, rows=64), p.sb["onesr_b"][0:1, 0:64],
                p.sb["dcol"][0:1, 3072 + q2 * 512:3072 + (q2 + 1) * 512], start=True, stop=True),
                inc="te")
            p.emit("dve", lambda e, s5=s5: e.tensor_tensor(
                p.sb["ot"][0:64, 0:512], p.sb["del0"][0:64, s5], bank(p, 4, rows=64), ALU.mult),
                wait=[("te", te2)], inc="dve")
            p.emit("dve", lambda e, s5=s5: e.tensor_tensor(
                p.sb["q1"][0:64, :], p.sb["del1"][0:64, s5], bank(p, 5, rows=64), ALU.mult),
                wait=[("te", te3)], inc="dve")
            p.emit("dve", lambda e, hp=hp, hdc=hdc, q2=q2: e.tensor_add(
                p.sb["outall"][hp:hp + 64, hdc, 1024 + q2 * 512:1024 + (q2 + 1) * 512],
                p.sb["ot"][0:64, 0:512], p.sb["q1"][0:64, :]), inc="dve")

    if debug_stop == "l3heads":
        return
    p.carve_rewind(mk3, ["qn", "kn", "vv", "bet", "q1", "q1b", "dtmp", "dtmp2", "dtmp3",
                         "dtmp4", "dcol", "del0", "del1", "ot", "rden3"])
    p.carve("xlf3", [128, DC, CH], F32)
    p.carve("wtile", [128, 2, DC, 128], BF16)
    slot_user = {0: p.cnt["te"], 1: p.cnt["te"]}
    dv_tail0 = p.cnt["dve"]
    for tg in range(NRUN):
        sl = slice(tg * 512, tg * 512 + 512)

        def bco(e, sl=sl, tg=tg):
            e.matmul(bank(p, 3), p.sb["onesr_b"][0:1, :], p.sb["stato"][0:1, sl], start=True, stop=True)
            return e.matmul(bank(p, 4), p.sb["onesr_b"][0:1, :],
                            p.sb["stato"][0:1, TLOC + tg * 512:TLOC + tg * 512 + 512],
                            start=True, stop=True)
        te = p.emit("te", bco, wait=[("dve", p.cnt["dve"]), ("act", p.cnt["act"])], inc="te")

        def xl1(e, sl=sl):
            mb = bank(p, 3).unsqueeze(1).broadcast_to([128, DC, 512])
            return e.tensor_tensor(p.sb["xlf3"][:, :, :], xres[:, :, sl], mb, ALU.subtract)
        p.emit("dve", xl1, wait=[("te", te)], inc="dve")

        def xl2(e):
            rb = bank(p, 4).unsqueeze(1).broadcast_to([128, DC, 512])
            return e.tensor_tensor(p.sb["xlf3"][:, :, :], p.sb["xlf3"][:, :, :], rb, ALU.mult)
        dvx = p.emit("dve", xl2, inc="dve")
        if debug_stop == "l3tail1":
            continue
        for oc in range(DC):
            slot = oc % 2
            wl = dma_pool(p, p.sb["wtile"][:, slot],
                          wout_in.rearrange("(dcc pp) o -> pp dcc o", pp=128)[:, :, oc * 128:(oc + 1) * 128],
                          wait=[("te", slot_user[slot]), ("dve", dv_tail0)])
            if debug_stop == "l3tail2":
                slot_user[slot] = p.cnt["te"]
                continue

            def mmo(e, slot=slot, sl=sl):
                for dcc in range(DC):
                    last = e.matmul(bank(p, 2), p.sb["wtile"][:, slot, dcc, :],
                                    p.sb["outall"][:, dcc, sl], start=(dcc == 0), stop=(dcc == DC - 1))
                return last
            te2 = p.emit("te", mmo, wait=[("wld", wl), ("dve", dvx), ("act", p.cnt["act"]),
                                          ("dve", p.cnt["dve"])], inc="te")
            slot_user[slot] = te2
            p.emit("dve", lambda e, oc=oc, sl=sl: e.tensor_add(
                p.sb["vtmp"][:, :], bank(p, 2), p.sb["xlf3"][:, oc, :]),
                wait=[("te", te2)], inc="dve")
            p.emit("dve", lambda e, oc=oc, sl=sl: e.tensor_add(
                xres[:, oc, sl], xres[:, oc, sl], p.sb["vtmp"][:, :]), inc="dve")


CAP1 = 2560
BIGM = 1.0e9


def emit_moe(p, lw, debug_stop=None):
    kind = lw["kind"]
    gwT = p.din(lw["gwT"], [D, 8])
    W1 = p.din(lw["W1"], [8, D, 4096])
    W2 = p.din(lw["W2"], [8, 2048 if kind == "l0f" else 4096, D])
    if kind == "l1f":
        gb = p.din("l1f_gb", [1, 8])
        b1f = p.din("l1f_b1", [8, 128, 32])
        b2f = p.din("l1f_b2", [8, 128, 8])
    xres = p.sb["xres"]
    INTER2 = 4096

    p.carve("gw_sb", [128, DC, 8], BF16)
    p.carve("zeroc", [128, 1], F32)
    p.carve("combT", [8, TLOC], F32)
    p.carve("gb_sb", [1, 8], BF16)
    p.carve("capc", [8, 1], F32)
    p.carve("nbig", [128, 1], F32)
    p.carve("xln", [128, DC, TLOC], BF16)
    mkA = p.carve_mark()
    emit_ln(p, keep_xln=True)
    p.carve("gprob", [128, 16, 8], F32)
    p.carve("w12c", [128, 16, 2], F32)
    p.carve("tkm", [128, 12], F32)
    p.carve("tk2", [128, 12], F32)
    p.carve("tk3", [128, 8], F32)
    p.carve("ohT", [8, 2, TLOC], BF16)
    p.carve("scanT", [8, TLOC], F32)
    p.carve("keepT", [8, TLOC], BF16)
    p.carve("wrow", [1, 2 * TLOC], F32)
    p.carve("thr", [8, 8], F32)
    p.carve("cnt8", [8, 8], F32)
    p.carve("pub", [4, 16], F32)
    p.carve("cagg", [32, 16], F32)
    p.carve("moff", [4, 16], F32)
    p.carve("scan2", [8, 2, TLOC], F32)

    ldg = dma_pool(p, p.sb["gw_sb"][:, :, :], gwT.rearrange("(dcc pp) o -> pp dcc o", pp=128))
    if kind == "l1f":
        ldg = dma_pool(p, p.sb["gb_sb"][:, :], gb[:, :])
    dvz = p.emit("dve", lambda e: e.memset(p.sb["zeroc"][:, :], 0.0), inc="dve")
    p.emit("dve", lambda e: e.memset(p.sb["combT"][:, :], 0.0), inc="dve")
    p.emit("dve", lambda e: e.memset(p.sb["capc"][0:8, 0:1], float(CAP1 + 1)), inc="dve")
    p.emit("dve", lambda e: e.memset(p.sb["nbig"][:, :], -BIGM), inc="dve")

    for tt in range(16):
        ts_ = slice(tt * 128, (tt + 1) * 128)

        if kind == "l1f":
            def gmm(e, ts_=ts_):
                for dcc in range(DC):
                    e.matmul(bank(p, 0, rows=128, width=8), p.sb["xln"][:, dcc, ts_],
                             p.sb["gw_sb"][:, dcc, :], start=(dcc == 0), stop=False)
                return e.matmul(bank(p, 0, rows=128, width=8), p.sb["onesr_b"][0:1, 0:128],
                                p.sb["gb_sb"][0:1, :], start=False, stop=True)
        else:
            def gmm(e, ts_=ts_):
                for dcc in range(DC):
                    last = e.matmul(bank(p, 0, rows=128, width=8), p.sb["xln"][:, dcc, ts_],
                                    p.sb["gw_sb"][:, dcc, :], start=(dcc == 0), stop=(dcc == DC - 1))
                return last
        te = p.emit("te", gmm, wait=[("wld", ldg), ("dve", p.cnt["dve"]),
                                     ("act", p.cnt["act"])], inc="te")
        dv = p.emit("dve", lambda e: e.reduce_max(p.sb["tkm"][:, 0:1], bank(p, 0, rows=128, width=8),
                                                  axis=mybir.AxisListType.X), wait=[("te", te)], inc="dve")
        dv = p.emit("dve", lambda e: e.tensor_sub(p.sb["tk2"][:, 0:1], p.sb["zeroc"][:, 0:1],
                                                  p.sb["tkm"][:, 0:1]), inc="dve")
        ac = p.emit("act", lambda e, tt=tt: e.activation(p.sb["gprob"][:, tt, :],
                                                         bank(p, 0, rows=128, width=8), AF.Exp,
                                                         bias=p.sb["tk2"][:, 0:1]),
                    wait=[("dve", dv)], inc="act")
        dv = p.emit("dve", lambda e, tt=tt: e.reduce_sum(p.sb["tkm"][:, 0:1], p.sb["gprob"][:, tt, :],
                                                         axis=mybir.AxisListType.X),
                    wait=[("act", ac)], inc="dve")
        dv = p.emit("dve", lambda e: e.reciprocal(p.sb["tk2"][:, 0:1], p.sb["tkm"][:, 0:1]), inc="dve")
        dv = p.emit("dve", lambda e, tt=tt: e.tensor_scalar(p.sb["gprob"][:, tt, :],
                                                            p.sb["gprob"][:, tt, :],
                                                            p.sb["tk2"][:, 0:1], None, ALU.mult),
                    inc="dve")
        gp = p.sb["gprob"]
        if kind == "l0f":
            p.emit("dve", lambda e, tt=tt: e.tensor_tensor(
                p.sb["tkm"][:, 0:4], gp[:, tt, 0::2], gp[:, tt, 1::2], ALU.max), inc="dve")
            p.emit("dve", lambda e: e.reduce_max(p.sb["tk2"][:, 0:1], p.sb["tkm"][:, 0:4],
                                                 axis=mybir.AxisListType.X), inc="dve")
            p.emit("dve", lambda e: e.tensor_scalar(p.sb["tk3"][:, 0:4], p.sb["tkm"][:, 0:4],
                                                    p.sb["tk2"][:, 0:1], None, ALU.is_ge), inc="dve")
            p.emit("dve", lambda e: e.scalar_tensor_tensor(p.sb["tk2"][:, 1:5], p.sb["tk3"][:, 0:4],
                                                           p.sb["nbig"][:, 0:1], p.sb["tkm"][:, 0:4],
                                                           ALU.mult, ALU.add), inc="dve")
            p.emit("dve", lambda e: e.reduce_max(p.sb["tk2"][:, 0:1], p.sb["tk2"][:, 1:5],
                                                 axis=mybir.AxisListType.X), inc="dve")
            p.emit("dve", lambda e: e.tensor_scalar(p.sb["tk3"][:, 0:4], p.sb["tkm"][:, 0:4],
                                                    p.sb["tk2"][:, 0:1], None, ALU.is_ge), inc="dve")
            p.emit("dve", lambda e: e.tensor_copy(p.sb["tkm"][:, 0:8][:, 0::2], p.sb["tk3"][:, 0:4]),
                   inc="dve")
            p.emit("dve", lambda e: e.tensor_copy(p.sb["tkm"][:, 0:8][:, 1::2], p.sb["tk3"][:, 0:4]),
                   inc="dve")
            p.emit("dve", lambda e, tt=tt: e.tensor_mul(p.sb["tkm"][:, 0:8], p.sb["tkm"][:, 0:8],
                                                        gp[:, tt, :]), inc="dve")
            masked = lambda: p.sb["tkm"][:, 0:8]
        else:
            p.emit("dve", lambda e, tt=tt: e.tensor_copy(p.sb["tkm"][:, 0:8], gp[:, tt, :]), inc="dve")
            masked = lambda: p.sb["tkm"][:, 0:8]
        p.emit("dve", lambda e: e.reduce_max(p.sb["tk2"][:, 0:1], masked(), axis=mybir.AxisListType.X),
               inc="dve")
        p.emit("dve", lambda e: e.tensor_scalar(p.sb["tk3"][:, 0:8], masked(), p.sb["tk2"][:, 0:1],
                                                None, ALU.is_ge), inc="dve")
        p.emit("dve", lambda e: e.scalar_tensor_tensor(p.sb["tk2"][:, 0:8], p.sb["tk3"][:, 0:8],
                                                       p.sb["nbig"][:, 0:1], masked(),
                                                       ALU.mult, ALU.add), inc="dve")
        p.emit("dve", lambda e: e.reduce_max(p.sb["tk2"][:, 8:9], p.sb["tk2"][:, 0:8],
                                             axis=mybir.AxisListType.X), inc="dve")
        p.emit("dve", lambda e: e.tensor_scalar(p.sb["tk2"][:, 0:8], masked(), p.sb["tk2"][:, 8:9],
                                                None, ALU.is_ge), inc="dve")
        p.emit("dve", lambda e: e.tensor_sub(p.sb["tk2"][:, 0:8], p.sb["tk2"][:, 0:8],
                                             p.sb["tk3"][:, 0:8]), inc="dve")
        p.emit("dve", lambda e: e.reduce_max(p.sb["tkm"][:, 8:9], masked(),
                                             axis=mybir.AxisListType.X), inc="dve")
        if kind == "l0f":
            p.emit("dve", lambda e: e.tensor_add(p.sb["tkm"][:, 9:10], p.sb["tkm"][:, 8:9],
                                                 p.sb["tk2"][:, 8:9]), inc="dve")
            p.emit("dve", lambda e: e.reciprocal(p.sb["tkm"][:, 10:11], p.sb["tkm"][:, 9:10]),
                   inc="dve")
            p.emit("dve", lambda e, tt=tt: e.tensor_tensor(p.sb["w12c"][:, tt, 0:1],
                                                           p.sb["tkm"][:, 8:9],
                                                           p.sb["tkm"][:, 10:11], ALU.mult), inc="dve")
            p.emit("dve", lambda e, tt=tt: e.tensor_tensor(p.sb["w12c"][:, tt, 1:2],
                                                           p.sb["tk2"][:, 8:9],
                                                           p.sb["tkm"][:, 10:11], ALU.mult), inc="dve")
        else:
            p.emit("dve", lambda e, tt=tt: e.tensor_copy(p.sb["w12c"][:, tt, 0:1],
                                                         p.sb["tkm"][:, 8:9]), inc="dve")
            p.emit("dve", lambda e, tt=tt: e.tensor_copy(p.sb["w12c"][:, tt, 1:2],
                                                         p.sb["tk2"][:, 8:9]), inc="dve")
        for ci, eqt in ((0, "tk3"), (1, "tk2")):
            dvq = p.emit("dve", lambda e, eqt=eqt: e.tensor_copy(p.sb["lnsq"][0:128, 0:8],
                                                                 p.sb[eqt][:, 0:8]), inc="dve")
            tt2 = p.emit("te", lambda e: e.transpose(p.pt[0:8, 0:128], p.sb["lnsq"][0:128, 0:8],
                                                     p.sb["idb"][:, :]),
                         wait=[("dve", dvq), ("dve", p.cnt["dve"])], inc="te")
            p.emit("dve", lambda e, ci=ci, tt=tt: e.tensor_copy(
                p.sb["ohT"][0:8, ci, tt * 128:(tt + 1) * 128], p.pt[0:8, 0:128]),
                wait=[("te", tt2)], inc="dve")
        for wi in range(2):
            dvq = p.emit("dve", lambda e, tt=tt, wi=wi: e.tensor_copy(
                p.sb["lnsq"][0:128, 0:1], p.sb["w12c"][:, tt, wi:wi + 1]), inc="dve")
            tt2 = p.emit("te", lambda e: e.transpose(p.pt[0:1, 0:128], p.sb["lnsq"][0:128, 0:1],
                                                     p.sb["idb"][:, :]),
                         wait=[("dve", dvq), ("dve", p.cnt["dve"])], inc="te")
            p.emit("dve", lambda e, tt=tt, wi=wi: e.tensor_copy(
                p.sb["wrow"][0:1, wi * TLOC + tt * 128:wi * TLOC + (tt + 1) * 128],
                p.pt[0:1, 0:128]), wait=[("te", tt2)], inc="dve")
    dv_rout = p.cnt["dve"]

    for ci in range(2):
        if kind == "l1f":
            if ci == 0:
                p.emit("dve", lambda e: e.memset(p.sb["keepT"][0:8, :], 1.0), inc="dve")
            p.emit("dve", lambda e, ci=ci: e.tensor_tensor_scan(
                p.sb["scanT"][0:8, :], p.sb["keepT"][0:8, :], p.sb["ohT"][0:8, ci, :],
                0.0, ALU.mult, ALU.add), wait=[("dve", dv_rout)], inc="dve")
            for r in range(NRUN):
                if r == 0:
                    p.emit("dve", lambda e, ci=ci: e.tensor_copy(
                        p.sb["cnt8"][0:8, ci * 4:ci * 4 + 1], p.sb["scanT"][0:8, 511:512]), inc="dve")
                else:
                    p.emit("dve", lambda e, ci=ci, r=r: e.tensor_sub(
                        p.sb["cnt8"][0:8, ci * 4 + r:ci * 4 + r + 1],
                        p.sb["scanT"][0:8, r * 512 + 511:r * 512 + 512],
                        p.sb["scanT"][0:8, r * 512 - 1:r * 512]), inc="dve")
            p.emit("dve", lambda e, ci=ci: e.memset(p.sb["thr"][0:8, ci * 4:ci * 4 + 1], 0.0),
                   inc="dve")
            for r in range(1, NRUN):
                p.emit("dve", lambda e, ci=ci, r=r: e.tensor_copy(
                    p.sb["thr"][0:8, ci * 4 + r:ci * 4 + r + 1],
                    p.sb["scanT"][0:8, r * 512 - 1:r * 512]), inc="dve")
            p.emit("dve", lambda e, ci=ci: e.tensor_copy(p.sb["scan2"][0:8, ci, :],
                                                         p.sb["scanT"][0:8, :]), inc="dve")
    if kind == "l1f":
        tt2 = p.emit("te", lambda e: e.transpose(bank(p, 3, rows=8, width=8),
                                                 p.sb["cnt8"][0:8, 0:8], p.sb["idf"][0:8, 0:8]),
                     wait=[("dve", p.cnt["dve"]), ("act", p.cnt["act"])], inc="te")
        p.carve("cnt8T", [8, 8], F32)
        dvp = p.emit("dve", lambda e: e.tensor_copy(p.sb["cnt8T"][0:8, 0:8],
                                                    bank(p, 3, rows=8, width=8)),
                     wait=[("te", tt2)], inc="dve")
        cnt_tmp = p.dint("l1f_cnt_tmp", [8, 8])
        st = dma_sp(p, cnt_tmp[:, :], p.sb["cnt8T"][0:8, :], wait=[("dve", dvp)])
        for cc2 in range(2):
            ld = dma_sp(p, p.sb["pub"][0:4, cc2 * 8:(cc2 + 1) * 8],
                        cnt_tmp[cc2 * 4:(cc2 + 1) * 4, :], wait=[("xld", st)])
        mo_in = p.dint("l1f_cnt_in", [4, 16])
        mo_all = p.dint("l1f_cnt_all", [32, 16])
        st = dma_sp(p, mo_in[:, :], p.sb["pub"][0:4, :], wait=[("dve", p.cnt["dve"])])
        cc = p.emit("pool", lambda e: e.collective_compute(
            "AllGather", ALU.bypass, replica_groups=[list(range(NCORE))],
            ins=[mo_in.opt()], outs=[mo_all.opt()]), wait=[("xld", st)], inc="cc")
        ld = dma_sp(p, p.sb["cagg"][0:32, :], mo_all[:, :], wait=[("cc", cc)])
        prec = p.din("c_moeprec", [32, 4])
        p.carve("prec32", [32, 4], F32)
        ld = dma_sp(p, p.sb["prec32"][:, :], prec[:, :])
        te = p.emit("te", lambda e: e.matmul(bank(p, 1, rows=4, width=16), p.sb["prec32"][:, :],
                                             p.sb["cagg"][0:32, :], start=True, stop=True),
                    wait=[("xld", ld)], inc="te")
        dv = p.emit("dve", lambda e: e.tensor_copy(p.sb["moff"][0:4, :], bank(p, 1, rows=4, width=16)),
                    wait=[("te", te)], inc="dve")
        tt2 = p.emit("te", lambda e: e.transpose(bank(p, 3, rows=16, width=4),
                                                 p.sb["moff"][0:4, :], p.sb["idf"][0:4, 0:4]),
                     wait=[("dve", p.cnt["dve"]), ("act", p.cnt["act"])], inc="te")
        p.carve("mof16", [16, 4], F32)
        dv = p.emit("dve", lambda e: e.tensor_copy(p.sb["mof16"][0:16, 0:4],
                                                   bank(p, 3, rows=16, width=4)),
                    wait=[("te", tt2)], inc="dve")
        mo_tmp = p.dint("l1f_mof_tmp", [16, 4])
        p.carve("mof8", [8, 8], F32)
        st = dma_sp(p, mo_tmp[:, :], p.sb["mof16"][0:16, :], wait=[("dve", dv)])
        for cc2 in range(2):
            ld = dma_sp(p, p.sb["mof8"][0:8, cc2 * 4:(cc2 + 1) * 4],
                        mo_tmp[cc2 * 8:(cc2 + 1) * 8, :], wait=[("xld", st)])
        dv = p.emit("dve", lambda e: e.tensor_sub(p.sb["thr"][0:8, :], p.sb["thr"][0:8, :],
                                                  p.sb["mof8"][0:8, :]),
                    wait=[("xld", ld)], inc="dve")
        dv = p.emit("dve", lambda e: e.tensor_scalar(p.sb["thr"][0:8, :], p.sb["thr"][0:8, :],
                                                     p.sb["capc"][0:8, 0:1], None, ALU.add),
                    inc="dve")
    dv_cap = p.cnt["dve"]

    for ci in range(2):
        if kind == "l1f":
            for r in range(NRUN):
                p.emit("dve", lambda e, ci=ci, r=r: e.tensor_scalar(
                    p.sb["keepT"][0:8, r * 512:(r + 1) * 512],
                    p.sb["scan2"][0:8, ci, r * 512:(r + 1) * 512],
                    p.sb["thr"][0:8, ci * 4 + r:ci * 4 + r + 1], None, ALU.is_lt),
                    wait=[("dve", dv_cap)], inc="dve")
        else:
            p.emit("dve", lambda e: e.memset(p.sb["keepT"][0:8, :], 1.0), inc="dve")
        for q4 in range(NRUN):
            s5 = slice(q4 * 512, (q4 + 1) * 512)
            te = p.emit("te", lambda e, ci=ci, q4=q4: e.matmul(
                bank(p, 2, rows=8), p.sb["onesr_f"][0:1, 0:8],
                p.sb["wrow"][0:1, ci * TLOC + q4 * 512:ci * TLOC + (q4 + 1) * 512],
                start=True, stop=True), wait=[("dve", p.cnt["dve"]), ("dve", dv_rout)], inc="te")
            p.emit("dve", lambda e, ci=ci, s5=s5: e.tensor_tensor(
                p.sb["scanT"][0:8, s5], p.sb["ohT"][0:8, ci, s5], p.sb["keepT"][0:8, s5],
                ALU.mult), inc="dve")
            p.emit("dve", lambda e, s5=s5: e.tensor_tensor(
                p.sb["scanT"][0:8, s5], p.sb["scanT"][0:8, s5], bank(p, 2, rows=8), ALU.mult),
                wait=[("te", te)], inc="dve")
            p.emit("dve", lambda e, s5=s5: e.tensor_add(
                p.sb["combT"][0:8, s5], p.sb["combT"][0:8, s5], p.sb["scanT"][0:8, s5]), inc="dve")
    dv_comb = p.cnt["dve"]

    if debug_stop == "route":
        return
    p.carve_rewind(mkA, ["stat_m", "stat_r", "lnst", "lnsq", "lnsq2", "gprob", "w12c",
                         "tkm", "tk2", "tk3", "ohT", "scanT", "keepT", "wrow", "thr", "cnt8",
                         "pub", "cagg", "moff", "scan2", "cnt8T", "prec32", "mof16", "mof8"])
    p.carve("mid", [128, 32, 512], BF16)
    p.carve("acts", [128, 16, 512], BF16)
    p.carve("wtile", [128, 2, DC, 128], BF16)
    p.carve("w2tile", [128, 2, 32, 128], BF16)
    p.carve("cb", [128, 512], F32)
    p.carve("crow", [1, TLOC], F32)
    p.carve("b1_sb", [128, 32], F32)
    p.carve("b2_sb", [128, 8], F32)
    p.carve("vt", [128, 512], F32)

    slot_user = {0: p.cnt["te"], 1: p.cnt["te"]}
    nfi = 32
    W2rows = 2048 if kind == "l0f" else 4096
    W1b = p.dint(f"{lw['tag']}_W1b", [8, D, 4096], BF16)
    W2b = p.dint(f"{lw['tag']}_W2b", [8, W2rows, D], BF16)
    cast_done = []
    for ei in range(8):
        dma_pool(p, W1b[ei], W1[ei])
        cast_done.append(dma_pool(p, W2b[ei], W2[ei]))
    cr_d = p.dint(f"{lw['tag']}_crow", [8, TLOC])
    for ei in range(8):
        if ei == 0:
            dma_sp(p, cr_d[:, :], p.sb["combT"][0:8, :], wait=[("dve", dv_comb)])
        crl = dma_sp(p, p.sb["crow"][0:1, :], cr_d[ei:ei + 1, :],
                     wait=[("te", p.cnt["te"])])
        if kind == "l1f":
            dma_sp(p, p.sb["b1_sb"][:, :], b1f[ei], wait=[("dve", p.cnt["dve"])])
            ldb = dma_sp(p, p.sb["b2_sb"][:, :], b2f[ei], wait=[("dve", p.cnt["dve"])])
        for tg in range(NRUN):
            sl = slice(tg * 512, tg * 512 + 512)
            for fi in range(nfi):
                slot = fi % 2
                wl = dma_sp(p, p.sb["wtile"][:, slot],
                            W1b[ei].rearrange("(dcc pp) o -> pp dcc o", pp=128)[:, :, fi * 128:(fi + 1) * 128],
                            wait=[("te", slot_user[slot]), ("dve", dv_comb),
                                  ("wld", cast_done[ei])])

                def mm1(e, slot=slot, sl=sl):
                    for dcc in range(DC):
                        last = e.matmul(bank(p, 0), p.sb["wtile"][:, slot, dcc, :],
                                        p.sb["xln"][:, dcc, sl], start=(dcc == 0), stop=(dcc == DC - 1))
                    return last
                te = p.emit("te", mm1, wait=[("xld", wl),
                                             ("dve", p.cnt["dve"]), ("act", p.cnt["act"])], inc="te")
                slot_user[slot] = te
                if kind == "l1f":
                    p.emit("dve", lambda e, fi=fi: e.tensor_scalar(
                        p.sb["mid"][:, fi, :], bank(p, 0), p.sb["b1_sb"][:, fi:fi + 1],
                        p.sb["zeroc"][:, 0:1], ALU.add, ALU.max), wait=[("te", te)], inc="dve")
                else:
                    if fi < 16:
                        p.emit("act", lambda e, fi=fi: e.activation(p.sb["mid"][:, fi, :],
                                                                    bank(p, 0), AF.Silu),
                               wait=[("te", te)], inc="act")
                    else:
                        p.emit("dve", lambda e, fi=fi: e.tensor_copy(p.sb["mid"][:, fi, :],
                                                                     bank(p, 0)),
                               wait=[("te", te)], inc="dve")
            if kind == "l0f":
                for fi in range(16):
                    p.emit("dve", lambda e, fi=fi: e.tensor_mul(
                        p.sb["acts"][:, fi, :], p.sb["mid"][:, fi, :], p.sb["mid"][:, 16 + fi, :]),
                        wait=[("act", p.cnt["act"])], inc="dve")
                msrc, nf2 = "acts", 16
            else:
                msrc, nf2 = "mid", 32
            te = p.emit("te", lambda e, sl=sl: e.matmul(bank(p, 6), p.sb["onesr_f"][0:1, :],
                                                        p.sb["crow"][0:1, sl], start=True, stop=True),
                        wait=[("xld", crl), ("dve", p.cnt["dve"]), ("act", p.cnt["act"])], inc="te")
            dvc = p.emit("dve", lambda e: e.tensor_copy(p.sb["cb"][:, :], bank(p, 6)),
                         wait=[("te", te)], inc="dve")
            for oc in range(DC):
                slot = oc % 2
                wl = dma_sp(p, p.sb["w2tile"][:, slot, 0:nf2, :],
                            W2b[ei].rearrange("(ii pp) o -> pp ii o", pp=128)[:, 0:nf2, oc * 128:(oc + 1) * 128],
                            wait=[("te", slot_user[slot]), ("dve", dv_comb),
                                  ("wld", cast_done[ei])])

                def mm2(e, slot=slot, nf2=nf2, msrc=msrc):
                    for ii in range(nf2):
                        last = e.matmul(bank(p, 1), p.sb["w2tile"][:, slot, ii, :],
                                        p.sb[msrc][:, ii, :], start=(ii == 0), stop=(ii == nf2 - 1))
                    return last
                te = p.emit("te", mm2, wait=[("xld", wl), ("dve", p.cnt["dve"]),
                                             ("act", p.cnt["act"])], inc="te")
                slot_user[slot] = te
                if kind == "l1f":
                    p.emit("dve", lambda e, oc=oc: e.tensor_scalar(
                        p.sb["vt"][:, :], bank(p, 1), p.sb["b2_sb"][:, oc:oc + 1], None, ALU.add),
                        wait=[("te", te)], inc="dve")
                else:
                    p.emit("dve", lambda e: e.tensor_copy(p.sb["vt"][:, :], bank(p, 1)),
                           wait=[("te", te)], inc="dve")
                p.emit("dve", lambda e: e.tensor_mul(p.sb["vt"][:, :], p.sb["vt"][:, :],
                                                     p.sb["cb"][:, :]), inc="dve")
                p.emit("dve", lambda e, oc=oc, sl=sl: e.tensor_add(
                    xres[:, oc, sl], xres[:, oc, sl], p.sb["vt"][:, :]), inc="dve")


def fm_vec(v):
    return np.ascontiguousarray(np.asarray(v).reshape(DC, 128).T.astype(np.float32))


def prep_host(inputs):
    g = {}
    pc = [dict() for _ in range(NCORE)]

    g["c_ident"] = np.eye(128, dtype=np.float32)
    hc = np.zeros((1, 16), dtype=np.float32)
    hc[0, HC_INV_D] = 1.0 / D
    hc[0, HC_EPS] = LN_EPS
    hc[0, HC_HALF] = 0.5
    hc[0, HC_Q2] = 0.125 * 0.125
    hc[0, HC_Q2L0] = (1.0 / 32) ** 2
    hc[0, HC_ZERO] = 0.0
    hc[0, HC_NEPS] = -LN_EPS
    hc[0, HC_ONE] = 1.0
    g["c_hc"] = hc

    x = np.asarray(inputs["L0_attn_x"])
    for c in range(NCORE):
        xr = np.empty((128, DC, TLOC), dtype=np.float32)
        for r, (b, j) in enumerate(runs_of_core(c)):
            seg = x[b, j * CH:(j + 1) * CH, :]
            xr[:, :, r * CH:(r + 1) * CH] = seg.T.reshape(DC, 128, CH).transpose(1, 0, 2)
        pc[c]["xres_in"] = np.ascontiguousarray(xr)

    for tag, pre in (("l2f", "L2_ffn"), ("l3f", "L3_ffn")):
        g[f"{tag}_WrT"] = np.ascontiguousarray(np.asarray(inputs[f"{pre}_W_r"]).T)
        g[f"{tag}_WkT"] = np.ascontiguousarray(np.asarray(inputs[f"{pre}_W_k"]).T)
        g[f"{tag}_WvT"] = np.ascontiguousarray(np.asarray(inputs[f"{pre}_W_v"]).T)
        g[f"{tag}_mur"] = fm_vec(inputs[f"{pre}_mu_r"])
        g[f"{tag}_muk"] = fm_vec(inputs[f"{pre}_mu_k"])

    WqT = np.asarray(inputs["L2_attn_W_q"]).T
    WkT = np.asarray(inputs["L2_attn_W_k"]).T
    WvT = np.asarray(inputs["L2_attn_W_v"]).T
    g["l2a_WoT"] = np.ascontiguousarray(np.asarray(inputs["L2_attn_W_o"]).T)
    g["l2a_bo"] = fm_vec(inputs["L2_attn_b_o"])
    for c in range(NCORE):
        hsl = slice(2 * c * 64, (2 * c + 2) * 64)
        pc[c]["l2a_WqT"] = np.ascontiguousarray(WqT[:, hsl])
        pc[c]["l2a_WkT"] = np.ascontiguousarray(WkT[:, hsl])
        pc[c]["l2a_WvT"] = np.ascontiguousarray(WvT[:, hsl])

    wkqv = np.asarray(inputs["L0_attn_w_kqv"])
    for c in range(NCORE):
        hsl = slice(2 * c * 64, (2 * c + 2) * 64)
        pc[c]["l0a_WkT"] = np.ascontiguousarray(wkqv[:, 0:1024][:, hsl])
        pc[c]["l0a_WqT"] = np.ascontiguousarray(wkqv[:, 1024:2048][:, hsl])
        pc[c]["l0a_WvT"] = np.ascontiguousarray(wkqv[:, 2048:3072][:, hsl])
        xx = (2.0 ** 8) ** (1.0 / 16)
        slopes = np.asarray([1.0 / xx ** (i + 1) for i in range(16)], dtype=np.float32)
        srow = np.zeros((128, 16), dtype=np.float32)
        srow[:, 0] = slopes[2 * c]
        srow[:, 1] = slopes[2 * c + 1]
        pc[c]["c_slopes"] = srow
    relh = np.zeros((128, 8, 512), dtype=np.float32)
    rell = np.zeros((128, 8, 512), dtype=np.float32)
    jq = np.arange(512)[None, :]
    for off in range(2):
        for sub in range(4):
            i = np.arange(128)[:, None]
            rel = (off - 1) * 512 + sub * 128 + i - jq
            lo = np.mod(rel, 128)
            hi = rel - lo
            relh[:, off * 4 + sub, :] = hi
            rell[:, off * 4 + sub, :] = lo
    g["c_relh"] = relh
    g["c_rell"] = rell
    wm = np.zeros((4, 128, 512), dtype=np.float32)
    for sub in range(4):
        for i in range(128):
            wm[sub, i, sub * 128 + i + 1:] = -30000.0
    g["c_winmask"] = wm

    Wq1 = np.asarray(inputs["L1_attn_Wq"]).reshape(D, 16 * 64)
    Wku = np.asarray(inputs["L1_attn_Wk_up"]).reshape(256, 16 * 64)
    Wvu = np.asarray(inputs["L1_attn_Wv_up"]).reshape(256, 16 * 64)
    g["l1a_Wkvd"] = np.ascontiguousarray(np.asarray(inputs["L1_attn_Wkv_down"]))
    g["l1a_WoT"] = np.ascontiguousarray(np.asarray(inputs["L1_attn_Wout"]).reshape(16 * 64, D))
    for c in range(NCORE):
        hsl = slice(2 * c * 64, (2 * c + 2) * 64)
        pc[c]["l1a_WqT"] = np.ascontiguousarray(Wq1[:, hsl])
        pc[c]["l1a_WkuT"] = np.ascontiguousarray(Wku[:, hsl])
        pc[c]["l1a_WvuT"] = np.ascontiguousarray(Wvu[:, hsl])

    dm = np.zeros((4, 128, 512), dtype=np.float32)
    for sub in range(4):
        for i in range(128):
            dm[sub, i, : min(512, sub * 128 + i)] = -30000.0
    g["c_diagmask"] = dm

    g["l0f_gwT"] = np.ascontiguousarray(np.asarray(inputs["L0_ffn_gate_w"]).T)
    g["l0f_W1"] = np.ascontiguousarray(np.asarray(inputs["L0_ffn_w13_weight"]).transpose(0, 2, 1))
    g["l0f_W2"] = np.ascontiguousarray(np.asarray(inputs["L0_ffn_w2_weight"]).transpose(0, 2, 1))
    g["l1f_gwT"] = np.ascontiguousarray(np.asarray(inputs["L1_ffn_gate_w"]).T)
    g["l1f_gb"] = np.ascontiguousarray(np.asarray(inputs["L1_ffn_gate_b"]).reshape(1, 8))
    g["l1f_W1"] = np.ascontiguousarray(np.asarray(inputs["L1_ffn_W1"]))
    g["l1f_W2"] = np.ascontiguousarray(np.asarray(inputs["L1_ffn_W2"]))
    g["l1f_b1"] = np.ascontiguousarray(np.asarray(inputs["L1_ffn_b1"]).reshape(8, 32, 128).transpose(0, 2, 1))
    g["l1f_b2"] = np.ascontiguousarray(np.asarray(inputs["L1_ffn_b2"]).reshape(8, 8, 128).transpose(0, 2, 1))
    for c in range(NCORE):
        own = runs_of_core(c)
        prec = np.zeros((32, 4), dtype=np.float32)
        for g_o in range(NCORE):
            for g_r, (gb_, gj) in enumerate(runs_of_core(g_o)):
                for r, (b_, j_) in enumerate(own):
                    if gb_ * 16 + gj < b_ * 16 + j_:
                        prec[g_o * 4 + g_r, r] = 1.0
        pc[c]["c_moeprec"] = prec

    g["l3a_wslow"] = np.ascontiguousarray(np.asarray(inputs["L3_attn_w_slow"]))
    g["l3a_wout"] = np.ascontiguousarray(np.asarray(inputs["L3_attn_w_out"]).reshape(D, D))
    g["l3a_lnw"] = fm_vec(inputs["L3_attn_ln_w"])
    g["l3a_lnb"] = fm_vec(inputs["L3_attn_ln_b"])

    for c in range(NCORE):
        sel = np.zeros((33, 4), dtype=np.float32)
        for r, (b, j) in enumerate(runs_of_core(c)):
            if j == 0:
                sel[32, r] = 1.0
            else:
                o = owner_of_chunk(j - 1)
                sel[o * 4 + run_index(o, b, j - 1), r] = 1.0
        pc[c]["c_bndsel"] = sel

    return g, pc


LAYER_RWKV = {
    "l2f": dict(tag="l2f", WrT="l2f_WrT", WkT="l2f_WkT", WvT="l2f_WvT", mur="l2f_mur", muk="l2f_muk"),
    "l3f": dict(tag="l3f", WrT="l3f_WrT", WkT="l3f_WkT", WvT="l3f_WvT", mur="l3f_mur", muk="l3f_muk"),
}


def build_program(block_ids, debug_stop=None):
    stack = ExitStack()
    nc = bass.Bass("TRN2", target_bir_lowering=False, debug=False)
    p = Prog(nc, stack)
    with stack:
        for s in ("wld", "xld", "te", "act", "dve", "cc"):
            p.sem(s)
        xin = p.din("xres_in", [128, DC, TLOC])
        yout = p.dout("xres_out", [128, DC, TLOC])
        alloc_common(p)
        ld = dma_sp(p, p.sb["xres"][:, :, :], xin[:, :, :])
        for e in ("te", "act", "dve", "pool"):
            p.waitpt(e, "xld", "wld", "dve")

        for bid in block_ids:
            p.block_fence()
            p.carve_reset()
            if bid in LAYER_RWKV:
                emit_rwkv(p, LAYER_RWKV[bid])
            elif bid == "l0a":
                emit_attn(p, dict(tag="l0a", kind="l0", WqT="l0a_WqT", WkT="l0a_WkT",
                                  WvT="l0a_WvT"), debug_stop=debug_stop)
            elif bid == "l2a":
                emit_attn(p, dict(tag="l2a", kind="l2", WqT="l2a_WqT", WkT="l2a_WkT",
                                  WvT="l2a_WvT", WoT="l2a_WoT", bo="l2a_bo"), debug_stop=debug_stop)
            elif bid == "l1a":
                emit_attn(p, dict(tag="l1a", kind="l1", WqT="l1a_WqT", Wkvd="l1a_Wkvd",
                                  WkuT="l1a_WkuT", WvuT="l1a_WvuT", WoT="l1a_WoT"),
                          debug_stop=debug_stop)
            elif bid == "l3a":
                emit_l3a(p, {}, debug_stop=debug_stop)
            elif bid == "l0f":
                emit_moe(p, dict(tag="l0f", kind="l0f", gwT="l0f_gwT", W1="l0f_W1", W2="l0f_W2"),
                         debug_stop=debug_stop)
            elif bid == "l1f":
                emit_moe(p, dict(tag="l1f", kind="l1f", gwT="l1f_gwT", W1="l1f_W1", W2="l1f_W2"),
                         debug_stop=debug_stop)
            else:
                raise NotImplementedError(bid)

        dma_sp(p, yout[:, :, :], p.sb["xres"][:, :, :],
               wait=[("dve", p.cnt["dve"]), ("te", p.cnt["te"]), ("act", p.cnt["act"])])
        p.waitpt("sp", "xld")
        p.build()
    return p


_GROUPS = [("l0a", "l0f", "l1a", "l1f", "l2a", "l2f", "l3a", "l3f")]


def kernel(**inputs):
    glob, pc = prep_host(inputs)
    xcur = [pc[c]["xres_in"] for c in range(NCORE)]
    for grp in _GROUPS:
        p = build_program(grp)
        in_maps = []
        for c in range(NCORE):
            m = {}
            for name in p.inputs:
                if name == "xres_in":
                    m[name] = xcur[c]
                elif name in pc[c]:
                    m[name] = pc[c][name]
                else:
                    m[name] = glob[name]
            in_maps.append(m)
        res = run_bass_kernel_spmd(p.nc, in_maps, core_ids=list(range(NCORE)))
        xcur = [res.results[c]["xres_out"] for c in range(NCORE)]
    out = np.empty((B, S, D), dtype=np.float32)
    for c in range(NCORE):
        for r, (b, j) in enumerate(runs_of_core(c)):
            seg = xcur[c][:, :, r * CH:(r + 1) * CH]
            out[b, j * CH:(j + 1) * CH, :] = seg.transpose(1, 0, 2).reshape(D, CH).T
    return out
